# version 08
# baseline (speedup 1.0000x reference)
import numpy as np
from contextlib import ExitStack

import concourse.bass as bass
import concourse.mybir as mybir
from concourse.alu_op_type import AluOpType
from concourse.bass_utils import run_bass_kernel_spmd

F32 = mybir.dt.float32
BF16 = mybir.dt.bfloat16
I16 = mybir.dt.int16
U16 = mybir.dt.uint16
AF = mybir.ActivationFunctionType
AX = mybir.AxisListType.X
OP = AluOpType

B, LSEQ, DM = 4, 4096, 1024
TOK = 2048
NT = TOK // 128
NPT = 17
TPAD = NPT * 128
D_INNER = 2048
KD, VD, HH = 768, 1536, 4
DK, DV = 192, 384
CH, NCH = 128, 16
FI = 3072
QMAX = 32767.0
EPS_LN = 1e-5


class Sem:
    __slots__ = ("h", "v")

    def __init__(self, ctx, nc, name):
        self.h = ctx.enter_context(nc.semaphore(name))
        self.v = 0

    def inc(self, n=1):
        self.v += n
        return self.h


def _levelmask_neg(C, s):
    E = np.zeros((C, C), np.float32)
    for i in range(C):
        lo = (i // (2 * s)) * (2 * s)
        if (i // s) % 2 == 1:
            E[i, lo:lo + s] = -1.0
    return E



class _ChainStub:
    __slots__ = ("chain",)

    def __init__(self):
        self.chain = []

    def then_inc(self, sem, n):
        self.chain.append((sem, n))
        return self


class _RecEngine:
    __slots__ = ("q",)

    def __init__(self, q):
        self.q = q

    def __getattr__(self, m):
        def call(*a, **kw):
            stub = _ChainStub()
            self.q.append((m, a, kw, stub))
            return stub
        return call


class _RecBlock:

    def __init__(self):
        self.qs = {e: [] for e in ("sync", "tensor", "vector", "scalar", "gpsimd")}
        self.realblock = None
        self.direct = set()

    def set_direct(self, *engs):
        self.materialize(self.realblock)
        self.direct.update(engs)

    def set_rec(self, *engs):
        self.direct.difference_update(engs)

    def _mk(self, eng):
        if eng in self.direct:
            return getattr(self.realblock, eng)
        def deco(f):
            f(_RecEngine(self.qs[eng]))
            return f
        return deco

    @property
    def sync(self):
        return self._mk("sync")

    @property
    def tensor(self):
        return self._mk("tensor")

    @property
    def vector(self):
        return self._mk("vector")

    @property
    def scalar(self):
        return self._mk("scalar")

    @property
    def gpsimd(self):
        return self._mk("gpsimd")

    def materialize(self, block):
        for eng in ("sync", "tensor", "vector", "scalar", "gpsimd"):
            q = self.qs[eng]
            if not q:
                continue

            def body(e, q=list(q)):
                for m, a, kw, stub in q:
                    inst = getattr(e, m)(*a, **kw)
                    for sem, n in stub.chain:
                        inst = inst.then_inc(sem, n)
            getattr(block, eng)(body)
            self.qs[eng] = []


class K:
    pass


def build_program(upto="all", dbg=(), mock_ag=False):
    nc = bass.Bass("TRN2", target_bir_lowering=False)
    ctx = ExitStack()
    k = K()
    k.nc = nc
    k.ctx = ctx
    k.dbg = dbg
    k.mock_ag = mock_ag
    k.dbg_out = {}

    d = k.d = {}
    d["x_pad"] = nc.dram_tensor("x_pad", [TPAD, DM], F32, kind="ExternalInput").ap()
    d["W_in"] = nc.dram_tensor("W_in", [DM, D_INNER], F32, kind="ExternalInput").ap()
    d["convw"] = nc.dram_tensor("convw", [D_INNER, 4], F32, kind="ExternalInput").ap()
    d["convb"] = nc.dram_tensor("convb", [D_INNER, 1], F32, kind="ExternalInput").ap()
    d["W_q"] = nc.dram_tensor("W_q", [DM, KD], F32, kind="ExternalInput").ap()
    d["W_k"] = nc.dram_tensor("W_k", [DM, KD], F32, kind="ExternalInput").ap()
    d["W_v"] = nc.dram_tensor("W_v", [DM, VD], F32, kind="ExternalInput").ap()
    d["W_g"] = nc.dram_tensor("W_g", [DM, VD], F32, kind="ExternalInput").ap()
    d["W_gkb"] = nc.dram_tensor("W_gkb", [DM, 8], F32, kind="ExternalInput").ap()
    d["gkb_bias"] = nc.dram_tensor("gkb_bias", [1, 8], F32, kind="ExternalInput").ap()
    d["A_log"] = nc.dram_tensor("A_log", [1, HH], F32, kind="ExternalInput").ap()
    d["g_norm_w"] = nc.dram_tensor("g_norm_w", [1, DV], F32, kind="ExternalInput").ap()
    d["W_so"] = nc.dram_tensor("W_so", [VD, DM], F32, kind="ExternalInput").ap()
    d["W_out"] = nc.dram_tensor("W_out", [D_INNER, DM], F32, kind="ExternalInput").ap()
    d["w_gate"] = nc.dram_tensor("w_gate", [FI, DM], F32, kind="ExternalInput").ap()
    d["w_up"] = nc.dram_tensor("w_up", [FI, DM], F32, kind="ExternalInput").ap()
    d["w_down"] = nc.dram_tensor("w_down", [DM, FI], F32, kind="ExternalInput").ap()
    d["smask"] = nc.dram_tensor("smask", [1, 8], F32, kind="ExternalInput").ap()
    d["wg_shard"] = nc.dram_tensor("wg_shard", [384, DM], F32, kind="ExternalInput").ap()
    d["wu_shard"] = nc.dram_tensor("wu_shard", [384, DM], F32, kind="ExternalInput").ap()
    d["wd_shard"] = nc.dram_tensor("wd_shard", [128, FI], F32, kind="ExternalInput").ap()
    d["out"] = nc.dram_tensor("out", [TOK, DM], F32, kind="ExternalOutput").ap()

    dr = k.dr = {}
    def dram_scratch(name, shape, dt=BF16):
        dr[name] = nc.dram_tensor(name, shape, dt).ap()
        return dr[name]
    dram_scratch("sp_uv", [NCH * HH * 128, DV])
    dram_scratch("sp_wfm", [NCH * HH * 128, 256])
    dram_scratch("sp_aqt", [NCH * HH * 128, 128])
    dram_scratch("sp_ktil", [NCH * HH * 128, 192])
    dram_scratch("sp_qfm", [NCH * HH * 128, 256])
    dram_scratch("sp_o", [TOK, VD])
    dram_scratch("sp_og", [TOK, VD])
    dram_scratch("sp_x1", [TOK, DM], F32)
    dram_scratch("sp_xq", [TOK, DM])
    dram_scratch("sp_up", [TOK, FI])
    dram_scratch("sp_gate", [TOK, FI])
    dram_scratch("sp_oq", [TOK, FI])
    dr["cc_state_in"] = nc.dram_tensor("cc_state_in", [192, HH * DV], F32).ap()
    dr["cc_state_out"] = nc.dram_tensor("cc_state_out", [8 * 192, HH * DV], F32).ap()
    dr["cc_wq_in"] = nc.dram_tensor("cc_wq_in", [1152, DM], BF16).ap()
    dr["cc_wq_out"] = nc.dram_tensor("cc_wq_out", [8 * 1152, DM], BF16).ap()
    for i in range(4):
        dr[f"cc_s{i}_in"] = nc.dram_tensor(f"cc_s{i}_in", [1, 2], F32).ap()
        dr[f"cc_s{i}_out"] = nc.dram_tensor(f"cc_s{i}_out", [8, 2], F32).ap()

    def dbg_tensor(name, shape):
        if name in dbg:
            ap = nc.dram_tensor("dbg_" + name, shape, F32, kind="ExternalOutput").ap()
            k.dbg_out[name] = ap
            return ap
        return None
    k.dbg_tensor = dbg_tensor

    c = k.c = {}
    c["idf"] = nc.inline_tensor(np.eye(128, dtype=np.float32), name="c_idf").ap()
    c["ones_row"] = nc.inline_tensor(np.ones((1, 128), np.float32), name="c_ones").ap()
    c["ut"] = nc.inline_tensor(np.triu(np.ones((128, 128), np.float32), 0), name="c_ut").ap()
    c["lts"] = nc.inline_tensor(np.tril(np.ones((128, 128), np.float32), -1), name="c_lts").ap()
    c["mlow"] = nc.inline_tensor(np.tril(np.ones((128, 128), np.float32), -1), name="c_mlow").ap()
    c["muic"] = nc.inline_tensor(np.triu(np.ones((128, 128), np.float32), 0), name="c_muic").ap()
    c["id128"] = nc.inline_tensor(np.eye(128, dtype=np.float32), name="c_id128").ap()
    for i, s in enumerate((1, 2, 4, 8, 16, 32, 64)):
        c[f"negE{i}"] = nc.inline_tensor(_levelmask_neg(128, s), name=f"c_negE{i}").ap()

    sb = lambda *a, **kw: ctx.enter_context(nc.sbuf_tensor(*a, **kw))
    k.sb = sb
    k.consts = sb("consts", [128, 512], F32)
    k.constsb = sb("constsb", [128, 1280], BF16)
    k.gnwb = sb("gnwb", [128, DV], F32)
    k.alogv = sb("alogv", [128, 16], F32)
    k.gkb = sb("gkb", [128, NT * 8], F32)
    k.stat = sb("stat", [128, 192], F32)
    k.sval = sb("sval", [128, 48], F32)
    k.fconst = sb("fconst", [128, 4], F32)

    k.pb = [ctx.enter_context(nc.psum_tensor(f"pb{i}", [128, 512], F32)) for i in range(8)]

    sem = lambda name: Sem(ctx, nc, name)
    k.sem = sem
    k.s_lc = sem("s_lc")
    k.s_lcg = sem("s_lcg")
    k.s_lw = sem("s_lw")
    k.s_lwg = sem("s_lwg")
    k.s_lx = [sem("s_lx0"), sem("s_lx1")]
    k.s_mm = sem("s_mm")
    k.s_v = sem("s_v")
    k.s_sc = sem("s_sc")
    k.s_sp = sem("s_sp")
    k.s_g = sem("s_g")
    k.s_st = sem("s_st")
    k.s_st2 = [sem("s_st2a"), sem("s_st2b")]

    k.realblock = ctx.enter_context(nc.Block())
    k.block = _RecBlock()
    k.block.realblock = k.realblock

    emit_prelude(k)
    emit_passA(k)
    if upto == "A":
        emit_finish_dbg(k)
        k.block.materialize(k.realblock)
        ctx.close()
        return nc, k.dbg_out
    k.block.materialize(k.realblock)
    emit_gkb(k)
    k.block.materialize(k.realblock)
    emit_delta(k, chain=1)
    k.block.materialize(k.realblock)
    emit_state_ag(k)
    k.block.materialize(k.realblock)
    emit_delta(k, chain=2)
    if upto == "delta":
        emit_finish_dbg(k)
        k.block.materialize(k.realblock)
        ctx.close()
        return nc, k.dbg_out
    k.block.materialize(k.realblock)
    emit_epilogue(k)
    if upto == "attn":
        emit_finish_dbg(k)
        k.block.materialize(k.realblock)
        ctx.close()
        return nc, k.dbg_out
    k.block.materialize(k.realblock)
    emit_ffn(k)
    k.block.materialize(k.realblock)
    emit_dbg_ffn(k)
    emit_finish(k)
    k.block.materialize(k.realblock)
    ctx.close()
    return nc, k.dbg_out


def emit_prelude(k):
    nc, block, c = k.nc, k.block, k.c
    s_lc, s_mm, s_v, s_sc = k.s_lc, k.s_mm, k.s_v, k.s_sc

    @block.sync
    def _(sync):
        sync.dma_start(out=k.consts[:, 0:128], in_=c["idf"]).then_inc(s_lc.inc(16), 16)
        sync.dma_start(out=k.consts[0:1, 128:256], in_=c["ones_row"]).then_inc(s_lc.inc(16), 16)
        sync.dma_start(out=k.consts[:, 256:384], in_=c["ut"]).then_inc(s_lc.inc(16), 16)
        sync.dma_start(out=k.consts[:, 384:512], in_=c["lts"]).then_inc(s_lc.inc(16), 16)
        sync.dma_start(out=k.gnwb[0:1, :], in_=k.d["g_norm_w"]).then_inc(s_lc.inc(16), 16)
        sync.dma_start(out=k.alogv[0:1, 0:4], in_=k.d["A_log"]).then_inc(s_lc.inc(16), 16)
        sync.dma_start(out=k.alogv[0:1, 4:12], in_=k.d["gkb_bias"]).then_inc(s_lc.inc(16), 16)

    @block.gpsimd
    def _(g):
        g.dma_start(out=k.constsb[:, 0:128], in_=c["mlow"]).then_inc(k.s_lcg.inc(16), 16)
        g.dma_start(out=k.constsb[:, 128:256], in_=c["muic"]).then_inc(k.s_lcg.inc(16), 16)
        g.dma_start(out=k.constsb[:, 256:384], in_=c["id128"]).then_inc(k.s_lcg.inc(16), 16)
        for i in range(7):
            g.dma_start(out=k.constsb[:, 384 + 128 * i:512 + 128 * i],
                        in_=c[f"negE{i}"]).then_inc(k.s_lcg.inc(16), 16)

    @block.tensor
    def _(t):
        t.wait_ge(s_lc.h, s_lc.v)
        t.wait_ge(k.s_lcg.h, k.s_lcg.v)
        t.matmul(k.pb[0][:, 0:384], k.consts[0:1, 128:256], k.gnwb[0:1, 0:384],
                 start=True, stop=True)
        t.matmul(k.pb[1][:, 0:12], k.consts[0:1, 128:256], k.alogv[0:1, 0:12],
                 start=True, stop=True).then_inc(s_mm.inc(), 1)

    @block.vector
    def _(v):
        v.wait_ge(s_mm.h, s_mm.v)
        v.wait_ge(k.s_lcg.h, k.s_lcg.v)
        v.memset(k.fconst[:, 0:1], 1e-6)
        v.memset(k.fconst[:, 1:2], EPS_LN)
        v.memset(k.sval[:, :], 0.0)
        v.tensor_copy(out=k.gnwb[:, :], in_=k.pb[0][:, 0:384])
        v.tensor_copy(out=k.alogv[:, 4:12], in_=k.pb[1][:, 4:12])
        v.tensor_copy(out=k.alogv[:, 12:16], in_=k.pb[1][:, 0:4]).then_inc(s_v.inc(), 1)

    @block.scalar
    def _(sc):
        sc.wait_ge(s_v.h, s_v.v)
        sc.wait_ge(k.s_lcg.h, k.s_lcg.v)
        sc.activation(out=k.alogv[:, 0:4], in_=k.alogv[:, 12:16], func=AF.Exp).then_inc(s_sc.inc(), 1)

    @block.vector
    def _(v):
        v.wait_ge(s_sc.h, s_sc.v)
        v.tensor_scalar(out=k.alogv[:, 12:16], in0=k.alogv[:, 0:4], scalar1=-1.0,
                        scalar2=None, op0=OP.mult).then_inc(s_v.inc(), 1)
    k.prelude_v = s_v.v


def emit_passA(k):
    nc, block = k.nc, k.block
    s_mm, s_v, s_sc = k.s_mm, k.s_v, k.s_sc
    pb = k.pb

    k.scope_zs = ExitStack()
    k.zs = k.scope_zs.enter_context(nc.sbuf_tensor("zs", [128, 8 * TOK], BF16))
    k.scope_xs = ExitStack()
    k.xs = k.scope_xs.enter_context(nc.sbuf_tensor("xs", [128, 8 * TOK], BF16))

    k.scopeA = ExitStack()
    sbA = lambda *a: k.scopeA.enter_context(nc.sbuf_tensor(*a))
    winb = sbA("winb", [128, 8 * D_INNER], BF16)
    diagw = sbA("diagw", [128, 64 * 128], BF16)
    convwv = sbA("convwv", [128, 64], F32)
    convbv = sbA("convbv", [128, 16], F32)
    xtile = sbA("xtile", [128, 2 * DM], F32)
    xfm = sbA("xfm", [128, 8 * TPAD], BF16)
    xzbuf = sbA("xzbuf", [128, 16 * 516], BF16)

    @block.gpsimd
    def _(g):
        for kc in range(8):
            g.dma_start(out=winb[:, kc * D_INNER:(kc + 1) * D_INNER],
                        in_=k.d["W_in"][128 * kc:128 * (kc + 1), :]).then_inc(k.s_lwg.inc(16), 16)

    @block.sync
    def _(sync):
        for cc in range(16):
            sync.dma_start(out=convwv[:, 4 * cc:4 * cc + 4],
                           in_=k.d["convw"][128 * cc:128 * (cc + 1), :]).then_inc(k.s_lw.inc(16), 16)
        for cc in range(16):
            sync.dma_start(out=convbv[:, cc:cc + 1],
                           in_=k.d["convb"][128 * cc:128 * (cc + 1), :]).then_inc(k.s_lw.inc(16), 16)

    @block.vector
    def _(v):
        v.wait_ge(k.s_lw.h, k.s_lw.v)
        for cc in range(16):
            for j in range(4):
                vi = v.tensor_scalar(out=diagw[:, (4 * cc + j) * 128:(4 * cc + j + 1) * 128],
                                     in0=k.consts[:, 0:128], scalar1=convwv[:, 4 * cc + j:4 * cc + j + 1],
                                     scalar2=None, op0=OP.mult)
        vi.then_inc(s_v.inc(), 1)
    v_diag = k.s_v.v

    base_mm = s_mm.v
    base_v = s_v.v

    @block.sync
    def _(sync):
        for tt in range(NPT):
            if tt >= 2:
                sync.wait_ge(s_mm.h, base_mm + tt - 1)
            sync.dma_start(out=xtile[:, (tt % 2) * DM:(tt % 2 + 1) * DM],
                           in_=k.d["x_pad"][128 * tt:128 * (tt + 1), :]).then_inc(k.s_lx[tt % 2].inc(16), 16)

    @block.tensor
    def _(t):
        t.wait_ge(s_v.h, 1)
        for tt in range(NPT):
            t.wait_ge(k.s_lx[tt % 2].h, 16 * (tt // 2 + 1))
            if tt >= 2:
                t.wait_ge(s_v.h, base_v + tt - 1)
            for kc in range(8):
                bank = pb[(tt % 2) * 2 + kc // 4]
                ti = t.transpose(bank[:, (kc % 4) * 128:(kc % 4 + 1) * 128],
                                 xtile[:, (tt % 2) * DM + kc * 128:(tt % 2) * DM + (kc + 1) * 128],
                                 k.consts[:, 0:128])
                if kc == 7:
                    ti.then_inc(s_mm.inc(), 1)

    @block.vector
    def _(v):
        for tt in range(NPT):
            v.wait_ge(s_mm.h, base_mm + tt + 1)
            for kc in range(8):
                bank = pb[(tt % 2) * 2 + kc // 4]
                vi = v.tensor_copy(out=xfm[:, kc * TPAD + 128 * tt:kc * TPAD + 128 * (tt + 1)],
                                   in_=bank[:, (kc % 4) * 128:(kc % 4 + 1) * 128])
            vi.then_inc(s_v.inc(), 1)

    mm_after_x = s_mm.v
    v_after_x = s_v.v

    base_mm2 = s_mm.v
    base_v2 = s_v.v
    base_sc3 = k.s_sc.v

    @block.tensor
    def _(t):
        t.wait_ge(k.s_lwg.h, k.s_lwg.v)
        t.wait_ge(s_v.h, v_after_x)
        t.wait_ge(s_v.h, v_diag)
        nmm = 0
        nconv = 0
        for g in range(4):
            t0 = 512 * g
            for oc in range(16):
                bank = pb[2 + (nmm % 2)]
                if nmm >= 2:
                    t.wait_ge(s_v.h, base_v2 + nmm - 1)
                for kc in range(8):
                    t.matmul(bank[:, 0:512],
                             winb[:, kc * D_INNER + oc * 128:kc * D_INNER + (oc + 1) * 128],
                             xfm[:, kc * TPAD + t0:kc * TPAD + t0 + 512],
                             start=(kc == 0), stop=False if kc < 7 else True)
                for kc in range(8):
                    mi = t.matmul(pb[4 + (nmm % 2)][:, 0:4],
                                  winb[:, kc * D_INNER + oc * 128:kc * D_INNER + (oc + 1) * 128],
                                  xfm[:, kc * TPAD + t0 + 512:kc * TPAD + t0 + 516],
                                  start=(kc == 0), stop=(kc == 7))
                    if kc == 7:
                        mi.then_inc(s_mm.inc(), 1)
                nmm += 1
            t.wait_ge(s_v.h, base_v2 + nmm)
            for cc in range(16):
                bank = pb[6 + (nconv % 2)]
                if nconv >= 2:
                    t.wait_ge(k.s_sc.h, base_sc3 + nconv - 1)
                for j in range(4):
                    mi = t.matmul(bank[:, 0:512],
                                  diagw[:, (4 * cc + j) * 128:(4 * cc + j + 1) * 128],
                                  xzbuf[:, cc * 516 + j:cc * 516 + j + 512],
                                  start=(j == 0), stop=(j == 3))
                    if j == 3:
                        mi.then_inc(s_mm.inc(), 1)
                nconv += 1

    @block.vector
    def _(v):
        nmm = 0
        for g in range(4):
            for oc in range(16):
                bank = pb[2 + (nmm % 2)]
                v.wait_ge(s_mm.h, base_mm2 + nmm + 1 + 16 * g)
                v.tensor_copy(out=xzbuf[:, oc * 516:oc * 516 + 512], in_=bank[:, 0:512])
                v.tensor_copy(out=xzbuf[:, oc * 516 + 512:oc * 516 + 516],
                              in_=pb[4 + (nmm % 2)][:, 0:4]).then_inc(s_v.inc(), 1)
                nmm += 1

    @block.scalar
    def _(sc):
        nconv = 0
        for g in range(4):
            t0 = 512 * g
            for cc in range(16):
                bank = pb[6 + (nconv % 2)]
                sc.wait_ge(s_mm.h, base_mm2 + 16 * (g + 1) + nconv + 1)
                dst = k.xs if cc < 8 else k.zs
                oc = cc if cc < 8 else cc - 8
                sc.activation(out=dst[:, oc * TOK + t0:oc * TOK + t0 + 512], in_=bank[:, 0:512],
                              func=AF.Silu, bias=convbv[:, cc:cc + 1],
                              scale=1.0).then_inc(k.s_sc.inc(), 1)
                nconv += 1

    k.A_done_sc = k.s_sc.v
    k.A_done_mm = s_mm.v

    dxs = k.dbg_tensor("xs", [128, 8 * TOK])
    dzs = k.dbg_tensor("zs", [128, 8 * TOK])
    if dxs is not None:
        @block.gpsimd
        def _(g):
            g.wait_ge(k.s_sc.h, k.A_done_sc)
            g.dma_start(out=dxs, in_=k.xs[:, :]).then_inc(k.s_sp.inc(16), 16)
            g.dma_start(out=dzs, in_=k.zs[:, :]).then_inc(k.s_sp.inc(16), 16)


def emit_gkb(k):
    nc, block, pb = k.nc, k.block, k.pb
    s_mm, s_v, s_sc = k.s_mm, k.s_v, k.s_sc
    k.scopeA.close()

    k.scopeD = ExitStack()
    sbD = lambda *a: k.scopeD.enter_context(nc.sbuf_tensor(*a))
    k.sbD = sbD
    wgkbb = sbD("wgkbb", [128, 64], BF16)
    gkscr = sbD("gkscr", [128, 32], F32)
    s_lgkb = k.sem("s_lgkb")

    @block.gpsimd
    def _(g):
        g.wait_ge(k.s_sc.h, k.A_done_sc)
        for kc in range(8):
            g.dma_start(out=wgkbb[:, 8 * kc:8 * kc + 8],
                        in_=k.d["W_gkb"][128 * kc:128 * (kc + 1), :]).then_inc(s_lgkb.inc(16), 16)

    vfirst = {}
    for t in range(NT):
        par = t % 2

        @block.tensor
        def _(tt_, t=t, par=par):
            tt_.wait_ge(s_lgkb.h, s_lgkb.v)
            tt_.wait_ge(k.s_sc.h, k.A_done_sc)
            if t >= 2:
                tt_.wait_ge(s_v.h, vfirst[t - 2])
            for kc in range(8):
                mi = tt_.matmul(pb[6 + par][:, 0:8],
                                k.xs[:, kc * TOK + 128 * t:kc * TOK + 128 * (t + 1)],
                                wgkbb[:, 8 * kc:8 * kc + 8],
                                start=(kc == 0), stop=(kc == 7))
            mi.then_inc(s_mm.inc(), 1)

        mmv = s_mm.v

        @block.vector
        def _(v, t=t, par=par):
            v.wait_ge(s_mm.h, mmv)
            v.tensor_tensor(out=gkscr[:, 0:8], in0=pb[6 + par][:, 0:8],
                            in1=k.alogv[:, 4:12], op=OP.add).then_inc(s_v.inc(), 1)
        vfirst[t] = s_v.v

        vv = s_v.v

        @block.scalar
        def _(sc, t=t):
            sc.wait_ge(s_v.h, vv)
            sc.activation(out=gkscr[:, 8:12], in_=gkscr[:, 0:4], func=AF.Exp).then_inc(s_sc.inc(), 1)
            sc.wait_ge(s_sc.h, s_sc.v)
            sc.activation(out=gkscr[:, 12:16], in_=gkscr[:, 8:12], func=AF.Ln, bias=1.0)
            sc.activation(out=gkscr[:, 16:20], in_=gkscr[:, 4:8], func=AF.Exp,
                          scale=-1.0).then_inc(s_sc.inc(), 1)

        scv = s_sc.v

        @block.vector
        def _(v, t=t):
            v.wait_ge(s_sc.h, scv)
            v.tensor_tensor(out=k.gkb[:, 8 * t:8 * t + 4], in0=gkscr[:, 12:16],
                            in1=k.alogv[:, 12:16], op=OP.mult)
            v.tensor_scalar(out=gkscr[:, 20:24], in0=gkscr[:, 16:20], scalar1=1.0,
                            scalar2=None, op0=OP.add).then_inc(s_v.inc(), 1)
            v.wait_ge(s_v.h, s_v.v)
            v.reciprocal(out=k.gkb[:, 8 * t + 4:8 * t + 8],
                         in_=gkscr[:, 20:24]).then_inc(s_v.inc(), 1)

    k.gkb_done_v = s_v.v

    dgk = k.dbg_tensor("gkb", [128, NT * 8])
    if dgk is not None:
        @block.gpsimd
        def _(g):
            g.wait_ge(s_v.h, k.gkb_done_v)
            g.dma_start(out=dgk, in_=k.gkb[:, :]).then_inc(k.s_sp.inc(16), 16)


def delta_alloc(k):
    sbD = k.sbD
    nc = k.nc
    k.wqb = sbD("wqb", [128, 8 * KD], BF16)
    k.wkb = sbD("wkb", [128, 8 * KD], BF16)
    k.wvb = sbD("wvb", [128, 8 * VD], BF16)
    k.q_tm = sbD("q_tm", [128, KD], F32)
    k.k_tm = sbD("k_tm", [128, KD], F32)
    k.qh_tm = sbD("qh_tm", [128, DK], F32)
    k.kh_tm = sbD("kh_tm", [128, DK], F32)
    k.sqv = sbD("sqv", [128, 8], F32)
    k.lamv = sbD("lamv", [128, 2 * 16], F32)
    k.lamlast = sbD("lamlast", [128, 2 * 4], F32)
    k.lamT = sbD("lamT", [4, 2 * 128], F32)
    k.dupS = sbD("dupS", [128, 128], F32)
    k.dloS = sbD("dloS", [128, 128], F32)
    k.dscr = sbD("dscr", [128, 128], F32)
    k.dscr2 = sbD("dscr2", [128, 128], F32)
    k.a_bf = sbD("a_bf", [128, 2 * 128], BF16)
    k.as_bf = sbD("as_bf", [128, 128], BF16)
    k.z0 = sbD("z0", [128, 128], BF16)
    k.z1 = sbD("z1", [128, 128], BF16)
    k.zt0 = sbD("zt0", [128, 128], BF16)
    k.zt1 = sbD("zt1", [128, 128], BF16)
    k.q1s = sbD("q1s", [128, 128], BF16)
    k.r_pan = sbD("r_pan", [128, 576], BF16)
    k.blcol = sbD("blcol", [128, 4], F32)
    k.uv = sbD("uv", [128, 2 * DV], BF16)
    k.w_tm = sbD("w_tm", [128, 2 * 192], BF16)
    k.wfm = sbD("wfm", [128, 2 * 256], BF16)
    k.aqt = sbD("aqt", [128, 2 * 128], BF16)
    k.ktil = sbD("ktil", [128, 2 * 192], BF16)
    k.qfm = sbD("qfm", [128, 2 * 256], BF16)
    k.kfm = sbD("kfm", [128, 256], BF16)
    k.ubf = sbD("ubf", [128, DV], BF16)
    k.oscr = sbD("oscr", [128, DV], F32)
    k.obuf = sbD("obuf", [128, 2 * VD], BF16)
    k.osq = sbD("osq", [128, DV], F32)
    k.s0 = [sbD("s0a", [128, HH * DV], F32), sbD("s0b", [128, HH * DV], F32)]
    k.s1 = [sbD("s1a", [128, HH * DV], F32), sbD("s1b", [128, HH * DV], F32)]
    k.s0b = [sbD("s0ab", [128, HH * DV], BF16), sbD("s0bb", [128, HH * DV], BF16)]
    k.s1b = [sbD("s1ab", [128, HH * DV], BF16), sbD("s1bb", [128, HH * DV], BF16)]
    k.c["hsel"] = nc.inline_tensor(
        np.stack([np.repeat(np.eye(4, dtype=np.float32)[:, hh:hh + 1], 128, 1) for hh in range(4)], 0)
        .transpose(1, 0, 2).reshape(4, 512), name="c_hsel").ap()
    k.hsel = sbD("hsel", [4, 512], F32)
    k.s_lqkv = k.sem("s_lqkv")
    k.s_lhsel = k.sem("s_lhsel")
    k.s_spp = [k.sem("s_spp0"), k.sem("s_spp1")]
    k.s_hd = [k.sem("s_hd1"), k.sem("s_hd2")]
    k.s_pan = [k.sem("s_pan0"), k.sem("s_pan1")]
    k.s_so = [k.sem("s_so0"), k.sem("s_so1")]

    @k.block.sync
    def _(sync):
        sync.wait_ge(k.s_sc.h, k.A_done_sc)
        sync.dma_start(out=k.hsel[:, :], in_=k.c["hsel"]).then_inc(k.s_lhsel.inc(16), 16)

    @k.block.gpsimd
    def _(g):
        g.wait_ge(k.s_sc.h, k.A_done_sc)
        for kc in range(8):
            g.dma_start(out=k.wqb[:, kc * KD:(kc + 1) * KD],
                        in_=k.d["W_q"][128 * kc:128 * (kc + 1), :]).then_inc(k.s_lqkv.inc(16), 16)
            g.dma_start(out=k.wkb[:, kc * KD:(kc + 1) * KD],
                        in_=k.d["W_k"][128 * kc:128 * (kc + 1), :]).then_inc(k.s_lqkv.inc(16), 16)
            g.dma_start(out=k.wvb[:, kc * VD:(kc + 1) * VD],
                        in_=k.d["W_v"][128 * kc:128 * (kc + 1), :]).then_inc(k.s_lqkv.inc(16), 16)


def emit_chunk_lambda(k, c, chain):
    block, pb = k.block, k.pb
    s_mm, s_v, s_sc = k.s_mm, k.s_v, k.s_sc
    par = c % 2
    lo = 16 * par

    vnow = s_v.v

    @block.tensor
    def _(t):
        t.wait_ge(s_v.h, vnow)
        t.matmul(pb[5][:, 384:388], k.consts[:, 256:384], k.gkb[:, 8 * c:8 * c + 4],
                 start=True, stop=True)
        t.matmul(pb[5][:, 388:392], k.consts[:, 384:512], k.gkb[:, 8 * c:8 * c + 4],
                 start=True, stop=True).then_inc(s_mm.inc(), 1)

    mmv = s_mm.v

    @block.vector
    def _(v):
        v.wait_ge(s_mm.h, mmv)
        v.tensor_copy(out=k.lamv[:, lo:lo + 8], in_=pb[5][:, 384:392]).then_inc(s_v.inc(), 1)
        v.wait_ge(s_v.h, s_v.v)
        v.tensor_tensor(out=k.lamv[:, lo + 12:lo + 16], in0=k.lamv[:, lo:lo + 4],
                        in1=k.lamv[:, lo + 4:lo + 8], op=OP.add).then_inc(s_v.inc(), 1)

    vv = s_v.v

    @block.scalar
    def _(sc):
        sc.wait_ge(s_v.h, vv)
        sc.activation(out=k.lamlast[:, 4 * par:4 * par + 4], in_=k.lamv[:, lo + 12:lo + 16],
                      func=AF.Exp).then_inc(s_sc.inc(), 1)
        sc.wait_ge(s_sc.h, s_sc.v)
        sc.activation(out=k.lamv[:, lo + 12:lo + 16], in_=k.lamv[:, lo + 4:lo + 8],
                      func=AF.Exp)
        sc.activation(out=k.lamv[:, lo + 8:lo + 12], in_=k.lamv[:, lo:lo + 4],
                      func=AF.Exp).then_inc(s_sc.inc(), 1)

    scv = s_sc.v

    @block.tensor
    def _(t):
        t.wait_ge(s_sc.h, scv)
        t.transpose(pb[4][0:4, 384:512], k.lamv[:, lo:lo + 4],
                    k.consts[:, 0:128]).then_inc(s_mm.inc(), 1)

    mmv2 = s_mm.v

    @block.vector
    def _(v):
        v.wait_ge(s_mm.h, mmv2)
        v.tensor_copy(out=k.lamT[:, 128 * par:128 * (par + 1)],
                      in_=pb[4][0:4, 384:512]).then_inc(s_v.inc(), 1)


def emit_head(k, c, h, chain):
    block, pb = k.block, k.pb
    s_mm, s_v, s_sc = k.s_mm, k.s_v, k.s_sc
    par = c % 2
    lo = 16 * par
    j = 4 * c + h
    jp = j % 2
    cur, nxt = par, 1 - par
    bcol = k.gkb[:, 8 * c + 4 + h:8 * c + 5 + h]
    lamcol = k.lamv[:, lo + h:lo + h + 1]
    Lamcol = k.lamv[:, lo + 8 + h:lo + 9 + h]
    Lamrcol = k.lamv[:, lo + 12 + h:lo + 13 + h]
    Lamlastcol = k.lamlast[:, 4 * par + h:4 * par + h + 1]
    UVo, WFo, AQo, KTo, QFo = (k.uv[:, jp * DV:(jp + 1) * DV],
                               k.wfm[:, jp * 256:(jp + 1) * 256],
                               k.aqt[:, jp * 128:(jp + 1) * 128],
                               k.ktil[:, jp * 192:(jp + 1) * 192],
                               k.qfm[:, jp * 256:(jp + 1) * 256])

    if chain == 1:
        vproj = s_v.v

        @block.tensor
        def _(t):
            t.wait_ge(k.s_lqkv.h, k.s_lqkv.v)
            t.wait_ge(k.s_lhsel.h, k.s_lhsel.v)
            t.wait_ge(s_v.h, vproj)
            for kc in range(8):
                t.matmul(pb[0][:, 0:192], k.xs[:, kc * TOK + 128 * c:kc * TOK + 128 * (c + 1)],
                         k.wqb[:, kc * KD + 192 * h:kc * KD + 192 * (h + 1)],
                         start=(kc == 0), stop=(kc == 7))
            for kc in range(8):
                t.matmul(pb[0][:, 192:384], k.xs[:, kc * TOK + 128 * c:kc * TOK + 128 * (c + 1)],
                         k.wkb[:, kc * KD + 192 * h:kc * KD + 192 * (h + 1)],
                         start=(kc == 0), stop=(kc == 7))
            for kc in range(8):
                mi = t.matmul(pb[1][:, 0:384], k.xs[:, kc * TOK + 128 * c:kc * TOK + 128 * (c + 1)],
                              k.wvb[:, kc * VD + 384 * h:kc * VD + 384 * (h + 1)],
                              start=(kc == 0), stop=(kc == 7))
            mi.then_inc(s_mm.inc(), 1)

        m1 = s_mm.v

        @block.vector
        def _(v):
            v.wait_ge(s_mm.h, m1)
            if j >= 2:
                v.wait_ge(k.s_spp[j % 2].h, 16 * 5 * (j // 2))
            v.tensor_copy(out=k.q_tm[:, 0:192], in_=pb[0][:, 0:192])
            v.tensor_copy(out=k.k_tm[:, 0:192], in_=pb[0][:, 192:384]).then_inc(s_v.inc(), 1)
            v.wait_ge(s_v.h, s_v.v)
            v.tensor_tensor(out=k.qh_tm[:, :], in0=k.q_tm[:, 0:192], in1=k.q_tm[:, 0:192], op=OP.mult)
            v.tensor_tensor(out=k.kh_tm[:, :], in0=k.k_tm[:, 0:192], in1=k.k_tm[:, 0:192],
                            op=OP.mult).then_inc(s_v.inc(), 1)
            v.wait_ge(s_v.h, s_v.v)
            v.tensor_reduce(out=k.sqv[:, 0:1], in_=k.qh_tm[:, :], axis=AX, op=OP.add)
            v.tensor_reduce(out=k.sqv[:, 1:2], in_=k.kh_tm[:, :], axis=AX,
                            op=OP.add).then_inc(s_v.inc(), 1)

        v1 = s_v.v

        @block.scalar
        def _(sc):
            sc.wait_ge(s_v.h, v1)
            sc.activation(out=k.sqv[:, 2:4], in_=k.sqv[:, 0:2], func=AF.Ln,
                          bias=k.fconst[:, 0:1]).then_inc(s_sc.inc(), 1)
            sc.wait_ge(s_sc.h, s_sc.v)
            sc.activation(out=k.sqv[:, 4:6], in_=k.sqv[:, 2:4], func=AF.Exp,
                          scale=-0.5).then_inc(s_sc.inc(), 1)

        sc1 = s_sc.v

        @block.vector
        def _(v):
            v.wait_ge(s_sc.h, sc1)
            v.tensor_scalar(out=k.qh_tm[:, :], in0=k.q_tm[:, 0:192], scalar1=k.sqv[:, 4:5],
                            scalar2=None, op0=OP.mult)
            v.tensor_scalar(out=k.kh_tm[:, :], in0=k.k_tm[:, 0:192], scalar1=k.sqv[:, 5:6],
                            scalar2=None, op0=OP.mult).then_inc(s_v.inc(), 1)

        v2 = s_v.v

        @block.tensor
        def _(t):
            t.wait_ge(s_v.h, v2)
            t.transpose(pb[2][:, 0:128], k.qh_tm[:, 0:128], k.consts[:, 0:128])
            t.transpose(pb[2][0:64, 128:256], k.qh_tm[:, 128:192], k.consts[:, 0:128])
            t.transpose(pb[3][:, 0:128], k.kh_tm[:, 0:128], k.consts[:, 0:128])
            t.transpose(pb[3][0:64, 128:256], k.kh_tm[:, 128:192], k.consts[:, 0:128])
            t.matmul(pb[2][:, 256:384], k.hsel[:, 128 * h:128 * (h + 1)],
                     k.lamT[:, 128 * par:128 * (par + 1)],
                     start=True, stop=True).then_inc(s_mm.inc(), 1)

        m2 = s_mm.v

        @block.vector
        def _(v):
            v.wait_ge(s_mm.h, m2)
            v.tensor_copy(out=QFo[:, 0:128], in_=pb[2][:, 0:128])
            v.tensor_copy(out=QFo[0:64, 128:256], in_=pb[2][0:64, 128:256])
            v.tensor_copy(out=k.kfm[:, 0:128], in_=pb[3][:, 0:128])
            v.tensor_copy(out=k.kfm[0:64, 128:256], in_=pb[3][0:64, 128:256])
            v.tensor_scalar(out=k.dscr[:, :], in0=pb[2][:, 256:384], scalar1=lamcol,
                            scalar2=0.0, op0=OP.subtract, op1=OP.min)
            v.tensor_scalar(out=k.dloS[:, :], in0=pb[2][:, 256:384], scalar1=lamcol,
                            scalar2=0.0, op0=OP.subtract, op1=OP.max).then_inc(s_v.inc(), 1)

        v3 = s_v.v

        @block.scalar
        def _(sc):
            sc.wait_ge(s_v.h, v3)
            sc.activation(out=k.dupS[:, :], in_=k.dscr[:, :], func=AF.Exp).then_inc(s_sc.inc(), 1)
            sc.wait_ge(s_sc.h, s_sc.v)
            sc.activation(out=k.dscr[:, :], in_=k.dloS[:, :], func=AF.Exp,
                          scale=-1.0).then_inc(s_sc.inc(), 1)

        sc2 = s_sc.v

        @block.tensor
        def _(t):
            t.wait_ge(s_v.h, v3)
            t.matmul(pb[4][:, 0:128], k.kfm[:, 0:128], k.kfm[:, 0:128], start=True, stop=False)
            t.matmul(pb[4][:, 0:128], k.kfm[0:64, 128:256], k.kfm[0:64, 128:256], start=False, stop=True)
            t.matmul(pb[4][:, 128:256], k.kfm[:, 0:128], QFo[:, 0:128], start=True, stop=False)
            mi = t.matmul(pb[4][:, 128:256], k.kfm[0:64, 128:256], QFo[0:64, 128:256],
                          start=False, stop=True)
            mi.then_inc(s_mm.inc(), 1)

        m3 = s_mm.v

        @block.vector
        def _(v):
            v.wait_ge(s_mm.h, m3)
            v.wait_ge(s_sc.h, sc2)
            v.scalar_tensor_tensor(out=k.dloS[:, :], in0=k.dscr[:, :], scalar=bcol,
                                   in1=pb[4][:, 0:128], op0=OP.mult, op1=OP.mult)
            v.tensor_tensor(out=k.dscr2[:, :], in0=k.dupS[:, :], in1=pb[4][:, 128:256],
                            op=OP.mult).then_inc(s_v.inc(), 1)
            v.wait_ge(s_v.h, s_v.v)
            v.tensor_tensor(out=k.a_bf[:, jp * 128:(jp + 1) * 128], in0=k.dloS[:, :],
                            in1=k.constsb[:, 0:128], op=OP.mult)
            v.tensor_tensor(out=AQo, in0=k.dscr2[:, :],
                            in1=k.constsb[:, 128:256], op=OP.mult).then_inc(s_v.inc(), 1)

        v4 = s_v.v
        abf = k.a_bf[:, jp * 128:(jp + 1) * 128]

        @block.vector
        def _(v):
            v.wait_ge(s_v.h, v4)
            v.tensor_tensor(out=k.as_bf[:, :], in0=abf, in1=k.constsb[:, 384:512],
                            op=OP.mult).then_inc(s_v.inc(), 1)
            v.wait_ge(s_v.h, s_v.v)
            v.tensor_tensor(out=k.z0[:, :], in0=k.as_bf[:, :], in1=k.constsb[:, 256:384],
                            op=OP.add).then_inc(s_v.inc(), 1)

        vd = s_v.v

        @block.tensor
        def _(t):
            t.wait_ge(s_v.h, vd)
            t.transpose(pb[5][:, 0:64].bitcast(BF16), k.z0[:, :],
                        k.constsb[:, 256:384]).then_inc(s_mm.inc(), 1)

        md = s_mm.v

        @block.vector
        def _(v):
            v.wait_ge(s_mm.h, md)
            v.tensor_copy(out=k.zt0[:, :], in_=pb[5][:, 0:64].bitcast(BF16)).then_inc(s_v.inc(), 1)

        zc, ztc = k.z0, k.zt0
        zn, ztn = k.z1, k.zt1
        for lv in range(1, 7):
            vpre = s_v.v

            @block.vector
            def _(v, lv=lv):
                v.tensor_tensor(out=k.as_bf[:, :], in0=abf,
                                in1=k.constsb[:, 384 + 128 * lv:512 + 128 * lv],
                                op=OP.mult).then_inc(s_v.inc(), 1)

            va = s_v.v

            @block.tensor
            def _(t, ztc=ztc):
                t.wait_ge(s_v.h, va)
                t.matmul(pb[5][:, 0:128], k.as_bf[:, :], ztc[:, :],
                         start=True, stop=True).then_inc(s_mm.inc(), 1)

            mq = s_mm.v

            @block.vector
            def _(v):
                v.wait_ge(s_mm.h, mq)
                v.tensor_copy(out=k.q1s[:, :], in_=pb[5][:, 0:128]).then_inc(s_v.inc(), 1)

            vq = s_v.v

            @block.tensor
            def _(t, zc=zc):
                t.wait_ge(s_v.h, vq)
                t.matmul(pb[6][:, 0:128], k.q1s[:, :], zc[:, :], start=True, stop=True)
                t.matmul(pb[6][:, 128:256], zc[:, :], k.q1s[:, :],
                         start=True, stop=True).then_inc(s_mm.inc(), 1)

            mp = s_mm.v

            @block.vector
            def _(v, zc=zc, ztc=ztc, zn=zn, ztn=ztn):
                v.wait_ge(s_mm.h, mp)
                v.tensor_tensor(out=zn[:, :], in0=zc[:, :], in1=pb[6][:, 0:128], op=OP.add)
                v.tensor_tensor(out=ztn[:, :], in0=ztc[:, :], in1=pb[6][:, 128:256],
                                op=OP.add).then_inc(s_v.inc(), 1)

            zc, zn = zn, zc
            ztc, ztn = ztn, ztc

        ztfinal = ztc

        vr0 = s_v.v

        @block.vector
        def _(v):
            v.tensor_tensor(out=k.blcol[:, 0:1], in0=bcol, in1=Lamcol, op=OP.mult)
            v.tensor_scalar(out=k.r_pan[:, 0:384], in0=pb[1][:, 0:384], scalar1=bcol,
                            scalar2=None, op0=OP.mult)
            v.tensor_scalar(out=KTo, in0=k.kh_tm[:, :], scalar1=Lamrcol,
                            scalar2=None, op0=OP.mult).then_inc(s_v.inc(), 1)
            v.wait_ge(s_v.h, s_v.v)
            v.tensor_scalar(out=k.r_pan[:, 384:576], in0=k.kh_tm[:, :], scalar1=k.blcol[:, 0:1],
                            scalar2=None, op0=OP.mult).then_inc(s_v.inc(), 1)

        vr = s_v.v

        @block.tensor
        def _(t, ztfinal=ztfinal):
            t.wait_ge(s_v.h, vr)
            t.matmul(pb[7][:, 0:512], ztfinal[:, :], k.r_pan[:, 0:512], start=True, stop=True)
            t.matmul(pb[5][:, 128:192], ztfinal[:, :], k.r_pan[:, 512:576],
                     start=True, stop=True).then_inc(s_mm.inc(), 1)

        mpan = s_mm.v

        @block.vector
        def _(v):
            v.wait_ge(s_mm.h, mpan)
            v.tensor_copy(out=UVo, in_=pb[7][:, 0:384])
            v.tensor_copy(out=k.w_tm[:, jp * 192:jp * 192 + 128], in_=pb[7][:, 384:512])
            v.tensor_copy(out=k.w_tm[:, jp * 192 + 128:jp * 192 + 192],
                          in_=pb[5][:, 128:192]).then_inc(s_v.inc(), 1)

        vw = s_v.v

        @block.tensor
        def _(t):
            t.wait_ge(s_v.h, vw)
            t.transpose(pb[6][:, 0:64].bitcast(BF16), k.w_tm[:, jp * 192:jp * 192 + 128],
                        k.constsb[:, 256:384])
            t.transpose(pb[6][0:64, 64:128].bitcast(BF16), k.w_tm[:, jp * 192 + 128:jp * 192 + 192],
                        k.constsb[:, 256:384]).then_inc(s_mm.inc(), 1)

        mwf = s_mm.v

        @block.vector
        def _(v):
            v.wait_ge(s_mm.h, mwf)
            v.tensor_copy(out=WFo[:, 0:128], in_=pb[6][:, 0:64].bitcast(BF16))
            v.tensor_copy(out=WFo[0:64, 128:256],
                          in_=pb[6][0:64, 64:128].bitcast(BF16)).then_inc(s_v.inc(), 1)

        vwf = s_v.v

        @block.gpsimd
        def _(g):
            g.wait_ge(s_v.h, vwf)
            sp = k.s_spp[j % 2]
            g.dma_start(out=k.dr["sp_uv"][128 * j:128 * (j + 1), :], in_=UVo).then_inc(sp.inc(16), 16)
            g.dma_start(out=k.dr["sp_wfm"][128 * j:128 * (j + 1), :], in_=WFo).then_inc(sp.inc(16), 16)
            g.dma_start(out=k.dr["sp_aqt"][128 * j:128 * (j + 1), :], in_=AQo).then_inc(sp.inc(16), 16)
            g.dma_start(out=k.dr["sp_ktil"][128 * j:128 * (j + 1), :], in_=KTo).then_inc(sp.inc(16), 16)
            g.dma_start(out=k.dr["sp_qfm"][128 * j:128 * (j + 1), :], in_=QFo).then_inc(sp.inc(16), 16)

    else:
        @block.sync
        def _(sync):
            if j >= 2:
                sync.wait_ge(k.s_v.h, k.head_done_v[2][j - 2])
            sp = k.s_pan[jp]
            sync.dma_start(out=UVo, in_=k.dr["sp_uv"][128 * j:128 * (j + 1), :]).then_inc(sp.inc(16), 16)
            sync.dma_start(out=WFo, in_=k.dr["sp_wfm"][128 * j:128 * (j + 1), :]).then_inc(sp.inc(16), 16)
            sync.dma_start(out=AQo, in_=k.dr["sp_aqt"][128 * j:128 * (j + 1), :]).then_inc(sp.inc(16), 16)
            sync.dma_start(out=KTo, in_=k.dr["sp_ktil"][128 * j:128 * (j + 1), :]).then_inc(sp.inc(16), 16)
            sync.dma_start(out=QFo, in_=k.dr["sp_qfm"][128 * j:128 * (j + 1), :]).then_inc(sp.inc(16), 16)

    panwait = (k.s_pan[jp], 16 * 5 * (j // 2 + 1)) if chain == 2 else None

    vws = s_v.v

    @block.tensor
    def _(t):
        if panwait is not None:
            t.wait_ge(panwait[0].h, panwait[1])
        t.wait_ge(s_v.h, vws)
        t.matmul(pb[6][:, 0:384], WFo[:, 0:128], k.s0b[cur][:, DV * h:DV * (h + 1)],
                 start=True, stop=False)
        t.matmul(pb[6][:, 0:384], WFo[0:64, 128:256], k.s1b[cur][0:64, DV * h:DV * (h + 1)],
                 start=False, stop=True).then_inc(s_mm.inc(), 1)

    mws = s_mm.v

    @block.vector
    def _(v):
        v.wait_ge(s_mm.h, mws)
        v.tensor_tensor(out=k.ubf[:, :], in0=UVo, in1=pb[6][:, 0:384],
                        op=OP.subtract).then_inc(s_v.inc(), 1)

    vu = s_v.v

    if chain == 2:
        @block.tensor
        def _(t):
            t.wait_ge(s_v.h, vu)
            t.matmul(pb[2][:, 0:384], QFo[:, 0:128], k.s0b[cur][:, DV * h:DV * (h + 1)],
                     start=True, stop=False)
            t.matmul(pb[2][:, 0:384], QFo[0:64, 128:256], k.s1b[cur][0:64, DV * h:DV * (h + 1)],
                     start=False, stop=True)
            t.matmul(pb[3][:, 0:384], AQo, k.ubf[:, :], start=True, stop=True).then_inc(s_mm.inc(), 1)

        mo = s_mm.v

        @block.vector
        def _(v):
            v.wait_ge(s_mm.h, mo)
            if c >= 2 and h == 0:
                v.wait_ge(k.s_so[c % 2].h, 16 * (c // 2))
            v.tensor_scalar(out=k.oscr[:, :], in0=pb[2][:, 0:384], scalar1=Lamcol,
                            scalar2=None, op0=OP.mult).then_inc(s_v.inc(), 1)
            v.wait_ge(s_v.h, s_v.v)
            v.tensor_tensor(out=k.obuf[:, par * VD + DV * h:par * VD + DV * (h + 1)],
                            in0=k.oscr[:, :], in1=pb[3][:, 0:384], op=OP.add).then_inc(s_v.inc(), 1)

        vo = s_v.v

        @block.scalar
        def _(sc):
            sc.wait_ge(s_v.h, vo)
            sc.wait_ge(s_sc.h, s_sc.v)
            sc.activation(out=k.osq[:, :], in_=k.obuf[:, par * VD + DV * h:par * VD + DV * (h + 1)],
                          func=AF.Square,
                          accum_out=k.stat[:, 4 * c + h:4 * c + h + 1]).then_inc(s_sc.inc(), 1)

    @block.tensor
    def _(t):
        t.wait_ge(s_v.h, vu)
        t.matmul(pb[0][:, 0:384], KTo[:, 0:128], k.ubf[:, :], start=True, stop=True)
        t.matmul(pb[1][0:64, 0:384], KTo[:, 128:192], k.ubf[:, :],
                 start=True, stop=True).then_inc(s_mm.inc(), 1)

    msu = s_mm.v

    @block.vector
    def _(v):
        v.wait_ge(s_mm.h, msu)
        v.scalar_tensor_tensor(out=k.s0[nxt][:, DV * h:DV * (h + 1)],
                               in0=k.s0[cur][:, DV * h:DV * (h + 1)], scalar=Lamlastcol,
                               in1=pb[0][:, 0:384], op0=OP.mult, op1=OP.add)
        v.scalar_tensor_tensor(out=k.s1[nxt][0:64, DV * h:DV * (h + 1)],
                               in0=k.s1[cur][0:64, DV * h:DV * (h + 1)], scalar=Lamlastcol[0:64, :],
                               in1=pb[1][0:64, 0:384], op0=OP.mult, op1=OP.add).then_inc(s_v.inc(), 1)
        v.wait_ge(s_v.h, s_v.v)
        v.tensor_copy(out=k.s0b[nxt][:, DV * h:DV * (h + 1)], in_=k.s0[nxt][:, DV * h:DV * (h + 1)])
        v.tensor_copy(out=k.s1b[nxt][0:64, DV * h:DV * (h + 1)],
                      in_=k.s1[nxt][0:64, DV * h:DV * (h + 1)]).then_inc(s_v.inc(), 1)
    k.head_done_v[chain][j] = k.s_v.v


def emit_delta(k, chain):
    nc, block = k.nc, k.block
    if not hasattr(k, "head_done_v"):
        k.head_done_v = {1: {}, 2: {}}
    if chain == 1:
        delta_alloc(k)

        @block.vector
        def _(v):
            v.memset(k.s0[0][:, :], 0.0)
            v.memset(k.s1[0][:, :], 0.0)
            v.memset(k.s0b[0][:, :], 0.0)
            v.memset(k.s1b[0][:, :], 0.0)
            v.memset(k.wfm[64:128, 128:256], 0.0)
            v.memset(k.wfm[64:128, 384:512], 0.0)
            v.memset(k.qfm[64:128, 128:256], 0.0)
            v.memset(k.qfm[64:128, 384:512], 0.0)
            v.memset(k.s1[1][64:128, :], 0.0)
            v.memset(k.s1b[1][64:128, :], 0.0).then_inc(k.s_v.inc(), 1)
    else:
        @block.sync
        def _(sync):
            sync.wait_ge(k.s_spp[0].h, k.s_spp[0].v)
            sync.wait_ge(k.s_spp[1].h, k.s_spp[1].v)
            sync.wait_ge(k.s_v.h, k.s_v.v)

    for c in range(NCH):
        emit_chunk_lambda(k, c, chain)
        for h in range(HH):
            emit_head(k, c, h, chain)
        if chain == 2:
            par = c % 2
            vdone = k.s_v.v

            @block.gpsimd
            def _(g, c=c, par=par):
                g.wait_ge(k.s_v.h, vdone)
                g.dma_start(out=k.dr["sp_o"][128 * c:128 * (c + 1), :],
                            in_=k.obuf[:, par * VD:(par + 1) * VD]).then_inc(k.s_so[c % 2].inc(16), 16)

    if chain == 1:
        k.chain1_hd = k.head_done_v[1][4 * NCH - 1]
    else:
        k.chain2_done_so = (k.s_so[0].v, k.s_so[1].v)
        k.chain2_hd = k.head_done_v[2][4 * NCH - 1]
        dbo = k.dbg_tensor("o", [TOK, VD])
        if dbo is not None:
            @block.gpsimd
            def _(g):
                g.wait_ge(k.s_so[0].h, k.s_so[0].v)
                g.wait_ge(k.s_so[1].h, k.s_so[1].v)
                g.dma_start(out=dbo, in_=k.dr["sp_o"][:, :]).then_inc(k.s_sp.inc(16), 16)


def emit_state_ag(k):
    nc, block = k.nc, k.block
    s_v = k.s_v
    mock = getattr(k, "mock_ag", False)

    if mock:
        @block.vector
        def _(v):
            v.wait_ge(s_v.h, k.chain1_hd)
            v.memset(k.s0[0][:, :], 0.0)
            v.memset(k.s1[0][:, :], 0.0)
            v.memset(k.s0b[0][:, :], 0.0)
            v.memset(k.s1b[0][:, :], 0.0).then_inc(s_v.inc(), 1)
        return

    s_ag = k.sem("s_ag")
    s_agl = k.sem("s_agl")
    cin, cout = k.dr["cc_state_in"], k.dr["cc_state_out"]
    fin = NCH % 2

    @block.gpsimd
    def _(g):
        g.wait_ge(k.s_v.h, k.chain1_hd)
        g.dma_start(out=cin[0:128, :], in_=k.s0[fin][:, :]).then_inc(s_ag.inc(16), 16)
        g.dma_start(out=cin[128:192, :], in_=k.s1[fin][0:64, :]).then_inc(s_ag.inc(16), 16)
        g.wait_ge(s_ag.h, s_ag.v)
        g.collective_compute(
            "AllGather", mybir.AluOpType.bypass,
            replica_groups=[[0, 1, 2, 3, 4, 5, 6, 7]],
            ins=[cin.opt()], outs=[cout.opt()],
        ).then_inc(s_agl.inc(1), 1)

    agv = k.s_ag_done = k.s_ag = s_ag

    s_msk = k.sem("s_msk")

    @block.sync
    def _(sync):
        sync.dma_start(out=k.sval[0:1, 8:16], in_=k.d["smask"]).then_inc(s_msk.inc(16), 16)

    @block.tensor
    def _(t):
        t.wait_ge(s_msk.h, s_msk.v)
        t.matmul(k.pb[4][:, 0:8], k.consts[0:1, 128:256], k.sval[0:1, 8:16],
                 start=True, stop=True).then_inc(k.s_mm.inc(), 1)

    mmv = k.s_mm.v

    @block.vector
    def _(v):
        v.wait_ge(k.s_mm.h, mmv)
        v.tensor_copy(out=k.sval[:, 0:8], in_=k.pb[4][:, 0:8]).then_inc(s_v.inc(), 1)

    slab0 = k.sbD("slab0", [128, HH * DV], F32)
    slab1 = k.sbD("slab1", [128, HH * DV], F32)

    @block.vector
    def _(v):
        v.wait_ge(s_ag.h, 32)
        v.memset(k.s0[0][:, :], 0.0)
        v.memset(k.s1[0][:, :], 0.0).then_inc(s_v.inc(), 1)

    for ci in range(8):
        sidx, dst = ci % 2, (ci + 1) % 2
        lv = k.sem(f"s_agld{ci}")
        vprev = s_v.v

        @block.sync
        def _(sync, ci=ci, lv=lv, vprev=vprev):
            sync.wait_ge(s_agl.h, s_agl.v)
            sync.wait_ge(s_v.h, vprev)
            sync.dma_start(out=slab0[:, :], in_=k.dr["cc_state_out"][192 * ci:192 * ci + 128, :]).then_inc(lv.inc(16), 16)
            sync.dma_start(out=slab1[0:64, :], in_=k.dr["cc_state_out"][192 * ci + 128:192 * (ci + 1), :]).then_inc(lv.inc(16), 16)

        @block.vector
        def _(v, ci=ci, sidx=sidx, dst=dst, lv=lv, vprev=vprev):
            v.wait_ge(lv.h, lv.v)
            v.scalar_tensor_tensor(out=k.s0[dst][:, :], in0=slab0[:, :],
                                   scalar=k.sval[:, ci:ci + 1], in1=k.s0[sidx][:, :],
                                   op0=OP.mult, op1=OP.add)
            v.scalar_tensor_tensor(out=k.s1[dst][0:64, :], in0=slab1[0:64, :],
                                   scalar=k.sval[0:64, ci:ci + 1], in1=k.s1[sidx][0:64, :],
                                   op0=OP.mult, op1=OP.add).then_inc(s_v.inc(), 1)

    vfin = s_v.v

    @block.vector
    def _(v):
        v.wait_ge(s_v.h, vfin)
        v.tensor_copy(out=k.s0b[0][:, :], in_=k.s0[0][:, :])
        v.tensor_copy(out=k.s1b[0][0:64, :], in_=k.s1[0][0:64, :]).then_inc(s_v.inc(), 1)

    k.stateag_done_v = s_v.v


def emit_epilogue(k):
    nc, block, pb = k.nc, k.block, k.pb
    s_mm, s_v, s_sc = k.s_mm, k.s_v, k.s_sc

    scv0 = s_sc.v

    @block.scalar
    def _(sc):
        sc.wait_ge(s_sc.h, s_sc.v)
        sc.wait_ge(s_v.h, k.chain2_hd)
        sc.activation(out=k.stat[:, 64:128], in_=k.stat[:, 0:64], func=AF.Ln,
                      scale=1.0 / DV, bias=k.fconst[:, 1:2]).then_inc(s_sc.inc(), 1)
        sc.wait_ge(s_sc.h, s_sc.v)
        sc.activation(out=k.stat[:, 0:64], in_=k.stat[:, 64:128], func=AF.Exp,
                      scale=-0.5).then_inc(s_sc.inc(), 1)
    rs_done_sc = s_sc.v

    k.scopeD.close()
    scope1 = ExitStack()
    sb1 = lambda *a: scope1.enter_context(nc.sbuf_tensor(*a))
    wgb = sb1("wgb", [128, 8 * VD], BF16)
    otile = sb1("otile", [128, 2 * VD], BF16)
    gs = sb1("gs", [128, 2 * VD], BF16)
    og = sb1("og", [128, 2 * VD], BF16)
    t1s = sb1("t1s", [128, VD], F32)
    s_lwg2 = k.sem("s_lwg2")
    s_lo = [k.sem("s_lo0"), k.sem("s_lo1")]
    s_og = [k.sem("s_og0"), k.sem("s_og1")]

    @block.gpsimd
    def _(g):
        g.wait_ge(s_v.h, k.chain2_hd)
        g.wait_ge(k.s_so[0].h, k.s_so[0].v)
        g.wait_ge(k.s_so[1].h, k.s_so[1].v)
        for kc in range(8):
            g.dma_start(out=wgb[:, kc * VD:(kc + 1) * VD],
                        in_=k.d["W_g"][128 * kc:128 * (kc + 1), :]).then_inc(s_lwg2.inc(16), 16)

    ot_ready = {}
    og_done_v = {}
    tile_v_done = {}

    @block.sync
    def _(sync):
        sync.wait_ge(s_v.h, k.chain2_hd)
        sync.wait_ge(k.s_so[0].h, k.s_so[0].v)
        sync.wait_ge(k.s_so[1].h, k.s_so[1].v)

    for t in range(NT):
        par = t % 2

        @block.sync
        def _(sync, t=t, par=par):
            if t >= 2:
                sync.wait_ge(s_v.h, tile_v_done[t - 2])
            sync.dma_start(out=otile[:, par * VD:(par + 1) * VD],
                           in_=k.dr["sp_o"][128 * t:128 * (t + 1), :]).then_inc(s_lo[par].inc(16), 16)

        vprev_g = s_v.v

        @block.tensor
        def _(tt_, t=t, par=par):
            tt_.wait_ge(s_lwg2.h, s_lwg2.v)
            tt_.wait_ge(s_v.h, vprev_g)
            for nc3 in range(3):
                for kc in range(8):
                    mi = tt_.matmul(pb[nc3][:, 0:512],
                                    k.xs[:, kc * TOK + 128 * t:kc * TOK + 128 * (t + 1)],
                                    wgb[:, kc * VD + 512 * nc3:kc * VD + 512 * (nc3 + 1)],
                                    start=(kc == 0), stop=(kc == 7))
            mi.then_inc(s_mm.inc(), 1)

        mg = s_mm.v

        @block.scalar
        def _(sc, t=t, par=par):
            sc.wait_ge(s_mm.h, mg)
            if t == 0:
                sc.wait_ge(s_sc.h, rs_done_sc)
            for nc3 in range(3):
                si = sc.activation(out=gs[:, par * VD + 512 * nc3:par * VD + 512 * (nc3 + 1)],
                                   in_=pb[nc3][:, 0:512], func=AF.Silu)
            si.then_inc(s_sc.inc(), 1)

        sg = s_sc.v

        @block.vector
        def _(v, t=t, par=par):
            v.wait_ge(s_sc.h, sg)
            v.wait_ge(s_lo[par].h, s_lo[par].v)
            if t >= 2:
                v.wait_ge(s_og[par].h, 16 * (t // 2))
            for h in range(HH):
                vi = v.scalar_tensor_tensor(out=t1s[:, DV * h:DV * (h + 1)],
                                            in0=otile[:, par * VD + DV * h:par * VD + DV * (h + 1)],
                                            scalar=k.stat[:, 4 * t + h:4 * t + h + 1],
                                            in1=k.gnwb[:, :], op0=OP.mult, op1=OP.mult)
            vi.then_inc(s_v.inc(), 1)
            v.wait_ge(s_v.h, s_v.v)
            v.tensor_tensor(out=og[:, par * VD:(par + 1) * VD], in0=t1s[:, :],
                            in1=gs[:, par * VD:(par + 1) * VD], op=OP.mult).then_inc(s_v.inc(), 1)
        tile_v_done[t] = s_v.v

        @block.gpsimd
        def _(g, t=t, par=par):
            g.wait_ge(s_v.h, tile_v_done[t])
            g.dma_start(out=k.dr["sp_og"][128 * t:128 * (t + 1), :],
                        in_=og[:, par * VD:(par + 1) * VD]).then_inc(s_og[par].inc(16), 16)

    c1_v = s_v.v
    c1_og = (s_og[0].v, s_og[1].v)

    scope1.close()
    k.scope_xs.close()
    scope2 = ExitStack()
    k.scope2 = scope2
    sb2 = lambda *a: scope2.enter_context(nc.sbuf_tensor(*a))
    wsob = sb2("wsob", [128, 12 * DM], BF16)
    woutb = sb2("woutb", [128, 16 * DM], BF16)
    ogfm = sb2("ogfm", [128, 12 * TOK], BF16)
    ybuf = sb2("ybuf", [128, 8 * TOK], BF16)
    xres = sb2("xres", [128, 2 * DM], F32)
    pre1 = sb2("pre1", [128, DM], F32)
    sqscr = sb2("sqscr", [128, DM], F32)
    x1buf = sb2("x1buf", [128, 2 * DM], F32)
    lnscr = sb2("lnscr", [128, 16], F32)
    s_lso = k.sem("s_lso")
    s_ogt = k.sem("s_ogt")
    s_lxr = [k.sem("s_lxr0"), k.sem("s_lxr1")]
    s_x1 = [k.sem("s_x10"), k.sem("s_x11")]

    @block.gpsimd
    def _(g):
        g.wait_ge(s_v.h, c1_v)
        for kc in range(12):
            g.dma_start(out=wsob[:, kc * DM:(kc + 1) * DM],
                        in_=k.d["W_so"][128 * kc:128 * (kc + 1), :]).then_inc(s_lso.inc(16), 16)
        for kc in range(16):
            g.dma_start(out=woutb[:, kc * DM:(kc + 1) * DM],
                        in_=k.d["W_out"][128 * kc:128 * (kc + 1), :]).then_inc(s_lso.inc(16), 16)

    @block.sync
    def _(sync):
        sync.wait_ge(s_og[0].h, c1_og[0])
        sync.wait_ge(s_og[1].h, c1_og[1])
        sync.wait_ge(s_v.h, c1_v)
        for ic in range(12):
            sync.dma_start(out=ogfm[:, ic * TOK:(ic + 1) * TOK],
                           in_=k.dr["sp_og"][0:TOK, 128 * ic:128 * (ic + 1)],
                           transpose=True).then_inc(s_ogt.inc(16), 16)

    yjobs = [(oc, tg) for oc in range(8) for tg in range(4)]
    yev = {}
    for i, (oc, tg) in enumerate(yjobs):
        bank = pb[i % 2]

        @block.tensor
        def _(tt_, i=i, oc=oc, tg=tg, bank=bank):
            if i == 0:
                tt_.wait_ge(s_lso.h, s_lso.v)
                tt_.wait_ge(s_ogt.h, s_ogt.v)
            if i >= 2:
                tt_.wait_ge(s_v.h, yev[i - 2])
            for kc in range(12):
                mi = tt_.matmul(bank[:, 0:512],
                                wsob[:, kc * DM + oc * 128:kc * DM + (oc + 1) * 128],
                                ogfm[:, kc * TOK + 512 * tg:kc * TOK + 512 * (tg + 1)],
                                start=(kc == 0), stop=(kc == 11))
            mi.then_inc(s_mm.inc(), 1)

        my = s_mm.v

        @block.vector
        def _(v, i=i, oc=oc, tg=tg, bank=bank, my=my):
            v.wait_ge(s_mm.h, my)
            v.tensor_copy(out=ybuf[:, oc * TOK + 512 * tg:oc * TOK + 512 * (tg + 1)],
                          in_=bank[:, 0:512]).then_inc(s_v.inc(), 1)
        yev[i] = s_v.v

    y_done_v = s_v.v

    x1_v_done = {}
    for t in range(NT):
        par = t % 2

        @block.sync
        def _(sync, t=t, par=par):
            if t >= 2:
                sync.wait_ge(s_v.h, x1_v_done[t - 2])
            sync.dma_start(out=xres[:, par * DM:(par + 1) * DM],
                           in_=k.d["x_pad"][128 * t + 1:128 * (t + 1) + 1, :]).then_inc(s_lxr[par].inc(16), 16)

        vprev_a = s_v.v

        @block.tensor
        def _(tt_, t=t, par=par):
            tt_.wait_ge(s_v.h, vprev_a)
            for nc2 in range(2):
                for kc in range(16):
                    srcb = ybuf if kc < 8 else k.zs
                    kcc = kc % 8
                    mi = tt_.matmul(pb[2 + 2 * par + nc2][:, 0:512],
                                    srcb[:, kcc * TOK + 128 * t:kcc * TOK + 128 * (t + 1)],
                                    woutb[:, kc * DM + 512 * nc2:kc * DM + 512 * (nc2 + 1)],
                                    start=(kc == 0), stop=(kc == 15))
            mi.then_inc(s_mm.inc(), 1)

        ma = s_mm.v

        @block.vector
        def _(v, t=t, par=par, ma=ma):
            v.wait_ge(s_mm.h, ma)
            v.wait_ge(s_lxr[par].h, s_lxr[par].v)
            if t >= 2:
                v.wait_ge(s_x1[par].h, 16 * (t // 2))
            v.tensor_tensor(out=pre1[:, 0:512], in0=pb[2 + 2 * par][:, 0:512],
                            in1=xres[:, par * DM:par * DM + 512], op=OP.add)
            v.tensor_tensor(out=pre1[:, 512:1024], in0=pb[3 + 2 * par][:, 0:512],
                            in1=xres[:, par * DM + 512:par * DM + 1024], op=OP.add).then_inc(s_v.inc(), 1)
            v.wait_ge(s_v.h, s_v.v)
            v.tensor_reduce(out=lnscr[:, 0:1], in_=pre1[:, :], axis=AX, op=OP.add).then_inc(s_v.inc(), 1)

        vpre = s_v.v

        @block.scalar
        def _(sc, t=t):
            sc.wait_ge(s_v.h, vpre)
            sc.wait_ge(s_sc.h, s_sc.v)
            sc.activation(out=sqscr[:, :], in_=pre1[:, :], func=AF.Square,
                          accum_out=lnscr[:, 1:2]).then_inc(s_sc.inc(), 1)

        sqd = s_sc.v

        @block.vector
        def _(v, t=t):
            v.wait_ge(s_sc.h, sqd)
            v.tensor_scalar(out=lnscr[:, 2:3], in0=lnscr[:, 0:1], scalar1=1.0 / DM,
                            scalar2=None, op0=OP.mult)
            v.tensor_scalar(out=lnscr[:, 3:4], in0=lnscr[:, 1:2], scalar1=1.0 / DM,
                            scalar2=None, op0=OP.mult).then_inc(s_v.inc(), 1)
            v.wait_ge(s_v.h, s_v.v)
            v.tensor_tensor(out=lnscr[:, 4:5], in0=lnscr[:, 2:3], in1=lnscr[:, 2:3], op=OP.mult).then_inc(s_v.inc(), 1)
            v.wait_ge(s_v.h, s_v.v)
            v.tensor_tensor(out=lnscr[:, 5:6], in0=lnscr[:, 3:4], in1=lnscr[:, 4:5],
                            op=OP.subtract).then_inc(s_v.inc(), 1)

        vvar = s_v.v

        @block.scalar
        def _(sc, t=t):
            sc.wait_ge(s_v.h, vvar)
            sc.activation(out=lnscr[:, 6:7], in_=lnscr[:, 5:6], func=AF.Ln,
                          bias=k.fconst[:, 1:2]).then_inc(s_sc.inc(), 1)
            sc.wait_ge(s_sc.h, s_sc.v)
            sc.activation(out=lnscr[:, 7:8], in_=lnscr[:, 6:7], func=AF.Exp,
                          scale=-0.5).then_inc(s_sc.inc(), 1)

        srs = s_sc.v

        @block.vector
        def _(v, t=t, par=par):
            v.wait_ge(s_sc.h, srs)
            v.tensor_tensor(out=lnscr[:, 8:9], in0=lnscr[:, 2:3], in1=lnscr[:, 7:8], op=OP.mult).then_inc(s_v.inc(), 1)
            v.wait_ge(s_v.h, s_v.v)
            v.tensor_scalar(out=lnscr[:, 9:10], in0=lnscr[:, 8:9], scalar1=-1.0,
                            scalar2=None, op0=OP.mult).then_inc(s_v.inc(), 1)
            v.wait_ge(s_v.h, s_v.v)
            v.tensor_scalar(out=x1buf[:, par * DM:(par + 1) * DM], in0=pre1[:, :],
                            scalar1=lnscr[:, 7:8], scalar2=lnscr[:, 9:10],
                            op0=OP.mult, op1=OP.add).then_inc(s_v.inc(), 1)
            v.wait_ge(s_v.h, s_v.v)
            v.tensor_reduce(out=k.stat[:, 160 + t:161 + t], in_=x1buf[:, par * DM:(par + 1) * DM],
                            axis=AX, op=OP.max, apply_absolute_value=True).then_inc(s_v.inc(), 1)
        x1_v_done[t] = s_v.v

        @block.gpsimd
        def _(g, t=t, par=par):
            g.wait_ge(s_v.h, x1_v_done[t])
            g.dma_start(out=k.dr["sp_x1"][128 * t:128 * (t + 1), :],
                        in_=x1buf[:, par * DM:(par + 1) * DM]).then_inc(s_x1[par].inc(16), 16)

    k.epi_v = s_v.v
    k.epi_x1 = (s_x1[0].v, s_x1[1].v)
    k.s_x1_sems = s_x1

    dbx1 = k.dbg_tensor("x1", [TOK, DM])
    if dbx1 is not None:
        @block.gpsimd
        def _(g):
            g.wait_ge(s_x1[0].h, s_x1[0].v)
            g.wait_ge(s_x1[1].h, s_x1[1].v)
            g.dma_start(out=dbx1, in_=k.dr["sp_x1"][:, :]).then_inc(k.s_sp.inc(16), 16)


def emit_ffn(k):
    nc, block, pb = k.nc, k.block, k.pb
    s_mm, s_v, s_sc = k.s_mm, k.s_v, k.s_sc

    def agather(cin, cout, sem):
        k._agctr = getattr(k, "_agctr", 0) + 1
        csem = k.sem(f"s_agc{k._agctr}")
        k.block.materialize(k.realblock)

        @k.realblock.gpsimd
        def _(g):
            g.wait_ge(sem.h, sem.v)
            g.collective_compute(
                "AllGather", mybir.AluOpType.bypass,
                replica_groups=[[0, 1, 2, 3, 4, 5, 6, 7]],
                ins=[cin.opt()], outs=[cout.opt()],
            ).then_inc(csem.inc(1), 1)
        return csem

    def bcast_scalar(pairs, waitv):
        @block.tensor
        def _(t):
            t.wait_ge(s_v.h, waitv)
            for i, (srcc, _) in enumerate(pairs):
                mi = t.matmul(pb[6][:, 8 + i:9 + i], k.consts[0:1, 128:256],
                              k.sval[0:1, srcc:srcc + 1], start=True, stop=True)
            mi.then_inc(s_mm.inc(), 1)
        mmv = s_mm.v

        @block.vector
        def _(v):
            v.wait_ge(s_mm.h, mmv)
            for i, (_, dstc) in enumerate(pairs):
                vi = v.tensor_copy(out=k.sval[:, dstc:dstc + 1], in_=pb[6][:, 8 + i:9 + i])
            vi.then_inc(s_v.inc(), 1)
        return s_v.v

    s_ag1 = k.sem("s_ag1")
    vx = s_v.v

    @block.vector
    def _(v):
        v.wait_ge(s_v.h, vx)
        v.tensor_reduce(out=k.sval[:, 20:21], in_=k.stat[:, 160:176], axis=AX,
                        op=OP.max).then_inc(s_v.inc(), 1)

    vmx = s_v.v

    @block.tensor
    def _(t):
        t.wait_ge(s_v.h, vmx)
        t.transpose(pb[6][0:1, 0:128], k.sval[:, 20:21], k.consts[:, 0:128]).then_inc(s_mm.inc(), 1)

    mmx = s_mm.v

    @block.vector
    def _(v):
        v.wait_ge(s_mm.h, mmx)
        v.memset(k.sval[0:1, 22:23], 0.0)
        v.tensor_reduce(out=k.sval[0:1, 21:22], in_=pb[6][0:1, 0:128], axis=AX,
                        op=OP.max).then_inc(s_v.inc(), 1)

    vgm = s_v.v

    @block.gpsimd
    def _(g):
        g.wait_ge(s_v.h, vgm)
        g.dma_start(out=k.dr["cc_s0_in"], in_=k.sval[0:1, 21:23]).then_inc(s_ag1.inc(16), 16)

    s_ag1c = agather(k.dr["cc_s0_in"], k.dr["cc_s0_out"], s_ag1)

    def max_from_ag(cout, agsem, dstcol, n=8):
        k._rdctr = getattr(k, "_rdctr", 0) + 1
        s_rd = k.sem(f"s_rd{dstcol}_{k._rdctr}")

        @block.sync
        def _(sync):
            sync.wait_ge(agsem.h, agsem.v)
            sync.dma_start(out=k.sval[0:1, 24:24 + 2 * n], in_=cout).then_inc(s_rd.inc(16), 16)

        @block.vector
        def _(v):
            v.wait_ge(s_rd.h, s_rd.v)
            g2 = k.sval[0:1, 24:24 + 2 * n].rearrange("p (a b) -> p b a", b=2)
            v.tensor_reduce(out=k.sval[0:1, dstcol:dstcol + 1], in_=g2[:, 0:1, :], axis=AX, op=OP.max)
            v.tensor_reduce(out=k.sval[0:1, dstcol + 1:dstcol + 2], in_=g2[:, 1:2, :],
                            axis=AX, op=OP.max).then_inc(s_v.inc(), 1)
        return s_v.v

    max_from_ag(k.dr["cc_s0_out"], s_ag1c, 16)

    @block.vector
    def _(v):
        v.wait_ge(s_v.h, s_v.v)
        v.tensor_scalar(out=k.sval[0:1, 19:20], in0=k.sval[0:1, 16:17],
                        scalar1=1.0 / QMAX, scalar2=None, op0=OP.mult).then_inc(s_v.inc(), 1)
        v.wait_ge(s_v.h, s_v.v)
        v.reciprocal(out=k.sval[0:1, 20:21], in_=k.sval[0:1, 19:20]).then_inc(s_v.inc(), 1)

    bc_sx_v = bcast_scalar([(19, 8), (20, 9)], s_v.v)

    k.scope2.close()
    scopeF = ExitStack()
    k.scopeF = scopeF
    wsh = scopeF.enter_context(nc.sbuf_tensor("wsh", [128, FI], F32))
    wi16 = scopeF.enter_context(nc.sbuf_tensor("wi16", [128, FI], I16))
    wdqb = scopeF.enter_context(nc.sbuf_tensor("wdqb", [128, FI], BF16))
    wsc = scopeF.enter_context(nc.sbuf_tensor("wsc", [128, 3 * 96], F32))
    wdqt = scopeF.enter_context(nc.sbuf_tensor("wdqt", [128, FI], BF16))
    s_lsh = k.sem("s_lsh")
    s_wq = k.sem("s_wq")
    block.materialize(k.realblock)
    dq = k.realblock

    for wt in range(7):
        if wt < 3:
            srcap = k.d["wg_shard"][128 * wt:128 * (wt + 1), :]
            ncols, nblk, dstrow = DM, 32, 128 * wt
        elif wt < 6:
            srcap = k.d["wu_shard"][128 * (wt - 3):128 * (wt - 2), :]
            ncols, nblk, dstrow = DM, 32, 384 + 128 * (wt - 3)
        else:
            srcap = k.d["wd_shard"]
            ncols, nblk, dstrow = FI, 96, 768

        vW = s_v.v

        @dq.sync
        def _(sync, wt=wt, srcap=srcap, ncols=ncols, vW=vW):
            if wt > 0:
                sync.wait_ge(s_v.h, vW)
            else:
                sync.wait_ge(s_v.h, k.epi_v)
                sync.wait_ge(k.s_x1_sems[0].h, k.epi_x1[0])
                sync.wait_ge(k.s_x1_sems[1].h, k.epi_x1[1])
            sync.dma_start(out=wsh[:, 0:ncols], in_=srcap).then_inc(s_lsh.inc(16), 16)

        ldW = s_lsh.v

        @dq.vector
        def _(v, wt=wt, ncols=ncols, nblk=nblk, ldW=ldW):
            v.wait_ge(s_lsh.h, ldW)
            if wt == 0:
                v.wait_ge(s_v.h, k.epi_v)
            v.tensor_reduce(out=wsc[:, 0:nblk], in_=wsh[:, 0:ncols].rearrange("p (a b) -> p a b", b=32),
                            axis=AX, op=OP.max, apply_absolute_value=True).then_inc(s_v.inc(), 1)
            v.wait_ge(s_v.h, s_v.v)
            v.tensor_scalar(out=wsc[:, 96:96 + nblk], in0=wsc[:, 0:nblk], scalar1=1.0 / 7.0,
                            scalar2=1e-12, op0=OP.mult, op1=OP.max).then_inc(s_v.inc(), 1)
            v.wait_ge(s_v.h, s_v.v)
            v.reciprocal(out=wsc[:, 192:192 + nblk], in_=wsc[:, 96:96 + nblk]).then_inc(s_v.inc(), 1)
            v.wait_ge(s_v.h, s_v.v)
            for b in range(nblk):
                vi = v.tensor_scalar(out=wi16[:, 32 * b:32 * (b + 1)], in0=wsh[:, 32 * b:32 * (b + 1)],
                                     scalar1=wsc[:, 192 + b:193 + b], scalar2=None, op0=OP.mult)
            vi.then_inc(s_v.inc(), 1)
            v.wait_ge(s_v.h, s_v.v)
            v.wait_ge(s_mm.h, s_mm.v)
            for b in range(nblk):
                vi = v.tensor_scalar(out=wdqb[:, 32 * b:32 * (b + 1)], in0=wi16[:, 32 * b:32 * (b + 1)],
                                     scalar1=wsc[:, 96 + b:97 + b], scalar2=None, op0=OP.mult)
            vi.then_inc(s_v.inc(), 1)

        wq_v = s_v.v

        nic = ncols // 128
        mm0 = s_mm.v
        v0 = s_v.v

        @dq.tensor
        def _(t, wq_v=wq_v, nic=nic, mm0=mm0, v0=v0):
            t.wait_ge(s_v.h, wq_v)
            for icc in range(nic):
                if icc >= 2:
                    t.wait_ge(s_v.h, v0 + icc - 1)
                t.transpose(pb[6 + icc % 2][:, 0:64].bitcast(BF16),
                            wdqb[:, 128 * icc:128 * (icc + 1)],
                            k.constsb[:, 256:384]).then_inc(s_mm.inc(), 1)

        @dq.vector
        def _(v, wt=wt, nic=nic, mm0=mm0):
            v.wait_ge(s_wq.h, 16 * wt)
            for icc in range(nic):
                v.wait_ge(s_mm.h, mm0 + icc + 1)
                v.tensor_copy(out=wdqt[:, 128 * icc:128 * (icc + 1)],
                              in_=pb[6 + icc % 2][:, 0:64].bitcast(BF16)).then_inc(s_v.inc(), 1)

        wqt_v = s_v.v

        @dq.gpsimd
        def _(g, wt=wt, nic=nic, wqt_v=wqt_v):
            g.wait_ge(s_v.h, wqt_v)
            flat1d = k.dr["cc_wq_in"].rearrange("r c -> (r c)")
            if wt < 3:
                dv = flat1d[0:393216].rearrange("(icc p o) -> p icc o", p=128, o=384)
                g.dma_start(out=dv[:, :, 128 * wt:128 * (wt + 1)][:, 0:8, :],
                            in_=wdqt[:, 0:1024].rearrange("p (icc o) -> p icc o", o=128),
                            allow_slow_non_contiguous=True).then_inc(s_wq.inc(16), 16)
            elif wt < 6:
                dv = flat1d[393216:786432].rearrange("(icc p o) -> p icc o", p=128, o=384)
                g.dma_start(out=dv[:, :, 128 * (wt - 3):128 * (wt - 2)][:, 0:8, :],
                            in_=wdqt[:, 0:1024].rearrange("p (icc o) -> p icc o", o=128),
                            allow_slow_non_contiguous=True).then_inc(s_wq.inc(16), 16)
            else:
                dv = flat1d[786432:1179648].rearrange("(icc p o) -> p icc o", p=128, o=128)
                g.dma_start(out=dv[:, 0:24, :],
                            in_=wdqt[:, 0:3072].rearrange("p (icc o) -> p icc o", o=128),
                            allow_slow_non_contiguous=True).then_inc(s_wq.inc(16), 16)

    k.wq_dma_v = s_wq.v
    k.s_wqc = agather(k.dr["cc_wq_in"], k.dr["cc_wq_out"], s_wq)
    k.s_wq = s_wq
    scopeF.close()

    scope3 = ExitStack()
    k.scopeF = scope3
    sb3 = lambda *a: scope3.enter_context(nc.sbuf_tensor(*a))
    xqfm = sb3("xqfm", [128, 8 * TOK], BF16)
    wupT = sb3("wupT", [128, 8 * FI], BF16)
    wgT = sb3("wgT", [128, 8 * FI], BF16)
    x1t = sb3("x1t", [128, 2 * DM], F32)
    xqi = sb3("xqi", [128, DM], I16)
    xqb = sb3("xqb", [128, 2 * DM], BF16)
    evb = sb3("evb", [128, 2 * FI], F32)
    mxa = sb3("mxa", [128, 8], F32)
    s_lx1 = [k.sem("s_lx1a"), k.sem("s_lx1b")]
    s_xq = [k.sem("s_xqa"), k.sem("s_xqb")]
    s_xqt = k.sem("s_xqt")
    s_lwt = k.sem("s_lwt")
    s_spu = [k.sem("s_spua"), k.sem("s_spub")]
    s_spg = [k.sem("s_spga"), k.sem("s_spgb")]

    @block.scalar
    def _(sc):
        sc.wait_ge(k.s_wqc.h, k.s_wqc.v)
        ccf = k.dr["cc_wq_out"].rearrange("(r z) c -> r (z c)", z=1152)
        for ic in range(8):
            gsrc = ccf[:, 0:393216].rearrange("r (i o) -> i r o", o=384)
            usrc = ccf[:, 393216:786432].rearrange("r (i o) -> i r o", o=384)
            sc.dma_start(out=wgT[:, ic * FI:(ic + 1) * FI].rearrange("p (r o) -> p r o", o=384),
                         in_=gsrc[128 * ic:128 * (ic + 1), :, :]).then_inc(s_lwt.inc(16), 16)
            sc.dma_start(out=wupT[:, ic * FI:(ic + 1) * FI].rearrange("p (r o) -> p r o", o=384),
                         in_=usrc[128 * ic:128 * (ic + 1), :, :]).then_inc(s_lwt.inc(16), 16)

    xq_v = {}
    for t in range(NT):
        par = t % 2

        @block.sync
        def _(sync, t=t, par=par):
            if t >= 2:
                sync.wait_ge(s_v.h, xq_v[t - 2])
            elif t == 0:
                sync.wait_ge(k.s_wq.h, k.wq_dma_v)
            sync.dma_start(out=x1t[:, par * DM:(par + 1) * DM],
                           in_=k.dr["sp_x1"][128 * t:128 * (t + 1), :]).then_inc(s_lx1[par].inc(16), 16)

        vt0 = s_v.v

        @block.vector
        def _(v, t=t, par=par, vt0=vt0):
            v.wait_ge(s_lx1[par].h, s_lx1[par].v)
            v.wait_ge(s_v.h, vt0)
            if t == 0:
                v.wait_ge(s_v.h, bc_sx_v)
            if t >= 2:
                v.wait_ge(s_xq[par].h, 16 * (t // 2))
            v.tensor_scalar(out=xqi[:, :], in0=x1t[:, par * DM:(par + 1) * DM],
                            scalar1=k.sval[:, 9:10], scalar2=None, op0=OP.mult).then_inc(s_v.inc(), 1)
            v.wait_ge(s_v.h, s_v.v)
            v.tensor_copy(out=xqb[:, par * DM:(par + 1) * DM], in_=xqi[:, :]).then_inc(s_v.inc(), 1)
        xq_v[t] = s_v.v

        @block.gpsimd
        def _(g, t=t, par=par):
            g.wait_ge(s_v.h, xq_v[t])
            g.dma_start(out=k.dr["sp_xq"][128 * t:128 * (t + 1), :],
                        in_=xqb[:, par * DM:(par + 1) * DM]).then_inc(s_xq[par].inc(16), 16)

    @block.scalar
    def _(sc):
        sc.wait_ge(s_xq[0].h, s_xq[0].v)
        sc.wait_ge(s_xq[1].h, s_xq[1].v)
        for ic in range(8):
            sc.dma_start(out=xqfm[:, ic * TOK:(ic + 1) * TOK],
                         in_=k.dr["sp_xq"][0:TOK, 128 * ic:128 * (ic + 1)],
                         transpose=True).then_inc(s_xqt.inc(16), 16)

    vinit = s_v.v

    @block.vector
    def _(v):
        v.wait_ge(s_v.h, vinit)
        v.memset(mxa[:, 0:4], 0.0).then_inc(s_v.inc(), 1)

    k.block.set_direct("tensor", "vector", "gpsimd", "sync", "scalar")
    vred = {}
    evc = {}
    runpar = [0, 0]
    nj = 0
    for t in range(NT):
        for w in range(2):
            par = t % 2
            wT = wupT if w == 0 else wgT
            spd = k.dr["sp_up"] if w == 0 else k.dr["sp_gate"]
            spsem = s_spu if w == 0 else s_spg
            for nc6 in range(6):
                bank = pb[nj % 2]

                @block.tensor
                def _(tt_, t=t, w=w, nc6=nc6, nj=nj, wT=wT, bank=bank):
                    if nj == 0:
                        tt_.wait_ge(s_xqt.h, s_xqt.v)
                        tt_.wait_ge(s_lwt.h, s_lwt.v)
                    if nj >= 2:
                        tt_.wait_ge(s_v.h, vred[nj - 2])
                        tt_.wait_ge(s_sc.h, evc[nj - 2])
                    for kc in range(8):
                        mi = tt_.matmul(bank[:, 0:512],
                                        xqfm[:, kc * TOK + 128 * t:kc * TOK + 128 * (t + 1)],
                                        wT[:, kc * FI + 512 * nc6:kc * FI + 512 * (nc6 + 1)],
                                        start=(kc == 0), stop=(kc == 7))
                    mi.then_inc(s_mm.inc(), 1)

                mmj = s_mm.v
                rsrc, rdst = runpar[w], 1 - runpar[w]
                runpar[w] = rdst

                vj0 = s_v.v

                @block.vector
                def _(v, w=w, bank=bank, mmj=mmj, rsrc=rsrc, rdst=rdst, vj0=vj0):
                    v.wait_ge(s_mm.h, mmj)
                    v.wait_ge(s_v.h, vj0)
                    v.tensor_reduce(out=mxa[:, 4:5], in_=bank[:, 0:512], axis=AX,
                                    op=OP.max, apply_absolute_value=True).then_inc(s_v.inc(), 1)
                    v.wait_ge(s_v.h, s_v.v)
                    v.tensor_tensor(out=mxa[:, 2 * rdst + w:2 * rdst + w + 1],
                                    in0=mxa[:, 2 * rsrc + w:2 * rsrc + w + 1],
                                    in1=mxa[:, 4:5], op=OP.max).then_inc(s_v.inc(), 1)
                vred[nj] = s_v.v

                jpar = (2 * t + w) % 2

                @block.scalar
                def _(sc, t=t, w=w, jpar=jpar, nc6=nc6, bank=bank, mmj=mmj):
                    sc.wait_ge(s_mm.h, mmj)
                    if nc6 == 0 and t >= 1:
                        sc.wait_ge(spsem[(t - 1) % 2].h, 16 * ((t - 1) // 2 + 1))
                    sc.copy(out=evb[:, jpar * FI + 512 * nc6:jpar * FI + 512 * (nc6 + 1)],
                            in_=bank[:, 0:512]).then_inc(s_sc.inc(), 1)
                evc[nj] = s_sc.v
                nj += 1

            jpar = (2 * t + w) % 2

            @block.gpsimd
            def _(g, t=t, jpar=jpar, spd=spd, spsem=spsem, scj=evc[nj - 1]):
                g.wait_ge(s_sc.h, scj)
                g.dma_start(out=spd[128 * t:128 * (t + 1), :],
                            in_=evb[:, jpar * FI:(jpar + 1) * FI]).then_inc(spsem[t % 2].inc(16), 16)


    s_ag2 = k.sem("s_ag2")
    vfin = s_v.v

    @block.vector
    def _(v):
        v.wait_ge(s_v.h, vfin)
        v.tensor_copy(out=k.sval[0:1, 24:25], in_=mxa[0:1, 0:1])

    @block.tensor
    def _(t):
        t.wait_ge(s_v.h, vfin)
        t.transpose(pb[6][0:2, 0:128], mxa[:, 0:2], k.consts[:, 0:128]).then_inc(s_mm.inc(), 1)

    mfin = s_mm.v

    @block.vector
    def _(v):
        v.wait_ge(s_mm.h, mfin)
        v.tensor_reduce(out=k.sval[0:2, 40:41], in_=pb[6][0:2, 0:128], axis=AX,
                        op=OP.max).then_inc(s_v.inc(), 1)

    vag2a = s_v.v

    @block.tensor
    def _(t):
        t.wait_ge(s_v.h, vag2a)
        t.transpose(pb[7][0:1, 0:2], k.sval[0:2, 40:41], k.consts[0:2, 0:2]).then_inc(s_mm.inc(), 1)

    mag2b = s_mm.v

    @block.vector
    def _(v):
        v.wait_ge(s_mm.h, mag2b)
        v.tensor_copy(out=k.sval[0:1, 21:23], in_=pb[7][0:1, 0:2]).then_inc(s_v.inc(), 1)

    vag2 = s_v.v

    @block.gpsimd
    def _(g):
        g.wait_ge(s_v.h, vag2)
        g.dma_start(out=k.dr["cc_s1_in"], in_=k.sval[0:1, 21:23]).then_inc(s_ag2.inc(16), 16)

    s_ag2c = agather(k.dr["cc_s1_in"], k.dr["cc_s1_out"], s_ag2)
    max_from_ag(k.dr["cc_s1_out"], s_ag2c, 16)

    @block.vector
    def _(v):
        v.wait_ge(s_v.h, s_v.v)
        v.tensor_tensor(out=k.sval[0:1, 19:20],
                        in0=k.sval[0:1, 16:17], in1=k.sval[0:1, 8:9], op=OP.mult)
        v.tensor_tensor(out=k.sval[0:1, 20:21], in0=k.sval[0:1, 17:18], in1=k.sval[0:1, 8:9],
                        op=OP.mult).then_inc(s_v.inc(), 1)
        v.wait_ge(s_v.h, s_v.v)
        v.tensor_scalar(out=k.sval[0:1, 21:22], in0=k.sval[0:1, 19:20],
                        scalar1=1.0 / QMAX, scalar2=None, op0=OP.mult)
        v.tensor_scalar(out=k.sval[0:1, 22:23], in0=k.sval[0:1, 20:21],
                        scalar1=1.0 / QMAX, scalar2=None, op0=OP.mult).then_inc(s_v.inc(), 1)
        v.wait_ge(s_v.h, s_v.v)
        v.reciprocal(out=k.sval[0:1, 30:31], in_=k.sval[0:1, 21:22])
        v.reciprocal(out=k.sval[0:1, 31:32], in_=k.sval[0:1, 22:23]).then_inc(s_v.inc(), 1)
        v.wait_ge(s_v.h, s_v.v)
        v.tensor_tensor(out=k.sval[0:1, 23:24], in0=k.sval[0:1, 8:9], in1=k.sval[0:1, 30:31],
                        op=OP.mult)
        v.tensor_tensor(out=k.sval[0:1, 24:25], in0=k.sval[0:1, 8:9], in1=k.sval[0:1, 31:32],
                        op=OP.mult).then_inc(s_v.inc(), 1)

    bc2 = bcast_scalar([(21, 10), (22, 11), (23, 12), (24, 13)], s_v.v)

    scope3.close()
    k.scope_zs.close()
    scope4 = ExitStack()
    k.scopeF = scope4
    sb4 = lambda *a: scope4.enter_context(nc.sbuf_tensor(*a))
    gint = sb4("gint", [128, NT * FI], I16)
    gtile = sb4("gtile", [128, 2 * FI], BF16)
    sgs = sb4("sgs", [128, 2 * FI], F32)
    sgu = sb4("sgu", [128, FI], U16)
    pts = sb4("pts", [128, FI], F32)
    upt = sb4("upt", [128, 2 * FI], BF16)
    upi = sb4("upi", [128, 2 * FI], I16)
    ots = sb4("ots", [128, 2 * FI], F32)
    mxb = sb4("mxb", [128, 8], F32)
    s_lg = [k.sem("s_lga"), k.sem("s_lgb")]
    s_lu = [k.sem("s_lua"), k.sem("s_lub")]
    s_ot = [k.sem("s_ota"), k.sem("s_otb")]

    vinitB = s_v.v

    @block.vector
    def _(v):
        v.wait_ge(s_v.h, vinitB)
        v.wait_ge(s_spu[0].h, s_spu[0].v)
        v.wait_ge(s_spu[1].h, s_spu[1].v)
        v.wait_ge(s_spg[0].h, s_spg[0].v)
        v.wait_ge(s_spg[1].h, s_spg[1].v)
        v.wait_ge(s_xq[0].h, s_xq[0].v)
        v.wait_ge(s_xq[1].h, s_xq[1].v)
        v.memset(mxb[:, 0:4], 0.0).then_inc(s_v.inc(), 1)
    vinitB2 = s_v.v

    runp = [0, 0]
    btile_v = {}
    ptsv = {}
    for t in range(NT):
        par = t % 2

        @block.sync
        def _(sync, t=t, par=par):
            if t >= 2:
                sync.wait_ge(s_v.h, btile_v[t - 2])
            else:
                sync.wait_ge(s_v.h, vinitB2)
            sync.dma_start(out=gtile[:, par * FI:(par + 1) * FI],
                           in_=k.dr["sp_gate"][128 * t:128 * (t + 1), :]).then_inc(s_lg[par].inc(16), 16)

        ldg = s_lg[par].v
        vprevB = btile_v[t - 2] if t >= 2 else vinitB2
        ptsvB = ptsv[t - 1] if t >= 1 else vinitB2

        @block.scalar
        def _(sc, t=t, par=par, ldg=ldg, vprevB=vprevB, ptsvB=ptsvB):
            sc.wait_ge(s_lg[par].h, ldg)
            sc.wait_ge(s_sc.h, s_sc.v)
            sc.wait_ge(s_v.h, vprevB)
            sc.activation(out=gint[:, t * FI:(t + 1) * FI], in_=gtile[:, par * FI:(par + 1) * FI],
                          func=AF.Copy, scale=k.sval[:, 13:14]).then_inc(s_sc.inc(), 1)
            sc.wait_ge(s_sc.h, s_sc.v)
            sc.activation(out=sgs[:, par * FI:(par + 1) * FI], in_=gint[:, t * FI:(t + 1) * FI],
                          func=AF.Sigmoid, scale=k.sval[:, 11:12]).then_inc(s_sc.inc(), 1)
            sc.wait_ge(s_sc.h, s_sc.v)
            sc.wait_ge(s_v.h, ptsvB)
            sc.activation(out=sgu[:, 0:FI], in_=sgs[:, par * FI:(par + 1) * FI],
                          func=AF.Copy, scale=65536.0).then_inc(s_sc.inc(), 1)

        scb = s_sc.v
        rs, rd = runp[0], 1 - runp[0]
        runp[0] = rd

        vb0 = s_v.v

        @block.vector
        def _(v, t=t, par=par, scb=scb, rs=rs, rd=rd, vb0=vb0):
            v.wait_ge(s_sc.h, scb)
            v.wait_ge(s_v.h, vb0)
            v.tensor_tensor(out=pts[:, :], in0=gint[:, t * FI:(t + 1) * FI],
                            in1=sgu[:, 0:FI],
                            op=OP.mult).then_inc(s_v.inc(), 1)
            ptsv[t] = s_v.v
            v.wait_ge(s_v.h, s_v.v)
            v.tensor_reduce(out=mxb[:, 4:5], in_=pts[:, :], axis=AX, op=OP.max,
                            apply_absolute_value=True).then_inc(s_v.inc(), 1)
            v.wait_ge(s_v.h, s_v.v)
            v.tensor_tensor(out=mxb[:, 2 * rd:2 * rd + 1], in0=mxb[:, 2 * rs:2 * rs + 1],
                            in1=mxb[:, 4:5], op=OP.max).then_inc(s_v.inc(), 1)
        btile_v[t] = s_v.v

    s_ag3 = k.sem("s_ag3")
    vfb = s_v.v

    @block.tensor
    def _(t):
        t.wait_ge(s_v.h, vfb)
        t.transpose(pb[6][0:1, 0:128], mxb[:, 0:1], k.consts[:, 0:128]).then_inc(s_mm.inc(), 1)

    mfb = s_mm.v

    @block.vector
    def _(v):
        v.wait_ge(s_mm.h, mfb)
        v.memset(k.sval[0:1, 22:23], 0.0)
        v.tensor_reduce(out=k.sval[0:1, 21:22], in_=pb[6][0:1, 0:128], axis=AX,
                        op=OP.max).then_inc(s_v.inc(), 1)

    vag3 = s_v.v

    @block.gpsimd
    def _(g):
        g.wait_ge(s_v.h, vag3)
        g.dma_start(out=k.dr["cc_s2_in"], in_=k.sval[0:1, 21:23]).then_inc(s_ag3.inc(16), 16)

    s_ag3c = agather(k.dr["cc_s2_in"], k.dr["cc_s2_out"], s_ag3)
    max_from_ag(k.dr["cc_s2_out"], s_ag3c, 16)

    @block.vector
    def _(v):
        v.wait_ge(s_v.h, s_v.v)
        v.tensor_scalar(out=k.sval[0:1, 26:27], in0=k.sval[0:1, 11:12],
                        scalar1=1.0 / 65536.0, scalar2=None, op0=OP.mult).then_inc(s_v.inc(), 1)
        v.wait_ge(s_v.h, s_v.v)
        v.tensor_tensor(out=k.sval[0:1, 27:28], in0=k.sval[0:1, 26:27], in1=k.sval[0:1, 16:17],
                        op=OP.mult).then_inc(s_v.inc(), 1)
        v.wait_ge(s_v.h, s_v.v)
        v.tensor_scalar(out=k.sval[0:1, 28:29], in0=k.sval[0:1, 27:28],
                        scalar1=1.0 / QMAX, scalar2=None, op0=OP.mult).then_inc(s_v.inc(), 1)
        v.wait_ge(s_v.h, s_v.v)
        v.reciprocal(out=k.sval[0:1, 32:33], in_=k.sval[0:1, 28:29]).then_inc(s_v.inc(), 1)
        v.wait_ge(s_v.h, s_v.v)
        v.tensor_tensor(out=k.sval[0:1, 29:30], in0=k.sval[0:1, 26:27], in1=k.sval[0:1, 32:33],
                        op=OP.mult).then_inc(s_v.inc(), 1)
    bc3 = bcast_scalar([(28, 14), (29, 15)], s_v.v)

    dtile_v = {}
    ptsv = {}
    for t in range(NT):
        par = t % 2

        @block.sync
        def _(sync, t=t, par=par):
            if t >= 2:
                sync.wait_ge(s_v.h, dtile_v[t - 2])
            sync.dma_start(out=upt[:, par * FI:(par + 1) * FI],
                           in_=k.dr["sp_up"][128 * t:128 * (t + 1), :]).then_inc(s_lu[par].inc(16), 16)

        ldu = s_lu[par].v
        vprevD = dtile_v[t - 2] if t >= 2 else None
        ptsvD = ptsv[t - 1] if t >= 1 else bc3

        @block.scalar
        def _(sc, t=t, par=par, ldu=ldu, vprevD=vprevD, ptsvD=ptsvD):
            sc.wait_ge(s_sc.h, s_sc.v)
            if vprevD is not None:
                sc.wait_ge(s_v.h, vprevD)
            else:
                sc.wait_ge(s_v.h, bc3)
            sc.activation(out=sgs[:, par * FI:(par + 1) * FI], in_=gint[:, t * FI:(t + 1) * FI],
                          func=AF.Sigmoid, scale=k.sval[:, 11:12]).then_inc(s_sc.inc(), 1)
            sc.wait_ge(s_sc.h, s_sc.v)
            sc.wait_ge(s_v.h, ptsvD)
            sc.activation(out=sgu[:, 0:FI], in_=sgs[:, par * FI:(par + 1) * FI],
                          func=AF.Copy, scale=65536.0).then_inc(s_sc.inc(), 1)
            sc.wait_ge(s_lu[par].h, ldu)
            sc.activation(out=upi[:, par * FI:(par + 1) * FI], in_=upt[:, par * FI:(par + 1) * FI],
                          func=AF.Copy, scale=k.sval[:, 12:13]).then_inc(s_sc.inc(), 1)

        scd = s_sc.v
        rs, rd = runp[1], 1 - runp[1]
        runp[1] = rd

        vd0 = s_v.v

        @block.vector
        def _(v, t=t, par=par, scd=scd, rs=rs, rd=rd, vd0=vd0):
            v.wait_ge(s_sc.h, scd)
            v.wait_ge(s_v.h, vd0)
            v.tensor_tensor(out=pts[:, :], in0=gint[:, t * FI:(t + 1) * FI],
                            in1=sgu[:, 0:FI],
                            op=OP.mult).then_inc(s_v.inc(), 1)
            ptsv[t] = s_v.v
            v.wait_ge(s_v.h, s_v.v)
            v.tensor_scalar(out=gint[:, t * FI:(t + 1) * FI], in0=pts[:, :],
                            scalar1=k.sval[:, 15:16], scalar2=None, op0=OP.mult).then_inc(s_v.inc(), 1)
            v.wait_ge(s_v.h, s_v.v)
            if t >= 2:
                v.wait_ge(s_ot[par].h, 16 * (t // 2))
            v.tensor_tensor(out=ots[:, par * FI:(par + 1) * FI],
                            in0=gint[:, t * FI:(t + 1) * FI],
                            in1=upi[:, par * FI:(par + 1) * FI],
                            op=OP.mult).then_inc(s_v.inc(), 1)
            v.wait_ge(s_v.h, s_v.v)
            v.tensor_reduce(out=mxb[:, 4:5], in_=ots[:, par * FI:(par + 1) * FI], axis=AX,
                            op=OP.max, apply_absolute_value=True).then_inc(s_v.inc(), 1)
            v.wait_ge(s_v.h, s_v.v)
            v.tensor_tensor(out=mxb[:, 2 * rd + 1:2 * rd + 2], in0=mxb[:, 2 * rs + 1:2 * rs + 2],
                            in1=mxb[:, 4:5], op=OP.max).then_inc(s_v.inc(), 1)
        dtile_v[t] = s_v.v

        @block.gpsimd
        def _(g, t=t, par=par):
            g.wait_ge(s_v.h, dtile_v[t])
            g.dma_start(out=k.dr["sp_oq"][128 * t:128 * (t + 1), :],
                        in_=ots[:, par * FI:(par + 1) * FI]).then_inc(s_ot[par].inc(16), 16)

    s_ag4 = k.sem("s_ag4")
    vfd = s_v.v

    @block.tensor
    def _(t):
        t.wait_ge(s_v.h, vfd)
        t.transpose(pb[6][0:1, 0:128], mxb[:, 1:2], k.consts[:, 0:128]).then_inc(s_mm.inc(), 1)

    mfd = s_mm.v

    @block.vector
    def _(v):
        v.wait_ge(s_mm.h, mfd)
        v.memset(k.sval[0:1, 22:23], 0.0)
        v.tensor_reduce(out=k.sval[0:1, 21:22], in_=pb[6][0:1, 0:128], axis=AX,
                        op=OP.max).then_inc(s_v.inc(), 1)

    vag4 = s_v.v

    @block.gpsimd
    def _(g):
        g.wait_ge(s_v.h, vag4)
        g.dma_start(out=k.dr["cc_s3_in"], in_=k.sval[0:1, 21:23]).then_inc(s_ag4.inc(16), 16)

    s_ag4c = agather(k.dr["cc_s3_in"], k.dr["cc_s3_out"], s_ag4)
    max_from_ag(k.dr["cc_s3_out"], s_ag4c, 16)

    @block.vector
    def _(v):
        v.wait_ge(s_v.h, s_v.v)
        v.tensor_tensor(out=k.sval[0:1, 26:27], in0=k.sval[0:1, 14:15], in1=k.sval[0:1, 10:11],
                        op=OP.mult).then_inc(s_v.inc(), 1)
        v.wait_ge(s_v.h, s_v.v)
        v.tensor_tensor(out=k.sval[0:1, 27:28], in0=k.sval[0:1, 26:27], in1=k.sval[0:1, 16:17],
                        op=OP.mult).then_inc(s_v.inc(), 1)
        v.wait_ge(s_v.h, s_v.v)
        v.tensor_scalar(out=k.sval[0:1, 28:29], in0=k.sval[0:1, 27:28],
                        scalar1=1.0 / QMAX, scalar2=None, op0=OP.mult).then_inc(s_v.inc(), 1)
        v.wait_ge(s_v.h, s_v.v)
        v.reciprocal(out=k.sval[0:1, 32:33], in_=k.sval[0:1, 28:29]).then_inc(s_v.inc(), 1)
        v.wait_ge(s_v.h, s_v.v)
        v.tensor_tensor(out=k.sval[0:1, 29:30], in0=k.sval[0:1, 26:27], in1=k.sval[0:1, 32:33],
                        op=OP.mult).then_inc(s_v.inc(), 1)
    bc4 = bcast_scalar([(28, 16), (29, 17)], s_v.v)

    scope4.close()
    scope5 = ExitStack()
    k.scopeF = scope5
    sb5 = lambda *a: scope5.enter_context(nc.sbuf_tensor(*a))
    ofm = sb5("ofm", [128, 24 * TOK], BF16)
    wdT = sb5("wdT", [128, 24 * DM], BF16)
    x1e = sb5("x1e", [128, 2 * DM], F32)
    pre2 = sb5("pre2", [128, DM], F32)
    sq2 = sb5("sq2", [128, DM], F32)
    outb = sb5("outb", [128, 2 * DM], F32)
    ln2s = sb5("ln2s", [128, 16], F32)
    oraw = sb5("oraw", [128, 2 * TOK], BF16)
    oq16 = sb5("oq16", [128, 2 * TOK], I16)
    s_oft = [k.sem("s_ofta"), k.sem("s_oftb")]
    s_lwd = k.sem("s_lwd")
    s_lx2 = [k.sem("s_lx2a"), k.sem("s_lx2b")]

    @block.sync
    def _(sync):
        sync.wait_ge(k.s_wqc.h, k.s_wqc.v)
        ccf = k.dr["cc_wq_out"].rearrange("(r z) c -> r (z c)", z=1152)
        dsrc = ccf[:, 786432:1179648].rearrange("r (i o) -> i r o", o=128)
        for ic in range(24):
            sync.dma_start(out=wdT[:, ic * DM:(ic + 1) * DM].rearrange("p (r o) -> p r o", o=128),
                           in_=dsrc[128 * ic:128 * (ic + 1), :, :]).then_inc(s_lwd.inc(16), 16)

    ofmcp = {}
    oqsc = {}
    for ic in range(25):
        par = ic % 2

        @block.scalar
        def _(sc, ic=ic, par=par):
            if ic < 24:
                if ic == 0:
                    sc.wait_ge(s_ot[0].h, s_ot[0].v)
                    sc.wait_ge(s_ot[1].h, s_ot[1].v)
                if ic >= 2:
                    sc.wait_ge(s_v.h, ofmcp[ic - 2])
                sc.dma_start(out=oraw[:, par * TOK:(par + 1) * TOK],
                             in_=k.dr["sp_oq"][0:TOK, 128 * ic:128 * (ic + 1)],
                             transpose=True).then_inc(s_oft[par].inc(16), 16)
            if ic >= 1:
                qp = (ic - 1) % 2
                sc.wait_ge(s_oft[qp].h, 16 * ((ic - 1) // 2 + 1))
                if ic == 1:
                    sc.wait_ge(s_v.h, bc4)
                sc.wait_ge(s_sc.h, s_sc.v)
                sc.activation(out=oq16[:, qp * TOK:(qp + 1) * TOK],
                              in_=oraw[:, qp * TOK:(qp + 1) * TOK],
                              func=AF.Copy, scale=k.sval[:, 17:18]).then_inc(s_sc.inc(), 1)
                oqsc[ic - 1] = s_sc.v

        if ic >= 1:
            @block.vector
            def _(v, ic=ic):
                qp = (ic - 1) % 2
                v.wait_ge(s_sc.h, oqsc[ic - 1])
                v.tensor_copy(out=ofm[:, (ic - 1) * TOK:ic * TOK],
                              in_=oq16[:, qp * TOK:(qp + 1) * TOK]).then_inc(s_v.inc(), 1)
            ofmcp[ic - 1] = s_v.v
    ofm_done = ofmcp[23]

    e2_v = {}
    for t in range(NT):
        par = t % 2

        @block.sync
        def _(sync, t=t, par=par):
            if t >= 2:
                sync.wait_ge(s_v.h, e2_v[t - 2])
            sync.dma_start(out=x1e[:, par * DM:(par + 1) * DM],
                           in_=k.dr["sp_x1"][128 * t:128 * (t + 1), :]).then_inc(s_lx2[par].inc(16), 16)

        vprev2 = s_v.v

        @block.tensor
        def _(tt_, t=t, par=par, vprev2=vprev2):
            if t == 0:
                tt_.wait_ge(s_v.h, ofm_done)
                tt_.wait_ge(s_lwd.h, s_lwd.v)
            tt_.wait_ge(s_v.h, vprev2)
            for nc2 in range(2):
                for kc in range(24):
                    mi = tt_.matmul(pb[2 * par + nc2][:, 0:512],
                                    ofm[:, kc * TOK + 128 * t:kc * TOK + 128 * (t + 1)],
                                    wdT[:, kc * DM + 512 * nc2:kc * DM + 512 * (nc2 + 1)],
                                    start=(kc == 0), stop=(kc == 23))
            mi.then_inc(s_mm.inc(), 1)

        md = s_mm.v

        @block.vector
        def _(v, t=t, par=par, md=md):
            v.wait_ge(s_mm.h, md)
            v.wait_ge(s_lx2[par].h, s_lx2[par].v)
            v.scalar_tensor_tensor(out=pre2[:, 0:512], in0=pb[2 * par][:, 0:512],
                                   scalar=k.sval[:, 16:17], in1=x1e[:, par * DM:par * DM + 512],
                                   op0=OP.mult, op1=OP.add)
            v.scalar_tensor_tensor(out=pre2[:, 512:1024], in0=pb[2 * par + 1][:, 0:512],
                                   scalar=k.sval[:, 16:17], in1=x1e[:, par * DM + 512:par * DM + 1024],
                                   op0=OP.mult, op1=OP.add).then_inc(s_v.inc(), 1)
            v.wait_ge(s_v.h, s_v.v)
            v.tensor_reduce(out=ln2s[:, 0:1], in_=pre2[:, :], axis=AX, op=OP.add).then_inc(s_v.inc(), 1)

        vpre2 = s_v.v

        @block.scalar
        def _(sc, t=t, vpre2=vpre2):
            sc.wait_ge(s_v.h, vpre2)
            sc.wait_ge(s_sc.h, s_sc.v)
            sc.activation(out=sq2[:, :], in_=pre2[:, :], func=AF.Square,
                          accum_out=ln2s[:, 1:2]).then_inc(s_sc.inc(), 1)

        sq2d = s_sc.v

        @block.vector
        def _(v, t=t, sq2d=sq2d):
            v.wait_ge(s_sc.h, sq2d)
            v.tensor_scalar(out=ln2s[:, 2:3], in0=ln2s[:, 0:1], scalar1=1.0 / DM,
                            scalar2=None, op0=OP.mult)
            v.tensor_scalar(out=ln2s[:, 3:4], in0=ln2s[:, 1:2], scalar1=1.0 / DM,
                            scalar2=None, op0=OP.mult).then_inc(s_v.inc(), 1)
            v.wait_ge(s_v.h, s_v.v)
            v.tensor_tensor(out=ln2s[:, 4:5], in0=ln2s[:, 2:3], in1=ln2s[:, 2:3],
                            op=OP.mult).then_inc(s_v.inc(), 1)
            v.wait_ge(s_v.h, s_v.v)
            v.tensor_tensor(out=ln2s[:, 5:6], in0=ln2s[:, 3:4], in1=ln2s[:, 4:5],
                            op=OP.subtract).then_inc(s_v.inc(), 1)

        vvar2 = s_v.v

        @block.scalar
        def _(sc, t=t, vvar2=vvar2):
            sc.wait_ge(s_v.h, vvar2)
            sc.activation(out=ln2s[:, 6:7], in_=ln2s[:, 5:6], func=AF.Ln,
                          bias=k.fconst[:, 1:2]).then_inc(s_sc.inc(), 1)
            sc.wait_ge(s_sc.h, s_sc.v)
            sc.activation(out=ln2s[:, 7:8], in_=ln2s[:, 6:7], func=AF.Exp,
                          scale=-0.5).then_inc(s_sc.inc(), 1)

        srs2 = s_sc.v

        @block.vector
        def _(v, t=t, par=par, srs2=srs2):
            v.wait_ge(s_sc.h, srs2)
            v.tensor_tensor(out=ln2s[:, 8:9], in0=ln2s[:, 2:3], in1=ln2s[:, 7:8],
                            op=OP.mult).then_inc(s_v.inc(), 1)
            v.wait_ge(s_v.h, s_v.v)
            v.tensor_scalar(out=ln2s[:, 9:10], in0=ln2s[:, 8:9], scalar1=-1.0,
                            scalar2=None, op0=OP.mult).then_inc(s_v.inc(), 1)
            v.wait_ge(s_v.h, s_v.v)
            if t >= 2:
                v.wait_ge(k.s_st2[par].h, 16 * (t // 2))
            v.tensor_scalar(out=outb[:, par * DM:(par + 1) * DM], in0=pre2[:, :],
                            scalar1=ln2s[:, 7:8], scalar2=ln2s[:, 9:10],
                            op0=OP.mult, op1=OP.add).then_inc(s_v.inc(), 1)
        e2_v[t] = s_v.v

        @block.sync
        def _(sync, t=t, par=par):
            sync.wait_ge(s_v.h, e2_v[t])
            sync.dma_start(out=k.d["out"][128 * t:128 * (t + 1), :],
                           in_=outb[:, par * DM:(par + 1) * DM]).then_inc(k.s_st2[par].inc(16), 16)

    k.scope5 = scope5


def emit_dbg_ffn(k):
    block = k.block
    dsv = k.dbg_tensor("sval", [128, 48])
    dgr = k.dbg_tensor("gateraw", [TOK, FI])
    doi = k.dbg_tensor("oint", [TOK, FI])
    dwq = k.dbg_tensor("wdq", [8 * 1152, DM])
    if dsv is not None:
        @block.gpsimd
        def _(g):
            g.wait_ge(k.s_st2[0].h, k.s_st2[0].v)
            g.wait_ge(k.s_st2[1].h, k.s_st2[1].v)
            g.dma_start(out=dsv, in_=k.sval[:, :]).then_inc(k.s_sp.inc(16), 16)
            if dgr is not None:
                g.dma_start(out=dgr, in_=k.dr["sp_gate"][:, :]).then_inc(k.s_sp.inc(16), 16)
            if doi is not None:
                g.dma_start(out=doi, in_=k.dr["sp_up"][:, :]).then_inc(k.s_sp.inc(16), 16)
            if dwq is not None:
                g.dma_start(out=dwq, in_=k.dr["cc_wq_out"]).then_inc(k.s_sp.inc(16), 16)
            g.wait_ge(k.s_sp.h, k.s_sp.v)


def emit_finish(k):
    block = k.block

    @block.sync
    def _(sync):
        sync.wait_ge(k.s_st2[0].h, k.s_st2[0].v)
        sync.wait_ge(k.s_st2[1].h, k.s_st2[1].v)

    @block.gpsimd
    def _(g):
        g.wait_ge(k.s_st2[0].h, k.s_st2[0].v)
        g.wait_ge(k.s_st2[1].h, k.s_st2[1].v)

    k.scope5.close()



def emit_finish_dbg(k):
    block = k.block
    for scope in ("scopeF", "scope2", "scopeD", "scopeA", "scope_xs", "scope_zs"):
        s = getattr(k, scope, None)
        if s is not None:
            try:
                s.close()
            except Exception:
                pass

    @block.gpsimd
    def _(g):
        if k.s_sp.v:
            g.wait_ge(k.s_sp.h, k.s_sp.v)
        for s in getattr(k, "s_so", []) or []:
            if s.v:
                g.wait_ge(s.h, s.v)


def shard_inputs(inputs, core):
    b, half = core // 2, core % 2
    x = np.asarray(inputs["L0_attn_x"])
    t0 = half * TOK
    xp = np.zeros((TPAD, DM), np.float32)
    lo, hi = t0 - 1, t0 + TOK + 2
    slo, shi = max(lo, 0), min(hi, LSEQ)
    xp[slo - lo:shi - lo, :] = x[b, slo:shi, :]
    convw = np.concatenate([np.asarray(inputs["L0_attn_conv_x_w"])[:, 0, :],
                            np.asarray(inputs["L0_attn_conv_z_w"])[:, 0, :]], axis=0)
    convb = np.concatenate([np.asarray(inputs["L0_attn_conv_x_b"]),
                            np.asarray(inputs["L0_attn_conv_z_b"])], axis=0)[:, None]
    wgkb = np.concatenate([np.asarray(inputs["L0_attn_W_gk"]),
                           np.asarray(inputs["L0_attn_W_b"])], axis=1)
    gkb_bias = np.concatenate([np.asarray(inputs["L0_attn_dt_bias"]),
                               np.asarray(inputs["L0_attn_b_b"])])[None, :]
    smask = np.zeros((1, 8), np.float32)
    if half == 1:
        smask[0, core - 1] = 1.0
    m = {
        "x_pad": np.ascontiguousarray(xp),
        "W_in": np.asarray(inputs["L0_attn_W_in"]),
        "convw": np.ascontiguousarray(convw),
        "convb": np.ascontiguousarray(convb),
        "W_q": np.asarray(inputs["L0_attn_W_q"]),
        "W_k": np.asarray(inputs["L0_attn_W_k"]),
        "W_v": np.asarray(inputs["L0_attn_W_v"]),
        "W_g": np.asarray(inputs["L0_attn_W_g"]),
        "W_gkb": np.ascontiguousarray(wgkb),
        "gkb_bias": np.ascontiguousarray(gkb_bias),
        "A_log": np.asarray(inputs["L0_attn_A_log"])[None, :],
        "g_norm_w": np.asarray(inputs["L0_attn_g_norm_w"])[None, :],
        "W_so": np.asarray(inputs["L0_attn_W_so"]),
        "W_out": np.asarray(inputs["L0_attn_W_out"]),
        "w_gate": np.asarray(inputs["L0_ffn_w_gate"]),
        "w_up": np.asarray(inputs["L0_ffn_w_up"]),
        "w_down": np.asarray(inputs["L0_ffn_w_down"]),
        "smask": smask,
        "wg_shard": np.asarray(inputs["L0_ffn_w_gate"])[384 * core:384 * (core + 1), :],
        "wu_shard": np.asarray(inputs["L0_ffn_w_up"])[384 * core:384 * (core + 1), :],
        "wd_shard": np.asarray(inputs["L0_ffn_w_down"])[128 * core:128 * (core + 1), :],
    }
    return {kk: np.ascontiguousarray(v, dtype=np.float32) for kk, v in m.items()}


def kernel(**inputs):
    nc, _ = build_program()
    in_maps = [shard_inputs(inputs, cidx) for cidx in range(8)]
    res = run_bass_kernel_spmd(nc, in_maps, core_ids=list(range(8)))
    outs = [res.results[cidx]["out"] for cidx in range(8)]
    return np.stack([np.concatenate([outs[2 * b], outs[2 * b + 1]], axis=0)
                     for b in range(B)]).astype(np.float32)


if __name__ == "__main__":
    nc, _ = build_program(upto="A", dbg=("xs", "zs"))
    print("built OK")


# version 09
# speedup vs baseline: 1.0100x; 1.0100x over previous
import numpy as np
from contextlib import ExitStack

import concourse.bass as bass
import concourse.mybir as mybir
from concourse.alu_op_type import AluOpType
from concourse.bass_utils import run_bass_kernel_spmd

F32 = mybir.dt.float32
BF16 = mybir.dt.bfloat16
I16 = mybir.dt.int16
U16 = mybir.dt.uint16
AF = mybir.ActivationFunctionType
AX = mybir.AxisListType.X
OP = AluOpType

B, LSEQ, DM = 4, 4096, 1024
TOK = 2048
NT = TOK // 128
NPT = 17
TPAD = NPT * 128
D_INNER = 2048
KD, VD, HH = 768, 1536, 4
DK, DV = 192, 384
CH, NCH = 128, 16
FI = 3072
QMAX = 32767.0
EPS_LN = 1e-5


class Sem:
    __slots__ = ("h", "v")

    def __init__(self, ctx, nc, name):
        self.h = ctx.enter_context(nc.semaphore(name))
        self.v = 0

    def inc(self, n=1):
        self.v += n
        return self.h


def _levelmask_neg(C, s):
    E = np.zeros((C, C), np.float32)
    for i in range(C):
        lo = (i // (2 * s)) * (2 * s)
        if (i // s) % 2 == 1:
            E[i, lo:lo + s] = -1.0
    return E



class _ChainStub:
    __slots__ = ("chain",)

    def __init__(self):
        self.chain = []

    def then_inc(self, sem, n):
        self.chain.append((sem, n))
        return self


class _RecEngine:
    __slots__ = ("q",)

    def __init__(self, q):
        self.q = q

    def __getattr__(self, m):
        def call(*a, **kw):
            stub = _ChainStub()
            self.q.append((m, a, kw, stub))
            return stub
        return call


class _RecBlock:

    def __init__(self):
        self.qs = {e: [] for e in ("sync", "tensor", "vector", "scalar", "gpsimd")}
        self.realblock = None
        self.direct = set()

    def set_direct(self, *engs):
        self.materialize(self.realblock)
        self.direct.update(engs)

    def set_rec(self, *engs):
        self.direct.difference_update(engs)

    def _mk(self, eng):
        if eng in self.direct:
            return getattr(self.realblock, eng)
        def deco(f):
            f(_RecEngine(self.qs[eng]))
            return f
        return deco

    @property
    def sync(self):
        return self._mk("sync")

    @property
    def tensor(self):
        return self._mk("tensor")

    @property
    def vector(self):
        return self._mk("vector")

    @property
    def scalar(self):
        return self._mk("scalar")

    @property
    def gpsimd(self):
        return self._mk("gpsimd")

    def materialize(self, block):
        for eng in ("sync", "tensor", "vector", "scalar", "gpsimd"):
            q = self.qs[eng]
            if not q:
                continue

            def body(e, q=list(q)):
                for m, a, kw, stub in q:
                    inst = getattr(e, m)(*a, **kw)
                    for sem, n in stub.chain:
                        inst = inst.then_inc(sem, n)
            getattr(block, eng)(body)
            self.qs[eng] = []


class K:
    pass


def build_program(upto="all", dbg=(), mock_ag=False):
    nc = bass.Bass("TRN2", target_bir_lowering=False)
    ctx = ExitStack()
    k = K()
    k.nc = nc
    k.ctx = ctx
    k.dbg = dbg
    k.mock_ag = mock_ag
    k.dbg_out = {}

    d = k.d = {}
    d["x_pad"] = nc.dram_tensor("x_pad", [TPAD, DM], F32, kind="ExternalInput").ap()
    d["W_in"] = nc.dram_tensor("W_in", [DM, D_INNER], F32, kind="ExternalInput").ap()
    d["convw"] = nc.dram_tensor("convw", [D_INNER, 4], F32, kind="ExternalInput").ap()
    d["convb"] = nc.dram_tensor("convb", [D_INNER, 1], F32, kind="ExternalInput").ap()
    d["W_q"] = nc.dram_tensor("W_q", [DM, KD], F32, kind="ExternalInput").ap()
    d["W_k"] = nc.dram_tensor("W_k", [DM, KD], F32, kind="ExternalInput").ap()
    d["W_v"] = nc.dram_tensor("W_v", [DM, VD], F32, kind="ExternalInput").ap()
    d["W_g"] = nc.dram_tensor("W_g", [DM, VD], F32, kind="ExternalInput").ap()
    d["W_gkb"] = nc.dram_tensor("W_gkb", [DM, 8], F32, kind="ExternalInput").ap()
    d["gkb_bias"] = nc.dram_tensor("gkb_bias", [1, 8], F32, kind="ExternalInput").ap()
    d["A_log"] = nc.dram_tensor("A_log", [1, HH], F32, kind="ExternalInput").ap()
    d["g_norm_w"] = nc.dram_tensor("g_norm_w", [1, DV], F32, kind="ExternalInput").ap()
    d["W_so"] = nc.dram_tensor("W_so", [VD, DM], F32, kind="ExternalInput").ap()
    d["W_out"] = nc.dram_tensor("W_out", [D_INNER, DM], F32, kind="ExternalInput").ap()
    d["w_gate"] = nc.dram_tensor("w_gate", [FI, DM], F32, kind="ExternalInput").ap()
    d["w_up"] = nc.dram_tensor("w_up", [FI, DM], F32, kind="ExternalInput").ap()
    d["w_down"] = nc.dram_tensor("w_down", [DM, FI], F32, kind="ExternalInput").ap()
    d["smask"] = nc.dram_tensor("smask", [1, 8], F32, kind="ExternalInput").ap()
    d["wg_shard"] = nc.dram_tensor("wg_shard", [384, DM], F32, kind="ExternalInput").ap()
    d["wu_shard"] = nc.dram_tensor("wu_shard", [384, DM], F32, kind="ExternalInput").ap()
    d["wd_shard"] = nc.dram_tensor("wd_shard", [128, FI], F32, kind="ExternalInput").ap()
    d["out"] = nc.dram_tensor("out", [TOK, DM], F32, kind="ExternalOutput").ap()

    dr = k.dr = {}
    def dram_scratch(name, shape, dt=BF16):
        dr[name] = nc.dram_tensor(name, shape, dt).ap()
        return dr[name]
    dram_scratch("sp_uv", [NCH * HH * 128, DV])
    dram_scratch("sp_wfm", [NCH * HH * 128, 256])
    dram_scratch("sp_aqt", [NCH * HH * 128, 128])
    dram_scratch("sp_ktil", [NCH * HH * 128, 192])
    dram_scratch("sp_qfm", [NCH * HH * 128, 256])
    dram_scratch("sp_o", [TOK, VD])
    dram_scratch("sp_og", [TOK, VD])
    dram_scratch("sp_x1", [TOK, DM], F32)
    dram_scratch("sp_xq", [TOK, DM])
    dram_scratch("sp_up", [TOK, FI])
    dram_scratch("sp_gate", [TOK, FI])
    dram_scratch("sp_oq", [TOK, FI])
    dr["cc_state_in"] = nc.dram_tensor("cc_state_in", [192, HH * DV], F32).ap()
    dr["cc_state_out"] = nc.dram_tensor("cc_state_out", [8 * 192, HH * DV], F32).ap()
    dr["cc_wq_in"] = nc.dram_tensor("cc_wq_in", [1152, DM], BF16).ap()
    dr["cc_wq_out"] = nc.dram_tensor("cc_wq_out", [8 * 1152, DM], BF16).ap()
    for i in range(4):
        dr[f"cc_s{i}_in"] = nc.dram_tensor(f"cc_s{i}_in", [1, 2], F32).ap()
        dr[f"cc_s{i}_out"] = nc.dram_tensor(f"cc_s{i}_out", [8, 2], F32).ap()

    def dbg_tensor(name, shape):
        if name in dbg:
            ap = nc.dram_tensor("dbg_" + name, shape, F32, kind="ExternalOutput").ap()
            k.dbg_out[name] = ap
            return ap
        return None
    k.dbg_tensor = dbg_tensor

    c = k.c = {}
    c["idf"] = nc.inline_tensor(np.eye(128, dtype=np.float32), name="c_idf").ap()
    c["ones_row"] = nc.inline_tensor(np.ones((1, 128), np.float32), name="c_ones").ap()
    c["ut"] = nc.inline_tensor(np.triu(np.ones((128, 128), np.float32), 0), name="c_ut").ap()
    c["lts"] = nc.inline_tensor(np.tril(np.ones((128, 128), np.float32), -1), name="c_lts").ap()
    c["mlow"] = nc.inline_tensor(np.tril(np.ones((128, 128), np.float32), -1), name="c_mlow").ap()
    c["muic"] = nc.inline_tensor(np.triu(np.ones((128, 128), np.float32), 0), name="c_muic").ap()
    c["id128"] = nc.inline_tensor(np.eye(128, dtype=np.float32), name="c_id128").ap()
    for i, s in enumerate((1, 2, 4, 8, 16, 32, 64)):
        c[f"negE{i}"] = nc.inline_tensor(_levelmask_neg(128, s), name=f"c_negE{i}").ap()

    sb = lambda *a, **kw: ctx.enter_context(nc.sbuf_tensor(*a, **kw))
    k.sb = sb
    k.consts = sb("consts", [128, 512], F32)
    k.constsb = sb("constsb", [128, 1280], BF16)
    k.gnwb = sb("gnwb", [128, DV], F32)
    k.alogv = sb("alogv", [128, 16], F32)
    k.gkb = sb("gkb", [128, NT * 8], F32)
    k.stat = sb("stat", [128, 192], F32)
    k.sval = sb("sval", [128, 48], F32)
    k.fconst = sb("fconst", [128, 4], F32)

    k.pb = [ctx.enter_context(nc.psum_tensor(f"pb{i}", [128, 512], F32)) for i in range(8)]

    sem = lambda name: Sem(ctx, nc, name)
    k.sem = sem
    k.s_lc = sem("s_lc")
    k.s_lcg = sem("s_lcg")
    k.s_lw = sem("s_lw")
    k.s_lwg = sem("s_lwg")
    k.s_lx = [sem("s_lx0"), sem("s_lx1")]
    k.s_mm = sem("s_mm")
    k.s_v = sem("s_v")
    k.s_sc = sem("s_sc")
    k.s_sp = sem("s_sp")
    k.s_g = sem("s_g")
    k.s_st = sem("s_st")
    k.s_st2 = [sem("s_st2a"), sem("s_st2b")]

    k.realblock = ctx.enter_context(nc.Block())
    k.block = _RecBlock()
    k.block.realblock = k.realblock

    emit_prelude(k)
    emit_passA(k)
    if upto == "A":
        emit_finish_dbg(k)
        k.block.materialize(k.realblock)
        ctx.close()
        return nc, k.dbg_out
    k.block.materialize(k.realblock)
    emit_gkb(k)
    k.block.materialize(k.realblock)
    emit_delta(k, chain=1)
    k.block.materialize(k.realblock)
    emit_state_ag(k)
    k.block.materialize(k.realblock)
    emit_delta(k, chain=2)
    if upto == "delta":
        emit_finish_dbg(k)
        k.block.materialize(k.realblock)
        ctx.close()
        return nc, k.dbg_out
    k.block.materialize(k.realblock)
    emit_epilogue(k)
    if upto == "attn":
        emit_finish_dbg(k)
        k.block.materialize(k.realblock)
        ctx.close()
        return nc, k.dbg_out
    k.block.materialize(k.realblock)
    emit_ffn(k)
    k.block.materialize(k.realblock)
    emit_dbg_ffn(k)
    emit_finish(k)
    k.block.materialize(k.realblock)
    ctx.close()
    return nc, k.dbg_out


def emit_prelude(k):
    nc, block, c = k.nc, k.block, k.c
    s_lc, s_mm, s_v, s_sc = k.s_lc, k.s_mm, k.s_v, k.s_sc

    @block.sync
    def _(sync):
        sync.dma_start(out=k.consts[:, 0:128], in_=c["idf"]).then_inc(s_lc.inc(16), 16)
        sync.dma_start(out=k.consts[0:1, 128:256], in_=c["ones_row"]).then_inc(s_lc.inc(16), 16)
        sync.dma_start(out=k.consts[:, 256:384], in_=c["ut"]).then_inc(s_lc.inc(16), 16)
        sync.dma_start(out=k.consts[:, 384:512], in_=c["lts"]).then_inc(s_lc.inc(16), 16)
        sync.dma_start(out=k.gnwb[0:1, :], in_=k.d["g_norm_w"]).then_inc(s_lc.inc(16), 16)
        sync.dma_start(out=k.alogv[0:1, 0:4], in_=k.d["A_log"]).then_inc(s_lc.inc(16), 16)
        sync.dma_start(out=k.alogv[0:1, 4:12], in_=k.d["gkb_bias"]).then_inc(s_lc.inc(16), 16)

    @block.gpsimd
    def _(g):
        g.dma_start(out=k.constsb[:, 0:128], in_=c["mlow"]).then_inc(k.s_lcg.inc(16), 16)
        g.dma_start(out=k.constsb[:, 128:256], in_=c["muic"]).then_inc(k.s_lcg.inc(16), 16)
        g.dma_start(out=k.constsb[:, 256:384], in_=c["id128"]).then_inc(k.s_lcg.inc(16), 16)
        for i in range(7):
            g.dma_start(out=k.constsb[:, 384 + 128 * i:512 + 128 * i],
                        in_=c[f"negE{i}"]).then_inc(k.s_lcg.inc(16), 16)

    @block.tensor
    def _(t):
        t.wait_ge(s_lc.h, s_lc.v)
        t.wait_ge(k.s_lcg.h, k.s_lcg.v)
        t.matmul(k.pb[0][:, 0:384], k.consts[0:1, 128:256], k.gnwb[0:1, 0:384],
                 start=True, stop=True)
        t.matmul(k.pb[1][:, 0:12], k.consts[0:1, 128:256], k.alogv[0:1, 0:12],
                 start=True, stop=True).then_inc(s_mm.inc(), 1)

    @block.vector
    def _(v):
        v.wait_ge(s_mm.h, s_mm.v)
        v.wait_ge(k.s_lcg.h, k.s_lcg.v)
        v.memset(k.fconst[:, 0:1], 1e-6)
        v.memset(k.fconst[:, 1:2], EPS_LN)
        v.memset(k.sval[:, :], 0.0)
        v.tensor_copy(out=k.gnwb[:, :], in_=k.pb[0][:, 0:384])
        v.tensor_copy(out=k.alogv[:, 4:12], in_=k.pb[1][:, 4:12])
        v.tensor_copy(out=k.alogv[:, 12:16], in_=k.pb[1][:, 0:4]).then_inc(s_v.inc(), 1)

    @block.scalar
    def _(sc):
        sc.wait_ge(s_v.h, s_v.v)
        sc.wait_ge(k.s_lcg.h, k.s_lcg.v)
        sc.activation(out=k.alogv[:, 0:4], in_=k.alogv[:, 12:16], func=AF.Exp).then_inc(s_sc.inc(), 1)

    @block.vector
    def _(v):
        v.wait_ge(s_sc.h, s_sc.v)
        v.tensor_scalar(out=k.alogv[:, 12:16], in0=k.alogv[:, 0:4], scalar1=-1.0,
                        scalar2=None, op0=OP.mult).then_inc(s_v.inc(), 1)
    k.prelude_v = s_v.v


def emit_passA(k):
    nc, block = k.nc, k.block
    s_mm, s_v, s_sc = k.s_mm, k.s_v, k.s_sc
    pb = k.pb

    k.scope_zs = ExitStack()
    k.zs = k.scope_zs.enter_context(nc.sbuf_tensor("zs", [128, 8 * TOK], BF16))
    k.scope_xs = ExitStack()
    k.xs = k.scope_xs.enter_context(nc.sbuf_tensor("xs", [128, 8 * TOK], BF16))

    k.scopeA = ExitStack()
    sbA = lambda *a: k.scopeA.enter_context(nc.sbuf_tensor(*a))
    winb = sbA("winb", [128, 8 * D_INNER], BF16)
    diagw = sbA("diagw", [128, 64 * 128], BF16)
    convwv = sbA("convwv", [128, 64], F32)
    convbv = sbA("convbv", [128, 16], F32)
    xtile = sbA("xtile", [128, 2 * DM], F32)
    xfm = sbA("xfm", [128, 8 * TPAD], BF16)
    xzbuf = sbA("xzbuf", [128, 16 * 516], BF16)

    @block.gpsimd
    def _(g):
        for kc in range(8):
            g.dma_start(out=winb[:, kc * D_INNER:(kc + 1) * D_INNER],
                        in_=k.d["W_in"][128 * kc:128 * (kc + 1), :]).then_inc(k.s_lwg.inc(16), 16)

    @block.sync
    def _(sync):
        for cc in range(16):
            sync.dma_start(out=convwv[:, 4 * cc:4 * cc + 4],
                           in_=k.d["convw"][128 * cc:128 * (cc + 1), :]).then_inc(k.s_lw.inc(16), 16)
        for cc in range(16):
            sync.dma_start(out=convbv[:, cc:cc + 1],
                           in_=k.d["convb"][128 * cc:128 * (cc + 1), :]).then_inc(k.s_lw.inc(16), 16)

    @block.vector
    def _(v):
        v.wait_ge(k.s_lw.h, k.s_lw.v)
        for cc in range(16):
            for j in range(4):
                vi = v.tensor_scalar(out=diagw[:, (4 * cc + j) * 128:(4 * cc + j + 1) * 128],
                                     in0=k.consts[:, 0:128], scalar1=convwv[:, 4 * cc + j:4 * cc + j + 1],
                                     scalar2=None, op0=OP.mult)
        vi.then_inc(s_v.inc(), 1)
    v_diag = k.s_v.v

    base_mm = s_mm.v
    base_v = s_v.v

    @block.sync
    def _(sync):
        for tt in range(NPT):
            if tt >= 2:
                sync.wait_ge(s_mm.h, base_mm + tt - 1)
            sync.dma_start(out=xtile[:, (tt % 2) * DM:(tt % 2 + 1) * DM],
                           in_=k.d["x_pad"][128 * tt:128 * (tt + 1), :]).then_inc(k.s_lx[tt % 2].inc(16), 16)

    @block.tensor
    def _(t):
        t.wait_ge(s_v.h, 1)
        for tt in range(NPT):
            t.wait_ge(k.s_lx[tt % 2].h, 16 * (tt // 2 + 1))
            if tt >= 2:
                t.wait_ge(s_v.h, base_v + tt - 1)
            for kc in range(8):
                bank = pb[(tt % 2) * 2 + kc // 4]
                ti = t.transpose(bank[:, (kc % 4) * 128:(kc % 4 + 1) * 128],
                                 xtile[:, (tt % 2) * DM + kc * 128:(tt % 2) * DM + (kc + 1) * 128],
                                 k.consts[:, 0:128])
                if kc == 7:
                    ti.then_inc(s_mm.inc(), 1)

    @block.vector
    def _(v):
        for tt in range(NPT):
            v.wait_ge(s_mm.h, base_mm + tt + 1)
            for kc in range(8):
                bank = pb[(tt % 2) * 2 + kc // 4]
                vi = v.tensor_copy(out=xfm[:, kc * TPAD + 128 * tt:kc * TPAD + 128 * (tt + 1)],
                                   in_=bank[:, (kc % 4) * 128:(kc % 4 + 1) * 128])
            vi.then_inc(s_v.inc(), 1)

    mm_after_x = s_mm.v
    v_after_x = s_v.v

    base_mm2 = s_mm.v
    base_v2 = s_v.v
    base_sc3 = k.s_sc.v

    @block.tensor
    def _(t):
        t.wait_ge(k.s_lwg.h, k.s_lwg.v)
        t.wait_ge(s_v.h, v_after_x)
        t.wait_ge(s_v.h, v_diag)
        nmm = 0
        nconv = 0
        for g in range(4):
            t0 = 512 * g
            for oc in range(16):
                bank = pb[2 + (nmm % 2)]
                if nmm >= 2:
                    t.wait_ge(s_v.h, base_v2 + nmm - 1)
                for kc in range(8):
                    t.matmul(bank[:, 0:512],
                             winb[:, kc * D_INNER + oc * 128:kc * D_INNER + (oc + 1) * 128],
                             xfm[:, kc * TPAD + t0:kc * TPAD + t0 + 512],
                             start=(kc == 0), stop=False if kc < 7 else True)
                for kc in range(8):
                    mi = t.matmul(pb[4 + (nmm % 2)][:, 0:4],
                                  winb[:, kc * D_INNER + oc * 128:kc * D_INNER + (oc + 1) * 128],
                                  xfm[:, kc * TPAD + t0 + 512:kc * TPAD + t0 + 516],
                                  start=(kc == 0), stop=(kc == 7))
                    if kc == 7:
                        mi.then_inc(s_mm.inc(), 1)
                nmm += 1
            t.wait_ge(s_v.h, base_v2 + nmm)
            for cc in range(16):
                bank = pb[6 + (nconv % 2)]
                if nconv >= 2:
                    t.wait_ge(k.s_sc.h, base_sc3 + nconv - 1)
                for j in range(4):
                    mi = t.matmul(bank[:, 0:512],
                                  diagw[:, (4 * cc + j) * 128:(4 * cc + j + 1) * 128],
                                  xzbuf[:, cc * 516 + j:cc * 516 + j + 512],
                                  start=(j == 0), stop=(j == 3))
                    if j == 3:
                        mi.then_inc(s_mm.inc(), 1)
                nconv += 1

    @block.vector
    def _(v):
        nmm = 0
        for g in range(4):
            for oc in range(16):
                bank = pb[2 + (nmm % 2)]
                v.wait_ge(s_mm.h, base_mm2 + nmm + 1 + 16 * g)
                v.tensor_copy(out=xzbuf[:, oc * 516:oc * 516 + 512], in_=bank[:, 0:512])
                v.tensor_copy(out=xzbuf[:, oc * 516 + 512:oc * 516 + 516],
                              in_=pb[4 + (nmm % 2)][:, 0:4]).then_inc(s_v.inc(), 1)
                nmm += 1

    @block.scalar
    def _(sc):
        nconv = 0
        for g in range(4):
            t0 = 512 * g
            for cc in range(16):
                bank = pb[6 + (nconv % 2)]
                sc.wait_ge(s_mm.h, base_mm2 + 16 * (g + 1) + nconv + 1)
                dst = k.xs if cc < 8 else k.zs
                oc = cc if cc < 8 else cc - 8
                sc.activation(out=dst[:, oc * TOK + t0:oc * TOK + t0 + 512], in_=bank[:, 0:512],
                              func=AF.Silu, bias=convbv[:, cc:cc + 1],
                              scale=1.0).then_inc(k.s_sc.inc(), 1)
                nconv += 1

    k.A_done_sc = k.s_sc.v
    k.A_done_mm = s_mm.v

    dxs = k.dbg_tensor("xs", [128, 8 * TOK])
    dzs = k.dbg_tensor("zs", [128, 8 * TOK])
    if dxs is not None:
        @block.gpsimd
        def _(g):
            g.wait_ge(k.s_sc.h, k.A_done_sc)
            g.dma_start(out=dxs, in_=k.xs[:, :]).then_inc(k.s_sp.inc(16), 16)
            g.dma_start(out=dzs, in_=k.zs[:, :]).then_inc(k.s_sp.inc(16), 16)


def emit_gkb(k):
    nc, block, pb = k.nc, k.block, k.pb
    s_mm, s_v, s_sc = k.s_mm, k.s_v, k.s_sc
    k.scopeA.close()

    k.scopeD = ExitStack()
    sbD = lambda *a: k.scopeD.enter_context(nc.sbuf_tensor(*a))
    k.sbD = sbD
    wgkbb = sbD("wgkbb", [128, 64], BF16)
    gkscr = sbD("gkscr", [128, 32], F32)
    s_lgkb = k.sem("s_lgkb")

    @block.gpsimd
    def _(g):
        g.wait_ge(k.s_sc.h, k.A_done_sc)
        for kc in range(8):
            g.dma_start(out=wgkbb[:, 8 * kc:8 * kc + 8],
                        in_=k.d["W_gkb"][128 * kc:128 * (kc + 1), :]).then_inc(s_lgkb.inc(16), 16)

    vfirst = {}
    for t in range(NT):
        par = t % 2

        @block.tensor
        def _(tt_, t=t, par=par):
            tt_.wait_ge(s_lgkb.h, s_lgkb.v)
            tt_.wait_ge(k.s_sc.h, k.A_done_sc)
            if t >= 2:
                tt_.wait_ge(s_v.h, vfirst[t - 2])
            for kc in range(8):
                mi = tt_.matmul(pb[6 + par][:, 0:8],
                                k.xs[:, kc * TOK + 128 * t:kc * TOK + 128 * (t + 1)],
                                wgkbb[:, 8 * kc:8 * kc + 8],
                                start=(kc == 0), stop=(kc == 7))
            mi.then_inc(s_mm.inc(), 1)

        mmv = s_mm.v

        @block.vector
        def _(v, t=t, par=par):
            v.wait_ge(s_mm.h, mmv)
            v.tensor_tensor(out=gkscr[:, 0:8], in0=pb[6 + par][:, 0:8],
                            in1=k.alogv[:, 4:12], op=OP.add).then_inc(s_v.inc(), 1)
        vfirst[t] = s_v.v

        vv = s_v.v

        @block.scalar
        def _(sc, t=t):
            sc.wait_ge(s_v.h, vv)
            sc.activation(out=gkscr[:, 8:12], in_=gkscr[:, 0:4], func=AF.Exp).then_inc(s_sc.inc(), 1)
            sc.wait_ge(s_sc.h, s_sc.v)
            sc.activation(out=gkscr[:, 12:16], in_=gkscr[:, 8:12], func=AF.Ln, bias=1.0)
            sc.activation(out=gkscr[:, 16:20], in_=gkscr[:, 4:8], func=AF.Exp,
                          scale=-1.0).then_inc(s_sc.inc(), 1)

        scv = s_sc.v

        @block.vector
        def _(v, t=t):
            v.wait_ge(s_sc.h, scv)
            v.tensor_tensor(out=k.gkb[:, 8 * t:8 * t + 4], in0=gkscr[:, 12:16],
                            in1=k.alogv[:, 12:16], op=OP.mult)
            v.tensor_scalar(out=gkscr[:, 20:24], in0=gkscr[:, 16:20], scalar1=1.0,
                            scalar2=None, op0=OP.add).then_inc(s_v.inc(), 1)
            v.wait_ge(s_v.h, s_v.v)
            v.reciprocal(out=k.gkb[:, 8 * t + 4:8 * t + 8],
                         in_=gkscr[:, 20:24]).then_inc(s_v.inc(), 1)

    k.gkb_done_v = s_v.v

    dgk = k.dbg_tensor("gkb", [128, NT * 8])
    if dgk is not None:
        @block.gpsimd
        def _(g):
            g.wait_ge(s_v.h, k.gkb_done_v)
            g.dma_start(out=dgk, in_=k.gkb[:, :]).then_inc(k.s_sp.inc(16), 16)


def delta_alloc(k):
    sbD = k.sbD
    nc = k.nc
    k.wqb = sbD("wqb", [128, 8 * KD], BF16)
    k.wkb = sbD("wkb", [128, 8 * KD], BF16)
    k.wvb = sbD("wvb", [128, 8 * VD], BF16)
    k.q_tm = sbD("q_tm", [128, KD], F32)
    k.k_tm = sbD("k_tm", [128, KD], F32)
    k.qh_tm = sbD("qh_tm", [128, DK], F32)
    k.kh_tm = sbD("kh_tm", [128, DK], F32)
    k.sqv = sbD("sqv", [128, 8], F32)
    k.lamv = sbD("lamv", [128, 2 * 16], F32)
    k.lamlast = sbD("lamlast", [128, 2 * 4], F32)
    k.lamT = sbD("lamT", [4, 2 * 128], F32)
    k.dupS = sbD("dupS", [128, 128], F32)
    k.dloS = sbD("dloS", [128, 128], F32)
    k.dscr = sbD("dscr", [128, 128], F32)
    k.dscr2 = sbD("dscr2", [128, 128], F32)
    k.a_bf = sbD("a_bf", [128, 2 * 128], BF16)
    k.as_bf = sbD("as_bf", [128, 128], BF16)
    k.z0 = sbD("z0", [128, 128], BF16)
    k.z1 = sbD("z1", [128, 128], BF16)
    k.zt0 = sbD("zt0", [128, 128], BF16)
    k.zt1 = sbD("zt1", [128, 128], BF16)
    k.q1s = sbD("q1s", [128, 128], BF16)
    k.r_pan = sbD("r_pan", [128, 576], BF16)
    k.blcol = sbD("blcol", [128, 4], F32)
    k.uv = sbD("uv", [128, 2 * DV], BF16)
    k.w_tm = sbD("w_tm", [128, 2 * 192], BF16)
    k.wfm = sbD("wfm", [128, 2 * 256], BF16)
    k.aqt = sbD("aqt", [128, 2 * 128], BF16)
    k.ktil = sbD("ktil", [128, 2 * 192], BF16)
    k.qfm = sbD("qfm", [128, 2 * 256], BF16)
    k.kfm = sbD("kfm", [128, 256], BF16)
    k.ubf = sbD("ubf", [128, DV], BF16)
    k.oscr = sbD("oscr", [128, DV], F32)
    k.obuf = sbD("obuf", [128, 2 * VD], BF16)
    k.osq = sbD("osq", [128, DV], F32)
    k.s0 = [sbD("s0a", [128, HH * DV], F32), sbD("s0b", [128, HH * DV], F32)]
    k.s1 = [sbD("s1a", [128, HH * DV], F32), sbD("s1b", [128, HH * DV], F32)]
    k.s0b = [sbD("s0ab", [128, HH * DV], BF16), sbD("s0bb", [128, HH * DV], BF16)]
    k.s1b = [sbD("s1ab", [128, HH * DV], BF16), sbD("s1bb", [128, HH * DV], BF16)]
    k.c["hsel"] = nc.inline_tensor(
        np.stack([np.repeat(np.eye(4, dtype=np.float32)[:, hh:hh + 1], 128, 1) for hh in range(4)], 0)
        .transpose(1, 0, 2).reshape(4, 512), name="c_hsel").ap()
    k.hsel = sbD("hsel", [4, 512], F32)
    k.s_lqkv = k.sem("s_lqkv")
    k.s_lhsel = k.sem("s_lhsel")
    k.s_spp = [k.sem("s_spp0"), k.sem("s_spp1")]
    k.s_hd = [k.sem("s_hd1"), k.sem("s_hd2")]
    k.s_pan = [k.sem("s_pan0"), k.sem("s_pan1")]
    k.s_so = [k.sem("s_so0"), k.sem("s_so1")]

    @k.block.sync
    def _(sync):
        sync.wait_ge(k.s_sc.h, k.A_done_sc)
        sync.dma_start(out=k.hsel[:, :], in_=k.c["hsel"]).then_inc(k.s_lhsel.inc(16), 16)

    @k.block.gpsimd
    def _(g):
        g.wait_ge(k.s_sc.h, k.A_done_sc)
        for kc in range(8):
            g.dma_start(out=k.wqb[:, kc * KD:(kc + 1) * KD],
                        in_=k.d["W_q"][128 * kc:128 * (kc + 1), :]).then_inc(k.s_lqkv.inc(16), 16)
            g.dma_start(out=k.wkb[:, kc * KD:(kc + 1) * KD],
                        in_=k.d["W_k"][128 * kc:128 * (kc + 1), :]).then_inc(k.s_lqkv.inc(16), 16)
            g.dma_start(out=k.wvb[:, kc * VD:(kc + 1) * VD],
                        in_=k.d["W_v"][128 * kc:128 * (kc + 1), :]).then_inc(k.s_lqkv.inc(16), 16)


def emit_chunk_lambda(k, c, chain):
    block, pb = k.block, k.pb
    s_mm, s_v, s_sc = k.s_mm, k.s_v, k.s_sc
    par = c % 2
    lo = 16 * par

    vnow = s_v.v

    @block.tensor
    def _(t):
        t.wait_ge(s_v.h, vnow)
        t.matmul(pb[5][:, 384:388], k.consts[:, 256:384], k.gkb[:, 8 * c:8 * c + 4],
                 start=True, stop=True)
        t.matmul(pb[5][:, 388:392], k.consts[:, 384:512], k.gkb[:, 8 * c:8 * c + 4],
                 start=True, stop=True).then_inc(s_mm.inc(), 1)

    mmv = s_mm.v

    @block.vector
    def _(v):
        v.wait_ge(s_mm.h, mmv)
        v.tensor_copy(out=k.lamv[:, lo:lo + 8], in_=pb[5][:, 384:392]).then_inc(s_v.inc(), 1)
        v.wait_ge(s_v.h, s_v.v)
        v.tensor_tensor(out=k.lamv[:, lo + 12:lo + 16], in0=k.lamv[:, lo:lo + 4],
                        in1=k.lamv[:, lo + 4:lo + 8], op=OP.add).then_inc(s_v.inc(), 1)

    vv = s_v.v

    @block.scalar
    def _(sc):
        sc.wait_ge(s_v.h, vv)
        sc.activation(out=k.lamlast[:, 4 * par:4 * par + 4], in_=k.lamv[:, lo + 12:lo + 16],
                      func=AF.Exp).then_inc(s_sc.inc(), 1)
        sc.wait_ge(s_sc.h, s_sc.v)
        sc.activation(out=k.lamv[:, lo + 12:lo + 16], in_=k.lamv[:, lo + 4:lo + 8],
                      func=AF.Exp)
        sc.activation(out=k.lamv[:, lo + 8:lo + 12], in_=k.lamv[:, lo:lo + 4],
                      func=AF.Exp).then_inc(s_sc.inc(), 1)

    scv = s_sc.v

    @block.tensor
    def _(t):
        t.wait_ge(s_sc.h, scv)
        t.transpose(pb[4][0:4, 384:512], k.lamv[:, lo:lo + 4],
                    k.consts[:, 0:128]).then_inc(s_mm.inc(), 1)

    mmv2 = s_mm.v

    @block.vector
    def _(v):
        v.wait_ge(s_mm.h, mmv2)
        v.tensor_copy(out=k.lamT[:, 128 * par:128 * (par + 1)],
                      in_=pb[4][0:4, 384:512]).then_inc(s_v.inc(), 1)


def emit_head(k, c, h, chain):
    block, pb = k.block, k.pb
    s_mm, s_v, s_sc = k.s_mm, k.s_v, k.s_sc
    par = c % 2
    lo = 16 * par
    j = 4 * c + h
    jp = j % 2
    cur, nxt = par, 1 - par
    bcol = k.gkb[:, 8 * c + 4 + h:8 * c + 5 + h]
    lamcol = k.lamv[:, lo + h:lo + h + 1]
    Lamcol = k.lamv[:, lo + 8 + h:lo + 9 + h]
    Lamrcol = k.lamv[:, lo + 12 + h:lo + 13 + h]
    Lamlastcol = k.lamlast[:, 4 * par + h:4 * par + h + 1]
    UVo, WFo, AQo, KTo, QFo = (k.uv[:, jp * DV:(jp + 1) * DV],
                               k.wfm[:, jp * 256:(jp + 1) * 256],
                               k.aqt[:, jp * 128:(jp + 1) * 128],
                               k.ktil[:, jp * 192:(jp + 1) * 192],
                               k.qfm[:, jp * 256:(jp + 1) * 256])

    if chain == 1:
        vproj = s_v.v

        @block.tensor
        def _(t):
            t.wait_ge(k.s_lqkv.h, k.s_lqkv.v)
            t.wait_ge(k.s_lhsel.h, k.s_lhsel.v)
            t.wait_ge(s_v.h, vproj)
            for kc in range(8):
                t.matmul(pb[0][:, 0:192], k.xs[:, kc * TOK + 128 * c:kc * TOK + 128 * (c + 1)],
                         k.wqb[:, kc * KD + 192 * h:kc * KD + 192 * (h + 1)],
                         start=(kc == 0), stop=(kc == 7))
            for kc in range(8):
                t.matmul(pb[0][:, 192:384], k.xs[:, kc * TOK + 128 * c:kc * TOK + 128 * (c + 1)],
                         k.wkb[:, kc * KD + 192 * h:kc * KD + 192 * (h + 1)],
                         start=(kc == 0), stop=(kc == 7))
            for kc in range(8):
                mi = t.matmul(pb[1][:, 0:384], k.xs[:, kc * TOK + 128 * c:kc * TOK + 128 * (c + 1)],
                              k.wvb[:, kc * VD + 384 * h:kc * VD + 384 * (h + 1)],
                              start=(kc == 0), stop=(kc == 7))
            mi.then_inc(s_mm.inc(), 1)

        m1 = s_mm.v

        @block.vector
        def _(v):
            v.wait_ge(s_mm.h, m1)
            if j >= 2:
                v.wait_ge(k.s_spp[j % 2].h, 16 * 5 * (j // 2))
            v.tensor_copy(out=k.q_tm[:, 0:192], in_=pb[0][:, 0:192])
            v.tensor_copy(out=k.k_tm[:, 0:192], in_=pb[0][:, 192:384]).then_inc(s_v.inc(), 1)
            v.wait_ge(s_v.h, s_v.v)
            v.tensor_tensor(out=k.qh_tm[:, :], in0=k.q_tm[:, 0:192], in1=k.q_tm[:, 0:192], op=OP.mult)
            v.tensor_tensor(out=k.kh_tm[:, :], in0=k.k_tm[:, 0:192], in1=k.k_tm[:, 0:192],
                            op=OP.mult).then_inc(s_v.inc(), 1)
            v.wait_ge(s_v.h, s_v.v)
            v.tensor_reduce(out=k.sqv[:, 0:1], in_=k.qh_tm[:, :], axis=AX, op=OP.add)
            v.tensor_reduce(out=k.sqv[:, 1:2], in_=k.kh_tm[:, :], axis=AX,
                            op=OP.add).then_inc(s_v.inc(), 1)

        v1 = s_v.v

        @block.scalar
        def _(sc):
            sc.wait_ge(s_v.h, v1)
            sc.activation(out=k.sqv[:, 2:4], in_=k.sqv[:, 0:2], func=AF.Ln,
                          bias=k.fconst[:, 0:1]).then_inc(s_sc.inc(), 1)
            sc.wait_ge(s_sc.h, s_sc.v)
            sc.activation(out=k.sqv[:, 4:6], in_=k.sqv[:, 2:4], func=AF.Exp,
                          scale=-0.5).then_inc(s_sc.inc(), 1)

        sc1 = s_sc.v

        @block.vector
        def _(v):
            v.wait_ge(s_sc.h, sc1)
            v.tensor_scalar(out=k.qh_tm[:, :], in0=k.q_tm[:, 0:192], scalar1=k.sqv[:, 4:5],
                            scalar2=None, op0=OP.mult)
            v.tensor_scalar(out=k.kh_tm[:, :], in0=k.k_tm[:, 0:192], scalar1=k.sqv[:, 5:6],
                            scalar2=None, op0=OP.mult).then_inc(s_v.inc(), 1)

        v2 = s_v.v

        @block.tensor
        def _(t):
            t.wait_ge(s_v.h, v2)
            t.transpose(pb[2][:, 0:128], k.qh_tm[:, 0:128], k.consts[:, 0:128])
            t.transpose(pb[2][0:64, 128:256], k.qh_tm[:, 128:192], k.consts[:, 0:128])
            t.transpose(pb[3][:, 0:128], k.kh_tm[:, 0:128], k.consts[:, 0:128])
            t.transpose(pb[3][0:64, 128:256], k.kh_tm[:, 128:192], k.consts[:, 0:128])
            t.matmul(pb[2][:, 256:384], k.hsel[:, 128 * h:128 * (h + 1)],
                     k.lamT[:, 128 * par:128 * (par + 1)],
                     start=True, stop=True).then_inc(s_mm.inc(), 1)

        m2 = s_mm.v

        @block.vector
        def _(v):
            v.wait_ge(s_mm.h, m2)
            v.tensor_copy(out=QFo[:, 0:128], in_=pb[2][:, 0:128])
            v.tensor_copy(out=QFo[0:64, 128:256], in_=pb[2][0:64, 128:256])
            v.tensor_copy(out=k.kfm[:, 0:128], in_=pb[3][:, 0:128])
            v.tensor_copy(out=k.kfm[0:64, 128:256], in_=pb[3][0:64, 128:256])
            v.tensor_scalar(out=k.dscr[:, :], in0=pb[2][:, 256:384], scalar1=lamcol,
                            scalar2=0.0, op0=OP.subtract, op1=OP.min)
            v.tensor_scalar(out=k.dloS[:, :], in0=pb[2][:, 256:384], scalar1=lamcol,
                            scalar2=0.0, op0=OP.subtract, op1=OP.max).then_inc(s_v.inc(), 1)

        v3 = s_v.v

        @block.scalar
        def _(sc):
            sc.wait_ge(s_v.h, v3)
            sc.activation(out=k.dupS[:, :], in_=k.dscr[:, :], func=AF.Exp).then_inc(s_sc.inc(), 1)
            sc.wait_ge(s_sc.h, s_sc.v)
            sc.activation(out=k.dscr[:, :], in_=k.dloS[:, :], func=AF.Exp,
                          scale=-1.0).then_inc(s_sc.inc(), 1)

        sc2 = s_sc.v

        @block.tensor
        def _(t):
            t.wait_ge(s_v.h, v3)
            t.matmul(pb[4][:, 0:128], k.kfm[:, 0:128], k.kfm[:, 0:128], start=True, stop=False)
            t.matmul(pb[4][:, 0:128], k.kfm[0:64, 128:256], k.kfm[0:64, 128:256], start=False, stop=True)
            t.matmul(pb[4][:, 128:256], k.kfm[:, 0:128], QFo[:, 0:128], start=True, stop=False)
            mi = t.matmul(pb[4][:, 128:256], k.kfm[0:64, 128:256], QFo[0:64, 128:256],
                          start=False, stop=True)
            mi.then_inc(s_mm.inc(), 1)

        m3 = s_mm.v

        @block.vector
        def _(v):
            v.wait_ge(s_mm.h, m3)
            v.wait_ge(s_sc.h, sc2)
            v.scalar_tensor_tensor(out=k.dloS[:, :], in0=k.dscr[:, :], scalar=bcol,
                                   in1=pb[4][:, 0:128], op0=OP.mult, op1=OP.mult)
            v.tensor_tensor(out=k.dscr2[:, :], in0=k.dupS[:, :], in1=pb[4][:, 128:256],
                            op=OP.mult).then_inc(s_v.inc(), 1)
            v.wait_ge(s_v.h, s_v.v)
            v.tensor_tensor(out=k.a_bf[:, jp * 128:(jp + 1) * 128], in0=k.dloS[:, :],
                            in1=k.constsb[:, 0:128], op=OP.mult)
            v.tensor_tensor(out=AQo, in0=k.dscr2[:, :],
                            in1=k.constsb[:, 128:256], op=OP.mult).then_inc(s_v.inc(), 1)

        v4 = s_v.v
        abf = k.a_bf[:, jp * 128:(jp + 1) * 128]

        @block.vector
        def _(v):
            v.wait_ge(s_v.h, v4)
            v.tensor_tensor(out=k.as_bf[:, :], in0=abf, in1=k.constsb[:, 384:512],
                            op=OP.mult).then_inc(s_v.inc(), 1)
            v.wait_ge(s_v.h, s_v.v)
            v.tensor_tensor(out=k.z0[:, :], in0=k.as_bf[:, :], in1=k.constsb[:, 256:384],
                            op=OP.add).then_inc(s_v.inc(), 1)

        vd = s_v.v

        @block.tensor
        def _(t):
            t.wait_ge(s_v.h, vd)
            t.transpose(pb[5][:, 0:64].bitcast(BF16), k.z0[:, :],
                        k.constsb[:, 256:384]).then_inc(s_mm.inc(), 1)

        md = s_mm.v

        @block.vector
        def _(v):
            v.wait_ge(s_mm.h, md)
            v.tensor_copy(out=k.zt0[:, :], in_=pb[5][:, 0:64].bitcast(BF16)).then_inc(s_v.inc(), 1)

        zc, ztc = k.z0, k.zt0
        zn, ztn = k.z1, k.zt1
        for lv in range(1, 7):
            vpre = s_v.v

            @block.vector
            def _(v, lv=lv):
                v.tensor_tensor(out=k.as_bf[:, :], in0=abf,
                                in1=k.constsb[:, 384 + 128 * lv:512 + 128 * lv],
                                op=OP.mult).then_inc(s_v.inc(), 1)

            va = s_v.v

            @block.tensor
            def _(t, ztc=ztc):
                t.wait_ge(s_v.h, va)
                t.matmul(pb[5][:, 0:128], k.as_bf[:, :], ztc[:, :],
                         start=True, stop=True).then_inc(s_mm.inc(), 1)

            mq = s_mm.v

            @block.vector
            def _(v):
                v.wait_ge(s_mm.h, mq)
                v.tensor_copy(out=k.q1s[:, :], in_=pb[5][:, 0:128]).then_inc(s_v.inc(), 1)

            vq = s_v.v

            @block.tensor
            def _(t, zc=zc):
                t.wait_ge(s_v.h, vq)
                t.matmul(pb[6][:, 0:128], k.q1s[:, :], zc[:, :], start=True, stop=True)
                t.matmul(pb[6][:, 128:256], zc[:, :], k.q1s[:, :],
                         start=True, stop=True).then_inc(s_mm.inc(), 1)

            mp = s_mm.v

            @block.vector
            def _(v, zc=zc, ztc=ztc, zn=zn, ztn=ztn):
                v.wait_ge(s_mm.h, mp)
                v.tensor_tensor(out=zn[:, :], in0=zc[:, :], in1=pb[6][:, 0:128], op=OP.add)
                v.tensor_tensor(out=ztn[:, :], in0=ztc[:, :], in1=pb[6][:, 128:256],
                                op=OP.add).then_inc(s_v.inc(), 1)

            zc, zn = zn, zc
            ztc, ztn = ztn, ztc

        ztfinal = ztc

        vr0 = s_v.v

        @block.vector
        def _(v):
            v.tensor_tensor(out=k.blcol[:, 0:1], in0=bcol, in1=Lamcol, op=OP.mult)
            v.tensor_scalar(out=k.r_pan[:, 0:384], in0=pb[1][:, 0:384], scalar1=bcol,
                            scalar2=None, op0=OP.mult)
            v.tensor_scalar(out=KTo, in0=k.kh_tm[:, :], scalar1=Lamrcol,
                            scalar2=None, op0=OP.mult).then_inc(s_v.inc(), 1)
            v.wait_ge(s_v.h, s_v.v)
            v.tensor_scalar(out=k.r_pan[:, 384:576], in0=k.kh_tm[:, :], scalar1=k.blcol[:, 0:1],
                            scalar2=None, op0=OP.mult).then_inc(s_v.inc(), 1)

        vr = s_v.v

        @block.tensor
        def _(t, ztfinal=ztfinal):
            t.wait_ge(s_v.h, vr)
            t.matmul(pb[7][:, 0:512], ztfinal[:, :], k.r_pan[:, 0:512], start=True, stop=True)
            t.matmul(pb[5][:, 128:192], ztfinal[:, :], k.r_pan[:, 512:576],
                     start=True, stop=True).then_inc(s_mm.inc(), 1)

        mpan = s_mm.v

        @block.vector
        def _(v):
            v.wait_ge(s_mm.h, mpan)
            v.tensor_copy(out=UVo, in_=pb[7][:, 0:384])
            v.tensor_copy(out=k.w_tm[:, jp * 192:jp * 192 + 128], in_=pb[7][:, 384:512])
            v.tensor_copy(out=k.w_tm[:, jp * 192 + 128:jp * 192 + 192],
                          in_=pb[5][:, 128:192]).then_inc(s_v.inc(), 1)

        vw = s_v.v

        @block.tensor
        def _(t):
            t.wait_ge(s_v.h, vw)
            t.transpose(pb[6][:, 0:64].bitcast(BF16), k.w_tm[:, jp * 192:jp * 192 + 128],
                        k.constsb[:, 256:384])
            t.transpose(pb[6][0:64, 64:128].bitcast(BF16), k.w_tm[:, jp * 192 + 128:jp * 192 + 192],
                        k.constsb[:, 256:384]).then_inc(s_mm.inc(), 1)

        mwf = s_mm.v

        @block.vector
        def _(v):
            v.wait_ge(s_mm.h, mwf)
            v.tensor_copy(out=WFo[:, 0:128], in_=pb[6][:, 0:64].bitcast(BF16))
            v.tensor_copy(out=WFo[0:64, 128:256],
                          in_=pb[6][0:64, 64:128].bitcast(BF16)).then_inc(s_v.inc(), 1)

        vwf = s_v.v

        @block.gpsimd
        def _(g):
            g.wait_ge(s_v.h, vwf)
            sp = k.s_spp[j % 2]
            g.dma_start(out=k.dr["sp_uv"][128 * j:128 * (j + 1), :], in_=UVo).then_inc(sp.inc(16), 16)
            g.dma_start(out=k.dr["sp_wfm"][128 * j:128 * (j + 1), :], in_=WFo).then_inc(sp.inc(16), 16)
            g.dma_start(out=k.dr["sp_aqt"][128 * j:128 * (j + 1), :], in_=AQo).then_inc(sp.inc(16), 16)
            g.dma_start(out=k.dr["sp_ktil"][128 * j:128 * (j + 1), :], in_=KTo).then_inc(sp.inc(16), 16)
            g.dma_start(out=k.dr["sp_qfm"][128 * j:128 * (j + 1), :], in_=QFo).then_inc(sp.inc(16), 16)

    else:
        @block.sync
        def _(sync):
            if j >= 2:
                sync.wait_ge(k.s_v.h, k.head_done_v[2][j - 2])
            sp = k.s_pan[jp]
            sync.dma_start(out=UVo, in_=k.dr["sp_uv"][128 * j:128 * (j + 1), :]).then_inc(sp.inc(16), 16)
            sync.dma_start(out=WFo, in_=k.dr["sp_wfm"][128 * j:128 * (j + 1), :]).then_inc(sp.inc(16), 16)
            sync.dma_start(out=AQo, in_=k.dr["sp_aqt"][128 * j:128 * (j + 1), :]).then_inc(sp.inc(16), 16)
            sync.dma_start(out=KTo, in_=k.dr["sp_ktil"][128 * j:128 * (j + 1), :]).then_inc(sp.inc(16), 16)
            sync.dma_start(out=QFo, in_=k.dr["sp_qfm"][128 * j:128 * (j + 1), :]).then_inc(sp.inc(16), 16)

    panwait = (k.s_pan[jp], 16 * 5 * (j // 2 + 1)) if chain == 2 else None

    vws = s_v.v

    @block.tensor
    def _(t):
        if panwait is not None:
            t.wait_ge(panwait[0].h, panwait[1])
        t.wait_ge(s_v.h, vws)
        t.matmul(pb[6][:, 0:384], WFo[:, 0:128], k.s0b[cur][:, DV * h:DV * (h + 1)],
                 start=True, stop=False)
        t.matmul(pb[6][:, 0:384], WFo[0:64, 128:256], k.s1b[cur][0:64, DV * h:DV * (h + 1)],
                 start=False, stop=True).then_inc(s_mm.inc(), 1)

    mws = s_mm.v

    @block.vector
    def _(v):
        v.wait_ge(s_mm.h, mws)
        v.tensor_tensor(out=k.ubf[:, :], in0=UVo, in1=pb[6][:, 0:384],
                        op=OP.subtract).then_inc(s_v.inc(), 1)

    vu = s_v.v

    if chain == 2:
        @block.tensor
        def _(t):
            t.wait_ge(s_v.h, vu)
            t.matmul(pb[2][:, 0:384], QFo[:, 0:128], k.s0b[cur][:, DV * h:DV * (h + 1)],
                     start=True, stop=False)
            t.matmul(pb[2][:, 0:384], QFo[0:64, 128:256], k.s1b[cur][0:64, DV * h:DV * (h + 1)],
                     start=False, stop=True)
            t.matmul(pb[3][:, 0:384], AQo, k.ubf[:, :], start=True, stop=True).then_inc(s_mm.inc(), 1)

        mo = s_mm.v

        @block.vector
        def _(v):
            v.wait_ge(s_mm.h, mo)
            if c >= 2 and h == 0:
                v.wait_ge(k.s_so[c % 2].h, 16 * (c // 2))
            v.tensor_scalar(out=k.oscr[:, :], in0=pb[2][:, 0:384], scalar1=Lamcol,
                            scalar2=None, op0=OP.mult).then_inc(s_v.inc(), 1)
            v.wait_ge(s_v.h, s_v.v)
            v.tensor_tensor(out=k.obuf[:, par * VD + DV * h:par * VD + DV * (h + 1)],
                            in0=k.oscr[:, :], in1=pb[3][:, 0:384], op=OP.add).then_inc(s_v.inc(), 1)

        vo = s_v.v

        @block.scalar
        def _(sc):
            sc.wait_ge(s_v.h, vo)
            sc.wait_ge(s_sc.h, s_sc.v)
            sc.activation(out=k.osq[:, :], in_=k.obuf[:, par * VD + DV * h:par * VD + DV * (h + 1)],
                          func=AF.Square,
                          accum_out=k.stat[:, 4 * c + h:4 * c + h + 1]).then_inc(s_sc.inc(), 1)

    @block.tensor
    def _(t):
        t.wait_ge(s_v.h, vu)
        t.matmul(pb[0][:, 0:384], KTo[:, 0:128], k.ubf[:, :], start=True, stop=True)
        t.matmul(pb[1][0:64, 0:384], KTo[:, 128:192], k.ubf[:, :],
                 start=True, stop=True).then_inc(s_mm.inc(), 1)

    msu = s_mm.v

    @block.vector
    def _(v):
        v.wait_ge(s_mm.h, msu)
        v.scalar_tensor_tensor(out=k.s0[nxt][:, DV * h:DV * (h + 1)],
                               in0=k.s0[cur][:, DV * h:DV * (h + 1)], scalar=Lamlastcol,
                               in1=pb[0][:, 0:384], op0=OP.mult, op1=OP.add)
        v.scalar_tensor_tensor(out=k.s1[nxt][0:64, DV * h:DV * (h + 1)],
                               in0=k.s1[cur][0:64, DV * h:DV * (h + 1)], scalar=Lamlastcol[0:64, :],
                               in1=pb[1][0:64, 0:384], op0=OP.mult, op1=OP.add).then_inc(s_v.inc(), 1)
        v.wait_ge(s_v.h, s_v.v)
        v.tensor_copy(out=k.s0b[nxt][:, DV * h:DV * (h + 1)], in_=k.s0[nxt][:, DV * h:DV * (h + 1)])
        v.tensor_copy(out=k.s1b[nxt][0:64, DV * h:DV * (h + 1)],
                      in_=k.s1[nxt][0:64, DV * h:DV * (h + 1)]).then_inc(s_v.inc(), 1)
    k.head_done_v[chain][j] = k.s_v.v


def emit_delta(k, chain):
    nc, block = k.nc, k.block
    if not hasattr(k, "head_done_v"):
        k.head_done_v = {1: {}, 2: {}}
    if chain == 1:
        delta_alloc(k)

        @block.vector
        def _(v):
            v.memset(k.s0[0][:, :], 0.0)
            v.memset(k.s1[0][:, :], 0.0)
            v.memset(k.s0b[0][:, :], 0.0)
            v.memset(k.s1b[0][:, :], 0.0)
            v.memset(k.wfm[64:128, 128:256], 0.0)
            v.memset(k.wfm[64:128, 384:512], 0.0)
            v.memset(k.qfm[64:128, 128:256], 0.0)
            v.memset(k.qfm[64:128, 384:512], 0.0)
            v.memset(k.s1[1][64:128, :], 0.0)
            v.memset(k.s1b[1][64:128, :], 0.0).then_inc(k.s_v.inc(), 1)
    else:
        @block.sync
        def _(sync):
            sync.wait_ge(k.s_spp[0].h, k.s_spp[0].v)
            sync.wait_ge(k.s_spp[1].h, k.s_spp[1].v)
            sync.wait_ge(k.s_v.h, k.s_v.v)

    for c in range(NCH):
        emit_chunk_lambda(k, c, chain)
        for h in range(HH):
            emit_head(k, c, h, chain)
        if chain == 2:
            par = c % 2
            vdone = k.s_v.v

            @block.gpsimd
            def _(g, c=c, par=par):
                g.wait_ge(k.s_v.h, vdone)
                g.dma_start(out=k.dr["sp_o"][128 * c:128 * (c + 1), :],
                            in_=k.obuf[:, par * VD:(par + 1) * VD]).then_inc(k.s_so[c % 2].inc(16), 16)

    if chain == 1:
        k.chain1_hd = k.head_done_v[1][4 * NCH - 1]
    else:
        k.chain2_done_so = (k.s_so[0].v, k.s_so[1].v)
        k.chain2_hd = k.head_done_v[2][4 * NCH - 1]
        dbo = k.dbg_tensor("o", [TOK, VD])
        if dbo is not None:
            @block.gpsimd
            def _(g):
                g.wait_ge(k.s_so[0].h, k.s_so[0].v)
                g.wait_ge(k.s_so[1].h, k.s_so[1].v)
                g.dma_start(out=dbo, in_=k.dr["sp_o"][:, :]).then_inc(k.s_sp.inc(16), 16)


def emit_state_ag(k):
    nc, block = k.nc, k.block
    s_v = k.s_v
    mock = getattr(k, "mock_ag", False)

    if mock:
        @block.vector
        def _(v):
            v.wait_ge(s_v.h, k.chain1_hd)
            v.memset(k.s0[0][:, :], 0.0)
            v.memset(k.s1[0][:, :], 0.0)
            v.memset(k.s0b[0][:, :], 0.0)
            v.memset(k.s1b[0][:, :], 0.0).then_inc(s_v.inc(), 1)
        return

    s_ag = k.sem("s_ag")
    s_agl = k.sem("s_agl")
    cin, cout = k.dr["cc_state_in"], k.dr["cc_state_out"]
    fin = NCH % 2

    @block.gpsimd
    def _(g):
        g.wait_ge(k.s_v.h, k.chain1_hd)
        g.dma_start(out=cin[0:128, :], in_=k.s0[fin][:, :]).then_inc(s_ag.inc(16), 16)
        g.dma_start(out=cin[128:192, :], in_=k.s1[fin][0:64, :]).then_inc(s_ag.inc(16), 16)
        g.wait_ge(s_ag.h, s_ag.v)
        g.collective_compute(
            "AllGather", mybir.AluOpType.bypass,
            replica_groups=[[0, 1, 2, 3, 4, 5, 6, 7]],
            ins=[cin.opt()], outs=[cout.opt()],
        ).then_inc(s_agl.inc(1), 1)

    agv = k.s_ag_done = k.s_ag = s_ag

    s_msk = k.sem("s_msk")

    @block.sync
    def _(sync):
        sync.dma_start(out=k.sval[0:1, 8:16], in_=k.d["smask"]).then_inc(s_msk.inc(16), 16)

    @block.tensor
    def _(t):
        t.wait_ge(s_msk.h, s_msk.v)
        t.matmul(k.pb[4][:, 0:8], k.consts[0:1, 128:256], k.sval[0:1, 8:16],
                 start=True, stop=True).then_inc(k.s_mm.inc(), 1)

    mmv = k.s_mm.v

    @block.vector
    def _(v):
        v.wait_ge(k.s_mm.h, mmv)
        v.tensor_copy(out=k.sval[:, 0:8], in_=k.pb[4][:, 0:8]).then_inc(s_v.inc(), 1)

    slab0 = k.sbD("slab0", [128, HH * DV], F32)
    slab1 = k.sbD("slab1", [128, HH * DV], F32)

    @block.vector
    def _(v):
        v.wait_ge(s_ag.h, 32)
        v.memset(k.s0[0][:, :], 0.0)
        v.memset(k.s1[0][:, :], 0.0).then_inc(s_v.inc(), 1)

    for ci in range(8):
        sidx, dst = ci % 2, (ci + 1) % 2
        lv = k.sem(f"s_agld{ci}")
        vprev = s_v.v

        @block.sync
        def _(sync, ci=ci, lv=lv, vprev=vprev):
            sync.wait_ge(s_agl.h, s_agl.v)
            sync.wait_ge(s_v.h, vprev)
            sync.dma_start(out=slab0[:, :], in_=k.dr["cc_state_out"][192 * ci:192 * ci + 128, :]).then_inc(lv.inc(16), 16)
            sync.dma_start(out=slab1[0:64, :], in_=k.dr["cc_state_out"][192 * ci + 128:192 * (ci + 1), :]).then_inc(lv.inc(16), 16)

        @block.vector
        def _(v, ci=ci, sidx=sidx, dst=dst, lv=lv, vprev=vprev):
            v.wait_ge(lv.h, lv.v)
            v.scalar_tensor_tensor(out=k.s0[dst][:, :], in0=slab0[:, :],
                                   scalar=k.sval[:, ci:ci + 1], in1=k.s0[sidx][:, :],
                                   op0=OP.mult, op1=OP.add)
            v.scalar_tensor_tensor(out=k.s1[dst][0:64, :], in0=slab1[0:64, :],
                                   scalar=k.sval[0:64, ci:ci + 1], in1=k.s1[sidx][0:64, :],
                                   op0=OP.mult, op1=OP.add).then_inc(s_v.inc(), 1)

    vfin = s_v.v

    @block.vector
    def _(v):
        v.wait_ge(s_v.h, vfin)
        v.tensor_copy(out=k.s0b[0][:, :], in_=k.s0[0][:, :])
        v.tensor_copy(out=k.s1b[0][0:64, :], in_=k.s1[0][0:64, :]).then_inc(s_v.inc(), 1)

    k.stateag_done_v = s_v.v


def emit_epilogue(k):
    nc, block, pb = k.nc, k.block, k.pb
    s_mm, s_v, s_sc = k.s_mm, k.s_v, k.s_sc

    scv0 = s_sc.v

    @block.scalar
    def _(sc):
        sc.wait_ge(s_sc.h, s_sc.v)
        sc.wait_ge(s_v.h, k.chain2_hd)
        sc.activation(out=k.stat[:, 64:128], in_=k.stat[:, 0:64], func=AF.Ln,
                      scale=1.0 / DV, bias=k.fconst[:, 1:2]).then_inc(s_sc.inc(), 1)
        sc.wait_ge(s_sc.h, s_sc.v)
        sc.activation(out=k.stat[:, 0:64], in_=k.stat[:, 64:128], func=AF.Exp,
                      scale=-0.5).then_inc(s_sc.inc(), 1)
    rs_done_sc = s_sc.v

    k.scopeD.close()
    scope1 = ExitStack()
    sb1 = lambda *a: scope1.enter_context(nc.sbuf_tensor(*a))
    wgb = sb1("wgb", [128, 8 * VD], BF16)
    otile = sb1("otile", [128, 2 * VD], BF16)
    gs = sb1("gs", [128, 2 * VD], BF16)
    og = sb1("og", [128, 2 * VD], BF16)
    t1s = sb1("t1s", [128, VD], F32)
    s_lwg2 = k.sem("s_lwg2")
    s_lo = [k.sem("s_lo0"), k.sem("s_lo1")]
    s_og = [k.sem("s_og0"), k.sem("s_og1")]

    @block.gpsimd
    def _(g):
        g.wait_ge(s_v.h, k.chain2_hd)
        g.wait_ge(k.s_so[0].h, k.s_so[0].v)
        g.wait_ge(k.s_so[1].h, k.s_so[1].v)
        for kc in range(8):
            g.dma_start(out=wgb[:, kc * VD:(kc + 1) * VD],
                        in_=k.d["W_g"][128 * kc:128 * (kc + 1), :]).then_inc(s_lwg2.inc(16), 16)

    ot_ready = {}
    og_done_v = {}
    tile_v_done = {}

    @block.sync
    def _(sync):
        sync.wait_ge(s_v.h, k.chain2_hd)
        sync.wait_ge(k.s_so[0].h, k.s_so[0].v)
        sync.wait_ge(k.s_so[1].h, k.s_so[1].v)

    for t in range(NT):
        par = t % 2

        @block.sync
        def _(sync, t=t, par=par):
            if t >= 2:
                sync.wait_ge(s_v.h, tile_v_done[t - 2])
            sync.dma_start(out=otile[:, par * VD:(par + 1) * VD],
                           in_=k.dr["sp_o"][128 * t:128 * (t + 1), :]).then_inc(s_lo[par].inc(16), 16)

        vprev_g = s_v.v

        @block.tensor
        def _(tt_, t=t, par=par):
            tt_.wait_ge(s_lwg2.h, s_lwg2.v)
            tt_.wait_ge(s_v.h, vprev_g)
            for nc3 in range(3):
                for kc in range(8):
                    mi = tt_.matmul(pb[nc3][:, 0:512],
                                    k.xs[:, kc * TOK + 128 * t:kc * TOK + 128 * (t + 1)],
                                    wgb[:, kc * VD + 512 * nc3:kc * VD + 512 * (nc3 + 1)],
                                    start=(kc == 0), stop=(kc == 7))
            mi.then_inc(s_mm.inc(), 1)

        mg = s_mm.v

        @block.scalar
        def _(sc, t=t, par=par):
            sc.wait_ge(s_mm.h, mg)
            if t == 0:
                sc.wait_ge(s_sc.h, rs_done_sc)
            for nc3 in range(3):
                si = sc.activation(out=gs[:, par * VD + 512 * nc3:par * VD + 512 * (nc3 + 1)],
                                   in_=pb[nc3][:, 0:512], func=AF.Silu)
            si.then_inc(s_sc.inc(), 1)

        sg = s_sc.v

        @block.vector
        def _(v, t=t, par=par):
            v.wait_ge(s_sc.h, sg)
            v.wait_ge(s_lo[par].h, s_lo[par].v)
            if t >= 2:
                v.wait_ge(s_og[par].h, 16 * (t // 2))
            for h in range(HH):
                vi = v.scalar_tensor_tensor(out=t1s[:, DV * h:DV * (h + 1)],
                                            in0=otile[:, par * VD + DV * h:par * VD + DV * (h + 1)],
                                            scalar=k.stat[:, 4 * t + h:4 * t + h + 1],
                                            in1=k.gnwb[:, :], op0=OP.mult, op1=OP.mult)
            vi.then_inc(s_v.inc(), 1)
            v.wait_ge(s_v.h, s_v.v)
            v.tensor_tensor(out=og[:, par * VD:(par + 1) * VD], in0=t1s[:, :],
                            in1=gs[:, par * VD:(par + 1) * VD], op=OP.mult).then_inc(s_v.inc(), 1)
        tile_v_done[t] = s_v.v

        @block.gpsimd
        def _(g, t=t, par=par):
            g.wait_ge(s_v.h, tile_v_done[t])
            g.dma_start(out=k.dr["sp_og"][128 * t:128 * (t + 1), :],
                        in_=og[:, par * VD:(par + 1) * VD]).then_inc(s_og[par].inc(16), 16)

    c1_v = s_v.v
    c1_og = (s_og[0].v, s_og[1].v)

    scope1.close()
    k.scope_xs.close()
    scope2 = ExitStack()
    k.scope2 = scope2
    sb2 = lambda *a: scope2.enter_context(nc.sbuf_tensor(*a))
    wsob = sb2("wsob", [128, 12 * DM], BF16)
    woutb = sb2("woutb", [128, 16 * DM], BF16)
    ogfm = sb2("ogfm", [128, 12 * TOK], BF16)
    ybuf = sb2("ybuf", [128, 8 * TOK], BF16)
    xres = sb2("xres", [128, 2 * DM], F32)
    pre1 = sb2("pre1", [128, DM], F32)
    sqscr = sb2("sqscr", [128, DM], F32)
    x1buf = sb2("x1buf", [128, 2 * DM], F32)
    lnscr = sb2("lnscr", [128, 16], F32)
    s_lso = k.sem("s_lso")
    s_ogt = k.sem("s_ogt")
    s_lxr = [k.sem("s_lxr0"), k.sem("s_lxr1")]
    s_x1 = [k.sem("s_x10"), k.sem("s_x11")]

    @block.gpsimd
    def _(g):
        g.wait_ge(s_v.h, c1_v)
        for kc in range(12):
            g.dma_start(out=wsob[:, kc * DM:(kc + 1) * DM],
                        in_=k.d["W_so"][128 * kc:128 * (kc + 1), :]).then_inc(s_lso.inc(16), 16)
        for kc in range(16):
            g.dma_start(out=woutb[:, kc * DM:(kc + 1) * DM],
                        in_=k.d["W_out"][128 * kc:128 * (kc + 1), :]).then_inc(s_lso.inc(16), 16)

    @block.sync
    def _(sync):
        sync.wait_ge(s_og[0].h, c1_og[0])
        sync.wait_ge(s_og[1].h, c1_og[1])
        sync.wait_ge(s_v.h, c1_v)
        for ic in range(12):
            sync.dma_start(out=ogfm[:, ic * TOK:(ic + 1) * TOK],
                           in_=k.dr["sp_og"][0:TOK, 128 * ic:128 * (ic + 1)],
                           transpose=True).then_inc(s_ogt.inc(16), 16)

    yjobs = [(oc, tg) for oc in range(8) for tg in range(4)]
    yev = {}
    for i, (oc, tg) in enumerate(yjobs):
        bank = pb[i % 2]

        @block.tensor
        def _(tt_, i=i, oc=oc, tg=tg, bank=bank):
            if i == 0:
                tt_.wait_ge(s_lso.h, s_lso.v)
                tt_.wait_ge(s_ogt.h, s_ogt.v)
            if i >= 2:
                tt_.wait_ge(s_v.h, yev[i - 2])
            for kc in range(12):
                mi = tt_.matmul(bank[:, 0:512],
                                wsob[:, kc * DM + oc * 128:kc * DM + (oc + 1) * 128],
                                ogfm[:, kc * TOK + 512 * tg:kc * TOK + 512 * (tg + 1)],
                                start=(kc == 0), stop=(kc == 11))
            mi.then_inc(s_mm.inc(), 1)

        my = s_mm.v

        @block.vector
        def _(v, i=i, oc=oc, tg=tg, bank=bank, my=my):
            v.wait_ge(s_mm.h, my)
            v.tensor_copy(out=ybuf[:, oc * TOK + 512 * tg:oc * TOK + 512 * (tg + 1)],
                          in_=bank[:, 0:512]).then_inc(s_v.inc(), 1)
        yev[i] = s_v.v

    y_done_v = s_v.v

    x1_v_done = {}
    for t in range(NT):
        par = t % 2

        @block.sync
        def _(sync, t=t, par=par):
            if t >= 2:
                sync.wait_ge(s_v.h, x1_v_done[t - 2])
            sync.dma_start(out=xres[:, par * DM:(par + 1) * DM],
                           in_=k.d["x_pad"][128 * t + 1:128 * (t + 1) + 1, :]).then_inc(s_lxr[par].inc(16), 16)

        vprev_a = s_v.v

        @block.tensor
        def _(tt_, t=t, par=par):
            tt_.wait_ge(s_v.h, vprev_a)
            for nc2 in range(2):
                for kc in range(16):
                    srcb = ybuf if kc < 8 else k.zs
                    kcc = kc % 8
                    mi = tt_.matmul(pb[2 + 2 * par + nc2][:, 0:512],
                                    srcb[:, kcc * TOK + 128 * t:kcc * TOK + 128 * (t + 1)],
                                    woutb[:, kc * DM + 512 * nc2:kc * DM + 512 * (nc2 + 1)],
                                    start=(kc == 0), stop=(kc == 15))
            mi.then_inc(s_mm.inc(), 1)

        ma = s_mm.v

        @block.vector
        def _(v, t=t, par=par, ma=ma):
            v.wait_ge(s_mm.h, ma)
            v.wait_ge(s_lxr[par].h, s_lxr[par].v)
            if t >= 2:
                v.wait_ge(s_x1[par].h, 16 * (t // 2))
            v.tensor_tensor(out=pre1[:, 0:512], in0=pb[2 + 2 * par][:, 0:512],
                            in1=xres[:, par * DM:par * DM + 512], op=OP.add)
            v.tensor_tensor(out=pre1[:, 512:1024], in0=pb[3 + 2 * par][:, 0:512],
                            in1=xres[:, par * DM + 512:par * DM + 1024], op=OP.add).then_inc(s_v.inc(), 1)
            v.wait_ge(s_v.h, s_v.v)
            v.tensor_reduce(out=lnscr[:, 0:1], in_=pre1[:, :], axis=AX, op=OP.add).then_inc(s_v.inc(), 1)

        vpre = s_v.v

        @block.scalar
        def _(sc, t=t):
            sc.wait_ge(s_v.h, vpre)
            sc.wait_ge(s_sc.h, s_sc.v)
            sc.activation(out=sqscr[:, :], in_=pre1[:, :], func=AF.Square,
                          accum_out=lnscr[:, 1:2]).then_inc(s_sc.inc(), 1)

        sqd = s_sc.v

        @block.vector
        def _(v, t=t):
            v.wait_ge(s_sc.h, sqd)
            v.tensor_scalar(out=lnscr[:, 2:3], in0=lnscr[:, 0:1], scalar1=1.0 / DM,
                            scalar2=None, op0=OP.mult)
            v.tensor_scalar(out=lnscr[:, 3:4], in0=lnscr[:, 1:2], scalar1=1.0 / DM,
                            scalar2=None, op0=OP.mult).then_inc(s_v.inc(), 1)
            v.wait_ge(s_v.h, s_v.v)
            v.tensor_tensor(out=lnscr[:, 4:5], in0=lnscr[:, 2:3], in1=lnscr[:, 2:3], op=OP.mult).then_inc(s_v.inc(), 1)
            v.wait_ge(s_v.h, s_v.v)
            v.tensor_tensor(out=lnscr[:, 5:6], in0=lnscr[:, 3:4], in1=lnscr[:, 4:5],
                            op=OP.subtract).then_inc(s_v.inc(), 1)

        vvar = s_v.v

        @block.scalar
        def _(sc, t=t):
            sc.wait_ge(s_v.h, vvar)
            sc.activation(out=lnscr[:, 6:7], in_=lnscr[:, 5:6], func=AF.Ln,
                          bias=k.fconst[:, 1:2]).then_inc(s_sc.inc(), 1)
            sc.wait_ge(s_sc.h, s_sc.v)
            sc.activation(out=lnscr[:, 7:8], in_=lnscr[:, 6:7], func=AF.Exp,
                          scale=-0.5).then_inc(s_sc.inc(), 1)

        srs = s_sc.v

        @block.vector
        def _(v, t=t, par=par):
            v.wait_ge(s_sc.h, srs)
            v.tensor_tensor(out=lnscr[:, 8:9], in0=lnscr[:, 2:3], in1=lnscr[:, 7:8], op=OP.mult).then_inc(s_v.inc(), 1)
            v.wait_ge(s_v.h, s_v.v)
            v.tensor_scalar(out=lnscr[:, 9:10], in0=lnscr[:, 8:9], scalar1=-1.0,
                            scalar2=None, op0=OP.mult).then_inc(s_v.inc(), 1)
            v.wait_ge(s_v.h, s_v.v)
            v.tensor_scalar(out=x1buf[:, par * DM:(par + 1) * DM], in0=pre1[:, :],
                            scalar1=lnscr[:, 7:8], scalar2=lnscr[:, 9:10],
                            op0=OP.mult, op1=OP.add).then_inc(s_v.inc(), 1)
            v.wait_ge(s_v.h, s_v.v)
            v.tensor_reduce(out=k.stat[:, 160 + t:161 + t], in_=x1buf[:, par * DM:(par + 1) * DM],
                            axis=AX, op=OP.max, apply_absolute_value=True).then_inc(s_v.inc(), 1)
        x1_v_done[t] = s_v.v

        @block.gpsimd
        def _(g, t=t, par=par):
            g.wait_ge(s_v.h, x1_v_done[t])
            g.dma_start(out=k.dr["sp_x1"][128 * t:128 * (t + 1), :],
                        in_=x1buf[:, par * DM:(par + 1) * DM]).then_inc(s_x1[par].inc(16), 16)

    k.epi_v = s_v.v
    k.epi_x1 = (s_x1[0].v, s_x1[1].v)
    k.s_x1_sems = s_x1

    dbx1 = k.dbg_tensor("x1", [TOK, DM])
    if dbx1 is not None:
        @block.gpsimd
        def _(g):
            g.wait_ge(s_x1[0].h, s_x1[0].v)
            g.wait_ge(s_x1[1].h, s_x1[1].v)
            g.dma_start(out=dbx1, in_=k.dr["sp_x1"][:, :]).then_inc(k.s_sp.inc(16), 16)


def emit_ffn(k):
    nc, block, pb = k.nc, k.block, k.pb
    s_mm, s_v, s_sc = k.s_mm, k.s_v, k.s_sc

    def agather(cin, cout, sem):
        k._agctr = getattr(k, "_agctr", 0) + 1
        csem = k.sem(f"s_agc{k._agctr}")
        k.block.materialize(k.realblock)

        @k.realblock.gpsimd
        def _(g):
            g.wait_ge(sem.h, sem.v)
            g.collective_compute(
                "AllGather", mybir.AluOpType.bypass,
                replica_groups=[[0, 1, 2, 3, 4, 5, 6, 7]],
                ins=[cin.opt()], outs=[cout.opt()],
            ).then_inc(csem.inc(1), 1)
        return csem

    def bcast_scalar(pairs, waitv):
        @block.tensor
        def _(t):
            t.wait_ge(s_v.h, waitv)
            for i, (srcc, _) in enumerate(pairs):
                mi = t.matmul(pb[6][:, 8 + i:9 + i], k.consts[0:1, 128:256],
                              k.sval[0:1, srcc:srcc + 1], start=True, stop=True)
            mi.then_inc(s_mm.inc(), 1)
        mmv = s_mm.v

        @block.vector
        def _(v):
            v.wait_ge(s_mm.h, mmv)
            for i, (_, dstc) in enumerate(pairs):
                vi = v.tensor_copy(out=k.sval[:, dstc:dstc + 1], in_=pb[6][:, 8 + i:9 + i])
            vi.then_inc(s_v.inc(), 1)
        return s_v.v

    s_ag1 = k.sem("s_ag1")
    vx = s_v.v

    @block.vector
    def _(v):
        v.wait_ge(s_v.h, vx)
        v.tensor_reduce(out=k.sval[:, 20:21], in_=k.stat[:, 160:176], axis=AX,
                        op=OP.max).then_inc(s_v.inc(), 1)

    vmx = s_v.v

    @block.tensor
    def _(t):
        t.wait_ge(s_v.h, vmx)
        t.transpose(pb[6][0:1, 0:128], k.sval[:, 20:21], k.consts[:, 0:128]).then_inc(s_mm.inc(), 1)

    mmx = s_mm.v

    @block.vector
    def _(v):
        v.wait_ge(s_mm.h, mmx)
        v.memset(k.sval[0:1, 22:23], 0.0)
        v.tensor_reduce(out=k.sval[0:1, 21:22], in_=pb[6][0:1, 0:128], axis=AX,
                        op=OP.max).then_inc(s_v.inc(), 1)

    vgm = s_v.v

    @block.gpsimd
    def _(g):
        g.wait_ge(s_v.h, vgm)
        g.dma_start(out=k.dr["cc_s0_in"], in_=k.sval[0:1, 21:23]).then_inc(s_ag1.inc(16), 16)

    s_ag1c = agather(k.dr["cc_s0_in"], k.dr["cc_s0_out"], s_ag1)

    def max_from_ag(cout, agsem, dstcol, n=8):
        k._rdctr = getattr(k, "_rdctr", 0) + 1
        s_rd = k.sem(f"s_rd{dstcol}_{k._rdctr}")

        @block.sync
        def _(sync):
            sync.wait_ge(agsem.h, agsem.v)
            sync.dma_start(out=k.sval[0:1, 24:24 + 2 * n], in_=cout).then_inc(s_rd.inc(16), 16)

        @block.vector
        def _(v):
            v.wait_ge(s_rd.h, s_rd.v)
            g2 = k.sval[0:1, 24:24 + 2 * n].rearrange("p (a b) -> p b a", b=2)
            v.tensor_reduce(out=k.sval[0:1, dstcol:dstcol + 1], in_=g2[:, 0:1, :], axis=AX, op=OP.max)
            v.tensor_reduce(out=k.sval[0:1, dstcol + 1:dstcol + 2], in_=g2[:, 1:2, :],
                            axis=AX, op=OP.max).then_inc(s_v.inc(), 1)
        return s_v.v

    max_from_ag(k.dr["cc_s0_out"], s_ag1c, 16)

    @block.vector
    def _(v):
        v.wait_ge(s_v.h, s_v.v)
        v.tensor_scalar(out=k.sval[0:1, 19:20], in0=k.sval[0:1, 16:17],
                        scalar1=1.0 / QMAX, scalar2=None, op0=OP.mult).then_inc(s_v.inc(), 1)
        v.wait_ge(s_v.h, s_v.v)
        v.reciprocal(out=k.sval[0:1, 20:21], in_=k.sval[0:1, 19:20]).then_inc(s_v.inc(), 1)

    bc_sx_v = bcast_scalar([(19, 8), (20, 9)], s_v.v)

    k.scope2.close()
    scopeF = ExitStack()
    k.scopeF = scopeF
    wsh = scopeF.enter_context(nc.sbuf_tensor("wsh", [128, FI], F32))
    wi16 = scopeF.enter_context(nc.sbuf_tensor("wi16", [128, FI], I16))
    wdqb = scopeF.enter_context(nc.sbuf_tensor("wdqb", [128, FI], BF16))
    wsc = scopeF.enter_context(nc.sbuf_tensor("wsc", [128, 3 * 96], F32))
    wdqt = scopeF.enter_context(nc.sbuf_tensor("wdqt", [128, FI], BF16))
    s_lsh = k.sem("s_lsh")
    s_wq = k.sem("s_wq")
    block.materialize(k.realblock)
    dq = k.realblock

    for wt in range(7):
        if wt < 3:
            srcap = k.d["wg_shard"][128 * wt:128 * (wt + 1), :]
            ncols, nblk, dstrow = DM, 32, 128 * wt
        elif wt < 6:
            srcap = k.d["wu_shard"][128 * (wt - 3):128 * (wt - 2), :]
            ncols, nblk, dstrow = DM, 32, 384 + 128 * (wt - 3)
        else:
            srcap = k.d["wd_shard"]
            ncols, nblk, dstrow = FI, 96, 768

        vW = s_v.v

        @dq.sync
        def _(sync, wt=wt, srcap=srcap, ncols=ncols, vW=vW):
            if wt > 0:
                sync.wait_ge(s_v.h, vW)
            else:
                sync.wait_ge(s_v.h, k.epi_v)
                sync.wait_ge(k.s_x1_sems[0].h, k.epi_x1[0])
                sync.wait_ge(k.s_x1_sems[1].h, k.epi_x1[1])
            sync.dma_start(out=wsh[:, 0:ncols], in_=srcap).then_inc(s_lsh.inc(16), 16)

        ldW = s_lsh.v

        @dq.vector
        def _(v, wt=wt, ncols=ncols, nblk=nblk, ldW=ldW):
            v.wait_ge(s_lsh.h, ldW)
            if wt == 0:
                v.wait_ge(s_v.h, k.epi_v)
            v.tensor_reduce(out=wsc[:, 0:nblk], in_=wsh[:, 0:ncols].rearrange("p (a b) -> p a b", b=32),
                            axis=AX, op=OP.max, apply_absolute_value=True).then_inc(s_v.inc(), 1)
            v.wait_ge(s_v.h, s_v.v)
            v.tensor_scalar(out=wsc[:, 96:96 + nblk], in0=wsc[:, 0:nblk], scalar1=1.0 / 7.0,
                            scalar2=1e-12, op0=OP.mult, op1=OP.max).then_inc(s_v.inc(), 1)
            v.wait_ge(s_v.h, s_v.v)
            v.reciprocal(out=wsc[:, 192:192 + nblk], in_=wsc[:, 96:96 + nblk]).then_inc(s_v.inc(), 1)
            v.wait_ge(s_v.h, s_v.v)
            for b in range(nblk):
                vi = v.tensor_scalar(out=wi16[:, 32 * b:32 * (b + 1)], in0=wsh[:, 32 * b:32 * (b + 1)],
                                     scalar1=wsc[:, 192 + b:193 + b], scalar2=None, op0=OP.mult)
            vi.then_inc(s_v.inc(), 1)
            v.wait_ge(s_v.h, s_v.v)
            v.wait_ge(s_mm.h, s_mm.v)
            for b in range(nblk):
                vi = v.tensor_scalar(out=wdqb[:, 32 * b:32 * (b + 1)], in0=wi16[:, 32 * b:32 * (b + 1)],
                                     scalar1=wsc[:, 96 + b:97 + b], scalar2=None, op0=OP.mult)
            vi.then_inc(s_v.inc(), 1)

        wq_v = s_v.v

        nic = ncols // 128
        mm0 = s_mm.v
        v0 = s_v.v

        @dq.tensor
        def _(t, wq_v=wq_v, nic=nic, mm0=mm0, v0=v0):
            t.wait_ge(s_v.h, wq_v)
            for icc in range(nic):
                if icc >= 2:
                    t.wait_ge(s_v.h, v0 + icc - 1)
                t.transpose(pb[6 + icc % 2][:, 0:64].bitcast(BF16),
                            wdqb[:, 128 * icc:128 * (icc + 1)],
                            k.constsb[:, 256:384]).then_inc(s_mm.inc(), 1)

        @dq.vector
        def _(v, wt=wt, nic=nic, mm0=mm0):
            v.wait_ge(s_wq.h, 16 * wt)
            for icc in range(nic):
                v.wait_ge(s_mm.h, mm0 + icc + 1)
                v.tensor_copy(out=wdqt[:, 128 * icc:128 * (icc + 1)],
                              in_=pb[6 + icc % 2][:, 0:64].bitcast(BF16)).then_inc(s_v.inc(), 1)

        wqt_v = s_v.v

        @dq.gpsimd
        def _(g, wt=wt, nic=nic, wqt_v=wqt_v):
            g.wait_ge(s_v.h, wqt_v)
            flat1d = k.dr["cc_wq_in"].rearrange("r c -> (r c)")
            if wt < 3:
                dv = flat1d[0:393216].rearrange("(icc p o) -> p icc o", p=128, o=384)
                g.dma_start(out=dv[:, :, 128 * wt:128 * (wt + 1)][:, 0:8, :],
                            in_=wdqt[:, 0:1024].rearrange("p (icc o) -> p icc o", o=128),
                            allow_slow_non_contiguous=True).then_inc(s_wq.inc(16), 16)
            elif wt < 6:
                dv = flat1d[393216:786432].rearrange("(icc p o) -> p icc o", p=128, o=384)
                g.dma_start(out=dv[:, :, 128 * (wt - 3):128 * (wt - 2)][:, 0:8, :],
                            in_=wdqt[:, 0:1024].rearrange("p (icc o) -> p icc o", o=128),
                            allow_slow_non_contiguous=True).then_inc(s_wq.inc(16), 16)
            else:
                dv = flat1d[786432:1179648].rearrange("(icc p o) -> p icc o", p=128, o=128)
                g.dma_start(out=dv[:, 0:24, :],
                            in_=wdqt[:, 0:3072].rearrange("p (icc o) -> p icc o", o=128),
                            allow_slow_non_contiguous=True).then_inc(s_wq.inc(16), 16)

    k.wq_dma_v = s_wq.v
    k.s_wqc = agather(k.dr["cc_wq_in"], k.dr["cc_wq_out"], s_wq)
    k.s_wq = s_wq
    scopeF.close()

    scope3 = ExitStack()
    k.scopeF = scope3
    sb3 = lambda *a: scope3.enter_context(nc.sbuf_tensor(*a))
    xqfm = sb3("xqfm", [128, 8 * TOK], BF16)
    wupT = sb3("wupT", [128, 8 * FI], BF16)
    wgT = sb3("wgT", [128, 8 * FI], BF16)
    x1t = sb3("x1t", [128, 2 * DM], F32)
    xqi = sb3("xqi", [128, DM], I16)
    xqb = sb3("xqb", [128, 2 * DM], BF16)
    evb = sb3("evb", [128, 2 * FI], F32)
    mxa = sb3("mxa", [128, 8], F32)
    s_lx1 = [k.sem("s_lx1a"), k.sem("s_lx1b")]
    s_xq = [k.sem("s_xqa"), k.sem("s_xqb")]
    s_xqt = k.sem("s_xqt")
    s_lwt = k.sem("s_lwt")
    s_spu = [k.sem("s_spua"), k.sem("s_spub")]
    s_spg = [k.sem("s_spga"), k.sem("s_spgb")]

    @block.scalar
    def _(sc):
        sc.wait_ge(k.s_wqc.h, k.s_wqc.v)
        ccf = k.dr["cc_wq_out"].rearrange("(r z) c -> r (z c)", z=1152)
        for ic in range(8):
            gsrc = ccf[:, 0:393216].rearrange("r (i o) -> i r o", o=384)
            usrc = ccf[:, 393216:786432].rearrange("r (i o) -> i r o", o=384)
            sc.dma_start(out=wgT[:, ic * FI:(ic + 1) * FI].rearrange("p (r o) -> p r o", o=384),
                         in_=gsrc[128 * ic:128 * (ic + 1), :, :]).then_inc(s_lwt.inc(16), 16)
            sc.dma_start(out=wupT[:, ic * FI:(ic + 1) * FI].rearrange("p (r o) -> p r o", o=384),
                         in_=usrc[128 * ic:128 * (ic + 1), :, :]).then_inc(s_lwt.inc(16), 16)

    xq_v = {}
    for t in range(NT):
        par = t % 2

        @block.sync
        def _(sync, t=t, par=par):
            if t >= 2:
                sync.wait_ge(s_v.h, xq_v[t - 2])
            elif t == 0:
                sync.wait_ge(k.s_wq.h, k.wq_dma_v)
            sync.dma_start(out=x1t[:, par * DM:(par + 1) * DM],
                           in_=k.dr["sp_x1"][128 * t:128 * (t + 1), :]).then_inc(s_lx1[par].inc(16), 16)

        vt0 = s_v.v

        @block.vector
        def _(v, t=t, par=par, vt0=vt0):
            v.wait_ge(s_lx1[par].h, s_lx1[par].v)
            v.wait_ge(s_v.h, vt0)
            if t == 0:
                v.wait_ge(s_v.h, bc_sx_v)
            if t >= 2:
                v.wait_ge(s_xq[par].h, 16 * (t // 2))
            v.tensor_scalar(out=xqi[:, :], in0=x1t[:, par * DM:(par + 1) * DM],
                            scalar1=k.sval[:, 9:10], scalar2=None, op0=OP.mult).then_inc(s_v.inc(), 1)
            v.wait_ge(s_v.h, s_v.v)
            v.tensor_copy(out=xqb[:, par * DM:(par + 1) * DM], in_=xqi[:, :]).then_inc(s_v.inc(), 1)
        xq_v[t] = s_v.v

        @block.gpsimd
        def _(g, t=t, par=par):
            g.wait_ge(s_v.h, xq_v[t])
            g.dma_start(out=k.dr["sp_xq"][128 * t:128 * (t + 1), :],
                        in_=xqb[:, par * DM:(par + 1) * DM]).then_inc(s_xq[par].inc(16), 16)

    @block.scalar
    def _(sc):
        sc.wait_ge(s_xq[0].h, s_xq[0].v)
        sc.wait_ge(s_xq[1].h, s_xq[1].v)
        for ic in range(8):
            sc.dma_start(out=xqfm[:, ic * TOK:(ic + 1) * TOK],
                         in_=k.dr["sp_xq"][0:TOK, 128 * ic:128 * (ic + 1)],
                         transpose=True).then_inc(s_xqt.inc(16), 16)

    vinit = s_v.v

    @block.vector
    def _(v):
        v.wait_ge(s_v.h, vinit)
        v.memset(mxa[:, 0:4], 0.0).then_inc(s_v.inc(), 1)

    k.block.set_direct("tensor", "vector", "gpsimd", "sync", "scalar")
    vred = {}
    evc = {}
    runpar = [0, 0]
    nj = 0
    for t in range(NT):
        for w in range(2):
            par = t % 2
            wT = wupT if w == 0 else wgT
            spd = k.dr["sp_up"] if w == 0 else k.dr["sp_gate"]
            spsem = s_spu if w == 0 else s_spg
            for nc6 in range(6):
                bank = pb[nj % 2]

                @block.tensor
                def _(tt_, t=t, w=w, nc6=nc6, nj=nj, wT=wT, bank=bank):
                    if nj == 0:
                        tt_.wait_ge(s_xqt.h, s_xqt.v)
                        tt_.wait_ge(s_lwt.h, s_lwt.v)
                    if nj >= 2:
                        tt_.wait_ge(s_v.h, vred[nj - 2])
                        tt_.wait_ge(s_sc.h, evc[nj - 2])
                    for kc in range(8):
                        mi = tt_.matmul(bank[:, 0:512],
                                        xqfm[:, kc * TOK + 128 * t:kc * TOK + 128 * (t + 1)],
                                        wT[:, kc * FI + 512 * nc6:kc * FI + 512 * (nc6 + 1)],
                                        start=(kc == 0), stop=(kc == 7))
                    mi.then_inc(s_mm.inc(), 1)

                mmj = s_mm.v
                rsrc, rdst = runpar[w], 1 - runpar[w]
                runpar[w] = rdst

                vj0 = s_v.v

                @block.vector
                def _(v, w=w, bank=bank, mmj=mmj, rsrc=rsrc, rdst=rdst, vj0=vj0):
                    v.wait_ge(s_mm.h, mmj)
                    v.wait_ge(s_v.h, vj0)
                    v.tensor_reduce(out=mxa[:, 4:5], in_=bank[:, 0:512], axis=AX,
                                    op=OP.max, apply_absolute_value=True).then_inc(s_v.inc(), 1)
                    v.wait_ge(s_v.h, s_v.v)
                    v.tensor_tensor(out=mxa[:, 2 * rdst + w:2 * rdst + w + 1],
                                    in0=mxa[:, 2 * rsrc + w:2 * rsrc + w + 1],
                                    in1=mxa[:, 4:5], op=OP.max).then_inc(s_v.inc(), 1)
                vred[nj] = s_v.v

                jpar = (2 * t + w) % 2

                @block.scalar
                def _(sc, t=t, w=w, jpar=jpar, nc6=nc6, bank=bank, mmj=mmj):
                    sc.wait_ge(s_mm.h, mmj)
                    if nc6 == 0 and t >= 1:
                        sc.wait_ge(spsem[(t - 1) % 2].h, 16 * ((t - 1) // 2 + 1))
                    sc.copy(out=evb[:, jpar * FI + 512 * nc6:jpar * FI + 512 * (nc6 + 1)],
                            in_=bank[:, 0:512]).then_inc(s_sc.inc(), 1)
                evc[nj] = s_sc.v
                nj += 1

            jpar = (2 * t + w) % 2

            @block.gpsimd
            def _(g, t=t, jpar=jpar, spd=spd, spsem=spsem, scj=evc[nj - 1]):
                g.wait_ge(s_sc.h, scj)
                g.dma_start(out=spd[128 * t:128 * (t + 1), :],
                            in_=evb[:, jpar * FI:(jpar + 1) * FI]).then_inc(spsem[t % 2].inc(16), 16)


    s_ag2 = k.sem("s_ag2")
    vfin = s_v.v

    @block.vector
    def _(v):
        v.wait_ge(s_v.h, vfin)
        v.tensor_copy(out=k.sval[0:1, 24:25], in_=mxa[0:1, 0:1])

    @block.tensor
    def _(t):
        t.wait_ge(s_v.h, vfin)
        t.transpose(pb[6][0:2, 0:128], mxa[:, 0:2], k.consts[:, 0:128]).then_inc(s_mm.inc(), 1)

    mfin = s_mm.v

    @block.vector
    def _(v):
        v.wait_ge(s_mm.h, mfin)
        v.tensor_reduce(out=k.sval[0:2, 40:41], in_=pb[6][0:2, 0:128], axis=AX,
                        op=OP.max).then_inc(s_v.inc(), 1)

    vag2a = s_v.v

    @block.tensor
    def _(t):
        t.wait_ge(s_v.h, vag2a)
        t.transpose(pb[7][0:1, 0:2], k.sval[0:2, 40:41], k.consts[0:2, 0:2]).then_inc(s_mm.inc(), 1)

    mag2b = s_mm.v

    @block.vector
    def _(v):
        v.wait_ge(s_mm.h, mag2b)
        v.tensor_copy(out=k.sval[0:1, 21:23], in_=pb[7][0:1, 0:2]).then_inc(s_v.inc(), 1)

    vag2 = s_v.v

    @block.gpsimd
    def _(g):
        g.wait_ge(s_v.h, vag2)
        g.dma_start(out=k.dr["cc_s1_in"], in_=k.sval[0:1, 21:23]).then_inc(s_ag2.inc(16), 16)

    s_ag2c = agather(k.dr["cc_s1_in"], k.dr["cc_s1_out"], s_ag2)
    max_from_ag(k.dr["cc_s1_out"], s_ag2c, 16)

    @block.vector
    def _(v):
        v.wait_ge(s_v.h, s_v.v)
        v.tensor_tensor(out=k.sval[0:1, 19:20],
                        in0=k.sval[0:1, 16:17], in1=k.sval[0:1, 8:9], op=OP.mult)
        v.tensor_tensor(out=k.sval[0:1, 20:21], in0=k.sval[0:1, 17:18], in1=k.sval[0:1, 8:9],
                        op=OP.mult).then_inc(s_v.inc(), 1)
        v.wait_ge(s_v.h, s_v.v)
        v.tensor_scalar(out=k.sval[0:1, 21:22], in0=k.sval[0:1, 19:20],
                        scalar1=1.0 / QMAX, scalar2=None, op0=OP.mult)
        v.tensor_scalar(out=k.sval[0:1, 22:23], in0=k.sval[0:1, 20:21],
                        scalar1=1.0 / QMAX, scalar2=None, op0=OP.mult).then_inc(s_v.inc(), 1)
        v.wait_ge(s_v.h, s_v.v)
        v.reciprocal(out=k.sval[0:1, 30:31], in_=k.sval[0:1, 21:22])
        v.reciprocal(out=k.sval[0:1, 31:32], in_=k.sval[0:1, 22:23]).then_inc(s_v.inc(), 1)
        v.wait_ge(s_v.h, s_v.v)
        v.tensor_tensor(out=k.sval[0:1, 23:24], in0=k.sval[0:1, 8:9], in1=k.sval[0:1, 30:31],
                        op=OP.mult)
        v.tensor_tensor(out=k.sval[0:1, 24:25], in0=k.sval[0:1, 8:9], in1=k.sval[0:1, 31:32],
                        op=OP.mult).then_inc(s_v.inc(), 1)

    bc2 = bcast_scalar([(21, 10), (22, 11), (23, 12), (24, 13)], s_v.v)

    scope3.close()
    k.scope_zs.close()
    scope4 = ExitStack()
    k.scopeF = scope4
    sb4 = lambda *a: scope4.enter_context(nc.sbuf_tensor(*a))
    gint = sb4("gint", [128, NT * FI], I16)
    gtile = sb4("gtile", [128, 2 * FI], BF16)
    sgs = sb4("sgs", [128, 2 * FI], F32)
    pts = sb4("pts", [128, FI], F32)
    upt = sb4("upt", [128, 2 * FI], BF16)
    upi = sb4("upi", [128, 2 * FI], I16)
    ots = sb4("ots", [128, 2 * FI], F32)
    mxb = sb4("mxb", [128, 8], F32)
    s_lg = [k.sem("s_lga"), k.sem("s_lgb")]
    s_lu = [k.sem("s_lua"), k.sem("s_lub")]
    s_ot = [k.sem("s_ota"), k.sem("s_otb")]

    vinitB = s_v.v

    @block.vector
    def _(v):
        v.wait_ge(s_v.h, vinitB)
        v.wait_ge(s_spu[0].h, s_spu[0].v)
        v.wait_ge(s_spu[1].h, s_spu[1].v)
        v.wait_ge(s_spg[0].h, s_spg[0].v)
        v.wait_ge(s_spg[1].h, s_spg[1].v)
        v.wait_ge(s_xq[0].h, s_xq[0].v)
        v.wait_ge(s_xq[1].h, s_xq[1].v)
        v.memset(mxb[:, 0:4], 0.0).then_inc(s_v.inc(), 1)
    vinitB2 = s_v.v

    runp = [0, 0]
    btile_v = {}
    for t in range(NT):
        par = t % 2

        @block.sync
        def _(sync, t=t, par=par):
            if t >= 2:
                sync.wait_ge(s_v.h, btile_v[t - 2])
            else:
                sync.wait_ge(s_v.h, vinitB2)
            sync.dma_start(out=gtile[:, par * FI:(par + 1) * FI],
                           in_=k.dr["sp_gate"][128 * t:128 * (t + 1), :]).then_inc(s_lg[par].inc(16), 16)

        ldg = s_lg[par].v
        vprevB = btile_v[t - 2] if t >= 2 else vinitB2

        @block.vector
        def _(v, t=t, par=par, ldg=ldg, vprevB=vprevB):
            v.wait_ge(s_lg[par].h, ldg)
            v.wait_ge(s_v.h, vprevB)
            v.tensor_scalar(out=gint[:, t * FI:(t + 1) * FI], in0=gtile[:, par * FI:(par + 1) * FI],
                            scalar1=k.sval[:, 13:14], scalar2=None,
                            op0=OP.mult).then_inc(s_v.inc(), 1)
        gq_v = s_v.v

        @block.scalar
        def _(sc, t=t, par=par, gq_v=gq_v):
            sc.wait_ge(s_v.h, gq_v)
            sc.wait_ge(s_sc.h, s_sc.v)
            sc.activation(out=sgs[:, par * FI:(par + 1) * FI], in_=gint[:, t * FI:(t + 1) * FI],
                          func=AF.Sigmoid, scale=k.sval[:, 11:12]).then_inc(s_sc.inc(), 1)

        scb = s_sc.v
        rs, rd = runp[0], 1 - runp[0]
        runp[0] = rd

        vb0 = s_v.v

        @block.vector
        def _(v, t=t, par=par, scb=scb, rs=rs, rd=rd, vb0=vb0):
            v.wait_ge(s_sc.h, scb)
            v.wait_ge(s_v.h, vb0)
            v.tensor_tensor(out=pts[:, :], in0=gint[:, t * FI:(t + 1) * FI],
                            in1=sgs[:, par * FI:(par + 1) * FI],
                            op=OP.mult).then_inc(s_v.inc(), 1)
            v.wait_ge(s_v.h, s_v.v)
            v.tensor_reduce(out=mxb[:, 4:5], in_=pts[:, :], axis=AX, op=OP.max,
                            apply_absolute_value=True).then_inc(s_v.inc(), 1)
            v.wait_ge(s_v.h, s_v.v)
            v.tensor_tensor(out=mxb[:, 2 * rd:2 * rd + 1], in0=mxb[:, 2 * rs:2 * rs + 1],
                            in1=mxb[:, 4:5], op=OP.max).then_inc(s_v.inc(), 1)
        btile_v[t] = s_v.v

    s_ag3 = k.sem("s_ag3")
    vfb = s_v.v

    @block.tensor
    def _(t):
        t.wait_ge(s_v.h, vfb)
        t.transpose(pb[6][0:1, 0:128], mxb[:, 0:1], k.consts[:, 0:128]).then_inc(s_mm.inc(), 1)

    mfb = s_mm.v

    @block.vector
    def _(v):
        v.wait_ge(s_mm.h, mfb)
        v.memset(k.sval[0:1, 22:23], 0.0)
        v.tensor_reduce(out=k.sval[0:1, 21:22], in_=pb[6][0:1, 0:128], axis=AX,
                        op=OP.max).then_inc(s_v.inc(), 1)

    vag3 = s_v.v

    @block.gpsimd
    def _(g):
        g.wait_ge(s_v.h, vag3)
        g.dma_start(out=k.dr["cc_s2_in"], in_=k.sval[0:1, 21:23]).then_inc(s_ag3.inc(16), 16)

    s_ag3c = agather(k.dr["cc_s2_in"], k.dr["cc_s2_out"], s_ag3)
    max_from_ag(k.dr["cc_s2_out"], s_ag3c, 16)

    @block.vector
    def _(v):
        v.wait_ge(s_v.h, s_v.v)
        v.tensor_scalar(out=k.sval[0:1, 26:27], in0=k.sval[0:1, 11:12],
                        scalar1=1.0, scalar2=None, op0=OP.mult).then_inc(s_v.inc(), 1)
        v.wait_ge(s_v.h, s_v.v)
        v.tensor_tensor(out=k.sval[0:1, 27:28], in0=k.sval[0:1, 26:27], in1=k.sval[0:1, 16:17],
                        op=OP.mult).then_inc(s_v.inc(), 1)
        v.wait_ge(s_v.h, s_v.v)
        v.tensor_scalar(out=k.sval[0:1, 28:29], in0=k.sval[0:1, 27:28],
                        scalar1=1.0 / QMAX, scalar2=None, op0=OP.mult).then_inc(s_v.inc(), 1)
        v.wait_ge(s_v.h, s_v.v)
        v.reciprocal(out=k.sval[0:1, 32:33], in_=k.sval[0:1, 28:29]).then_inc(s_v.inc(), 1)
        v.wait_ge(s_v.h, s_v.v)
        v.tensor_tensor(out=k.sval[0:1, 29:30], in0=k.sval[0:1, 26:27], in1=k.sval[0:1, 32:33],
                        op=OP.mult).then_inc(s_v.inc(), 1)
    bc3 = bcast_scalar([(28, 14), (29, 15)], s_v.v)

    dtile_v = {}
    for t in range(NT):
        par = t % 2

        @block.sync
        def _(sync, t=t, par=par):
            if t >= 2:
                sync.wait_ge(s_v.h, dtile_v[t - 2])
            sync.dma_start(out=upt[:, par * FI:(par + 1) * FI],
                           in_=k.dr["sp_up"][128 * t:128 * (t + 1), :]).then_inc(s_lu[par].inc(16), 16)

        ldu = s_lu[par].v
        vprevD = dtile_v[t - 2] if t >= 2 else None

        @block.scalar
        def _(sc, t=t, par=par, ldu=ldu, vprevD=vprevD):
            sc.wait_ge(s_sc.h, s_sc.v)
            if vprevD is not None:
                sc.wait_ge(s_v.h, vprevD)
            else:
                sc.wait_ge(s_v.h, bc3)
            sc.activation(out=sgs[:, par * FI:(par + 1) * FI], in_=gint[:, t * FI:(t + 1) * FI],
                          func=AF.Sigmoid, scale=k.sval[:, 11:12]).then_inc(s_sc.inc(), 1)
            sc.wait_ge(s_lu[par].h, ldu)
            sc.wait_ge(s_sc.h, s_sc.v)
            sc.activation(out=upi[:, par * FI:(par + 1) * FI], in_=upt[:, par * FI:(par + 1) * FI],
                          func=AF.Copy, scale=k.sval[:, 12:13]).then_inc(s_sc.inc(), 1)

        scd = s_sc.v
        rs, rd = runp[1], 1 - runp[1]
        runp[1] = rd

        vd0 = s_v.v

        @block.vector
        def _(v, t=t, par=par, scd=scd, rs=rs, rd=rd, vd0=vd0):
            v.wait_ge(s_sc.h, scd)
            v.wait_ge(s_v.h, vd0)
            v.tensor_tensor(out=pts[:, :], in0=gint[:, t * FI:(t + 1) * FI],
                            in1=sgs[:, par * FI:(par + 1) * FI],
                            op=OP.mult).then_inc(s_v.inc(), 1)
            v.wait_ge(s_v.h, s_v.v)
            v.tensor_scalar(out=gint[:, t * FI:(t + 1) * FI], in0=pts[:, :],
                            scalar1=k.sval[:, 15:16], scalar2=None, op0=OP.mult).then_inc(s_v.inc(), 1)
            v.wait_ge(s_v.h, s_v.v)
            if t >= 2:
                v.wait_ge(s_ot[par].h, 16 * (t // 2))
            v.tensor_tensor(out=ots[:, par * FI:(par + 1) * FI],
                            in0=gint[:, t * FI:(t + 1) * FI],
                            in1=upi[:, par * FI:(par + 1) * FI],
                            op=OP.mult).then_inc(s_v.inc(), 1)
            v.wait_ge(s_v.h, s_v.v)
            v.tensor_reduce(out=mxb[:, 4:5], in_=ots[:, par * FI:(par + 1) * FI], axis=AX,
                            op=OP.max, apply_absolute_value=True).then_inc(s_v.inc(), 1)
            v.wait_ge(s_v.h, s_v.v)
            v.tensor_tensor(out=mxb[:, 2 * rd + 1:2 * rd + 2], in0=mxb[:, 2 * rs + 1:2 * rs + 2],
                            in1=mxb[:, 4:5], op=OP.max).then_inc(s_v.inc(), 1)
        dtile_v[t] = s_v.v

        @block.gpsimd
        def _(g, t=t, par=par):
            g.wait_ge(s_v.h, dtile_v[t])
            g.dma_start(out=k.dr["sp_oq"][128 * t:128 * (t + 1), :],
                        in_=ots[:, par * FI:(par + 1) * FI]).then_inc(s_ot[par].inc(16), 16)

    s_ag4 = k.sem("s_ag4")
    vfd = s_v.v

    @block.tensor
    def _(t):
        t.wait_ge(s_v.h, vfd)
        t.transpose(pb[6][0:1, 0:128], mxb[:, 1:2], k.consts[:, 0:128]).then_inc(s_mm.inc(), 1)

    mfd = s_mm.v

    @block.vector
    def _(v):
        v.wait_ge(s_mm.h, mfd)
        v.memset(k.sval[0:1, 22:23], 0.0)
        v.tensor_reduce(out=k.sval[0:1, 21:22], in_=pb[6][0:1, 0:128], axis=AX,
                        op=OP.max).then_inc(s_v.inc(), 1)

    vag4 = s_v.v

    @block.gpsimd
    def _(g):
        g.wait_ge(s_v.h, vag4)
        g.dma_start(out=k.dr["cc_s3_in"], in_=k.sval[0:1, 21:23]).then_inc(s_ag4.inc(16), 16)

    s_ag4c = agather(k.dr["cc_s3_in"], k.dr["cc_s3_out"], s_ag4)
    max_from_ag(k.dr["cc_s3_out"], s_ag4c, 16)

    @block.vector
    def _(v):
        v.wait_ge(s_v.h, s_v.v)
        v.tensor_tensor(out=k.sval[0:1, 26:27], in0=k.sval[0:1, 14:15], in1=k.sval[0:1, 10:11],
                        op=OP.mult).then_inc(s_v.inc(), 1)
        v.wait_ge(s_v.h, s_v.v)
        v.tensor_tensor(out=k.sval[0:1, 27:28], in0=k.sval[0:1, 26:27], in1=k.sval[0:1, 16:17],
                        op=OP.mult).then_inc(s_v.inc(), 1)
        v.wait_ge(s_v.h, s_v.v)
        v.tensor_scalar(out=k.sval[0:1, 28:29], in0=k.sval[0:1, 27:28],
                        scalar1=1.0 / QMAX, scalar2=None, op0=OP.mult).then_inc(s_v.inc(), 1)
        v.wait_ge(s_v.h, s_v.v)
        v.reciprocal(out=k.sval[0:1, 32:33], in_=k.sval[0:1, 28:29]).then_inc(s_v.inc(), 1)
        v.wait_ge(s_v.h, s_v.v)
        v.tensor_tensor(out=k.sval[0:1, 29:30], in0=k.sval[0:1, 26:27], in1=k.sval[0:1, 32:33],
                        op=OP.mult).then_inc(s_v.inc(), 1)
    bc4 = bcast_scalar([(28, 16), (29, 17)], s_v.v)

    scope4.close()
    scope5 = ExitStack()
    k.scopeF = scope5
    sb5 = lambda *a: scope5.enter_context(nc.sbuf_tensor(*a))
    ofm = sb5("ofm", [128, 24 * TOK], BF16)
    wdT = sb5("wdT", [128, 24 * DM], BF16)
    x1e = sb5("x1e", [128, 2 * DM], F32)
    pre2 = sb5("pre2", [128, DM], F32)
    sq2 = sb5("sq2", [128, DM], F32)
    outb = sb5("outb", [128, 2 * DM], F32)
    ln2s = sb5("ln2s", [128, 16], F32)
    oraw = sb5("oraw", [128, 4 * TOK], BF16)
    oq16 = sb5("oq16", [128, 4 * TOK], I16)
    s_oft = [k.sem(f"s_oft{i}") for i in range(4)]
    s_lwd = k.sem("s_lwd")
    s_lx2 = [k.sem("s_lx2a"), k.sem("s_lx2b")]

    @block.sync
    def _(sync):
        sync.wait_ge(k.s_wqc.h, k.s_wqc.v)
        ccf = k.dr["cc_wq_out"].rearrange("(r z) c -> r (z c)", z=1152)
        dsrc = ccf[:, 786432:1179648].rearrange("r (i o) -> i r o", o=128)
        for ic in range(24):
            sync.dma_start(out=wdT[:, ic * DM:(ic + 1) * DM].rearrange("p (r o) -> p r o", o=128),
                           in_=dsrc[128 * ic:128 * (ic + 1), :, :]).then_inc(s_lwd.inc(16), 16)

    ofmcp = {}
    oqsc = {}
    for ic in range(26):
        par = ic % 4

        @block.scalar
        def _(sc, ic=ic, par=par):
            if ic < 24:
                if ic == 0:
                    sc.wait_ge(s_ot[0].h, s_ot[0].v)
                    sc.wait_ge(s_ot[1].h, s_ot[1].v)
                if ic >= 4:
                    sc.wait_ge(s_v.h, ofmcp[ic - 4])
                sc.dma_start(out=oraw[:, par * TOK:(par + 1) * TOK],
                             in_=k.dr["sp_oq"][0:TOK, 128 * ic:128 * (ic + 1)],
                             transpose=True).then_inc(s_oft[par].inc(16), 16)
            if ic >= 2:
                qp = (ic - 2) % 4
                sc.wait_ge(s_oft[qp].h, 16 * ((ic - 2) // 4 + 1))
                if ic == 2:
                    sc.wait_ge(s_v.h, bc4)
                sc.wait_ge(s_sc.h, s_sc.v)
                sc.activation(out=oq16[:, qp * TOK:(qp + 1) * TOK],
                              in_=oraw[:, qp * TOK:(qp + 1) * TOK],
                              func=AF.Copy, scale=k.sval[:, 17:18]).then_inc(s_sc.inc(), 1)
                oqsc[ic - 2] = s_sc.v

        if ic >= 2:
            @block.vector
            def _(v, ic=ic):
                qp = (ic - 2) % 4
                v.wait_ge(s_sc.h, oqsc[ic - 2])
                v.tensor_copy(out=ofm[:, (ic - 2) * TOK:(ic - 1) * TOK],
                              in_=oq16[:, qp * TOK:(qp + 1) * TOK]).then_inc(s_v.inc(), 1)
            ofmcp[ic - 2] = s_v.v
    ofm_done = ofmcp[23]

    e2_v = {}
    for t in range(NT):
        par = t % 2

        @block.sync
        def _(sync, t=t, par=par):
            if t >= 2:
                sync.wait_ge(s_v.h, e2_v[t - 2])
            sync.dma_start(out=x1e[:, par * DM:(par + 1) * DM],
                           in_=k.dr["sp_x1"][128 * t:128 * (t + 1), :]).then_inc(s_lx2[par].inc(16), 16)

        vprev2 = s_v.v

        @block.tensor
        def _(tt_, t=t, par=par, vprev2=vprev2):
            if t == 0:
                tt_.wait_ge(s_v.h, ofm_done)
                tt_.wait_ge(s_lwd.h, s_lwd.v)
            tt_.wait_ge(s_v.h, vprev2)
            for nc2 in range(2):
                for kc in range(24):
                    mi = tt_.matmul(pb[2 * par + nc2][:, 0:512],
                                    ofm[:, kc * TOK + 128 * t:kc * TOK + 128 * (t + 1)],
                                    wdT[:, kc * DM + 512 * nc2:kc * DM + 512 * (nc2 + 1)],
                                    start=(kc == 0), stop=(kc == 23))
            mi.then_inc(s_mm.inc(), 1)

        md = s_mm.v

        @block.vector
        def _(v, t=t, par=par, md=md):
            v.wait_ge(s_mm.h, md)
            v.wait_ge(s_lx2[par].h, s_lx2[par].v)
            v.scalar_tensor_tensor(out=pre2[:, 0:512], in0=pb[2 * par][:, 0:512],
                                   scalar=k.sval[:, 16:17], in1=x1e[:, par * DM:par * DM + 512],
                                   op0=OP.mult, op1=OP.add)
            v.scalar_tensor_tensor(out=pre2[:, 512:1024], in0=pb[2 * par + 1][:, 0:512],
                                   scalar=k.sval[:, 16:17], in1=x1e[:, par * DM + 512:par * DM + 1024],
                                   op0=OP.mult, op1=OP.add).then_inc(s_v.inc(), 1)
            v.wait_ge(s_v.h, s_v.v)
            v.tensor_reduce(out=ln2s[:, 0:1], in_=pre2[:, :], axis=AX, op=OP.add).then_inc(s_v.inc(), 1)

        vpre2 = s_v.v

        @block.scalar
        def _(sc, t=t, vpre2=vpre2):
            sc.wait_ge(s_v.h, vpre2)
            sc.wait_ge(s_sc.h, s_sc.v)
            sc.activation(out=sq2[:, :], in_=pre2[:, :], func=AF.Square,
                          accum_out=ln2s[:, 1:2]).then_inc(s_sc.inc(), 1)

        sq2d = s_sc.v

        @block.vector
        def _(v, t=t, sq2d=sq2d):
            v.wait_ge(s_sc.h, sq2d)
            v.tensor_scalar(out=ln2s[:, 2:3], in0=ln2s[:, 0:1], scalar1=1.0 / DM,
                            scalar2=None, op0=OP.mult)
            v.tensor_scalar(out=ln2s[:, 3:4], in0=ln2s[:, 1:2], scalar1=1.0 / DM,
                            scalar2=None, op0=OP.mult).then_inc(s_v.inc(), 1)
            v.wait_ge(s_v.h, s_v.v)
            v.tensor_tensor(out=ln2s[:, 4:5], in0=ln2s[:, 2:3], in1=ln2s[:, 2:3],
                            op=OP.mult).then_inc(s_v.inc(), 1)
            v.wait_ge(s_v.h, s_v.v)
            v.tensor_tensor(out=ln2s[:, 5:6], in0=ln2s[:, 3:4], in1=ln2s[:, 4:5],
                            op=OP.subtract).then_inc(s_v.inc(), 1)

        vvar2 = s_v.v

        @block.scalar
        def _(sc, t=t, vvar2=vvar2):
            sc.wait_ge(s_v.h, vvar2)
            sc.activation(out=ln2s[:, 6:7], in_=ln2s[:, 5:6], func=AF.Ln,
                          bias=k.fconst[:, 1:2]).then_inc(s_sc.inc(), 1)
            sc.wait_ge(s_sc.h, s_sc.v)
            sc.activation(out=ln2s[:, 7:8], in_=ln2s[:, 6:7], func=AF.Exp,
                          scale=-0.5).then_inc(s_sc.inc(), 1)

        srs2 = s_sc.v

        @block.vector
        def _(v, t=t, par=par, srs2=srs2):
            v.wait_ge(s_sc.h, srs2)
            v.tensor_tensor(out=ln2s[:, 8:9], in0=ln2s[:, 2:3], in1=ln2s[:, 7:8],
                            op=OP.mult).then_inc(s_v.inc(), 1)
            v.wait_ge(s_v.h, s_v.v)
            v.tensor_scalar(out=ln2s[:, 9:10], in0=ln2s[:, 8:9], scalar1=-1.0,
                            scalar2=None, op0=OP.mult).then_inc(s_v.inc(), 1)
            v.wait_ge(s_v.h, s_v.v)
            if t >= 2:
                v.wait_ge(k.s_st2[par].h, 16 * (t // 2))
            v.tensor_scalar(out=outb[:, par * DM:(par + 1) * DM], in0=pre2[:, :],
                            scalar1=ln2s[:, 7:8], scalar2=ln2s[:, 9:10],
                            op0=OP.mult, op1=OP.add).then_inc(s_v.inc(), 1)
        e2_v[t] = s_v.v

        @block.sync
        def _(sync, t=t, par=par):
            sync.wait_ge(s_v.h, e2_v[t])
            sync.dma_start(out=k.d["out"][128 * t:128 * (t + 1), :],
                           in_=outb[:, par * DM:(par + 1) * DM]).then_inc(k.s_st2[par].inc(16), 16)

    k.scope5 = scope5


def emit_dbg_ffn(k):
    block = k.block
    dsv = k.dbg_tensor("sval", [128, 48])
    dgr = k.dbg_tensor("gateraw", [TOK, FI])
    doi = k.dbg_tensor("oint", [TOK, FI])
    dwq = k.dbg_tensor("wdq", [8 * 1152, DM])
    if dsv is not None:
        @block.gpsimd
        def _(g):
            g.wait_ge(k.s_st2[0].h, k.s_st2[0].v)
            g.wait_ge(k.s_st2[1].h, k.s_st2[1].v)
            g.dma_start(out=dsv, in_=k.sval[:, :]).then_inc(k.s_sp.inc(16), 16)
            if dgr is not None:
                g.dma_start(out=dgr, in_=k.dr["sp_gate"][:, :]).then_inc(k.s_sp.inc(16), 16)
            if doi is not None:
                g.dma_start(out=doi, in_=k.dr["sp_up"][:, :]).then_inc(k.s_sp.inc(16), 16)
            if dwq is not None:
                g.dma_start(out=dwq, in_=k.dr["cc_wq_out"]).then_inc(k.s_sp.inc(16), 16)
            g.wait_ge(k.s_sp.h, k.s_sp.v)


def emit_finish(k):
    block = k.block

    @block.sync
    def _(sync):
        sync.wait_ge(k.s_st2[0].h, k.s_st2[0].v)
        sync.wait_ge(k.s_st2[1].h, k.s_st2[1].v)

    @block.gpsimd
    def _(g):
        g.wait_ge(k.s_st2[0].h, k.s_st2[0].v)
        g.wait_ge(k.s_st2[1].h, k.s_st2[1].v)

    k.scope5.close()



def emit_finish_dbg(k):
    block = k.block
    for scope in ("scopeF", "scope2", "scopeD", "scopeA", "scope_xs", "scope_zs"):
        s = getattr(k, scope, None)
        if s is not None:
            try:
                s.close()
            except Exception:
                pass

    @block.gpsimd
    def _(g):
        if k.s_sp.v:
            g.wait_ge(k.s_sp.h, k.s_sp.v)
        for s in getattr(k, "s_so", []) or []:
            if s.v:
                g.wait_ge(s.h, s.v)


def shard_inputs(inputs, core):
    b, half = core // 2, core % 2
    x = np.asarray(inputs["L0_attn_x"])
    t0 = half * TOK
    xp = np.zeros((TPAD, DM), np.float32)
    lo, hi = t0 - 1, t0 + TOK + 2
    slo, shi = max(lo, 0), min(hi, LSEQ)
    xp[slo - lo:shi - lo, :] = x[b, slo:shi, :]
    convw = np.concatenate([np.asarray(inputs["L0_attn_conv_x_w"])[:, 0, :],
                            np.asarray(inputs["L0_attn_conv_z_w"])[:, 0, :]], axis=0)
    convb = np.concatenate([np.asarray(inputs["L0_attn_conv_x_b"]),
                            np.asarray(inputs["L0_attn_conv_z_b"])], axis=0)[:, None]
    wgkb = np.concatenate([np.asarray(inputs["L0_attn_W_gk"]),
                           np.asarray(inputs["L0_attn_W_b"])], axis=1)
    gkb_bias = np.concatenate([np.asarray(inputs["L0_attn_dt_bias"]),
                               np.asarray(inputs["L0_attn_b_b"])])[None, :]
    smask = np.zeros((1, 8), np.float32)
    if half == 1:
        smask[0, core - 1] = 1.0
    m = {
        "x_pad": np.ascontiguousarray(xp),
        "W_in": np.asarray(inputs["L0_attn_W_in"]),
        "convw": np.ascontiguousarray(convw),
        "convb": np.ascontiguousarray(convb),
        "W_q": np.asarray(inputs["L0_attn_W_q"]),
        "W_k": np.asarray(inputs["L0_attn_W_k"]),
        "W_v": np.asarray(inputs["L0_attn_W_v"]),
        "W_g": np.asarray(inputs["L0_attn_W_g"]),
        "W_gkb": np.ascontiguousarray(wgkb),
        "gkb_bias": np.ascontiguousarray(gkb_bias),
        "A_log": np.asarray(inputs["L0_attn_A_log"])[None, :],
        "g_norm_w": np.asarray(inputs["L0_attn_g_norm_w"])[None, :],
        "W_so": np.asarray(inputs["L0_attn_W_so"]),
        "W_out": np.asarray(inputs["L0_attn_W_out"]),
        "w_gate": np.asarray(inputs["L0_ffn_w_gate"]),
        "w_up": np.asarray(inputs["L0_ffn_w_up"]),
        "w_down": np.asarray(inputs["L0_ffn_w_down"]),
        "smask": smask,
        "wg_shard": np.asarray(inputs["L0_ffn_w_gate"])[384 * core:384 * (core + 1), :],
        "wu_shard": np.asarray(inputs["L0_ffn_w_up"])[384 * core:384 * (core + 1), :],
        "wd_shard": np.asarray(inputs["L0_ffn_w_down"])[128 * core:128 * (core + 1), :],
    }
    return {kk: np.ascontiguousarray(v, dtype=np.float32) for kk, v in m.items()}


def kernel(**inputs):
    nc, _ = build_program()
    in_maps = [shard_inputs(inputs, cidx) for cidx in range(8)]
    res = run_bass_kernel_spmd(nc, in_maps, core_ids=list(range(8)))
    outs = [res.results[cidx]["out"] for cidx in range(8)]
    return np.stack([np.concatenate([outs[2 * b], outs[2 * b + 1]], axis=0)
                     for b in range(B)]).astype(np.float32)


if __name__ == "__main__":
    nc, _ = build_program(upto="A", dbg=("xs", "zs"))
    print("built OK")


# version 10
# speedup vs baseline: 1.0137x; 1.0037x over previous
import numpy as np
from contextlib import ExitStack

import concourse.bass as bass
import concourse.mybir as mybir
from concourse.alu_op_type import AluOpType
from concourse.bass_utils import run_bass_kernel_spmd

F32 = mybir.dt.float32
BF16 = mybir.dt.bfloat16
I16 = mybir.dt.int16
U16 = mybir.dt.uint16
AF = mybir.ActivationFunctionType
AX = mybir.AxisListType.X
OP = AluOpType

B, LSEQ, DM = 4, 4096, 1024
TOK = 2048
NT = TOK // 128
NPT = 17
TPAD = NPT * 128
D_INNER = 2048
KD, VD, HH = 768, 1536, 4
DK, DV = 192, 384
CH, NCH = 128, 16
FI = 3072
QMAX = 32767.0
EPS_LN = 1e-5


class Sem:
    __slots__ = ("h", "v")

    def __init__(self, ctx, nc, name):
        self.h = ctx.enter_context(nc.semaphore(name))
        self.v = 0

    def inc(self, n=1):
        self.v += n
        return self.h


def _levelmask_neg(C, s):
    E = np.zeros((C, C), np.float32)
    for i in range(C):
        lo = (i // (2 * s)) * (2 * s)
        if (i // s) % 2 == 1:
            E[i, lo:lo + s] = -1.0
    return E



class _ChainStub:
    __slots__ = ("chain",)

    def __init__(self):
        self.chain = []

    def then_inc(self, sem, n):
        self.chain.append((sem, n))
        return self


class _RecEngine:
    __slots__ = ("q",)

    def __init__(self, q):
        self.q = q

    def __getattr__(self, m):
        def call(*a, **kw):
            stub = _ChainStub()
            self.q.append((m, a, kw, stub))
            return stub
        return call


class _RecBlock:

    def __init__(self):
        self.qs = {e: [] for e in ("sync", "tensor", "vector", "scalar", "gpsimd")}
        self.realblock = None
        self.direct = set()

    def set_direct(self, *engs):
        self.materialize(self.realblock)
        self.direct.update(engs)

    def set_rec(self, *engs):
        self.direct.difference_update(engs)

    def _mk(self, eng):
        if eng in self.direct:
            return getattr(self.realblock, eng)
        def deco(f):
            f(_RecEngine(self.qs[eng]))
            return f
        return deco

    @property
    def sync(self):
        return self._mk("sync")

    @property
    def tensor(self):
        return self._mk("tensor")

    @property
    def vector(self):
        return self._mk("vector")

    @property
    def scalar(self):
        return self._mk("scalar")

    @property
    def gpsimd(self):
        return self._mk("gpsimd")

    def materialize(self, block):
        for eng in ("sync", "tensor", "vector", "scalar", "gpsimd"):
            q = self.qs[eng]
            if not q:
                continue

            def body(e, q=list(q)):
                for m, a, kw, stub in q:
                    inst = getattr(e, m)(*a, **kw)
                    for sem, n in stub.chain:
                        inst = inst.then_inc(sem, n)
            getattr(block, eng)(body)
            self.qs[eng] = []


class K:
    pass


def build_program(upto="all", dbg=(), mock_ag=False):
    nc = bass.Bass("TRN2", target_bir_lowering=False)
    ctx = ExitStack()
    k = K()
    k.nc = nc
    k.ctx = ctx
    k.dbg = dbg
    k.mock_ag = mock_ag
    k.dbg_out = {}

    d = k.d = {}
    d["x_pad"] = nc.dram_tensor("x_pad", [TPAD, DM], F32, kind="ExternalInput").ap()
    d["W_in"] = nc.dram_tensor("W_in", [DM, D_INNER], F32, kind="ExternalInput").ap()
    d["convw"] = nc.dram_tensor("convw", [D_INNER, 4], F32, kind="ExternalInput").ap()
    d["convb"] = nc.dram_tensor("convb", [D_INNER, 1], F32, kind="ExternalInput").ap()
    d["W_q"] = nc.dram_tensor("W_q", [DM, KD], F32, kind="ExternalInput").ap()
    d["W_k"] = nc.dram_tensor("W_k", [DM, KD], F32, kind="ExternalInput").ap()
    d["W_v"] = nc.dram_tensor("W_v", [DM, VD], F32, kind="ExternalInput").ap()
    d["W_g"] = nc.dram_tensor("W_g", [DM, VD], F32, kind="ExternalInput").ap()
    d["W_gkb"] = nc.dram_tensor("W_gkb", [DM, 8], F32, kind="ExternalInput").ap()
    d["gkb_bias"] = nc.dram_tensor("gkb_bias", [1, 8], F32, kind="ExternalInput").ap()
    d["A_log"] = nc.dram_tensor("A_log", [1, HH], F32, kind="ExternalInput").ap()
    d["g_norm_w"] = nc.dram_tensor("g_norm_w", [1, DV], F32, kind="ExternalInput").ap()
    d["W_so"] = nc.dram_tensor("W_so", [VD, DM], F32, kind="ExternalInput").ap()
    d["W_out"] = nc.dram_tensor("W_out", [D_INNER, DM], F32, kind="ExternalInput").ap()
    d["w_gate"] = nc.dram_tensor("w_gate", [FI, DM], F32, kind="ExternalInput").ap()
    d["w_up"] = nc.dram_tensor("w_up", [FI, DM], F32, kind="ExternalInput").ap()
    d["w_down"] = nc.dram_tensor("w_down", [DM, FI], F32, kind="ExternalInput").ap()
    d["smask"] = nc.dram_tensor("smask", [1, 8], F32, kind="ExternalInput").ap()
    d["wg_shard"] = nc.dram_tensor("wg_shard", [384, DM], F32, kind="ExternalInput").ap()
    d["wu_shard"] = nc.dram_tensor("wu_shard", [384, DM], F32, kind="ExternalInput").ap()
    d["wd_shard"] = nc.dram_tensor("wd_shard", [128, FI], F32, kind="ExternalInput").ap()
    d["out"] = nc.dram_tensor("out", [TOK, DM], F32, kind="ExternalOutput").ap()

    dr = k.dr = {}
    def dram_scratch(name, shape, dt=BF16):
        dr[name] = nc.dram_tensor(name, shape, dt).ap()
        return dr[name]
    dram_scratch("sp_uv", [NCH * HH * 128, DV])
    dram_scratch("sp_wfm", [NCH * HH * 128, 256])
    dram_scratch("sp_aqt", [NCH * HH * 128, 128])
    dram_scratch("sp_ktil", [NCH * HH * 128, 192])
    dram_scratch("sp_qfm", [NCH * HH * 128, 256])
    dram_scratch("sp_o", [TOK, VD])
    dram_scratch("sp_og", [TOK, VD])
    dram_scratch("sp_x1", [TOK, DM], F32)
    dram_scratch("sp_xq", [TOK, DM])
    dram_scratch("sp_up", [TOK, FI])
    dram_scratch("sp_gate", [TOK, FI])
    dram_scratch("sp_oq", [TOK, FI])
    dr["cc_state_in"] = nc.dram_tensor("cc_state_in", [192, HH * DV], F32).ap()
    dr["cc_state_out"] = nc.dram_tensor("cc_state_out", [8 * 192, HH * DV], F32).ap()
    dr["cc_wq_in"] = nc.dram_tensor("cc_wq_in", [1152, DM], BF16).ap()
    dr["cc_wq_out"] = nc.dram_tensor("cc_wq_out", [8 * 1152, DM], BF16).ap()
    dr["cc_wg_in"] = nc.dram_tensor("cc_wg_in", [768, DM], BF16).ap()
    dr["cc_wg_out"] = nc.dram_tensor("cc_wg_out", [8 * 768, DM], BF16).ap()
    dr["cc_wd_in"] = nc.dram_tensor("cc_wd_in", [384, DM], BF16).ap()
    dr["cc_wd_out"] = nc.dram_tensor("cc_wd_out", [8 * 384, DM], BF16).ap()
    for i in range(4):
        dr[f"cc_s{i}_in"] = nc.dram_tensor(f"cc_s{i}_in", [1, 2], F32).ap()
        dr[f"cc_s{i}_out"] = nc.dram_tensor(f"cc_s{i}_out", [8, 2], F32).ap()

    def dbg_tensor(name, shape):
        if name in dbg:
            ap = nc.dram_tensor("dbg_" + name, shape, F32, kind="ExternalOutput").ap()
            k.dbg_out[name] = ap
            return ap
        return None
    k.dbg_tensor = dbg_tensor

    c = k.c = {}
    c["idf"] = nc.inline_tensor(np.eye(128, dtype=np.float32), name="c_idf").ap()
    c["ones_row"] = nc.inline_tensor(np.ones((1, 128), np.float32), name="c_ones").ap()
    c["ut"] = nc.inline_tensor(np.triu(np.ones((128, 128), np.float32), 0), name="c_ut").ap()
    c["lts"] = nc.inline_tensor(np.tril(np.ones((128, 128), np.float32), -1), name="c_lts").ap()
    c["mlow"] = nc.inline_tensor(np.tril(np.ones((128, 128), np.float32), -1), name="c_mlow").ap()
    c["muic"] = nc.inline_tensor(np.triu(np.ones((128, 128), np.float32), 0), name="c_muic").ap()
    c["id128"] = nc.inline_tensor(np.eye(128, dtype=np.float32), name="c_id128").ap()
    for i, s in enumerate((1, 2, 4, 8, 16, 32, 64)):
        c[f"negE{i}"] = nc.inline_tensor(_levelmask_neg(128, s), name=f"c_negE{i}").ap()

    sb = lambda *a, **kw: ctx.enter_context(nc.sbuf_tensor(*a, **kw))
    k.sb = sb
    k.consts = sb("consts", [128, 512], F32)
    k.constsb = sb("constsb", [128, 1280], BF16)
    k.gnwb = sb("gnwb", [128, DV], F32)
    k.alogv = sb("alogv", [128, 16], F32)
    k.gkb = sb("gkb", [128, NT * 8], F32)
    k.stat = sb("stat", [128, 192], F32)
    k.sval = sb("sval", [128, 48], F32)
    k.fconst = sb("fconst", [128, 4], F32)

    k.pb = [ctx.enter_context(nc.psum_tensor(f"pb{i}", [128, 512], F32)) for i in range(8)]

    sem = lambda name: Sem(ctx, nc, name)
    k.sem = sem
    k.s_lc = sem("s_lc")
    k.s_lcg = sem("s_lcg")
    k.s_lw = sem("s_lw")
    k.s_lwg = sem("s_lwg")
    k.s_lx = [sem("s_lx0"), sem("s_lx1")]
    k.s_mm = sem("s_mm")
    k.s_v = sem("s_v")
    k.s_sc = sem("s_sc")
    k.s_sp = sem("s_sp")
    k.s_g = sem("s_g")
    k.s_st = sem("s_st")
    k.s_st2 = [sem("s_st2a"), sem("s_st2b")]

    k.realblock = ctx.enter_context(nc.Block())
    k.block = _RecBlock()
    k.block.realblock = k.realblock

    emit_prelude(k)
    emit_passA(k)
    if upto == "A":
        emit_finish_dbg(k)
        k.block.materialize(k.realblock)
        ctx.close()
        return nc, k.dbg_out
    k.block.materialize(k.realblock)
    emit_gkb(k)
    k.block.materialize(k.realblock)
    emit_delta(k, chain=1)
    k.block.materialize(k.realblock)
    emit_state_ag(k)
    k.block.materialize(k.realblock)
    emit_delta(k, chain=2)
    if upto == "delta":
        emit_finish_dbg(k)
        k.block.materialize(k.realblock)
        ctx.close()
        return nc, k.dbg_out
    k.block.materialize(k.realblock)
    emit_epilogue(k)
    if upto == "attn":
        emit_finish_dbg(k)
        k.block.materialize(k.realblock)
        ctx.close()
        return nc, k.dbg_out
    k.block.materialize(k.realblock)
    emit_ffn(k)
    k.block.materialize(k.realblock)
    emit_dbg_ffn(k)
    emit_finish(k)
    k.block.materialize(k.realblock)
    ctx.close()
    return nc, k.dbg_out


def emit_prelude(k):
    nc, block, c = k.nc, k.block, k.c
    s_lc, s_mm, s_v, s_sc = k.s_lc, k.s_mm, k.s_v, k.s_sc

    @block.sync
    def _(sync):
        sync.dma_start(out=k.consts[:, 0:128], in_=c["idf"]).then_inc(s_lc.inc(16), 16)
        sync.dma_start(out=k.consts[0:1, 128:256], in_=c["ones_row"]).then_inc(s_lc.inc(16), 16)
        sync.dma_start(out=k.consts[:, 256:384], in_=c["ut"]).then_inc(s_lc.inc(16), 16)
        sync.dma_start(out=k.consts[:, 384:512], in_=c["lts"]).then_inc(s_lc.inc(16), 16)
        sync.dma_start(out=k.gnwb[0:1, :], in_=k.d["g_norm_w"]).then_inc(s_lc.inc(16), 16)
        sync.dma_start(out=k.alogv[0:1, 0:4], in_=k.d["A_log"]).then_inc(s_lc.inc(16), 16)
        sync.dma_start(out=k.alogv[0:1, 4:12], in_=k.d["gkb_bias"]).then_inc(s_lc.inc(16), 16)

    @block.gpsimd
    def _(g):
        g.dma_start(out=k.constsb[:, 0:128], in_=c["mlow"]).then_inc(k.s_lcg.inc(16), 16)
        g.dma_start(out=k.constsb[:, 128:256], in_=c["muic"]).then_inc(k.s_lcg.inc(16), 16)
        g.dma_start(out=k.constsb[:, 256:384], in_=c["id128"]).then_inc(k.s_lcg.inc(16), 16)
        for i in range(7):
            g.dma_start(out=k.constsb[:, 384 + 128 * i:512 + 128 * i],
                        in_=c[f"negE{i}"]).then_inc(k.s_lcg.inc(16), 16)

    @block.tensor
    def _(t):
        t.wait_ge(s_lc.h, s_lc.v)
        t.wait_ge(k.s_lcg.h, k.s_lcg.v)
        t.matmul(k.pb[0][:, 0:384], k.consts[0:1, 128:256], k.gnwb[0:1, 0:384],
                 start=True, stop=True)
        t.matmul(k.pb[1][:, 0:12], k.consts[0:1, 128:256], k.alogv[0:1, 0:12],
                 start=True, stop=True).then_inc(s_mm.inc(), 1)

    @block.vector
    def _(v):
        v.wait_ge(s_mm.h, s_mm.v)
        v.wait_ge(k.s_lcg.h, k.s_lcg.v)
        v.memset(k.fconst[:, 0:1], 1e-6)
        v.memset(k.fconst[:, 1:2], EPS_LN)
        v.memset(k.sval[:, :], 0.0)
        v.tensor_copy(out=k.gnwb[:, :], in_=k.pb[0][:, 0:384])
        v.tensor_copy(out=k.alogv[:, 4:12], in_=k.pb[1][:, 4:12])
        v.tensor_copy(out=k.alogv[:, 12:16], in_=k.pb[1][:, 0:4]).then_inc(s_v.inc(), 1)

    @block.scalar
    def _(sc):
        sc.wait_ge(s_v.h, s_v.v)
        sc.wait_ge(k.s_lcg.h, k.s_lcg.v)
        sc.activation(out=k.alogv[:, 0:4], in_=k.alogv[:, 12:16], func=AF.Exp).then_inc(s_sc.inc(), 1)

    @block.vector
    def _(v):
        v.wait_ge(s_sc.h, s_sc.v)
        v.tensor_scalar(out=k.alogv[:, 12:16], in0=k.alogv[:, 0:4], scalar1=-1.0,
                        scalar2=None, op0=OP.mult).then_inc(s_v.inc(), 1)
    k.prelude_v = s_v.v


def emit_passA(k):
    nc, block = k.nc, k.block
    s_mm, s_v, s_sc = k.s_mm, k.s_v, k.s_sc
    pb = k.pb

    k.scope_zs = ExitStack()
    k.zs = k.scope_zs.enter_context(nc.sbuf_tensor("zs", [128, 8 * TOK], BF16))
    k.scope_xs = ExitStack()
    k.xs = k.scope_xs.enter_context(nc.sbuf_tensor("xs", [128, 8 * TOK], BF16))

    k.scopeA = ExitStack()
    sbA = lambda *a: k.scopeA.enter_context(nc.sbuf_tensor(*a))
    winb = sbA("winb", [128, 8 * D_INNER], BF16)
    diagw = sbA("diagw", [128, 64 * 128], BF16)
    convwv = sbA("convwv", [128, 64], F32)
    convbv = sbA("convbv", [128, 16], F32)
    xtile = sbA("xtile", [128, 2 * DM], F32)
    xfm = sbA("xfm", [128, 8 * TPAD], BF16)
    xzbuf = sbA("xzbuf", [128, 16 * 516], BF16)

    @block.gpsimd
    def _(g):
        for kc in range(8):
            g.dma_start(out=winb[:, kc * D_INNER:(kc + 1) * D_INNER],
                        in_=k.d["W_in"][128 * kc:128 * (kc + 1), :]).then_inc(k.s_lwg.inc(16), 16)

    @block.sync
    def _(sync):
        for cc in range(16):
            sync.dma_start(out=convwv[:, 4 * cc:4 * cc + 4],
                           in_=k.d["convw"][128 * cc:128 * (cc + 1), :]).then_inc(k.s_lw.inc(16), 16)
        for cc in range(16):
            sync.dma_start(out=convbv[:, cc:cc + 1],
                           in_=k.d["convb"][128 * cc:128 * (cc + 1), :]).then_inc(k.s_lw.inc(16), 16)

    @block.vector
    def _(v):
        v.wait_ge(k.s_lw.h, k.s_lw.v)
        for cc in range(16):
            for j in range(4):
                vi = v.tensor_scalar(out=diagw[:, (4 * cc + j) * 128:(4 * cc + j + 1) * 128],
                                     in0=k.consts[:, 0:128], scalar1=convwv[:, 4 * cc + j:4 * cc + j + 1],
                                     scalar2=None, op0=OP.mult)
        vi.then_inc(s_v.inc(), 1)
    v_diag = k.s_v.v

    base_mm = s_mm.v
    base_v = s_v.v

    @block.sync
    def _(sync):
        for tt in range(NPT):
            if tt >= 2:
                sync.wait_ge(s_mm.h, base_mm + tt - 1)
            sync.dma_start(out=xtile[:, (tt % 2) * DM:(tt % 2 + 1) * DM],
                           in_=k.d["x_pad"][128 * tt:128 * (tt + 1), :]).then_inc(k.s_lx[tt % 2].inc(16), 16)

    @block.tensor
    def _(t):
        t.wait_ge(s_v.h, 1)
        for tt in range(NPT):
            t.wait_ge(k.s_lx[tt % 2].h, 16 * (tt // 2 + 1))
            if tt >= 2:
                t.wait_ge(s_v.h, base_v + tt - 1)
            for kc in range(8):
                bank = pb[(tt % 2) * 2 + kc // 4]
                ti = t.transpose(bank[:, (kc % 4) * 128:(kc % 4 + 1) * 128],
                                 xtile[:, (tt % 2) * DM + kc * 128:(tt % 2) * DM + (kc + 1) * 128],
                                 k.consts[:, 0:128])
                if kc == 7:
                    ti.then_inc(s_mm.inc(), 1)

    @block.vector
    def _(v):
        for tt in range(NPT):
            v.wait_ge(s_mm.h, base_mm + tt + 1)
            for kc in range(8):
                bank = pb[(tt % 2) * 2 + kc // 4]
                vi = v.tensor_copy(out=xfm[:, kc * TPAD + 128 * tt:kc * TPAD + 128 * (tt + 1)],
                                   in_=bank[:, (kc % 4) * 128:(kc % 4 + 1) * 128])
            vi.then_inc(s_v.inc(), 1)

    mm_after_x = s_mm.v
    v_after_x = s_v.v

    base_mm2 = s_mm.v
    base_v2 = s_v.v
    base_sc3 = k.s_sc.v

    @block.tensor
    def _(t):
        t.wait_ge(k.s_lwg.h, k.s_lwg.v)
        t.wait_ge(s_v.h, v_after_x)
        t.wait_ge(s_v.h, v_diag)
        nmm = 0
        nconv = 0
        for g in range(4):
            t0 = 512 * g
            for oc in range(16):
                bank = pb[2 + (nmm % 2)]
                if nmm >= 2:
                    t.wait_ge(s_v.h, base_v2 + nmm - 1)
                for kc in range(8):
                    t.matmul(bank[:, 0:512],
                             winb[:, kc * D_INNER + oc * 128:kc * D_INNER + (oc + 1) * 128],
                             xfm[:, kc * TPAD + t0:kc * TPAD + t0 + 512],
                             start=(kc == 0), stop=False if kc < 7 else True)
                for kc in range(8):
                    mi = t.matmul(pb[4 + (nmm % 2)][:, 0:4],
                                  winb[:, kc * D_INNER + oc * 128:kc * D_INNER + (oc + 1) * 128],
                                  xfm[:, kc * TPAD + t0 + 512:kc * TPAD + t0 + 516],
                                  start=(kc == 0), stop=(kc == 7))
                    if kc == 7:
                        mi.then_inc(s_mm.inc(), 1)
                nmm += 1
            t.wait_ge(s_v.h, base_v2 + nmm)
            for cc in range(16):
                bank = pb[6 + (nconv % 2)]
                if nconv >= 2:
                    t.wait_ge(k.s_sc.h, base_sc3 + nconv - 1)
                for j in range(4):
                    mi = t.matmul(bank[:, 0:512],
                                  diagw[:, (4 * cc + j) * 128:(4 * cc + j + 1) * 128],
                                  xzbuf[:, cc * 516 + j:cc * 516 + j + 512],
                                  start=(j == 0), stop=(j == 3))
                    if j == 3:
                        mi.then_inc(s_mm.inc(), 1)
                nconv += 1

    @block.vector
    def _(v):
        nmm = 0
        for g in range(4):
            for oc in range(16):
                bank = pb[2 + (nmm % 2)]
                v.wait_ge(s_mm.h, base_mm2 + nmm + 1 + 16 * g)
                v.tensor_copy(out=xzbuf[:, oc * 516:oc * 516 + 512], in_=bank[:, 0:512])
                v.tensor_copy(out=xzbuf[:, oc * 516 + 512:oc * 516 + 516],
                              in_=pb[4 + (nmm % 2)][:, 0:4]).then_inc(s_v.inc(), 1)
                nmm += 1

    @block.scalar
    def _(sc):
        nconv = 0
        for g in range(4):
            t0 = 512 * g
            for cc in range(16):
                bank = pb[6 + (nconv % 2)]
                sc.wait_ge(s_mm.h, base_mm2 + 16 * (g + 1) + nconv + 1)
                dst = k.xs if cc < 8 else k.zs
                oc = cc if cc < 8 else cc - 8
                sc.activation(out=dst[:, oc * TOK + t0:oc * TOK + t0 + 512], in_=bank[:, 0:512],
                              func=AF.Silu, bias=convbv[:, cc:cc + 1],
                              scale=1.0).then_inc(k.s_sc.inc(), 1)
                nconv += 1

    k.A_done_sc = k.s_sc.v
    k.A_done_mm = s_mm.v

    dxs = k.dbg_tensor("xs", [128, 8 * TOK])
    dzs = k.dbg_tensor("zs", [128, 8 * TOK])
    if dxs is not None:
        @block.gpsimd
        def _(g):
            g.wait_ge(k.s_sc.h, k.A_done_sc)
            g.dma_start(out=dxs, in_=k.xs[:, :]).then_inc(k.s_sp.inc(16), 16)
            g.dma_start(out=dzs, in_=k.zs[:, :]).then_inc(k.s_sp.inc(16), 16)


def emit_gkb(k):
    nc, block, pb = k.nc, k.block, k.pb
    s_mm, s_v, s_sc = k.s_mm, k.s_v, k.s_sc
    k.scopeA.close()

    k.scopeD = ExitStack()
    sbD = lambda *a: k.scopeD.enter_context(nc.sbuf_tensor(*a))
    k.sbD = sbD
    wgkbb = sbD("wgkbb", [128, 64], BF16)
    gkscr = sbD("gkscr", [128, 32], F32)
    s_lgkb = k.sem("s_lgkb")

    @block.gpsimd
    def _(g):
        g.wait_ge(k.s_sc.h, k.A_done_sc)
        for kc in range(8):
            g.dma_start(out=wgkbb[:, 8 * kc:8 * kc + 8],
                        in_=k.d["W_gkb"][128 * kc:128 * (kc + 1), :]).then_inc(s_lgkb.inc(16), 16)

    vfirst = {}
    for t in range(NT):
        par = t % 2

        @block.tensor
        def _(tt_, t=t, par=par):
            tt_.wait_ge(s_lgkb.h, s_lgkb.v)
            tt_.wait_ge(k.s_sc.h, k.A_done_sc)
            if t >= 2:
                tt_.wait_ge(s_v.h, vfirst[t - 2])
            for kc in range(8):
                mi = tt_.matmul(pb[6 + par][:, 0:8],
                                k.xs[:, kc * TOK + 128 * t:kc * TOK + 128 * (t + 1)],
                                wgkbb[:, 8 * kc:8 * kc + 8],
                                start=(kc == 0), stop=(kc == 7))
            mi.then_inc(s_mm.inc(), 1)

        mmv = s_mm.v

        @block.vector
        def _(v, t=t, par=par):
            v.wait_ge(s_mm.h, mmv)
            v.tensor_tensor(out=gkscr[:, 0:8], in0=pb[6 + par][:, 0:8],
                            in1=k.alogv[:, 4:12], op=OP.add).then_inc(s_v.inc(), 1)
        vfirst[t] = s_v.v

        vv = s_v.v

        @block.scalar
        def _(sc, t=t):
            sc.wait_ge(s_v.h, vv)
            sc.activation(out=gkscr[:, 8:12], in_=gkscr[:, 0:4], func=AF.Exp).then_inc(s_sc.inc(), 1)
            sc.wait_ge(s_sc.h, s_sc.v)
            sc.activation(out=gkscr[:, 12:16], in_=gkscr[:, 8:12], func=AF.Ln, bias=1.0)
            sc.activation(out=gkscr[:, 16:20], in_=gkscr[:, 4:8], func=AF.Exp,
                          scale=-1.0).then_inc(s_sc.inc(), 1)

        scv = s_sc.v

        @block.vector
        def _(v, t=t):
            v.wait_ge(s_sc.h, scv)
            v.tensor_tensor(out=k.gkb[:, 8 * t:8 * t + 4], in0=gkscr[:, 12:16],
                            in1=k.alogv[:, 12:16], op=OP.mult)
            v.tensor_scalar(out=gkscr[:, 20:24], in0=gkscr[:, 16:20], scalar1=1.0,
                            scalar2=None, op0=OP.add).then_inc(s_v.inc(), 1)
            v.wait_ge(s_v.h, s_v.v)
            v.reciprocal(out=k.gkb[:, 8 * t + 4:8 * t + 8],
                         in_=gkscr[:, 20:24]).then_inc(s_v.inc(), 1)

    k.gkb_done_v = s_v.v

    dgk = k.dbg_tensor("gkb", [128, NT * 8])
    if dgk is not None:
        @block.gpsimd
        def _(g):
            g.wait_ge(s_v.h, k.gkb_done_v)
            g.dma_start(out=dgk, in_=k.gkb[:, :]).then_inc(k.s_sp.inc(16), 16)


def delta_alloc(k):
    sbD = k.sbD
    nc = k.nc
    k.wqb = sbD("wqb", [128, 8 * KD], BF16)
    k.wkb = sbD("wkb", [128, 8 * KD], BF16)
    k.wvb = sbD("wvb", [128, 8 * VD], BF16)
    k.q_tm = sbD("q_tm", [128, KD], F32)
    k.k_tm = sbD("k_tm", [128, KD], F32)
    k.qh_tm = sbD("qh_tm", [128, DK], F32)
    k.kh_tm = sbD("kh_tm", [128, DK], F32)
    k.sqv = sbD("sqv", [128, 8], F32)
    k.lamv = sbD("lamv", [128, 2 * 16], F32)
    k.lamlast = sbD("lamlast", [128, 2 * 4], F32)
    k.lamT = sbD("lamT", [4, 2 * 128], F32)
    k.dupS = sbD("dupS", [128, 128], F32)
    k.dloS = sbD("dloS", [128, 128], F32)
    k.dscr = sbD("dscr", [128, 128], F32)
    k.dscr2 = sbD("dscr2", [128, 128], F32)
    k.a_bf = sbD("a_bf", [128, 2 * 128], BF16)
    k.as_bf = sbD("as_bf", [128, 128], BF16)
    k.z0 = sbD("z0", [128, 128], BF16)
    k.z1 = sbD("z1", [128, 128], BF16)
    k.zt0 = sbD("zt0", [128, 128], BF16)
    k.zt1 = sbD("zt1", [128, 128], BF16)
    k.q1s = sbD("q1s", [128, 128], BF16)
    k.r_pan = sbD("r_pan", [128, 576], BF16)
    k.blcol = sbD("blcol", [128, 4], F32)
    k.uv = sbD("uv", [128, 2 * DV], BF16)
    k.w_tm = sbD("w_tm", [128, 2 * 192], BF16)
    k.wfm = sbD("wfm", [128, 2 * 256], BF16)
    k.aqt = sbD("aqt", [128, 2 * 128], BF16)
    k.ktil = sbD("ktil", [128, 2 * 192], BF16)
    k.qfm = sbD("qfm", [128, 2 * 256], BF16)
    k.kfm = sbD("kfm", [128, 256], BF16)
    k.ubf = sbD("ubf", [128, DV], BF16)
    k.oscr = sbD("oscr", [128, DV], F32)
    k.obuf = sbD("obuf", [128, 2 * VD], BF16)
    k.osq = sbD("osq", [128, DV], F32)
    k.s0 = [sbD("s0a", [128, HH * DV], F32), sbD("s0b", [128, HH * DV], F32)]
    k.s1 = [sbD("s1a", [128, HH * DV], F32), sbD("s1b", [128, HH * DV], F32)]
    k.s0b = [sbD("s0ab", [128, HH * DV], BF16), sbD("s0bb", [128, HH * DV], BF16)]
    k.s1b = [sbD("s1ab", [128, HH * DV], BF16), sbD("s1bb", [128, HH * DV], BF16)]
    k.c["hsel"] = nc.inline_tensor(
        np.stack([np.repeat(np.eye(4, dtype=np.float32)[:, hh:hh + 1], 128, 1) for hh in range(4)], 0)
        .transpose(1, 0, 2).reshape(4, 512), name="c_hsel").ap()
    k.hsel = sbD("hsel", [4, 512], F32)
    k.s_lqkv = k.sem("s_lqkv")
    k.s_lhsel = k.sem("s_lhsel")
    k.s_spp = [k.sem("s_spp0"), k.sem("s_spp1")]
    k.s_hd = [k.sem("s_hd1"), k.sem("s_hd2")]
    k.s_pan = [k.sem("s_pan0"), k.sem("s_pan1")]
    k.s_so = [k.sem("s_so0"), k.sem("s_so1")]

    @k.block.sync
    def _(sync):
        sync.wait_ge(k.s_sc.h, k.A_done_sc)
        sync.dma_start(out=k.hsel[:, :], in_=k.c["hsel"]).then_inc(k.s_lhsel.inc(16), 16)

    @k.block.gpsimd
    def _(g):
        g.wait_ge(k.s_sc.h, k.A_done_sc)
        for kc in range(8):
            g.dma_start(out=k.wqb[:, kc * KD:(kc + 1) * KD],
                        in_=k.d["W_q"][128 * kc:128 * (kc + 1), :]).then_inc(k.s_lqkv.inc(16), 16)
            g.dma_start(out=k.wkb[:, kc * KD:(kc + 1) * KD],
                        in_=k.d["W_k"][128 * kc:128 * (kc + 1), :]).then_inc(k.s_lqkv.inc(16), 16)
            g.dma_start(out=k.wvb[:, kc * VD:(kc + 1) * VD],
                        in_=k.d["W_v"][128 * kc:128 * (kc + 1), :]).then_inc(k.s_lqkv.inc(16), 16)


def emit_chunk_lambda(k, c, chain):
    block, pb = k.block, k.pb
    s_mm, s_v, s_sc = k.s_mm, k.s_v, k.s_sc
    par = c % 2
    lo = 16 * par

    vnow = s_v.v

    @block.tensor
    def _(t):
        t.wait_ge(s_v.h, vnow)
        t.matmul(pb[5][:, 384:388], k.consts[:, 256:384], k.gkb[:, 8 * c:8 * c + 4],
                 start=True, stop=True)
        t.matmul(pb[5][:, 388:392], k.consts[:, 384:512], k.gkb[:, 8 * c:8 * c + 4],
                 start=True, stop=True).then_inc(s_mm.inc(), 1)

    mmv = s_mm.v

    @block.vector
    def _(v):
        v.wait_ge(s_mm.h, mmv)
        v.tensor_copy(out=k.lamv[:, lo:lo + 8], in_=pb[5][:, 384:392]).then_inc(s_v.inc(), 1)
        v.wait_ge(s_v.h, s_v.v)
        v.tensor_tensor(out=k.lamv[:, lo + 12:lo + 16], in0=k.lamv[:, lo:lo + 4],
                        in1=k.lamv[:, lo + 4:lo + 8], op=OP.add).then_inc(s_v.inc(), 1)

    vv = s_v.v

    @block.scalar
    def _(sc):
        sc.wait_ge(s_v.h, vv)
        sc.activation(out=k.lamlast[:, 4 * par:4 * par + 4], in_=k.lamv[:, lo + 12:lo + 16],
                      func=AF.Exp).then_inc(s_sc.inc(), 1)
        sc.wait_ge(s_sc.h, s_sc.v)
        sc.activation(out=k.lamv[:, lo + 12:lo + 16], in_=k.lamv[:, lo + 4:lo + 8],
                      func=AF.Exp)
        sc.activation(out=k.lamv[:, lo + 8:lo + 12], in_=k.lamv[:, lo:lo + 4],
                      func=AF.Exp).then_inc(s_sc.inc(), 1)

    scv = s_sc.v

    @block.tensor
    def _(t):
        t.wait_ge(s_sc.h, scv)
        t.transpose(pb[4][0:4, 384:512], k.lamv[:, lo:lo + 4],
                    k.consts[:, 0:128]).then_inc(s_mm.inc(), 1)

    mmv2 = s_mm.v

    @block.vector
    def _(v):
        v.wait_ge(s_mm.h, mmv2)
        v.tensor_copy(out=k.lamT[:, 128 * par:128 * (par + 1)],
                      in_=pb[4][0:4, 384:512]).then_inc(s_v.inc(), 1)


def emit_head(k, c, h, chain):
    block, pb = k.block, k.pb
    s_mm, s_v, s_sc = k.s_mm, k.s_v, k.s_sc
    par = c % 2
    lo = 16 * par
    j = 4 * c + h
    jp = j % 2
    cur, nxt = par, 1 - par
    bcol = k.gkb[:, 8 * c + 4 + h:8 * c + 5 + h]
    lamcol = k.lamv[:, lo + h:lo + h + 1]
    Lamcol = k.lamv[:, lo + 8 + h:lo + 9 + h]
    Lamrcol = k.lamv[:, lo + 12 + h:lo + 13 + h]
    Lamlastcol = k.lamlast[:, 4 * par + h:4 * par + h + 1]
    UVo, WFo, AQo, KTo, QFo = (k.uv[:, jp * DV:(jp + 1) * DV],
                               k.wfm[:, jp * 256:(jp + 1) * 256],
                               k.aqt[:, jp * 128:(jp + 1) * 128],
                               k.ktil[:, jp * 192:(jp + 1) * 192],
                               k.qfm[:, jp * 256:(jp + 1) * 256])

    if chain == 1:
        vproj = s_v.v

        @block.tensor
        def _(t):
            t.wait_ge(k.s_lqkv.h, k.s_lqkv.v)
            t.wait_ge(k.s_lhsel.h, k.s_lhsel.v)
            t.wait_ge(s_v.h, vproj)
            for kc in range(8):
                t.matmul(pb[0][:, 0:192], k.xs[:, kc * TOK + 128 * c:kc * TOK + 128 * (c + 1)],
                         k.wqb[:, kc * KD + 192 * h:kc * KD + 192 * (h + 1)],
                         start=(kc == 0), stop=(kc == 7))
            for kc in range(8):
                t.matmul(pb[0][:, 192:384], k.xs[:, kc * TOK + 128 * c:kc * TOK + 128 * (c + 1)],
                         k.wkb[:, kc * KD + 192 * h:kc * KD + 192 * (h + 1)],
                         start=(kc == 0), stop=(kc == 7))
            for kc in range(8):
                mi = t.matmul(pb[1][:, 0:384], k.xs[:, kc * TOK + 128 * c:kc * TOK + 128 * (c + 1)],
                              k.wvb[:, kc * VD + 384 * h:kc * VD + 384 * (h + 1)],
                              start=(kc == 0), stop=(kc == 7))
            mi.then_inc(s_mm.inc(), 1)

        m1 = s_mm.v

        @block.vector
        def _(v):
            v.wait_ge(s_mm.h, m1)
            if j >= 2:
                v.wait_ge(k.s_spp[j % 2].h, 16 * 5 * (j // 2))
            v.tensor_copy(out=k.q_tm[:, 0:192], in_=pb[0][:, 0:192])
            v.tensor_copy(out=k.k_tm[:, 0:192], in_=pb[0][:, 192:384]).then_inc(s_v.inc(), 1)
            v.wait_ge(s_v.h, s_v.v)
            v.tensor_tensor(out=k.qh_tm[:, :], in0=k.q_tm[:, 0:192], in1=k.q_tm[:, 0:192], op=OP.mult)
            v.tensor_tensor(out=k.kh_tm[:, :], in0=k.k_tm[:, 0:192], in1=k.k_tm[:, 0:192],
                            op=OP.mult).then_inc(s_v.inc(), 1)
            v.wait_ge(s_v.h, s_v.v)
            v.tensor_reduce(out=k.sqv[:, 0:1], in_=k.qh_tm[:, :], axis=AX, op=OP.add)
            v.tensor_reduce(out=k.sqv[:, 1:2], in_=k.kh_tm[:, :], axis=AX,
                            op=OP.add).then_inc(s_v.inc(), 1)

        v1 = s_v.v

        @block.scalar
        def _(sc):
            sc.wait_ge(s_v.h, v1)
            sc.activation(out=k.sqv[:, 2:4], in_=k.sqv[:, 0:2], func=AF.Ln,
                          bias=k.fconst[:, 0:1]).then_inc(s_sc.inc(), 1)
            sc.wait_ge(s_sc.h, s_sc.v)
            sc.activation(out=k.sqv[:, 4:6], in_=k.sqv[:, 2:4], func=AF.Exp,
                          scale=-0.5).then_inc(s_sc.inc(), 1)

        sc1 = s_sc.v

        @block.vector
        def _(v):
            v.wait_ge(s_sc.h, sc1)
            v.tensor_scalar(out=k.qh_tm[:, :], in0=k.q_tm[:, 0:192], scalar1=k.sqv[:, 4:5],
                            scalar2=None, op0=OP.mult)
            v.tensor_scalar(out=k.kh_tm[:, :], in0=k.k_tm[:, 0:192], scalar1=k.sqv[:, 5:6],
                            scalar2=None, op0=OP.mult).then_inc(s_v.inc(), 1)

        v2 = s_v.v

        @block.tensor
        def _(t):
            t.wait_ge(s_v.h, v2)
            t.transpose(pb[2][:, 0:128], k.qh_tm[:, 0:128], k.consts[:, 0:128])
            t.transpose(pb[2][0:64, 128:256], k.qh_tm[:, 128:192], k.consts[:, 0:128])
            t.transpose(pb[3][:, 0:128], k.kh_tm[:, 0:128], k.consts[:, 0:128])
            t.transpose(pb[3][0:64, 128:256], k.kh_tm[:, 128:192], k.consts[:, 0:128])
            t.matmul(pb[2][:, 256:384], k.hsel[:, 128 * h:128 * (h + 1)],
                     k.lamT[:, 128 * par:128 * (par + 1)],
                     start=True, stop=True).then_inc(s_mm.inc(), 1)

        m2 = s_mm.v

        @block.vector
        def _(v):
            v.wait_ge(s_mm.h, m2)
            v.tensor_copy(out=QFo[:, 0:128], in_=pb[2][:, 0:128])
            v.tensor_copy(out=QFo[0:64, 128:256], in_=pb[2][0:64, 128:256])
            v.tensor_copy(out=k.kfm[:, 0:128], in_=pb[3][:, 0:128])
            v.tensor_copy(out=k.kfm[0:64, 128:256], in_=pb[3][0:64, 128:256])
            v.tensor_scalar(out=k.dscr[:, :], in0=pb[2][:, 256:384], scalar1=lamcol,
                            scalar2=0.0, op0=OP.subtract, op1=OP.min)
            v.tensor_scalar(out=k.dloS[:, :], in0=pb[2][:, 256:384], scalar1=lamcol,
                            scalar2=0.0, op0=OP.subtract, op1=OP.max).then_inc(s_v.inc(), 1)

        v3 = s_v.v

        @block.scalar
        def _(sc):
            sc.wait_ge(s_v.h, v3)
            sc.activation(out=k.dupS[:, :], in_=k.dscr[:, :], func=AF.Exp).then_inc(s_sc.inc(), 1)
            sc.wait_ge(s_sc.h, s_sc.v)
            sc.activation(out=k.dscr[:, :], in_=k.dloS[:, :], func=AF.Exp,
                          scale=-1.0).then_inc(s_sc.inc(), 1)

        sc2 = s_sc.v

        @block.tensor
        def _(t):
            t.wait_ge(s_v.h, v3)
            t.matmul(pb[4][:, 0:128], k.kfm[:, 0:128], k.kfm[:, 0:128], start=True, stop=False)
            t.matmul(pb[4][:, 0:128], k.kfm[0:64, 128:256], k.kfm[0:64, 128:256], start=False, stop=True)
            t.matmul(pb[4][:, 128:256], k.kfm[:, 0:128], QFo[:, 0:128], start=True, stop=False)
            mi = t.matmul(pb[4][:, 128:256], k.kfm[0:64, 128:256], QFo[0:64, 128:256],
                          start=False, stop=True)
            mi.then_inc(s_mm.inc(), 1)

        m3 = s_mm.v

        @block.vector
        def _(v):
            v.wait_ge(s_mm.h, m3)
            v.wait_ge(s_sc.h, sc2)
            v.scalar_tensor_tensor(out=k.dloS[:, :], in0=k.dscr[:, :], scalar=bcol,
                                   in1=pb[4][:, 0:128], op0=OP.mult, op1=OP.mult)
            v.tensor_tensor(out=k.dscr2[:, :], in0=k.dupS[:, :], in1=pb[4][:, 128:256],
                            op=OP.mult).then_inc(s_v.inc(), 1)
            v.wait_ge(s_v.h, s_v.v)
            v.tensor_tensor(out=k.a_bf[:, jp * 128:(jp + 1) * 128], in0=k.dloS[:, :],
                            in1=k.constsb[:, 0:128], op=OP.mult)
            v.tensor_tensor(out=AQo, in0=k.dscr2[:, :],
                            in1=k.constsb[:, 128:256], op=OP.mult).then_inc(s_v.inc(), 1)

        v4 = s_v.v
        abf = k.a_bf[:, jp * 128:(jp + 1) * 128]

        @block.vector
        def _(v):
            v.wait_ge(s_v.h, v4)
            v.tensor_tensor(out=k.as_bf[:, :], in0=abf, in1=k.constsb[:, 384:512],
                            op=OP.mult).then_inc(s_v.inc(), 1)
            v.wait_ge(s_v.h, s_v.v)
            v.tensor_tensor(out=k.z0[:, :], in0=k.as_bf[:, :], in1=k.constsb[:, 256:384],
                            op=OP.add).then_inc(s_v.inc(), 1)

        vd = s_v.v

        @block.tensor
        def _(t):
            t.wait_ge(s_v.h, vd)
            t.transpose(pb[5][:, 0:64].bitcast(BF16), k.z0[:, :],
                        k.constsb[:, 256:384]).then_inc(s_mm.inc(), 1)

        md = s_mm.v

        @block.vector
        def _(v):
            v.wait_ge(s_mm.h, md)
            v.tensor_copy(out=k.zt0[:, :], in_=pb[5][:, 0:64].bitcast(BF16)).then_inc(s_v.inc(), 1)

        zc, ztc = k.z0, k.zt0
        zn, ztn = k.z1, k.zt1
        for lv in range(1, 7):
            vpre = s_v.v

            @block.vector
            def _(v, lv=lv):
                v.tensor_tensor(out=k.as_bf[:, :], in0=abf,
                                in1=k.constsb[:, 384 + 128 * lv:512 + 128 * lv],
                                op=OP.mult).then_inc(s_v.inc(), 1)

            va = s_v.v

            @block.tensor
            def _(t, ztc=ztc):
                t.wait_ge(s_v.h, va)
                t.matmul(pb[5][:, 0:128], k.as_bf[:, :], ztc[:, :],
                         start=True, stop=True).then_inc(s_mm.inc(), 1)

            mq = s_mm.v

            @block.vector
            def _(v):
                v.wait_ge(s_mm.h, mq)
                v.tensor_copy(out=k.q1s[:, :], in_=pb[5][:, 0:128]).then_inc(s_v.inc(), 1)

            vq = s_v.v

            @block.tensor
            def _(t, zc=zc):
                t.wait_ge(s_v.h, vq)
                t.matmul(pb[6][:, 0:128], k.q1s[:, :], zc[:, :], start=True, stop=True)
                t.matmul(pb[6][:, 128:256], zc[:, :], k.q1s[:, :],
                         start=True, stop=True).then_inc(s_mm.inc(), 1)

            mp = s_mm.v

            @block.vector
            def _(v, zc=zc, ztc=ztc, zn=zn, ztn=ztn):
                v.wait_ge(s_mm.h, mp)
                v.tensor_tensor(out=zn[:, :], in0=zc[:, :], in1=pb[6][:, 0:128], op=OP.add)
                v.tensor_tensor(out=ztn[:, :], in0=ztc[:, :], in1=pb[6][:, 128:256],
                                op=OP.add).then_inc(s_v.inc(), 1)

            zc, zn = zn, zc
            ztc, ztn = ztn, ztc

        ztfinal = ztc

        vr0 = s_v.v

        @block.vector
        def _(v):
            v.tensor_tensor(out=k.blcol[:, 0:1], in0=bcol, in1=Lamcol, op=OP.mult)
            v.tensor_scalar(out=k.r_pan[:, 0:384], in0=pb[1][:, 0:384], scalar1=bcol,
                            scalar2=None, op0=OP.mult)
            v.tensor_scalar(out=KTo, in0=k.kh_tm[:, :], scalar1=Lamrcol,
                            scalar2=None, op0=OP.mult).then_inc(s_v.inc(), 1)
            v.wait_ge(s_v.h, s_v.v)
            v.tensor_scalar(out=k.r_pan[:, 384:576], in0=k.kh_tm[:, :], scalar1=k.blcol[:, 0:1],
                            scalar2=None, op0=OP.mult).then_inc(s_v.inc(), 1)

        vr = s_v.v

        @block.tensor
        def _(t, ztfinal=ztfinal):
            t.wait_ge(s_v.h, vr)
            t.matmul(pb[7][:, 0:512], ztfinal[:, :], k.r_pan[:, 0:512], start=True, stop=True)
            t.matmul(pb[5][:, 128:192], ztfinal[:, :], k.r_pan[:, 512:576],
                     start=True, stop=True).then_inc(s_mm.inc(), 1)

        mpan = s_mm.v

        @block.vector
        def _(v):
            v.wait_ge(s_mm.h, mpan)
            v.tensor_copy(out=UVo, in_=pb[7][:, 0:384])
            v.tensor_copy(out=k.w_tm[:, jp * 192:jp * 192 + 128], in_=pb[7][:, 384:512])
            v.tensor_copy(out=k.w_tm[:, jp * 192 + 128:jp * 192 + 192],
                          in_=pb[5][:, 128:192]).then_inc(s_v.inc(), 1)

        vw = s_v.v

        @block.tensor
        def _(t):
            t.wait_ge(s_v.h, vw)
            t.transpose(pb[6][:, 0:64].bitcast(BF16), k.w_tm[:, jp * 192:jp * 192 + 128],
                        k.constsb[:, 256:384])
            t.transpose(pb[6][0:64, 64:128].bitcast(BF16), k.w_tm[:, jp * 192 + 128:jp * 192 + 192],
                        k.constsb[:, 256:384]).then_inc(s_mm.inc(), 1)

        mwf = s_mm.v

        @block.vector
        def _(v):
            v.wait_ge(s_mm.h, mwf)
            v.tensor_copy(out=WFo[:, 0:128], in_=pb[6][:, 0:64].bitcast(BF16))
            v.tensor_copy(out=WFo[0:64, 128:256],
                          in_=pb[6][0:64, 64:128].bitcast(BF16)).then_inc(s_v.inc(), 1)

        vwf = s_v.v

        @block.gpsimd
        def _(g):
            g.wait_ge(s_v.h, vwf)
            sp = k.s_spp[j % 2]
            g.dma_start(out=k.dr["sp_uv"][128 * j:128 * (j + 1), :], in_=UVo).then_inc(sp.inc(16), 16)
            g.dma_start(out=k.dr["sp_wfm"][128 * j:128 * (j + 1), :], in_=WFo).then_inc(sp.inc(16), 16)
            g.dma_start(out=k.dr["sp_aqt"][128 * j:128 * (j + 1), :], in_=AQo).then_inc(sp.inc(16), 16)
            g.dma_start(out=k.dr["sp_ktil"][128 * j:128 * (j + 1), :], in_=KTo).then_inc(sp.inc(16), 16)
            g.dma_start(out=k.dr["sp_qfm"][128 * j:128 * (j + 1), :], in_=QFo).then_inc(sp.inc(16), 16)

    else:
        @block.sync
        def _(sync):
            if j >= 2:
                sync.wait_ge(k.s_v.h, k.head_done_v[2][j - 2])
            sp = k.s_pan[jp]
            sync.dma_start(out=UVo, in_=k.dr["sp_uv"][128 * j:128 * (j + 1), :]).then_inc(sp.inc(16), 16)
            sync.dma_start(out=WFo, in_=k.dr["sp_wfm"][128 * j:128 * (j + 1), :]).then_inc(sp.inc(16), 16)
            sync.dma_start(out=AQo, in_=k.dr["sp_aqt"][128 * j:128 * (j + 1), :]).then_inc(sp.inc(16), 16)
            sync.dma_start(out=KTo, in_=k.dr["sp_ktil"][128 * j:128 * (j + 1), :]).then_inc(sp.inc(16), 16)
            sync.dma_start(out=QFo, in_=k.dr["sp_qfm"][128 * j:128 * (j + 1), :]).then_inc(sp.inc(16), 16)

    panwait = (k.s_pan[jp], 16 * 5 * (j // 2 + 1)) if chain == 2 else None

    vws = s_v.v

    @block.tensor
    def _(t):
        if panwait is not None:
            t.wait_ge(panwait[0].h, panwait[1])
        t.wait_ge(s_v.h, vws)
        t.matmul(pb[6][:, 0:384], WFo[:, 0:128], k.s0b[cur][:, DV * h:DV * (h + 1)],
                 start=True, stop=False)
        t.matmul(pb[6][:, 0:384], WFo[0:64, 128:256], k.s1b[cur][0:64, DV * h:DV * (h + 1)],
                 start=False, stop=True).then_inc(s_mm.inc(), 1)

    mws = s_mm.v

    @block.vector
    def _(v):
        v.wait_ge(s_mm.h, mws)
        v.tensor_tensor(out=k.ubf[:, :], in0=UVo, in1=pb[6][:, 0:384],
                        op=OP.subtract).then_inc(s_v.inc(), 1)

    vu = s_v.v

    if chain == 2:
        @block.tensor
        def _(t):
            t.wait_ge(s_v.h, vu)
            t.matmul(pb[2][:, 0:384], QFo[:, 0:128], k.s0b[cur][:, DV * h:DV * (h + 1)],
                     start=True, stop=False)
            t.matmul(pb[2][:, 0:384], QFo[0:64, 128:256], k.s1b[cur][0:64, DV * h:DV * (h + 1)],
                     start=False, stop=True)
            t.matmul(pb[3][:, 0:384], AQo, k.ubf[:, :], start=True, stop=True).then_inc(s_mm.inc(), 1)

        mo = s_mm.v

        @block.vector
        def _(v):
            v.wait_ge(s_mm.h, mo)
            if c >= 2 and h == 0:
                v.wait_ge(k.s_so[c % 2].h, 16 * (c // 2))
            v.tensor_scalar(out=k.oscr[:, :], in0=pb[2][:, 0:384], scalar1=Lamcol,
                            scalar2=None, op0=OP.mult).then_inc(s_v.inc(), 1)
            v.wait_ge(s_v.h, s_v.v)
            v.tensor_tensor(out=k.obuf[:, par * VD + DV * h:par * VD + DV * (h + 1)],
                            in0=k.oscr[:, :], in1=pb[3][:, 0:384], op=OP.add).then_inc(s_v.inc(), 1)

        vo = s_v.v

        @block.scalar
        def _(sc):
            sc.wait_ge(s_v.h, vo)
            sc.wait_ge(s_sc.h, s_sc.v)
            sc.activation(out=k.osq[:, :], in_=k.obuf[:, par * VD + DV * h:par * VD + DV * (h + 1)],
                          func=AF.Square,
                          accum_out=k.stat[:, 4 * c + h:4 * c + h + 1]).then_inc(s_sc.inc(), 1)

    @block.tensor
    def _(t):
        t.wait_ge(s_v.h, vu)
        t.matmul(pb[0][:, 0:384], KTo[:, 0:128], k.ubf[:, :], start=True, stop=True)
        t.matmul(pb[1][0:64, 0:384], KTo[:, 128:192], k.ubf[:, :],
                 start=True, stop=True).then_inc(s_mm.inc(), 1)

    msu = s_mm.v

    @block.vector
    def _(v):
        v.wait_ge(s_mm.h, msu)
        v.scalar_tensor_tensor(out=k.s0[nxt][:, DV * h:DV * (h + 1)],
                               in0=k.s0[cur][:, DV * h:DV * (h + 1)], scalar=Lamlastcol,
                               in1=pb[0][:, 0:384], op0=OP.mult, op1=OP.add)
        v.scalar_tensor_tensor(out=k.s1[nxt][0:64, DV * h:DV * (h + 1)],
                               in0=k.s1[cur][0:64, DV * h:DV * (h + 1)], scalar=Lamlastcol[0:64, :],
                               in1=pb[1][0:64, 0:384], op0=OP.mult, op1=OP.add).then_inc(s_v.inc(), 1)
        v.wait_ge(s_v.h, s_v.v)
        v.tensor_copy(out=k.s0b[nxt][:, DV * h:DV * (h + 1)], in_=k.s0[nxt][:, DV * h:DV * (h + 1)])
        v.tensor_copy(out=k.s1b[nxt][0:64, DV * h:DV * (h + 1)],
                      in_=k.s1[nxt][0:64, DV * h:DV * (h + 1)]).then_inc(s_v.inc(), 1)
    k.head_done_v[chain][j] = k.s_v.v


def emit_delta(k, chain):
    nc, block = k.nc, k.block
    if not hasattr(k, "head_done_v"):
        k.head_done_v = {1: {}, 2: {}}
    if chain == 1:
        delta_alloc(k)

        @block.vector
        def _(v):
            v.memset(k.s0[0][:, :], 0.0)
            v.memset(k.s1[0][:, :], 0.0)
            v.memset(k.s0b[0][:, :], 0.0)
            v.memset(k.s1b[0][:, :], 0.0)
            v.memset(k.wfm[64:128, 128:256], 0.0)
            v.memset(k.wfm[64:128, 384:512], 0.0)
            v.memset(k.qfm[64:128, 128:256], 0.0)
            v.memset(k.qfm[64:128, 384:512], 0.0)
            v.memset(k.s1[1][64:128, :], 0.0)
            v.memset(k.s1b[1][64:128, :], 0.0).then_inc(k.s_v.inc(), 1)
    else:
        @block.sync
        def _(sync):
            sync.wait_ge(k.s_spp[0].h, k.s_spp[0].v)
            sync.wait_ge(k.s_spp[1].h, k.s_spp[1].v)
            sync.wait_ge(k.s_v.h, k.s_v.v)

    for c in range(NCH):
        emit_chunk_lambda(k, c, chain)
        for h in range(HH):
            emit_head(k, c, h, chain)
        if chain == 2:
            par = c % 2
            vdone = k.s_v.v

            @block.gpsimd
            def _(g, c=c, par=par):
                g.wait_ge(k.s_v.h, vdone)
                g.dma_start(out=k.dr["sp_o"][128 * c:128 * (c + 1), :],
                            in_=k.obuf[:, par * VD:(par + 1) * VD]).then_inc(k.s_so[c % 2].inc(16), 16)

    if chain == 1:
        k.chain1_hd = k.head_done_v[1][4 * NCH - 1]
    else:
        k.chain2_done_so = (k.s_so[0].v, k.s_so[1].v)
        k.chain2_hd = k.head_done_v[2][4 * NCH - 1]
        dbo = k.dbg_tensor("o", [TOK, VD])
        if dbo is not None:
            @block.gpsimd
            def _(g):
                g.wait_ge(k.s_so[0].h, k.s_so[0].v)
                g.wait_ge(k.s_so[1].h, k.s_so[1].v)
                g.dma_start(out=dbo, in_=k.dr["sp_o"][:, :]).then_inc(k.s_sp.inc(16), 16)


def emit_state_ag(k):
    nc, block = k.nc, k.block
    s_v = k.s_v
    mock = getattr(k, "mock_ag", False)

    if mock:
        @block.vector
        def _(v):
            v.wait_ge(s_v.h, k.chain1_hd)
            v.memset(k.s0[0][:, :], 0.0)
            v.memset(k.s1[0][:, :], 0.0)
            v.memset(k.s0b[0][:, :], 0.0)
            v.memset(k.s1b[0][:, :], 0.0).then_inc(s_v.inc(), 1)
        return

    s_ag = k.sem("s_ag")
    s_agl = k.sem("s_agl")
    cin, cout = k.dr["cc_state_in"], k.dr["cc_state_out"]
    fin = NCH % 2

    @block.gpsimd
    def _(g):
        g.wait_ge(k.s_v.h, k.chain1_hd)
        g.dma_start(out=cin[0:128, :], in_=k.s0[fin][:, :]).then_inc(s_ag.inc(16), 16)
        g.dma_start(out=cin[128:192, :], in_=k.s1[fin][0:64, :]).then_inc(s_ag.inc(16), 16)
        g.wait_ge(s_ag.h, s_ag.v)
        g.collective_compute(
            "AllGather", mybir.AluOpType.bypass,
            replica_groups=[[0, 1, 2, 3, 4, 5, 6, 7]],
            ins=[cin.opt()], outs=[cout.opt()],
        ).then_inc(s_agl.inc(1), 1)

    agv = k.s_ag_done = k.s_ag = s_ag

    s_msk = k.sem("s_msk")

    @block.sync
    def _(sync):
        sync.dma_start(out=k.sval[0:1, 8:16], in_=k.d["smask"]).then_inc(s_msk.inc(16), 16)

    @block.tensor
    def _(t):
        t.wait_ge(s_msk.h, s_msk.v)
        t.matmul(k.pb[4][:, 0:8], k.consts[0:1, 128:256], k.sval[0:1, 8:16],
                 start=True, stop=True).then_inc(k.s_mm.inc(), 1)

    mmv = k.s_mm.v

    @block.vector
    def _(v):
        v.wait_ge(k.s_mm.h, mmv)
        v.tensor_copy(out=k.sval[:, 0:8], in_=k.pb[4][:, 0:8]).then_inc(s_v.inc(), 1)

    slab0 = k.sbD("slab0", [128, HH * DV], F32)
    slab1 = k.sbD("slab1", [128, HH * DV], F32)

    @block.vector
    def _(v):
        v.wait_ge(s_ag.h, 32)
        v.memset(k.s0[0][:, :], 0.0)
        v.memset(k.s1[0][:, :], 0.0).then_inc(s_v.inc(), 1)

    for ci in range(8):
        sidx, dst = ci % 2, (ci + 1) % 2
        lv = k.sem(f"s_agld{ci}")
        vprev = s_v.v

        @block.sync
        def _(sync, ci=ci, lv=lv, vprev=vprev):
            sync.wait_ge(s_agl.h, s_agl.v)
            sync.wait_ge(s_v.h, vprev)
            sync.dma_start(out=slab0[:, :], in_=k.dr["cc_state_out"][192 * ci:192 * ci + 128, :]).then_inc(lv.inc(16), 16)
            sync.dma_start(out=slab1[0:64, :], in_=k.dr["cc_state_out"][192 * ci + 128:192 * (ci + 1), :]).then_inc(lv.inc(16), 16)

        @block.vector
        def _(v, ci=ci, sidx=sidx, dst=dst, lv=lv, vprev=vprev):
            v.wait_ge(lv.h, lv.v)
            v.scalar_tensor_tensor(out=k.s0[dst][:, :], in0=slab0[:, :],
                                   scalar=k.sval[:, ci:ci + 1], in1=k.s0[sidx][:, :],
                                   op0=OP.mult, op1=OP.add)
            v.scalar_tensor_tensor(out=k.s1[dst][0:64, :], in0=slab1[0:64, :],
                                   scalar=k.sval[0:64, ci:ci + 1], in1=k.s1[sidx][0:64, :],
                                   op0=OP.mult, op1=OP.add).then_inc(s_v.inc(), 1)

    vfin = s_v.v

    @block.vector
    def _(v):
        v.wait_ge(s_v.h, vfin)
        v.tensor_copy(out=k.s0b[0][:, :], in_=k.s0[0][:, :])
        v.tensor_copy(out=k.s1b[0][0:64, :], in_=k.s1[0][0:64, :]).then_inc(s_v.inc(), 1)

    k.stateag_done_v = s_v.v


def emit_epilogue(k):
    nc, block, pb = k.nc, k.block, k.pb
    s_mm, s_v, s_sc = k.s_mm, k.s_v, k.s_sc

    scv0 = s_sc.v

    @block.scalar
    def _(sc):
        sc.wait_ge(s_sc.h, s_sc.v)
        sc.wait_ge(s_v.h, k.chain2_hd)
        sc.activation(out=k.stat[:, 64:128], in_=k.stat[:, 0:64], func=AF.Ln,
                      scale=1.0 / DV, bias=k.fconst[:, 1:2]).then_inc(s_sc.inc(), 1)
        sc.wait_ge(s_sc.h, s_sc.v)
        sc.activation(out=k.stat[:, 0:64], in_=k.stat[:, 64:128], func=AF.Exp,
                      scale=-0.5).then_inc(s_sc.inc(), 1)
    rs_done_sc = s_sc.v

    k.scopeD.close()
    scope1 = ExitStack()
    sb1 = lambda *a: scope1.enter_context(nc.sbuf_tensor(*a))
    wgb = sb1("wgb", [128, 8 * VD], BF16)
    otile = sb1("otile", [128, 2 * VD], BF16)
    gs = sb1("gs", [128, 2 * VD], BF16)
    og = sb1("og", [128, 2 * VD], BF16)
    t1s = sb1("t1s", [128, VD], F32)
    s_lwg2 = k.sem("s_lwg2")
    s_lo = [k.sem("s_lo0"), k.sem("s_lo1")]
    s_og = [k.sem("s_og0"), k.sem("s_og1")]

    @block.gpsimd
    def _(g):
        g.wait_ge(s_v.h, k.chain2_hd)
        g.wait_ge(k.s_so[0].h, k.s_so[0].v)
        g.wait_ge(k.s_so[1].h, k.s_so[1].v)
        for kc in range(8):
            g.dma_start(out=wgb[:, kc * VD:(kc + 1) * VD],
                        in_=k.d["W_g"][128 * kc:128 * (kc + 1), :]).then_inc(s_lwg2.inc(16), 16)

    ot_ready = {}
    og_done_v = {}
    tile_v_done = {}

    @block.sync
    def _(sync):
        sync.wait_ge(s_v.h, k.chain2_hd)
        sync.wait_ge(k.s_so[0].h, k.s_so[0].v)
        sync.wait_ge(k.s_so[1].h, k.s_so[1].v)

    for t in range(NT):
        par = t % 2

        @block.sync
        def _(sync, t=t, par=par):
            if t >= 2:
                sync.wait_ge(s_v.h, tile_v_done[t - 2])
            sync.dma_start(out=otile[:, par * VD:(par + 1) * VD],
                           in_=k.dr["sp_o"][128 * t:128 * (t + 1), :]).then_inc(s_lo[par].inc(16), 16)

        vprev_g = s_v.v

        @block.tensor
        def _(tt_, t=t, par=par):
            tt_.wait_ge(s_lwg2.h, s_lwg2.v)
            tt_.wait_ge(s_v.h, vprev_g)
            for nc3 in range(3):
                for kc in range(8):
                    mi = tt_.matmul(pb[nc3][:, 0:512],
                                    k.xs[:, kc * TOK + 128 * t:kc * TOK + 128 * (t + 1)],
                                    wgb[:, kc * VD + 512 * nc3:kc * VD + 512 * (nc3 + 1)],
                                    start=(kc == 0), stop=(kc == 7))
            mi.then_inc(s_mm.inc(), 1)

        mg = s_mm.v

        @block.scalar
        def _(sc, t=t, par=par):
            sc.wait_ge(s_mm.h, mg)
            if t == 0:
                sc.wait_ge(s_sc.h, rs_done_sc)
            for nc3 in range(3):
                si = sc.activation(out=gs[:, par * VD + 512 * nc3:par * VD + 512 * (nc3 + 1)],
                                   in_=pb[nc3][:, 0:512], func=AF.Silu)
            si.then_inc(s_sc.inc(), 1)

        sg = s_sc.v

        @block.vector
        def _(v, t=t, par=par):
            v.wait_ge(s_sc.h, sg)
            v.wait_ge(s_lo[par].h, s_lo[par].v)
            if t >= 2:
                v.wait_ge(s_og[par].h, 16 * (t // 2))
            for h in range(HH):
                vi = v.scalar_tensor_tensor(out=t1s[:, DV * h:DV * (h + 1)],
                                            in0=otile[:, par * VD + DV * h:par * VD + DV * (h + 1)],
                                            scalar=k.stat[:, 4 * t + h:4 * t + h + 1],
                                            in1=k.gnwb[:, :], op0=OP.mult, op1=OP.mult)
            vi.then_inc(s_v.inc(), 1)
            v.wait_ge(s_v.h, s_v.v)
            v.tensor_tensor(out=og[:, par * VD:(par + 1) * VD], in0=t1s[:, :],
                            in1=gs[:, par * VD:(par + 1) * VD], op=OP.mult).then_inc(s_v.inc(), 1)
        tile_v_done[t] = s_v.v

        @block.gpsimd
        def _(g, t=t, par=par):
            g.wait_ge(s_v.h, tile_v_done[t])
            g.dma_start(out=k.dr["sp_og"][128 * t:128 * (t + 1), :],
                        in_=og[:, par * VD:(par + 1) * VD]).then_inc(s_og[par].inc(16), 16)

    c1_v = s_v.v
    c1_og = (s_og[0].v, s_og[1].v)

    scope1.close()
    k.scope_xs.close()
    scope2 = ExitStack()
    k.scope2 = scope2
    sb2 = lambda *a: scope2.enter_context(nc.sbuf_tensor(*a))
    wsob = sb2("wsob", [128, 12 * DM], BF16)
    woutb = sb2("woutb", [128, 16 * DM], BF16)
    ogfm = sb2("ogfm", [128, 12 * TOK], BF16)
    ybuf = sb2("ybuf", [128, 8 * TOK], BF16)
    xres = sb2("xres", [128, 2 * DM], F32)
    pre1 = sb2("pre1", [128, DM], F32)
    sqscr = sb2("sqscr", [128, DM], F32)
    x1buf = sb2("x1buf", [128, 2 * DM], F32)
    lnscr = sb2("lnscr", [128, 16], F32)
    s_lso = k.sem("s_lso")
    s_ogt = k.sem("s_ogt")
    s_lxr = [k.sem("s_lxr0"), k.sem("s_lxr1")]
    s_x1 = [k.sem("s_x10"), k.sem("s_x11")]

    @block.gpsimd
    def _(g):
        g.wait_ge(s_v.h, c1_v)
        for kc in range(12):
            g.dma_start(out=wsob[:, kc * DM:(kc + 1) * DM],
                        in_=k.d["W_so"][128 * kc:128 * (kc + 1), :]).then_inc(s_lso.inc(16), 16)
        for kc in range(16):
            g.dma_start(out=woutb[:, kc * DM:(kc + 1) * DM],
                        in_=k.d["W_out"][128 * kc:128 * (kc + 1), :]).then_inc(s_lso.inc(16), 16)

    @block.sync
    def _(sync):
        sync.wait_ge(s_og[0].h, c1_og[0])
        sync.wait_ge(s_og[1].h, c1_og[1])
        sync.wait_ge(s_v.h, c1_v)
        for ic in range(12):
            sync.dma_start(out=ogfm[:, ic * TOK:(ic + 1) * TOK],
                           in_=k.dr["sp_og"][0:TOK, 128 * ic:128 * (ic + 1)],
                           transpose=True).then_inc(s_ogt.inc(16), 16)

    yjobs = [(oc, tg) for oc in range(8) for tg in range(4)]
    yev = {}
    for i, (oc, tg) in enumerate(yjobs):
        bank = pb[i % 2]

        @block.tensor
        def _(tt_, i=i, oc=oc, tg=tg, bank=bank):
            if i == 0:
                tt_.wait_ge(s_lso.h, s_lso.v)
                tt_.wait_ge(s_ogt.h, s_ogt.v)
            if i >= 2:
                tt_.wait_ge(s_v.h, yev[i - 2])
            for kc in range(12):
                mi = tt_.matmul(bank[:, 0:512],
                                wsob[:, kc * DM + oc * 128:kc * DM + (oc + 1) * 128],
                                ogfm[:, kc * TOK + 512 * tg:kc * TOK + 512 * (tg + 1)],
                                start=(kc == 0), stop=(kc == 11))
            mi.then_inc(s_mm.inc(), 1)

        my = s_mm.v

        @block.vector
        def _(v, i=i, oc=oc, tg=tg, bank=bank, my=my):
            v.wait_ge(s_mm.h, my)
            v.tensor_copy(out=ybuf[:, oc * TOK + 512 * tg:oc * TOK + 512 * (tg + 1)],
                          in_=bank[:, 0:512]).then_inc(s_v.inc(), 1)
        yev[i] = s_v.v

    y_done_v = s_v.v

    x1_v_done = {}
    for t in range(NT):
        par = t % 2

        @block.sync
        def _(sync, t=t, par=par):
            if t >= 2:
                sync.wait_ge(s_v.h, x1_v_done[t - 2])
            sync.dma_start(out=xres[:, par * DM:(par + 1) * DM],
                           in_=k.d["x_pad"][128 * t + 1:128 * (t + 1) + 1, :]).then_inc(s_lxr[par].inc(16), 16)

        vprev_a = s_v.v

        @block.tensor
        def _(tt_, t=t, par=par):
            tt_.wait_ge(s_v.h, vprev_a)
            for nc2 in range(2):
                for kc in range(16):
                    srcb = ybuf if kc < 8 else k.zs
                    kcc = kc % 8
                    mi = tt_.matmul(pb[2 + 2 * par + nc2][:, 0:512],
                                    srcb[:, kcc * TOK + 128 * t:kcc * TOK + 128 * (t + 1)],
                                    woutb[:, kc * DM + 512 * nc2:kc * DM + 512 * (nc2 + 1)],
                                    start=(kc == 0), stop=(kc == 15))
            mi.then_inc(s_mm.inc(), 1)

        ma = s_mm.v

        @block.vector
        def _(v, t=t, par=par, ma=ma):
            v.wait_ge(s_mm.h, ma)
            v.wait_ge(s_lxr[par].h, s_lxr[par].v)
            if t >= 2:
                v.wait_ge(s_x1[par].h, 16 * (t // 2))
            v.tensor_tensor(out=pre1[:, 0:512], in0=pb[2 + 2 * par][:, 0:512],
                            in1=xres[:, par * DM:par * DM + 512], op=OP.add)
            v.tensor_tensor(out=pre1[:, 512:1024], in0=pb[3 + 2 * par][:, 0:512],
                            in1=xres[:, par * DM + 512:par * DM + 1024], op=OP.add).then_inc(s_v.inc(), 1)
            v.wait_ge(s_v.h, s_v.v)
            v.tensor_reduce(out=lnscr[:, 0:1], in_=pre1[:, :], axis=AX, op=OP.add).then_inc(s_v.inc(), 1)

        vpre = s_v.v

        @block.scalar
        def _(sc, t=t):
            sc.wait_ge(s_v.h, vpre)
            sc.wait_ge(s_sc.h, s_sc.v)
            sc.activation(out=sqscr[:, :], in_=pre1[:, :], func=AF.Square,
                          accum_out=lnscr[:, 1:2]).then_inc(s_sc.inc(), 1)

        sqd = s_sc.v

        @block.vector
        def _(v, t=t):
            v.wait_ge(s_sc.h, sqd)
            v.tensor_scalar(out=lnscr[:, 2:3], in0=lnscr[:, 0:1], scalar1=1.0 / DM,
                            scalar2=None, op0=OP.mult)
            v.tensor_scalar(out=lnscr[:, 3:4], in0=lnscr[:, 1:2], scalar1=1.0 / DM,
                            scalar2=None, op0=OP.mult).then_inc(s_v.inc(), 1)
            v.wait_ge(s_v.h, s_v.v)
            v.tensor_tensor(out=lnscr[:, 4:5], in0=lnscr[:, 2:3], in1=lnscr[:, 2:3], op=OP.mult).then_inc(s_v.inc(), 1)
            v.wait_ge(s_v.h, s_v.v)
            v.tensor_tensor(out=lnscr[:, 5:6], in0=lnscr[:, 3:4], in1=lnscr[:, 4:5],
                            op=OP.subtract).then_inc(s_v.inc(), 1)

        vvar = s_v.v

        @block.scalar
        def _(sc, t=t):
            sc.wait_ge(s_v.h, vvar)
            sc.activation(out=lnscr[:, 6:7], in_=lnscr[:, 5:6], func=AF.Ln,
                          bias=k.fconst[:, 1:2]).then_inc(s_sc.inc(), 1)
            sc.wait_ge(s_sc.h, s_sc.v)
            sc.activation(out=lnscr[:, 7:8], in_=lnscr[:, 6:7], func=AF.Exp,
                          scale=-0.5).then_inc(s_sc.inc(), 1)

        srs = s_sc.v

        @block.vector
        def _(v, t=t, par=par):
            v.wait_ge(s_sc.h, srs)
            v.tensor_tensor(out=lnscr[:, 8:9], in0=lnscr[:, 2:3], in1=lnscr[:, 7:8], op=OP.mult).then_inc(s_v.inc(), 1)
            v.wait_ge(s_v.h, s_v.v)
            v.tensor_scalar(out=lnscr[:, 9:10], in0=lnscr[:, 8:9], scalar1=-1.0,
                            scalar2=None, op0=OP.mult).then_inc(s_v.inc(), 1)
            v.wait_ge(s_v.h, s_v.v)
            v.tensor_scalar(out=x1buf[:, par * DM:(par + 1) * DM], in0=pre1[:, :],
                            scalar1=lnscr[:, 7:8], scalar2=lnscr[:, 9:10],
                            op0=OP.mult, op1=OP.add).then_inc(s_v.inc(), 1)
            v.wait_ge(s_v.h, s_v.v)
            v.tensor_reduce(out=k.stat[:, 160 + t:161 + t], in_=x1buf[:, par * DM:(par + 1) * DM],
                            axis=AX, op=OP.max, apply_absolute_value=True).then_inc(s_v.inc(), 1)
        x1_v_done[t] = s_v.v

        @block.gpsimd
        def _(g, t=t, par=par):
            g.wait_ge(s_v.h, x1_v_done[t])
            g.dma_start(out=k.dr["sp_x1"][128 * t:128 * (t + 1), :],
                        in_=x1buf[:, par * DM:(par + 1) * DM]).then_inc(s_x1[par].inc(16), 16)

    k.epi_v = s_v.v
    k.epi_x1 = (s_x1[0].v, s_x1[1].v)
    k.s_x1_sems = s_x1

    dbx1 = k.dbg_tensor("x1", [TOK, DM])
    if dbx1 is not None:
        @block.gpsimd
        def _(g):
            g.wait_ge(s_x1[0].h, s_x1[0].v)
            g.wait_ge(s_x1[1].h, s_x1[1].v)
            g.dma_start(out=dbx1, in_=k.dr["sp_x1"][:, :]).then_inc(k.s_sp.inc(16), 16)


def emit_ffn(k):
    nc, block, pb = k.nc, k.block, k.pb
    s_mm, s_v, s_sc = k.s_mm, k.s_v, k.s_sc

    def agather(cin, cout, sem):
        k._agctr = getattr(k, "_agctr", 0) + 1
        csem = k.sem(f"s_agc{k._agctr}")
        k.block.materialize(k.realblock)

        @k.realblock.gpsimd
        def _(g):
            g.wait_ge(sem.h, sem.v)
            g.collective_compute(
                "AllGather", mybir.AluOpType.bypass,
                replica_groups=[[0, 1, 2, 3, 4, 5, 6, 7]],
                ins=[cin.opt()], outs=[cout.opt()],
            ).then_inc(csem.inc(1), 1)
        return csem

    def bcast_scalar(pairs, waitv):
        @block.tensor
        def _(t):
            t.wait_ge(s_v.h, waitv)
            for i, (srcc, _) in enumerate(pairs):
                mi = t.matmul(pb[6][:, 8 + i:9 + i], k.consts[0:1, 128:256],
                              k.sval[0:1, srcc:srcc + 1], start=True, stop=True)
            mi.then_inc(s_mm.inc(), 1)
        mmv = s_mm.v

        @block.vector
        def _(v):
            v.wait_ge(s_mm.h, mmv)
            for i, (_, dstc) in enumerate(pairs):
                vi = v.tensor_copy(out=k.sval[:, dstc:dstc + 1], in_=pb[6][:, 8 + i:9 + i])
            vi.then_inc(s_v.inc(), 1)
        return s_v.v

    k.scope2.close()
    scopeF = ExitStack()
    k.scopeF = scopeF
    wsh = scopeF.enter_context(nc.sbuf_tensor("wsh", [128, FI], F32))
    wi16 = scopeF.enter_context(nc.sbuf_tensor("wi16", [128, FI], I16))
    wdqb = scopeF.enter_context(nc.sbuf_tensor("wdqb", [128, FI], BF16))
    wsc = scopeF.enter_context(nc.sbuf_tensor("wsc", [128, 3 * 96], F32))
    wdqt = scopeF.enter_context(nc.sbuf_tensor("wdqt", [128, FI], BF16))
    s_lsh = k.sem("s_lsh")
    s_wq = k.sem("s_wq")
    block.materialize(k.realblock)
    dq = k.realblock

    for wt in range(7):
        if wt < 3:
            srcap = k.d["wg_shard"][128 * wt:128 * (wt + 1), :]
            ncols, nblk, dstrow = DM, 32, 128 * wt
        elif wt < 6:
            srcap = k.d["wu_shard"][128 * (wt - 3):128 * (wt - 2), :]
            ncols, nblk, dstrow = DM, 32, 384 + 128 * (wt - 3)
        else:
            srcap = k.d["wd_shard"]
            ncols, nblk, dstrow = FI, 96, 768

        vW = s_v.v

        @dq.sync
        def _(sync, wt=wt, srcap=srcap, ncols=ncols, vW=vW):
            if wt > 0:
                sync.wait_ge(s_v.h, vW)
            else:
                sync.wait_ge(s_v.h, k.epi_v)
                sync.wait_ge(k.s_x1_sems[0].h, k.epi_x1[0])
                sync.wait_ge(k.s_x1_sems[1].h, k.epi_x1[1])
            sync.dma_start(out=wsh[:, 0:ncols], in_=srcap).then_inc(s_lsh.inc(16), 16)

        ldW = s_lsh.v

        @dq.vector
        def _(v, wt=wt, ncols=ncols, nblk=nblk, ldW=ldW):
            v.wait_ge(s_lsh.h, ldW)
            if wt == 0:
                v.wait_ge(s_v.h, k.epi_v)
            v.tensor_reduce(out=wsc[:, 0:nblk], in_=wsh[:, 0:ncols].rearrange("p (a b) -> p a b", b=32),
                            axis=AX, op=OP.max, apply_absolute_value=True).then_inc(s_v.inc(), 1)
            v.wait_ge(s_v.h, s_v.v)
            v.tensor_scalar(out=wsc[:, 96:96 + nblk], in0=wsc[:, 0:nblk], scalar1=1.0 / 7.0,
                            scalar2=1e-12, op0=OP.mult, op1=OP.max).then_inc(s_v.inc(), 1)
            v.wait_ge(s_v.h, s_v.v)
            v.reciprocal(out=wsc[:, 192:192 + nblk], in_=wsc[:, 96:96 + nblk]).then_inc(s_v.inc(), 1)
            v.wait_ge(s_v.h, s_v.v)
            for b in range(nblk):
                vi = v.tensor_scalar(out=wi16[:, 32 * b:32 * (b + 1)], in0=wsh[:, 32 * b:32 * (b + 1)],
                                     scalar1=wsc[:, 192 + b:193 + b], scalar2=None, op0=OP.mult)
            vi.then_inc(s_v.inc(), 1)
            v.wait_ge(s_v.h, s_v.v)
            v.wait_ge(s_mm.h, s_mm.v)
            for b in range(nblk):
                vi = v.tensor_scalar(out=wdqb[:, 32 * b:32 * (b + 1)], in0=wi16[:, 32 * b:32 * (b + 1)],
                                     scalar1=wsc[:, 96 + b:97 + b], scalar2=None, op0=OP.mult)
            vi.then_inc(s_v.inc(), 1)

        wq_v = s_v.v

        nic = ncols // 128
        mm0 = s_mm.v
        v0 = s_v.v

        @dq.tensor
        def _(t, wq_v=wq_v, nic=nic, mm0=mm0, v0=v0):
            t.wait_ge(s_v.h, wq_v)
            for icc in range(nic):
                if icc >= 2:
                    t.wait_ge(s_v.h, v0 + icc - 1)
                t.transpose(pb[6 + icc % 2][:, 0:64].bitcast(BF16),
                            wdqb[:, 128 * icc:128 * (icc + 1)],
                            k.constsb[:, 256:384]).then_inc(s_mm.inc(), 1)

        @dq.vector
        def _(v, wt=wt, nic=nic, mm0=mm0):
            v.wait_ge(s_wq.h, 16 * wt)
            for icc in range(nic):
                v.wait_ge(s_mm.h, mm0 + icc + 1)
                v.tensor_copy(out=wdqt[:, 128 * icc:128 * (icc + 1)],
                              in_=pb[6 + icc % 2][:, 0:64].bitcast(BF16)).then_inc(s_v.inc(), 1)

        wqt_v = s_v.v

        @dq.gpsimd
        def _(g, wt=wt, nic=nic, wqt_v=wqt_v):
            g.wait_ge(s_v.h, wqt_v)
            flat1d = k.dr["cc_wg_in"].rearrange("r c -> (r c)")
            flat1dd = k.dr["cc_wd_in"].rearrange("r c -> (r c)")
            if wt < 3:
                dv = flat1d[0:393216].rearrange("(icc p o) -> p icc o", p=128, o=384)
                g.dma_start(out=dv[:, :, 128 * wt:128 * (wt + 1)][:, 0:8, :],
                            in_=wdqt[:, 0:1024].rearrange("p (icc o) -> p icc o", o=128),
                            allow_slow_non_contiguous=True).then_inc(s_wq.inc(16), 16)
            elif wt < 6:
                dv = flat1d[393216:786432].rearrange("(icc p o) -> p icc o", p=128, o=384)
                g.dma_start(out=dv[:, :, 128 * (wt - 3):128 * (wt - 2)][:, 0:8, :],
                            in_=wdqt[:, 0:1024].rearrange("p (icc o) -> p icc o", o=128),
                            allow_slow_non_contiguous=True).then_inc(s_wq.inc(16), 16)
            else:
                dv = flat1dd[0:393216].rearrange("(icc p o) -> p icc o", p=128, o=128)
                g.dma_start(out=dv[:, 0:24, :],
                            in_=wdqt[:, 0:3072].rearrange("p (icc o) -> p icc o", o=128),
                            allow_slow_non_contiguous=True).then_inc(s_wq.inc(16), 16)

        if wt == 5:
            k.s_wqc = agather(k.dr["cc_wg_in"], k.dr["cc_wg_out"], s_wq)

    k.wq_dma_v = s_wq.v
    k.s_wdc = agather(k.dr["cc_wd_in"], k.dr["cc_wd_out"], s_wq)
    k.s_wq = s_wq
    scopeF.close()

    s_ag1 = k.sem("s_ag1")
    vx = s_v.v

    @block.vector
    def _(v):
        v.wait_ge(s_v.h, vx)
        v.tensor_reduce(out=k.sval[:, 20:21], in_=k.stat[:, 160:176], axis=AX,
                        op=OP.max).then_inc(s_v.inc(), 1)

    vmx = s_v.v

    @block.tensor
    def _(t):
        t.wait_ge(s_v.h, vmx)
        t.transpose(pb[6][0:1, 0:128], k.sval[:, 20:21], k.consts[:, 0:128]).then_inc(s_mm.inc(), 1)

    mmx = s_mm.v

    @block.vector
    def _(v):
        v.wait_ge(s_mm.h, mmx)
        v.memset(k.sval[0:1, 22:23], 0.0)
        v.tensor_reduce(out=k.sval[0:1, 21:22], in_=pb[6][0:1, 0:128], axis=AX,
                        op=OP.max).then_inc(s_v.inc(), 1)

    vgm = s_v.v

    @block.gpsimd
    def _(g):
        g.wait_ge(s_v.h, vgm)
        g.dma_start(out=k.dr["cc_s0_in"], in_=k.sval[0:1, 21:23]).then_inc(s_ag1.inc(16), 16)

    s_ag1c = agather(k.dr["cc_s0_in"], k.dr["cc_s0_out"], s_ag1)

    def max_from_ag(cout, agsem, dstcol, n=8):
        k._rdctr = getattr(k, "_rdctr", 0) + 1
        s_rd = k.sem(f"s_rd{dstcol}_{k._rdctr}")

        @block.sync
        def _(sync):
            sync.wait_ge(agsem.h, agsem.v)
            sync.dma_start(out=k.sval[0:1, 24:24 + 2 * n], in_=cout).then_inc(s_rd.inc(16), 16)

        @block.vector
        def _(v):
            v.wait_ge(s_rd.h, s_rd.v)
            g2 = k.sval[0:1, 24:24 + 2 * n].rearrange("p (a b) -> p b a", b=2)
            v.tensor_reduce(out=k.sval[0:1, dstcol:dstcol + 1], in_=g2[:, 0:1, :], axis=AX, op=OP.max)
            v.tensor_reduce(out=k.sval[0:1, dstcol + 1:dstcol + 2], in_=g2[:, 1:2, :],
                            axis=AX, op=OP.max).then_inc(s_v.inc(), 1)
        return s_v.v

    max_from_ag(k.dr["cc_s0_out"], s_ag1c, 16)

    @block.vector
    def _(v):
        v.wait_ge(s_v.h, s_v.v)
        v.tensor_scalar(out=k.sval[0:1, 19:20], in0=k.sval[0:1, 16:17],
                        scalar1=1.0 / QMAX, scalar2=None, op0=OP.mult).then_inc(s_v.inc(), 1)
        v.wait_ge(s_v.h, s_v.v)
        v.reciprocal(out=k.sval[0:1, 20:21], in_=k.sval[0:1, 19:20]).then_inc(s_v.inc(), 1)

    bc_sx_v = bcast_scalar([(19, 8), (20, 9)], s_v.v)

    scope3 = ExitStack()
    k.scopeF = scope3
    sb3 = lambda *a: scope3.enter_context(nc.sbuf_tensor(*a))
    xqfm = sb3("xqfm", [128, 8 * TOK], BF16)
    wupT = sb3("wupT", [128, 8 * FI], BF16)
    wgT = sb3("wgT", [128, 8 * FI], BF16)
    x1t = sb3("x1t", [128, 2 * DM], F32)
    xqi = sb3("xqi", [128, DM], I16)
    xqb = sb3("xqb", [128, 2 * DM], BF16)
    evb = sb3("evb", [128, 2 * FI], F32)
    mxa = sb3("mxa", [128, 8], F32)
    s_lx1 = [k.sem("s_lx1a"), k.sem("s_lx1b")]
    s_xq = [k.sem("s_xqa"), k.sem("s_xqb")]
    s_xqt = k.sem("s_xqt")
    s_lwt = k.sem("s_lwt")
    s_spu = [k.sem("s_spua"), k.sem("s_spub")]
    s_spg = [k.sem("s_spga"), k.sem("s_spgb")]

    @block.scalar
    def _(sc):
        sc.wait_ge(k.s_wqc.h, k.s_wqc.v)
        ccf = k.dr["cc_wg_out"].rearrange("(r z) c -> r (z c)", z=768)
        for ic in range(8):
            gsrc = ccf[:, 0:393216].rearrange("r (i o) -> i r o", o=384)
            usrc = ccf[:, 393216:786432].rearrange("r (i o) -> i r o", o=384)
            sc.dma_start(out=wgT[:, ic * FI:(ic + 1) * FI].rearrange("p (r o) -> p r o", o=384),
                         in_=gsrc[128 * ic:128 * (ic + 1), :, :]).then_inc(s_lwt.inc(16), 16)
            sc.dma_start(out=wupT[:, ic * FI:(ic + 1) * FI].rearrange("p (r o) -> p r o", o=384),
                         in_=usrc[128 * ic:128 * (ic + 1), :, :]).then_inc(s_lwt.inc(16), 16)

    xq_v = {}
    for t in range(NT):
        par = t % 2

        @block.sync
        def _(sync, t=t, par=par):
            if t >= 2:
                sync.wait_ge(s_v.h, xq_v[t - 2])
            elif t == 0:
                sync.wait_ge(k.s_wq.h, k.wq_dma_v)
            sync.dma_start(out=x1t[:, par * DM:(par + 1) * DM],
                           in_=k.dr["sp_x1"][128 * t:128 * (t + 1), :]).then_inc(s_lx1[par].inc(16), 16)

        vt0 = s_v.v

        @block.vector
        def _(v, t=t, par=par, vt0=vt0):
            v.wait_ge(s_lx1[par].h, s_lx1[par].v)
            v.wait_ge(s_v.h, vt0)
            if t == 0:
                v.wait_ge(s_v.h, bc_sx_v)
            if t >= 2:
                v.wait_ge(s_xq[par].h, 16 * (t // 2))
            v.tensor_scalar(out=xqi[:, :], in0=x1t[:, par * DM:(par + 1) * DM],
                            scalar1=k.sval[:, 9:10], scalar2=None, op0=OP.mult).then_inc(s_v.inc(), 1)
            v.wait_ge(s_v.h, s_v.v)
            v.tensor_copy(out=xqb[:, par * DM:(par + 1) * DM], in_=xqi[:, :]).then_inc(s_v.inc(), 1)
        xq_v[t] = s_v.v

        @block.gpsimd
        def _(g, t=t, par=par):
            g.wait_ge(s_v.h, xq_v[t])
            g.dma_start(out=k.dr["sp_xq"][128 * t:128 * (t + 1), :],
                        in_=xqb[:, par * DM:(par + 1) * DM]).then_inc(s_xq[par].inc(16), 16)

    @block.scalar
    def _(sc):
        sc.wait_ge(s_xq[0].h, s_xq[0].v)
        sc.wait_ge(s_xq[1].h, s_xq[1].v)
        for ic in range(8):
            sc.dma_start(out=xqfm[:, ic * TOK:(ic + 1) * TOK],
                         in_=k.dr["sp_xq"][0:TOK, 128 * ic:128 * (ic + 1)],
                         transpose=True).then_inc(s_xqt.inc(16), 16)

    vinit = s_v.v

    @block.vector
    def _(v):
        v.wait_ge(s_v.h, vinit)
        v.memset(mxa[:, 0:4], 0.0).then_inc(s_v.inc(), 1)

    k.block.set_direct("tensor", "vector", "gpsimd", "sync", "scalar")
    vred = {}
    evc = {}
    runpar = [0, 0]
    nj = 0
    for t in range(NT):
        for w in range(2):
            par = t % 2
            wT = wupT if w == 0 else wgT
            spd = k.dr["sp_up"] if w == 0 else k.dr["sp_gate"]
            spsem = s_spu if w == 0 else s_spg
            for nc6 in range(6):
                bank = pb[nj % 2]

                @block.tensor
                def _(tt_, t=t, w=w, nc6=nc6, nj=nj, wT=wT, bank=bank):
                    if nj == 0:
                        tt_.wait_ge(s_xqt.h, s_xqt.v)
                        tt_.wait_ge(s_lwt.h, s_lwt.v)
                    if nj >= 2:
                        tt_.wait_ge(s_v.h, vred[nj - 2])
                        tt_.wait_ge(s_sc.h, evc[nj - 2])
                    for kc in range(8):
                        mi = tt_.matmul(bank[:, 0:512],
                                        xqfm[:, kc * TOK + 128 * t:kc * TOK + 128 * (t + 1)],
                                        wT[:, kc * FI + 512 * nc6:kc * FI + 512 * (nc6 + 1)],
                                        start=(kc == 0), stop=(kc == 7))
                    mi.then_inc(s_mm.inc(), 1)

                mmj = s_mm.v
                rsrc, rdst = runpar[w], 1 - runpar[w]
                runpar[w] = rdst

                vj0 = s_v.v

                @block.vector
                def _(v, w=w, bank=bank, mmj=mmj, rsrc=rsrc, rdst=rdst, vj0=vj0):
                    v.wait_ge(s_mm.h, mmj)
                    v.wait_ge(s_v.h, vj0)
                    v.tensor_reduce(out=mxa[:, 4:5], in_=bank[:, 0:512], axis=AX,
                                    op=OP.max, apply_absolute_value=True).then_inc(s_v.inc(), 1)
                    v.wait_ge(s_v.h, s_v.v)
                    v.tensor_tensor(out=mxa[:, 2 * rdst + w:2 * rdst + w + 1],
                                    in0=mxa[:, 2 * rsrc + w:2 * rsrc + w + 1],
                                    in1=mxa[:, 4:5], op=OP.max).then_inc(s_v.inc(), 1)
                vred[nj] = s_v.v

                jpar = (2 * t + w) % 2

                @block.scalar
                def _(sc, t=t, w=w, jpar=jpar, nc6=nc6, bank=bank, mmj=mmj):
                    sc.wait_ge(s_mm.h, mmj)
                    if nc6 == 0 and t >= 1:
                        sc.wait_ge(spsem[(t - 1) % 2].h, 16 * ((t - 1) // 2 + 1))
                    sc.copy(out=evb[:, jpar * FI + 512 * nc6:jpar * FI + 512 * (nc6 + 1)],
                            in_=bank[:, 0:512]).then_inc(s_sc.inc(), 1)
                evc[nj] = s_sc.v
                nj += 1

            jpar = (2 * t + w) % 2

            @block.gpsimd
            def _(g, t=t, jpar=jpar, spd=spd, spsem=spsem, scj=evc[nj - 1]):
                g.wait_ge(s_sc.h, scj)
                g.dma_start(out=spd[128 * t:128 * (t + 1), :],
                            in_=evb[:, jpar * FI:(jpar + 1) * FI]).then_inc(spsem[t % 2].inc(16), 16)


    s_ag2 = k.sem("s_ag2")
    vfin = s_v.v

    @block.vector
    def _(v):
        v.wait_ge(s_v.h, vfin)
        v.tensor_copy(out=k.sval[0:1, 24:25], in_=mxa[0:1, 0:1])

    @block.tensor
    def _(t):
        t.wait_ge(s_v.h, vfin)
        t.transpose(pb[6][0:2, 0:128], mxa[:, 0:2], k.consts[:, 0:128]).then_inc(s_mm.inc(), 1)

    mfin = s_mm.v

    @block.vector
    def _(v):
        v.wait_ge(s_mm.h, mfin)
        v.tensor_reduce(out=k.sval[0:2, 40:41], in_=pb[6][0:2, 0:128], axis=AX,
                        op=OP.max).then_inc(s_v.inc(), 1)

    vag2a = s_v.v

    @block.tensor
    def _(t):
        t.wait_ge(s_v.h, vag2a)
        t.transpose(pb[7][0:1, 0:2], k.sval[0:2, 40:41], k.consts[0:2, 0:2]).then_inc(s_mm.inc(), 1)

    mag2b = s_mm.v

    @block.vector
    def _(v):
        v.wait_ge(s_mm.h, mag2b)
        v.tensor_copy(out=k.sval[0:1, 21:23], in_=pb[7][0:1, 0:2]).then_inc(s_v.inc(), 1)

    vag2 = s_v.v

    @block.gpsimd
    def _(g):
        g.wait_ge(s_v.h, vag2)
        g.dma_start(out=k.dr["cc_s1_in"], in_=k.sval[0:1, 21:23]).then_inc(s_ag2.inc(16), 16)

    s_ag2c = agather(k.dr["cc_s1_in"], k.dr["cc_s1_out"], s_ag2)
    max_from_ag(k.dr["cc_s1_out"], s_ag2c, 16)

    @block.vector
    def _(v):
        v.wait_ge(s_v.h, s_v.v)
        v.tensor_tensor(out=k.sval[0:1, 19:20],
                        in0=k.sval[0:1, 16:17], in1=k.sval[0:1, 8:9], op=OP.mult)
        v.tensor_tensor(out=k.sval[0:1, 20:21], in0=k.sval[0:1, 17:18], in1=k.sval[0:1, 8:9],
                        op=OP.mult).then_inc(s_v.inc(), 1)
        v.wait_ge(s_v.h, s_v.v)
        v.tensor_scalar(out=k.sval[0:1, 21:22], in0=k.sval[0:1, 19:20],
                        scalar1=1.0 / QMAX, scalar2=None, op0=OP.mult)
        v.tensor_scalar(out=k.sval[0:1, 22:23], in0=k.sval[0:1, 20:21],
                        scalar1=1.0 / QMAX, scalar2=None, op0=OP.mult).then_inc(s_v.inc(), 1)
        v.wait_ge(s_v.h, s_v.v)
        v.reciprocal(out=k.sval[0:1, 30:31], in_=k.sval[0:1, 21:22])
        v.reciprocal(out=k.sval[0:1, 31:32], in_=k.sval[0:1, 22:23]).then_inc(s_v.inc(), 1)
        v.wait_ge(s_v.h, s_v.v)
        v.tensor_tensor(out=k.sval[0:1, 23:24], in0=k.sval[0:1, 8:9], in1=k.sval[0:1, 30:31],
                        op=OP.mult)
        v.tensor_tensor(out=k.sval[0:1, 24:25], in0=k.sval[0:1, 8:9], in1=k.sval[0:1, 31:32],
                        op=OP.mult).then_inc(s_v.inc(), 1)

    bc2 = bcast_scalar([(21, 10), (22, 11), (23, 12), (24, 13)], s_v.v)

    scope3.close()
    k.scope_zs.close()
    scope4 = ExitStack()
    k.scopeF = scope4
    sb4 = lambda *a: scope4.enter_context(nc.sbuf_tensor(*a))
    gint = sb4("gint", [128, NT * FI], I16)
    gtile = sb4("gtile", [128, 2 * FI], BF16)
    sgs = sb4("sgs", [128, 2 * FI], F32)
    sgu = sb4("sgu", [128, FI], U16)
    pts = sb4("pts", [128, FI], F32)
    upt = sb4("upt", [128, 2 * FI], BF16)
    upi = sb4("upi", [128, 2 * FI], I16)
    ots = sb4("ots", [128, 2 * FI], F32)
    mxb = sb4("mxb", [128, 8], F32)
    s_lg = [k.sem("s_lga"), k.sem("s_lgb")]
    s_lu = [k.sem("s_lua"), k.sem("s_lub")]
    s_ot = [k.sem("s_ota"), k.sem("s_otb")]

    vinitB = s_v.v

    @block.vector
    def _(v):
        v.wait_ge(s_v.h, vinitB)
        v.wait_ge(s_spu[0].h, s_spu[0].v)
        v.wait_ge(s_spu[1].h, s_spu[1].v)
        v.wait_ge(s_spg[0].h, s_spg[0].v)
        v.wait_ge(s_spg[1].h, s_spg[1].v)
        v.wait_ge(s_xq[0].h, s_xq[0].v)
        v.wait_ge(s_xq[1].h, s_xq[1].v)
        v.memset(mxb[:, 0:4], 0.0).then_inc(s_v.inc(), 1)
    vinitB2 = s_v.v

    runp = [0, 0]
    btile_v = {}
    ptsv = {}
    for t in range(NT):
        par = t % 2

        @block.sync
        def _(sync, t=t, par=par):
            if t >= 2:
                sync.wait_ge(s_v.h, btile_v[t - 2])
            else:
                sync.wait_ge(s_v.h, vinitB2)
            sync.dma_start(out=gtile[:, par * FI:(par + 1) * FI],
                           in_=k.dr["sp_gate"][128 * t:128 * (t + 1), :]).then_inc(s_lg[par].inc(16), 16)

        ldg = s_lg[par].v
        vprevB = btile_v[t - 2] if t >= 2 else vinitB2
        ptsvB = ptsv[t - 1] if t >= 1 else vinitB2

        @block.scalar
        def _(sc, t=t, par=par, ldg=ldg, vprevB=vprevB, ptsvB=ptsvB):
            sc.wait_ge(s_lg[par].h, ldg)
            sc.wait_ge(s_sc.h, s_sc.v)
            sc.wait_ge(s_v.h, vprevB)
            sc.activation(out=gint[:, t * FI:(t + 1) * FI], in_=gtile[:, par * FI:(par + 1) * FI],
                          func=AF.Copy, scale=k.sval[:, 13:14]).then_inc(s_sc.inc(), 1)
            sc.wait_ge(s_sc.h, s_sc.v)
            sc.activation(out=sgs[:, par * FI:(par + 1) * FI], in_=gint[:, t * FI:(t + 1) * FI],
                          func=AF.Sigmoid, scale=k.sval[:, 11:12]).then_inc(s_sc.inc(), 1)
            sc.wait_ge(s_sc.h, s_sc.v)
            sc.wait_ge(s_v.h, ptsvB)
            sc.activation(out=sgu[:, 0:FI], in_=sgs[:, par * FI:(par + 1) * FI],
                          func=AF.Copy, scale=65536.0).then_inc(s_sc.inc(), 1)

        scb = s_sc.v
        rs, rd = runp[0], 1 - runp[0]
        runp[0] = rd

        vb0 = s_v.v

        @block.vector
        def _(v, t=t, par=par, scb=scb, rs=rs, rd=rd, vb0=vb0):
            v.wait_ge(s_sc.h, scb)
            v.wait_ge(s_v.h, vb0)
            v.tensor_tensor(out=pts[:, :], in0=gint[:, t * FI:(t + 1) * FI],
                            in1=sgu[:, 0:FI],
                            op=OP.mult).then_inc(s_v.inc(), 1)
            ptsv[t] = s_v.v
            v.wait_ge(s_v.h, s_v.v)
            v.tensor_reduce(out=mxb[:, 4:5], in_=pts[:, :], axis=AX, op=OP.max,
                            apply_absolute_value=True).then_inc(s_v.inc(), 1)
            v.wait_ge(s_v.h, s_v.v)
            v.tensor_tensor(out=mxb[:, 2 * rd:2 * rd + 1], in0=mxb[:, 2 * rs:2 * rs + 1],
                            in1=mxb[:, 4:5], op=OP.max).then_inc(s_v.inc(), 1)
        btile_v[t] = s_v.v

    s_ag3 = k.sem("s_ag3")
    vfb = s_v.v

    @block.tensor
    def _(t):
        t.wait_ge(s_v.h, vfb)
        t.transpose(pb[6][0:1, 0:128], mxb[:, 0:1], k.consts[:, 0:128]).then_inc(s_mm.inc(), 1)

    mfb = s_mm.v

    @block.vector
    def _(v):
        v.wait_ge(s_mm.h, mfb)
        v.memset(k.sval[0:1, 22:23], 0.0)
        v.tensor_reduce(out=k.sval[0:1, 21:22], in_=pb[6][0:1, 0:128], axis=AX,
                        op=OP.max).then_inc(s_v.inc(), 1)

    vag3 = s_v.v

    @block.gpsimd
    def _(g):
        g.wait_ge(s_v.h, vag3)
        g.dma_start(out=k.dr["cc_s2_in"], in_=k.sval[0:1, 21:23]).then_inc(s_ag3.inc(16), 16)

    s_ag3c = agather(k.dr["cc_s2_in"], k.dr["cc_s2_out"], s_ag3)
    max_from_ag(k.dr["cc_s2_out"], s_ag3c, 16)

    @block.vector
    def _(v):
        v.wait_ge(s_v.h, s_v.v)
        v.tensor_scalar(out=k.sval[0:1, 26:27], in0=k.sval[0:1, 11:12],
                        scalar1=1.0 / 65536.0, scalar2=None, op0=OP.mult).then_inc(s_v.inc(), 1)
        v.wait_ge(s_v.h, s_v.v)
        v.tensor_tensor(out=k.sval[0:1, 27:28], in0=k.sval[0:1, 26:27], in1=k.sval[0:1, 16:17],
                        op=OP.mult).then_inc(s_v.inc(), 1)
        v.wait_ge(s_v.h, s_v.v)
        v.tensor_scalar(out=k.sval[0:1, 28:29], in0=k.sval[0:1, 27:28],
                        scalar1=1.0 / QMAX, scalar2=None, op0=OP.mult).then_inc(s_v.inc(), 1)
        v.wait_ge(s_v.h, s_v.v)
        v.reciprocal(out=k.sval[0:1, 32:33], in_=k.sval[0:1, 28:29]).then_inc(s_v.inc(), 1)
        v.wait_ge(s_v.h, s_v.v)
        v.tensor_tensor(out=k.sval[0:1, 29:30], in0=k.sval[0:1, 26:27], in1=k.sval[0:1, 32:33],
                        op=OP.mult).then_inc(s_v.inc(), 1)
    bc3 = bcast_scalar([(28, 14), (29, 15)], s_v.v)

    dtile_v = {}
    ptsv = {}
    for t in range(NT):
        par = t % 2

        @block.sync
        def _(sync, t=t, par=par):
            if t >= 2:
                sync.wait_ge(s_v.h, dtile_v[t - 2])
            sync.dma_start(out=upt[:, par * FI:(par + 1) * FI],
                           in_=k.dr["sp_up"][128 * t:128 * (t + 1), :]).then_inc(s_lu[par].inc(16), 16)

        ldu = s_lu[par].v
        vprevD = dtile_v[t - 2] if t >= 2 else None
        ptsvD = ptsv[t - 1] if t >= 1 else bc3

        @block.scalar
        def _(sc, t=t, par=par, ldu=ldu, vprevD=vprevD, ptsvD=ptsvD):
            sc.wait_ge(s_sc.h, s_sc.v)
            if vprevD is not None:
                sc.wait_ge(s_v.h, vprevD)
            else:
                sc.wait_ge(s_v.h, bc3)
            sc.activation(out=sgs[:, par * FI:(par + 1) * FI], in_=gint[:, t * FI:(t + 1) * FI],
                          func=AF.Sigmoid, scale=k.sval[:, 11:12]).then_inc(s_sc.inc(), 1)
            sc.wait_ge(s_sc.h, s_sc.v)
            sc.wait_ge(s_v.h, ptsvD)
            sc.activation(out=sgu[:, 0:FI], in_=sgs[:, par * FI:(par + 1) * FI],
                          func=AF.Copy, scale=65536.0).then_inc(s_sc.inc(), 1)
            sc.wait_ge(s_lu[par].h, ldu)
            sc.activation(out=upi[:, par * FI:(par + 1) * FI], in_=upt[:, par * FI:(par + 1) * FI],
                          func=AF.Copy, scale=k.sval[:, 12:13]).then_inc(s_sc.inc(), 1)

        scd = s_sc.v
        rs, rd = runp[1], 1 - runp[1]
        runp[1] = rd

        vd0 = s_v.v

        @block.vector
        def _(v, t=t, par=par, scd=scd, rs=rs, rd=rd, vd0=vd0):
            v.wait_ge(s_sc.h, scd)
            v.wait_ge(s_v.h, vd0)
            v.tensor_tensor(out=pts[:, :], in0=gint[:, t * FI:(t + 1) * FI],
                            in1=sgu[:, 0:FI],
                            op=OP.mult).then_inc(s_v.inc(), 1)
            ptsv[t] = s_v.v
            v.wait_ge(s_v.h, s_v.v)
            v.tensor_scalar(out=gint[:, t * FI:(t + 1) * FI], in0=pts[:, :],
                            scalar1=k.sval[:, 15:16], scalar2=None, op0=OP.mult).then_inc(s_v.inc(), 1)
            v.wait_ge(s_v.h, s_v.v)
            if t >= 2:
                v.wait_ge(s_ot[par].h, 16 * (t // 2))
            v.tensor_tensor(out=ots[:, par * FI:(par + 1) * FI],
                            in0=gint[:, t * FI:(t + 1) * FI],
                            in1=upi[:, par * FI:(par + 1) * FI],
                            op=OP.mult).then_inc(s_v.inc(), 1)
            v.wait_ge(s_v.h, s_v.v)
            v.tensor_reduce(out=mxb[:, 4:5], in_=ots[:, par * FI:(par + 1) * FI], axis=AX,
                            op=OP.max, apply_absolute_value=True).then_inc(s_v.inc(), 1)
            v.wait_ge(s_v.h, s_v.v)
            v.tensor_tensor(out=mxb[:, 2 * rd + 1:2 * rd + 2], in0=mxb[:, 2 * rs + 1:2 * rs + 2],
                            in1=mxb[:, 4:5], op=OP.max).then_inc(s_v.inc(), 1)
        dtile_v[t] = s_v.v

        @block.gpsimd
        def _(g, t=t, par=par):
            g.wait_ge(s_v.h, dtile_v[t])
            g.dma_start(out=k.dr["sp_oq"][128 * t:128 * (t + 1), :],
                        in_=ots[:, par * FI:(par + 1) * FI]).then_inc(s_ot[par].inc(16), 16)

    s_ag4 = k.sem("s_ag4")
    vfd = s_v.v

    @block.tensor
    def _(t):
        t.wait_ge(s_v.h, vfd)
        t.transpose(pb[6][0:1, 0:128], mxb[:, 1:2], k.consts[:, 0:128]).then_inc(s_mm.inc(), 1)

    mfd = s_mm.v

    @block.vector
    def _(v):
        v.wait_ge(s_mm.h, mfd)
        v.memset(k.sval[0:1, 22:23], 0.0)
        v.tensor_reduce(out=k.sval[0:1, 21:22], in_=pb[6][0:1, 0:128], axis=AX,
                        op=OP.max).then_inc(s_v.inc(), 1)

    vag4 = s_v.v

    @block.gpsimd
    def _(g):
        g.wait_ge(s_v.h, vag4)
        g.dma_start(out=k.dr["cc_s3_in"], in_=k.sval[0:1, 21:23]).then_inc(s_ag4.inc(16), 16)

    s_ag4c = agather(k.dr["cc_s3_in"], k.dr["cc_s3_out"], s_ag4)
    max_from_ag(k.dr["cc_s3_out"], s_ag4c, 16)

    @block.vector
    def _(v):
        v.wait_ge(s_v.h, s_v.v)
        v.tensor_tensor(out=k.sval[0:1, 26:27], in0=k.sval[0:1, 14:15], in1=k.sval[0:1, 10:11],
                        op=OP.mult).then_inc(s_v.inc(), 1)
        v.wait_ge(s_v.h, s_v.v)
        v.tensor_tensor(out=k.sval[0:1, 27:28], in0=k.sval[0:1, 26:27], in1=k.sval[0:1, 16:17],
                        op=OP.mult).then_inc(s_v.inc(), 1)
        v.wait_ge(s_v.h, s_v.v)
        v.tensor_scalar(out=k.sval[0:1, 28:29], in0=k.sval[0:1, 27:28],
                        scalar1=1.0 / QMAX, scalar2=None, op0=OP.mult).then_inc(s_v.inc(), 1)
        v.wait_ge(s_v.h, s_v.v)
        v.reciprocal(out=k.sval[0:1, 32:33], in_=k.sval[0:1, 28:29]).then_inc(s_v.inc(), 1)
        v.wait_ge(s_v.h, s_v.v)
        v.tensor_tensor(out=k.sval[0:1, 29:30], in0=k.sval[0:1, 26:27], in1=k.sval[0:1, 32:33],
                        op=OP.mult).then_inc(s_v.inc(), 1)
    bc4 = bcast_scalar([(28, 16), (29, 17)], s_v.v)

    scope4.close()
    scope5 = ExitStack()
    k.scopeF = scope5
    sb5 = lambda *a: scope5.enter_context(nc.sbuf_tensor(*a))
    ofm = sb5("ofm", [128, 24 * TOK], BF16)
    wdT = sb5("wdT", [128, 24 * DM], BF16)
    x1e = sb5("x1e", [128, 2 * DM], F32)
    pre2 = sb5("pre2", [128, DM], F32)
    sq2 = sb5("sq2", [128, DM], F32)
    outb = sb5("outb", [128, 2 * DM], F32)
    ln2s = sb5("ln2s", [128, 16], F32)
    oraw = sb5("oraw", [128, 2 * TOK], BF16)
    oq16 = sb5("oq16", [128, 2 * TOK], I16)
    s_oft = [k.sem("s_ofta"), k.sem("s_oftb")]
    s_lwd = k.sem("s_lwd")
    s_lx2 = [k.sem("s_lx2a"), k.sem("s_lx2b")]

    @block.sync
    def _(sync):
        sync.wait_ge(k.s_wdc.h, k.s_wdc.v)
        ccf = k.dr["cc_wd_out"].rearrange("(r z) c -> r (z c)", z=384)
        dsrc = ccf[:, 0:393216].rearrange("r (i o) -> i r o", o=128)
        for ic in range(24):
            sync.dma_start(out=wdT[:, ic * DM:(ic + 1) * DM].rearrange("p (r o) -> p r o", o=128),
                           in_=dsrc[128 * ic:128 * (ic + 1), :, :]).then_inc(s_lwd.inc(16), 16)

    ofmcp = {}
    oqsc = {}
    for ic in range(25):
        par = ic % 2

        @block.scalar
        def _(sc, ic=ic, par=par):
            if ic < 24:
                if ic == 0:
                    sc.wait_ge(s_ot[0].h, s_ot[0].v)
                    sc.wait_ge(s_ot[1].h, s_ot[1].v)
                if ic >= 2:
                    sc.wait_ge(s_v.h, ofmcp[ic - 2])
                sc.dma_start(out=oraw[:, par * TOK:(par + 1) * TOK],
                             in_=k.dr["sp_oq"][0:TOK, 128 * ic:128 * (ic + 1)],
                             transpose=True).then_inc(s_oft[par].inc(16), 16)
            if ic >= 1:
                qp = (ic - 1) % 2
                sc.wait_ge(s_oft[qp].h, 16 * ((ic - 1) // 2 + 1))
                if ic == 1:
                    sc.wait_ge(s_v.h, bc4)
                sc.wait_ge(s_sc.h, s_sc.v)
                sc.activation(out=oq16[:, qp * TOK:(qp + 1) * TOK],
                              in_=oraw[:, qp * TOK:(qp + 1) * TOK],
                              func=AF.Copy, scale=k.sval[:, 17:18]).then_inc(s_sc.inc(), 1)
                oqsc[ic - 1] = s_sc.v

        if ic >= 1:
            @block.vector
            def _(v, ic=ic):
                qp = (ic - 1) % 2
                v.wait_ge(s_sc.h, oqsc[ic - 1])
                v.tensor_copy(out=ofm[:, (ic - 1) * TOK:ic * TOK],
                              in_=oq16[:, qp * TOK:(qp + 1) * TOK]).then_inc(s_v.inc(), 1)
            ofmcp[ic - 1] = s_v.v
    ofm_done = ofmcp[23]

    e2_v = {}
    for t in range(NT):
        par = t % 2

        @block.sync
        def _(sync, t=t, par=par):
            if t >= 2:
                sync.wait_ge(s_v.h, e2_v[t - 2])
            sync.dma_start(out=x1e[:, par * DM:(par + 1) * DM],
                           in_=k.dr["sp_x1"][128 * t:128 * (t + 1), :]).then_inc(s_lx2[par].inc(16), 16)

        vprev2 = s_v.v

        @block.tensor
        def _(tt_, t=t, par=par, vprev2=vprev2):
            if t == 0:
                tt_.wait_ge(s_v.h, ofm_done)
                tt_.wait_ge(s_lwd.h, s_lwd.v)
            tt_.wait_ge(s_v.h, vprev2)
            for nc2 in range(2):
                for kc in range(24):
                    mi = tt_.matmul(pb[2 * par + nc2][:, 0:512],
                                    ofm[:, kc * TOK + 128 * t:kc * TOK + 128 * (t + 1)],
                                    wdT[:, kc * DM + 512 * nc2:kc * DM + 512 * (nc2 + 1)],
                                    start=(kc == 0), stop=(kc == 23))
            mi.then_inc(s_mm.inc(), 1)

        md = s_mm.v

        @block.vector
        def _(v, t=t, par=par, md=md):
            v.wait_ge(s_mm.h, md)
            v.wait_ge(s_lx2[par].h, s_lx2[par].v)
            v.scalar_tensor_tensor(out=pre2[:, 0:512], in0=pb[2 * par][:, 0:512],
                                   scalar=k.sval[:, 16:17], in1=x1e[:, par * DM:par * DM + 512],
                                   op0=OP.mult, op1=OP.add)
            v.scalar_tensor_tensor(out=pre2[:, 512:1024], in0=pb[2 * par + 1][:, 0:512],
                                   scalar=k.sval[:, 16:17], in1=x1e[:, par * DM + 512:par * DM + 1024],
                                   op0=OP.mult, op1=OP.add).then_inc(s_v.inc(), 1)
            v.wait_ge(s_v.h, s_v.v)
            v.tensor_reduce(out=ln2s[:, 0:1], in_=pre2[:, :], axis=AX, op=OP.add).then_inc(s_v.inc(), 1)

        vpre2 = s_v.v

        @block.scalar
        def _(sc, t=t, vpre2=vpre2):
            sc.wait_ge(s_v.h, vpre2)
            sc.wait_ge(s_sc.h, s_sc.v)
            sc.activation(out=sq2[:, :], in_=pre2[:, :], func=AF.Square,
                          accum_out=ln2s[:, 1:2]).then_inc(s_sc.inc(), 1)

        sq2d = s_sc.v

        @block.vector
        def _(v, t=t, sq2d=sq2d):
            v.wait_ge(s_sc.h, sq2d)
            v.tensor_scalar(out=ln2s[:, 2:3], in0=ln2s[:, 0:1], scalar1=1.0 / DM,
                            scalar2=None, op0=OP.mult)
            v.tensor_scalar(out=ln2s[:, 3:4], in0=ln2s[:, 1:2], scalar1=1.0 / DM,
                            scalar2=None, op0=OP.mult).then_inc(s_v.inc(), 1)
            v.wait_ge(s_v.h, s_v.v)
            v.tensor_tensor(out=ln2s[:, 4:5], in0=ln2s[:, 2:3], in1=ln2s[:, 2:3],
                            op=OP.mult).then_inc(s_v.inc(), 1)
            v.wait_ge(s_v.h, s_v.v)
            v.tensor_tensor(out=ln2s[:, 5:6], in0=ln2s[:, 3:4], in1=ln2s[:, 4:5],
                            op=OP.subtract).then_inc(s_v.inc(), 1)

        vvar2 = s_v.v

        @block.scalar
        def _(sc, t=t, vvar2=vvar2):
            sc.wait_ge(s_v.h, vvar2)
            sc.activation(out=ln2s[:, 6:7], in_=ln2s[:, 5:6], func=AF.Ln,
                          bias=k.fconst[:, 1:2]).then_inc(s_sc.inc(), 1)
            sc.wait_ge(s_sc.h, s_sc.v)
            sc.activation(out=ln2s[:, 7:8], in_=ln2s[:, 6:7], func=AF.Exp,
                          scale=-0.5).then_inc(s_sc.inc(), 1)

        srs2 = s_sc.v

        @block.vector
        def _(v, t=t, par=par, srs2=srs2):
            v.wait_ge(s_sc.h, srs2)
            v.tensor_tensor(out=ln2s[:, 8:9], in0=ln2s[:, 2:3], in1=ln2s[:, 7:8],
                            op=OP.mult).then_inc(s_v.inc(), 1)
            v.wait_ge(s_v.h, s_v.v)
            v.tensor_scalar(out=ln2s[:, 9:10], in0=ln2s[:, 8:9], scalar1=-1.0,
                            scalar2=None, op0=OP.mult).then_inc(s_v.inc(), 1)
            v.wait_ge(s_v.h, s_v.v)
            if t >= 2:
                v.wait_ge(k.s_st2[par].h, 16 * (t // 2))
            v.tensor_scalar(out=outb[:, par * DM:(par + 1) * DM], in0=pre2[:, :],
                            scalar1=ln2s[:, 7:8], scalar2=ln2s[:, 9:10],
                            op0=OP.mult, op1=OP.add).then_inc(s_v.inc(), 1)
        e2_v[t] = s_v.v

        @block.sync
        def _(sync, t=t, par=par):
            sync.wait_ge(s_v.h, e2_v[t])
            sync.dma_start(out=k.d["out"][128 * t:128 * (t + 1), :],
                           in_=outb[:, par * DM:(par + 1) * DM]).then_inc(k.s_st2[par].inc(16), 16)

    k.scope5 = scope5


def emit_dbg_ffn(k):
    block = k.block
    dsv = k.dbg_tensor("sval", [128, 48])
    dgr = k.dbg_tensor("gateraw", [TOK, FI])
    doi = k.dbg_tensor("oint", [TOK, FI])
    dwq = k.dbg_tensor("wdq", [8 * 1152, DM])
    if dsv is not None:
        @block.gpsimd
        def _(g):
            g.wait_ge(k.s_st2[0].h, k.s_st2[0].v)
            g.wait_ge(k.s_st2[1].h, k.s_st2[1].v)
            g.dma_start(out=dsv, in_=k.sval[:, :]).then_inc(k.s_sp.inc(16), 16)
            if dgr is not None:
                g.dma_start(out=dgr, in_=k.dr["sp_gate"][:, :]).then_inc(k.s_sp.inc(16), 16)
            if doi is not None:
                g.dma_start(out=doi, in_=k.dr["sp_up"][:, :]).then_inc(k.s_sp.inc(16), 16)
            if dwq is not None:
                g.dma_start(out=dwq, in_=k.dr["cc_wq_out"]).then_inc(k.s_sp.inc(16), 16)
            g.wait_ge(k.s_sp.h, k.s_sp.v)


def emit_finish(k):
    block = k.block

    @block.sync
    def _(sync):
        sync.wait_ge(k.s_st2[0].h, k.s_st2[0].v)
        sync.wait_ge(k.s_st2[1].h, k.s_st2[1].v)

    @block.gpsimd
    def _(g):
        g.wait_ge(k.s_st2[0].h, k.s_st2[0].v)
        g.wait_ge(k.s_st2[1].h, k.s_st2[1].v)

    k.scope5.close()



def emit_finish_dbg(k):
    block = k.block
    for scope in ("scopeF", "scope2", "scopeD", "scopeA", "scope_xs", "scope_zs"):
        s = getattr(k, scope, None)
        if s is not None:
            try:
                s.close()
            except Exception:
                pass

    @block.gpsimd
    def _(g):
        if k.s_sp.v:
            g.wait_ge(k.s_sp.h, k.s_sp.v)
        for s in getattr(k, "s_so", []) or []:
            if s.v:
                g.wait_ge(s.h, s.v)


def shard_inputs(inputs, core):
    b, half = core // 2, core % 2
    x = np.asarray(inputs["L0_attn_x"])
    t0 = half * TOK
    xp = np.zeros((TPAD, DM), np.float32)
    lo, hi = t0 - 1, t0 + TOK + 2
    slo, shi = max(lo, 0), min(hi, LSEQ)
    xp[slo - lo:shi - lo, :] = x[b, slo:shi, :]
    convw = np.concatenate([np.asarray(inputs["L0_attn_conv_x_w"])[:, 0, :],
                            np.asarray(inputs["L0_attn_conv_z_w"])[:, 0, :]], axis=0)
    convb = np.concatenate([np.asarray(inputs["L0_attn_conv_x_b"]),
                            np.asarray(inputs["L0_attn_conv_z_b"])], axis=0)[:, None]
    wgkb = np.concatenate([np.asarray(inputs["L0_attn_W_gk"]),
                           np.asarray(inputs["L0_attn_W_b"])], axis=1)
    gkb_bias = np.concatenate([np.asarray(inputs["L0_attn_dt_bias"]),
                               np.asarray(inputs["L0_attn_b_b"])])[None, :]
    smask = np.zeros((1, 8), np.float32)
    if half == 1:
        smask[0, core - 1] = 1.0
    m = {
        "x_pad": np.ascontiguousarray(xp),
        "W_in": np.asarray(inputs["L0_attn_W_in"]),
        "convw": np.ascontiguousarray(convw),
        "convb": np.ascontiguousarray(convb),
        "W_q": np.asarray(inputs["L0_attn_W_q"]),
        "W_k": np.asarray(inputs["L0_attn_W_k"]),
        "W_v": np.asarray(inputs["L0_attn_W_v"]),
        "W_g": np.asarray(inputs["L0_attn_W_g"]),
        "W_gkb": np.ascontiguousarray(wgkb),
        "gkb_bias": np.ascontiguousarray(gkb_bias),
        "A_log": np.asarray(inputs["L0_attn_A_log"])[None, :],
        "g_norm_w": np.asarray(inputs["L0_attn_g_norm_w"])[None, :],
        "W_so": np.asarray(inputs["L0_attn_W_so"]),
        "W_out": np.asarray(inputs["L0_attn_W_out"]),
        "w_gate": np.asarray(inputs["L0_ffn_w_gate"]),
        "w_up": np.asarray(inputs["L0_ffn_w_up"]),
        "w_down": np.asarray(inputs["L0_ffn_w_down"]),
        "smask": smask,
        "wg_shard": np.asarray(inputs["L0_ffn_w_gate"])[384 * core:384 * (core + 1), :],
        "wu_shard": np.asarray(inputs["L0_ffn_w_up"])[384 * core:384 * (core + 1), :],
        "wd_shard": np.asarray(inputs["L0_ffn_w_down"])[128 * core:128 * (core + 1), :],
    }
    return {kk: np.ascontiguousarray(v, dtype=np.float32) for kk, v in m.items()}


def kernel(**inputs):
    nc, _ = build_program()
    in_maps = [shard_inputs(inputs, cidx) for cidx in range(8)]
    res = run_bass_kernel_spmd(nc, in_maps, core_ids=list(range(8)))
    outs = [res.results[cidx]["out"] for cidx in range(8)]
    return np.stack([np.concatenate([outs[2 * b], outs[2 * b + 1]], axis=0)
                     for b in range(B)]).astype(np.float32)


if __name__ == "__main__":
    nc, _ = build_program(upto="A", dbg=("xs", "zs"))
    print("built OK")


# version 11 (final)
# speedup vs baseline: 1.0204x; 1.0066x over previous
import numpy as np
from contextlib import ExitStack

import concourse.bass as bass
import concourse.mybir as mybir
from concourse.alu_op_type import AluOpType
from concourse.bass_utils import run_bass_kernel_spmd

F32 = mybir.dt.float32
BF16 = mybir.dt.bfloat16
I16 = mybir.dt.int16
U16 = mybir.dt.uint16
AF = mybir.ActivationFunctionType
AX = mybir.AxisListType.X
OP = AluOpType

B, LSEQ, DM = 4, 4096, 1024
TOK = 2048
NT = TOK // 128
NPT = 17
TPAD = NPT * 128
D_INNER = 2048
KD, VD, HH = 768, 1536, 4
DK, DV = 192, 384
CH, NCH = 128, 16
FI = 3072
QMAX = 32767.0
EPS_LN = 1e-5


class Sem:
    __slots__ = ("h", "v")

    def __init__(self, ctx, nc, name):
        self.h = ctx.enter_context(nc.semaphore(name))
        self.v = 0

    def inc(self, n=1):
        self.v += n
        return self.h


def _levelmask_neg(C, s):
    E = np.zeros((C, C), np.float32)
    for i in range(C):
        lo = (i // (2 * s)) * (2 * s)
        if (i // s) % 2 == 1:
            E[i, lo:lo + s] = -1.0
    return E



class _ChainStub:
    __slots__ = ("chain",)

    def __init__(self):
        self.chain = []

    def then_inc(self, sem, n):
        self.chain.append((sem, n))
        return self


class _RecEngine:
    __slots__ = ("q",)

    def __init__(self, q):
        self.q = q

    def __getattr__(self, m):
        def call(*a, **kw):
            stub = _ChainStub()
            self.q.append((m, a, kw, stub))
            return stub
        return call


class _RecBlock:

    def __init__(self):
        self.qs = {e: [] for e in ("sync", "tensor", "vector", "scalar", "gpsimd")}
        self.realblock = None
        self.direct = set()

    def set_direct(self, *engs):
        self.materialize(self.realblock)
        self.direct.update(engs)

    def set_rec(self, *engs):
        self.direct.difference_update(engs)

    def _mk(self, eng):
        if eng in self.direct:
            return getattr(self.realblock, eng)
        def deco(f):
            f(_RecEngine(self.qs[eng]))
            return f
        return deco

    @property
    def sync(self):
        return self._mk("sync")

    @property
    def tensor(self):
        return self._mk("tensor")

    @property
    def vector(self):
        return self._mk("vector")

    @property
    def scalar(self):
        return self._mk("scalar")

    @property
    def gpsimd(self):
        return self._mk("gpsimd")

    def materialize(self, block):
        for eng in ("sync", "tensor", "vector", "scalar", "gpsimd"):
            q = self.qs[eng]
            if not q:
                continue

            def body(e, q=list(q)):
                for m, a, kw, stub in q:
                    inst = getattr(e, m)(*a, **kw)
                    for sem, n in stub.chain:
                        inst = inst.then_inc(sem, n)
            getattr(block, eng)(body)
            self.qs[eng] = []


class K:
    pass


def build_program(upto="all", dbg=(), mock_ag=False):
    nc = bass.Bass("TRN2", target_bir_lowering=False)
    ctx = ExitStack()
    k = K()
    k.nc = nc
    k.ctx = ctx
    k.dbg = dbg
    k.mock_ag = mock_ag
    k.dbg_out = {}

    d = k.d = {}
    d["x_pad"] = nc.dram_tensor("x_pad", [TPAD, DM], F32, kind="ExternalInput").ap()
    d["W_in"] = nc.dram_tensor("W_in", [DM, D_INNER], F32, kind="ExternalInput").ap()
    d["convw"] = nc.dram_tensor("convw", [D_INNER, 4], F32, kind="ExternalInput").ap()
    d["convb"] = nc.dram_tensor("convb", [D_INNER, 1], F32, kind="ExternalInput").ap()
    d["W_q"] = nc.dram_tensor("W_q", [DM, KD], F32, kind="ExternalInput").ap()
    d["W_k"] = nc.dram_tensor("W_k", [DM, KD], F32, kind="ExternalInput").ap()
    d["W_v"] = nc.dram_tensor("W_v", [DM, VD], F32, kind="ExternalInput").ap()
    d["W_g"] = nc.dram_tensor("W_g", [DM, VD], F32, kind="ExternalInput").ap()
    d["W_gkb"] = nc.dram_tensor("W_gkb", [DM, 8], F32, kind="ExternalInput").ap()
    d["gkb_bias"] = nc.dram_tensor("gkb_bias", [1, 8], F32, kind="ExternalInput").ap()
    d["A_log"] = nc.dram_tensor("A_log", [1, HH], F32, kind="ExternalInput").ap()
    d["g_norm_w"] = nc.dram_tensor("g_norm_w", [1, DV], F32, kind="ExternalInput").ap()
    d["W_so"] = nc.dram_tensor("W_so", [VD, DM], F32, kind="ExternalInput").ap()
    d["W_out"] = nc.dram_tensor("W_out", [D_INNER, DM], F32, kind="ExternalInput").ap()
    d["w_gate"] = nc.dram_tensor("w_gate", [FI, DM], F32, kind="ExternalInput").ap()
    d["w_up"] = nc.dram_tensor("w_up", [FI, DM], F32, kind="ExternalInput").ap()
    d["w_down"] = nc.dram_tensor("w_down", [DM, FI], F32, kind="ExternalInput").ap()
    d["smask"] = nc.dram_tensor("smask", [1, 8], F32, kind="ExternalInput").ap()
    d["wg_shard"] = nc.dram_tensor("wg_shard", [384, DM], F32, kind="ExternalInput").ap()
    d["wu_shard"] = nc.dram_tensor("wu_shard", [384, DM], F32, kind="ExternalInput").ap()
    d["wd_shard"] = nc.dram_tensor("wd_shard", [128, FI], F32, kind="ExternalInput").ap()
    d["out"] = nc.dram_tensor("out", [TOK, DM], F32, kind="ExternalOutput").ap()

    dr = k.dr = {}
    def dram_scratch(name, shape, dt=BF16):
        dr[name] = nc.dram_tensor(name, shape, dt).ap()
        return dr[name]
    dram_scratch("sp_uv", [NCH * HH * 128, DV])
    dram_scratch("sp_wfm", [NCH * HH * 128, 256])
    dram_scratch("sp_aqt", [NCH * HH * 128, 128])
    dram_scratch("sp_ktil", [NCH * HH * 128, 192])
    dram_scratch("sp_qfm", [NCH * HH * 128, 256])
    dram_scratch("sp_o", [TOK, VD])
    dram_scratch("sp_og", [TOK, VD])
    dram_scratch("sp_x1", [TOK, DM], F32)
    dram_scratch("sp_xq", [TOK, DM])
    dram_scratch("sp_up", [TOK, FI])
    dram_scratch("sp_gate", [TOK, FI])
    dram_scratch("sp_oq", [TOK, FI])
    dr["cc_state_in"] = nc.dram_tensor("cc_state_in", [192, HH * DV], F32).ap()
    dr["cc_state_out"] = nc.dram_tensor("cc_state_out", [8 * 192, HH * DV], F32).ap()
    dr["cc_wq_in"] = nc.dram_tensor("cc_wq_in", [1152, DM], BF16).ap()
    dr["cc_wq_out"] = nc.dram_tensor("cc_wq_out", [8 * 1152, DM], BF16).ap()
    dr["cc_wg_in"] = nc.dram_tensor("cc_wg_in", [768, DM], BF16).ap()
    dr["cc_wg_out"] = nc.dram_tensor("cc_wg_out", [8 * 768, DM], BF16).ap()
    dr["cc_wd_in"] = nc.dram_tensor("cc_wd_in", [384, DM], BF16).ap()
    dr["cc_wd_out"] = nc.dram_tensor("cc_wd_out", [8 * 384, DM], BF16).ap()
    for i in range(4):
        dr[f"cc_s{i}_in"] = nc.dram_tensor(f"cc_s{i}_in", [1, 2], F32).ap()
        dr[f"cc_s{i}_out"] = nc.dram_tensor(f"cc_s{i}_out", [8, 2], F32).ap()

    def dbg_tensor(name, shape):
        if name in dbg:
            ap = nc.dram_tensor("dbg_" + name, shape, F32, kind="ExternalOutput").ap()
            k.dbg_out[name] = ap
            return ap
        return None
    k.dbg_tensor = dbg_tensor

    c = k.c = {}
    c["idf"] = nc.inline_tensor(np.eye(128, dtype=np.float32), name="c_idf").ap()
    c["ones_row"] = nc.inline_tensor(np.ones((1, 128), np.float32), name="c_ones").ap()
    c["ut"] = nc.inline_tensor(np.triu(np.ones((128, 128), np.float32), 0), name="c_ut").ap()
    c["lts"] = nc.inline_tensor(np.tril(np.ones((128, 128), np.float32), -1), name="c_lts").ap()
    c["mlow"] = nc.inline_tensor(np.tril(np.ones((128, 128), np.float32), -1), name="c_mlow").ap()
    c["muic"] = nc.inline_tensor(np.triu(np.ones((128, 128), np.float32), 0), name="c_muic").ap()
    c["id128"] = nc.inline_tensor(np.eye(128, dtype=np.float32), name="c_id128").ap()
    for i, s in enumerate((1, 2, 4, 8, 16, 32, 64)):
        c[f"negE{i}"] = nc.inline_tensor(_levelmask_neg(128, s), name=f"c_negE{i}").ap()

    sb = lambda *a, **kw: ctx.enter_context(nc.sbuf_tensor(*a, **kw))
    k.sb = sb
    k.consts = sb("consts", [128, 512], F32)
    k.constsb = sb("constsb", [128, 1280], BF16)
    k.gnwb = sb("gnwb", [128, DV], F32)
    k.alogv = sb("alogv", [128, 16], F32)
    k.gkb = sb("gkb", [128, NT * 8], F32)
    k.stat = sb("stat", [128, 192], F32)
    k.sval = sb("sval", [128, 48], F32)
    k.fconst = sb("fconst", [128, 4], F32)

    k.pb = [ctx.enter_context(nc.psum_tensor(f"pb{i}", [128, 512], F32)) for i in range(8)]

    sem = lambda name: Sem(ctx, nc, name)
    k.sem = sem
    k.s_lc = sem("s_lc")
    k.s_lcg = sem("s_lcg")
    k.s_lw = sem("s_lw")
    k.s_lwg = sem("s_lwg")
    k.s_lx = [sem("s_lx0"), sem("s_lx1")]
    k.s_mm = sem("s_mm")
    k.s_v = sem("s_v")
    k.s_sc = sem("s_sc")
    k.s_sp = sem("s_sp")
    k.s_g = sem("s_g")
    k.s_st = sem("s_st")
    k.s_st2 = [sem("s_st2a"), sem("s_st2b")]

    k.realblock = ctx.enter_context(nc.Block())
    k.block = _RecBlock()
    k.block.realblock = k.realblock

    emit_prelude(k)
    emit_passA(k)
    if upto == "A":
        emit_finish_dbg(k)
        k.block.materialize(k.realblock)
        ctx.close()
        return nc, k.dbg_out
    k.block.materialize(k.realblock)
    emit_gkb(k)
    k.block.materialize(k.realblock)
    emit_delta(k, chain=1)
    k.block.materialize(k.realblock)
    emit_state_ag(k)
    k.block.materialize(k.realblock)
    emit_delta(k, chain=2)
    if upto == "delta":
        emit_finish_dbg(k)
        k.block.materialize(k.realblock)
        ctx.close()
        return nc, k.dbg_out
    k.block.materialize(k.realblock)
    emit_epilogue(k)
    if upto == "attn":
        emit_finish_dbg(k)
        k.block.materialize(k.realblock)
        ctx.close()
        return nc, k.dbg_out
    k.block.materialize(k.realblock)
    emit_ffn(k)
    k.block.materialize(k.realblock)
    emit_dbg_ffn(k)
    emit_finish(k)
    k.block.materialize(k.realblock)
    ctx.close()
    return nc, k.dbg_out


def emit_prelude(k):
    nc, block, c = k.nc, k.block, k.c
    s_lc, s_mm, s_v, s_sc = k.s_lc, k.s_mm, k.s_v, k.s_sc

    @block.sync
    def _(sync):
        sync.dma_start(out=k.consts[:, 0:128], in_=c["idf"]).then_inc(s_lc.inc(16), 16)
        sync.dma_start(out=k.consts[0:1, 128:256], in_=c["ones_row"]).then_inc(s_lc.inc(16), 16)
        sync.dma_start(out=k.consts[:, 256:384], in_=c["ut"]).then_inc(s_lc.inc(16), 16)
        sync.dma_start(out=k.consts[:, 384:512], in_=c["lts"]).then_inc(s_lc.inc(16), 16)
        sync.dma_start(out=k.gnwb[0:1, :], in_=k.d["g_norm_w"]).then_inc(s_lc.inc(16), 16)
        sync.dma_start(out=k.alogv[0:1, 0:4], in_=k.d["A_log"]).then_inc(s_lc.inc(16), 16)
        sync.dma_start(out=k.alogv[0:1, 4:12], in_=k.d["gkb_bias"]).then_inc(s_lc.inc(16), 16)

    @block.gpsimd
    def _(g):
        g.dma_start(out=k.constsb[:, 0:128], in_=c["mlow"]).then_inc(k.s_lcg.inc(16), 16)
        g.dma_start(out=k.constsb[:, 128:256], in_=c["muic"]).then_inc(k.s_lcg.inc(16), 16)
        g.dma_start(out=k.constsb[:, 256:384], in_=c["id128"]).then_inc(k.s_lcg.inc(16), 16)
        for i in range(7):
            g.dma_start(out=k.constsb[:, 384 + 128 * i:512 + 128 * i],
                        in_=c[f"negE{i}"]).then_inc(k.s_lcg.inc(16), 16)

    @block.tensor
    def _(t):
        t.wait_ge(s_lc.h, s_lc.v)
        t.wait_ge(k.s_lcg.h, k.s_lcg.v)
        t.matmul(k.pb[0][:, 0:384], k.consts[0:1, 128:256], k.gnwb[0:1, 0:384],
                 start=True, stop=True)
        t.matmul(k.pb[1][:, 0:12], k.consts[0:1, 128:256], k.alogv[0:1, 0:12],
                 start=True, stop=True).then_inc(s_mm.inc(), 1)

    @block.vector
    def _(v):
        v.wait_ge(s_mm.h, s_mm.v)
        v.wait_ge(k.s_lcg.h, k.s_lcg.v)
        v.memset(k.fconst[:, 0:1], 1e-6)
        v.memset(k.fconst[:, 1:2], EPS_LN)
        v.memset(k.sval[:, :], 0.0)
        v.tensor_copy(out=k.gnwb[:, :], in_=k.pb[0][:, 0:384])
        v.tensor_copy(out=k.alogv[:, 4:12], in_=k.pb[1][:, 4:12])
        v.tensor_copy(out=k.alogv[:, 12:16], in_=k.pb[1][:, 0:4]).then_inc(s_v.inc(), 1)

    @block.scalar
    def _(sc):
        sc.wait_ge(s_v.h, s_v.v)
        sc.wait_ge(k.s_lcg.h, k.s_lcg.v)
        sc.activation(out=k.alogv[:, 0:4], in_=k.alogv[:, 12:16], func=AF.Exp).then_inc(s_sc.inc(), 1)

    @block.vector
    def _(v):
        v.wait_ge(s_sc.h, s_sc.v)
        v.tensor_scalar(out=k.alogv[:, 12:16], in0=k.alogv[:, 0:4], scalar1=-1.0,
                        scalar2=None, op0=OP.mult).then_inc(s_v.inc(), 1)
    k.prelude_v = s_v.v


def emit_passA(k):
    nc, block = k.nc, k.block
    s_mm, s_v, s_sc = k.s_mm, k.s_v, k.s_sc
    pb = k.pb

    k.scope_zs = ExitStack()
    k.zs = k.scope_zs.enter_context(nc.sbuf_tensor("zs", [128, 8 * TOK], BF16))
    k.scope_xs = ExitStack()
    k.xs = k.scope_xs.enter_context(nc.sbuf_tensor("xs", [128, 8 * TOK], BF16))

    k.scopeA = ExitStack()
    sbA = lambda *a: k.scopeA.enter_context(nc.sbuf_tensor(*a))
    winb = sbA("winb", [128, 8 * D_INNER], BF16)
    diagw = sbA("diagw", [128, 64 * 128], BF16)
    convwv = sbA("convwv", [128, 64], F32)
    convbv = sbA("convbv", [128, 16], F32)
    xtile = sbA("xtile", [128, 2 * DM], F32)
    xfm = sbA("xfm", [128, 8 * TPAD], BF16)
    xzbuf = sbA("xzbuf", [128, 16 * 516], BF16)

    @block.gpsimd
    def _(g):
        for kc in range(8):
            g.dma_start(out=winb[:, kc * D_INNER:(kc + 1) * D_INNER],
                        in_=k.d["W_in"][128 * kc:128 * (kc + 1), :]).then_inc(k.s_lwg.inc(16), 16)

    @block.sync
    def _(sync):
        for cc in range(16):
            sync.dma_start(out=convwv[:, 4 * cc:4 * cc + 4],
                           in_=k.d["convw"][128 * cc:128 * (cc + 1), :]).then_inc(k.s_lw.inc(16), 16)
        for cc in range(16):
            sync.dma_start(out=convbv[:, cc:cc + 1],
                           in_=k.d["convb"][128 * cc:128 * (cc + 1), :]).then_inc(k.s_lw.inc(16), 16)

    @block.vector
    def _(v):
        v.wait_ge(k.s_lw.h, k.s_lw.v)
        for cc in range(16):
            for j in range(4):
                vi = v.tensor_scalar(out=diagw[:, (4 * cc + j) * 128:(4 * cc + j + 1) * 128],
                                     in0=k.consts[:, 0:128], scalar1=convwv[:, 4 * cc + j:4 * cc + j + 1],
                                     scalar2=None, op0=OP.mult)
        vi.then_inc(s_v.inc(), 1)
    v_diag = k.s_v.v

    base_mm = s_mm.v
    base_v = s_v.v

    @block.sync
    def _(sync):
        for tt in range(NPT):
            if tt >= 2:
                sync.wait_ge(s_mm.h, base_mm + tt - 1)
            sync.dma_start(out=xtile[:, (tt % 2) * DM:(tt % 2 + 1) * DM],
                           in_=k.d["x_pad"][128 * tt:128 * (tt + 1), :]).then_inc(k.s_lx[tt % 2].inc(16), 16)

    @block.tensor
    def _(t):
        t.wait_ge(s_v.h, 1)
        for tt in range(NPT):
            t.wait_ge(k.s_lx[tt % 2].h, 16 * (tt // 2 + 1))
            if tt >= 2:
                t.wait_ge(s_v.h, base_v + tt - 1)
            for kc in range(8):
                bank = pb[(tt % 2) * 2 + kc // 4]
                ti = t.transpose(bank[:, (kc % 4) * 128:(kc % 4 + 1) * 128],
                                 xtile[:, (tt % 2) * DM + kc * 128:(tt % 2) * DM + (kc + 1) * 128],
                                 k.consts[:, 0:128])
                if kc == 7:
                    ti.then_inc(s_mm.inc(), 1)

    @block.vector
    def _(v):
        for tt in range(NPT):
            v.wait_ge(s_mm.h, base_mm + tt + 1)
            for kc in range(8):
                bank = pb[(tt % 2) * 2 + kc // 4]
                vi = v.tensor_copy(out=xfm[:, kc * TPAD + 128 * tt:kc * TPAD + 128 * (tt + 1)],
                                   in_=bank[:, (kc % 4) * 128:(kc % 4 + 1) * 128])
            vi.then_inc(s_v.inc(), 1)

    mm_after_x = s_mm.v
    v_after_x = s_v.v

    base_mm2 = s_mm.v
    base_v2 = s_v.v
    base_sc3 = k.s_sc.v

    @block.tensor
    def _(t):
        t.wait_ge(k.s_lwg.h, k.s_lwg.v)
        t.wait_ge(s_v.h, v_after_x)
        t.wait_ge(s_v.h, v_diag)
        nmm = 0
        nconv = 0
        for g in range(4):
            t0 = 512 * g
            for oc in range(16):
                bank = pb[2 + (nmm % 2)]
                if nmm >= 2:
                    t.wait_ge(s_v.h, base_v2 + nmm - 1)
                for kc in range(8):
                    t.matmul(bank[:, 0:512],
                             winb[:, kc * D_INNER + oc * 128:kc * D_INNER + (oc + 1) * 128],
                             xfm[:, kc * TPAD + t0:kc * TPAD + t0 + 512],
                             start=(kc == 0), stop=False if kc < 7 else True)
                for kc in range(8):
                    mi = t.matmul(pb[4 + (nmm % 2)][:, 0:4],
                                  winb[:, kc * D_INNER + oc * 128:kc * D_INNER + (oc + 1) * 128],
                                  xfm[:, kc * TPAD + t0 + 512:kc * TPAD + t0 + 516],
                                  start=(kc == 0), stop=(kc == 7))
                    if kc == 7:
                        mi.then_inc(s_mm.inc(), 1)
                nmm += 1
            t.wait_ge(s_v.h, base_v2 + nmm)
            for cc in range(16):
                bank = pb[6 + (nconv % 2)]
                if nconv >= 2:
                    t.wait_ge(k.s_sc.h, base_sc3 + nconv - 1)
                for j in range(4):
                    mi = t.matmul(bank[:, 0:512],
                                  diagw[:, (4 * cc + j) * 128:(4 * cc + j + 1) * 128],
                                  xzbuf[:, cc * 516 + j:cc * 516 + j + 512],
                                  start=(j == 0), stop=(j == 3))
                    if j == 3:
                        mi.then_inc(s_mm.inc(), 1)
                nconv += 1

    @block.vector
    def _(v):
        nmm = 0
        for g in range(4):
            for oc in range(16):
                bank = pb[2 + (nmm % 2)]
                v.wait_ge(s_mm.h, base_mm2 + nmm + 1 + 16 * g)
                v.tensor_copy(out=xzbuf[:, oc * 516:oc * 516 + 512], in_=bank[:, 0:512])
                v.tensor_copy(out=xzbuf[:, oc * 516 + 512:oc * 516 + 516],
                              in_=pb[4 + (nmm % 2)][:, 0:4]).then_inc(s_v.inc(), 1)
                nmm += 1

    @block.scalar
    def _(sc):
        nconv = 0
        for g in range(4):
            t0 = 512 * g
            for cc in range(16):
                bank = pb[6 + (nconv % 2)]
                sc.wait_ge(s_mm.h, base_mm2 + 16 * (g + 1) + nconv + 1)
                dst = k.xs if cc < 8 else k.zs
                oc = cc if cc < 8 else cc - 8
                sc.activation(out=dst[:, oc * TOK + t0:oc * TOK + t0 + 512], in_=bank[:, 0:512],
                              func=AF.Silu, bias=convbv[:, cc:cc + 1],
                              scale=1.0).then_inc(k.s_sc.inc(), 1)
                nconv += 1

    k.A_done_sc = k.s_sc.v
    k.A_done_mm = s_mm.v

    dxs = k.dbg_tensor("xs", [128, 8 * TOK])
    dzs = k.dbg_tensor("zs", [128, 8 * TOK])
    if dxs is not None:
        @block.gpsimd
        def _(g):
            g.wait_ge(k.s_sc.h, k.A_done_sc)
            g.dma_start(out=dxs, in_=k.xs[:, :]).then_inc(k.s_sp.inc(16), 16)
            g.dma_start(out=dzs, in_=k.zs[:, :]).then_inc(k.s_sp.inc(16), 16)


def emit_gkb(k):
    nc, block, pb = k.nc, k.block, k.pb
    s_mm, s_v, s_sc = k.s_mm, k.s_v, k.s_sc
    k.scopeA.close()

    k.scopeD = ExitStack()
    sbD = lambda *a: k.scopeD.enter_context(nc.sbuf_tensor(*a))
    k.sbD = sbD
    wgkbb = sbD("wgkbb", [128, 64], BF16)
    gkscr = sbD("gkscr", [128, 32], F32)
    s_lgkb = k.sem("s_lgkb")

    @block.gpsimd
    def _(g):
        g.wait_ge(k.s_sc.h, k.A_done_sc)
        for kc in range(8):
            g.dma_start(out=wgkbb[:, 8 * kc:8 * kc + 8],
                        in_=k.d["W_gkb"][128 * kc:128 * (kc + 1), :]).then_inc(s_lgkb.inc(16), 16)

    vfirst = {}
    for t in range(NT):
        par = t % 2

        @block.tensor
        def _(tt_, t=t, par=par):
            tt_.wait_ge(s_lgkb.h, s_lgkb.v)
            tt_.wait_ge(k.s_sc.h, k.A_done_sc)
            if t >= 2:
                tt_.wait_ge(s_v.h, vfirst[t - 2])
            for kc in range(8):
                mi = tt_.matmul(pb[6 + par][:, 0:8],
                                k.xs[:, kc * TOK + 128 * t:kc * TOK + 128 * (t + 1)],
                                wgkbb[:, 8 * kc:8 * kc + 8],
                                start=(kc == 0), stop=(kc == 7))
            mi.then_inc(s_mm.inc(), 1)

        mmv = s_mm.v

        @block.vector
        def _(v, t=t, par=par):
            v.wait_ge(s_mm.h, mmv)
            v.tensor_tensor(out=gkscr[:, 0:8], in0=pb[6 + par][:, 0:8],
                            in1=k.alogv[:, 4:12], op=OP.add).then_inc(s_v.inc(), 1)
        vfirst[t] = s_v.v

        vv = s_v.v

        @block.scalar
        def _(sc, t=t):
            sc.wait_ge(s_v.h, vv)
            sc.activation(out=gkscr[:, 8:12], in_=gkscr[:, 0:4], func=AF.Exp).then_inc(s_sc.inc(), 1)
            sc.wait_ge(s_sc.h, s_sc.v)
            sc.activation(out=gkscr[:, 12:16], in_=gkscr[:, 8:12], func=AF.Ln, bias=1.0)
            sc.activation(out=gkscr[:, 16:20], in_=gkscr[:, 4:8], func=AF.Exp,
                          scale=-1.0).then_inc(s_sc.inc(), 1)

        scv = s_sc.v

        @block.vector
        def _(v, t=t):
            v.wait_ge(s_sc.h, scv)
            v.tensor_tensor(out=k.gkb[:, 8 * t:8 * t + 4], in0=gkscr[:, 12:16],
                            in1=k.alogv[:, 12:16], op=OP.mult)
            v.tensor_scalar(out=gkscr[:, 20:24], in0=gkscr[:, 16:20], scalar1=1.0,
                            scalar2=None, op0=OP.add).then_inc(s_v.inc(), 1)
            v.wait_ge(s_v.h, s_v.v)
            v.reciprocal(out=k.gkb[:, 8 * t + 4:8 * t + 8],
                         in_=gkscr[:, 20:24]).then_inc(s_v.inc(), 1)

    k.gkb_done_v = s_v.v

    dgk = k.dbg_tensor("gkb", [128, NT * 8])
    if dgk is not None:
        @block.gpsimd
        def _(g):
            g.wait_ge(s_v.h, k.gkb_done_v)
            g.dma_start(out=dgk, in_=k.gkb[:, :]).then_inc(k.s_sp.inc(16), 16)


def delta_alloc(k):
    sbD = k.sbD
    nc = k.nc
    k.wqb = sbD("wqb", [128, 8 * KD], BF16)
    k.wkb = sbD("wkb", [128, 8 * KD], BF16)
    k.wvb = sbD("wvb", [128, 8 * VD], BF16)
    k.q_tm = sbD("q_tm", [128, KD], F32)
    k.k_tm = sbD("k_tm", [128, KD], F32)
    k.qh_tm = sbD("qh_tm", [128, DK], F32)
    k.kh_tm = sbD("kh_tm", [128, DK], F32)
    k.sqv = sbD("sqv", [128, 8], F32)
    k.lamv = sbD("lamv", [128, 2 * 16], F32)
    k.lamlast = sbD("lamlast", [128, 2 * 4], F32)
    k.lamT = sbD("lamT", [4, 2 * 128], F32)
    k.dupS = sbD("dupS", [128, 128], F32)
    k.dloS = sbD("dloS", [128, 128], F32)
    k.dscr = sbD("dscr", [128, 128], F32)
    k.dscr2 = sbD("dscr2", [128, 128], F32)
    k.a_bf = sbD("a_bf", [128, 2 * 128], BF16)
    k.as_bf = sbD("as_bf", [128, 128], BF16)
    k.z0 = sbD("z0", [128, 128], BF16)
    k.z1 = sbD("z1", [128, 128], BF16)
    k.zt0 = sbD("zt0", [128, 128], BF16)
    k.zt1 = sbD("zt1", [128, 128], BF16)
    k.q1s = sbD("q1s", [128, 128], BF16)
    k.r_pan = sbD("r_pan", [128, 576], BF16)
    k.blcol = sbD("blcol", [128, 4], F32)
    k.uv = sbD("uv", [128, 2 * DV], BF16)
    k.w_tm = sbD("w_tm", [128, 2 * 192], BF16)
    k.wfm = sbD("wfm", [128, 2 * 256], BF16)
    k.aqt = sbD("aqt", [128, 2 * 128], BF16)
    k.ktil = sbD("ktil", [128, 2 * 192], BF16)
    k.qfm = sbD("qfm", [128, 2 * 256], BF16)
    k.kfm = sbD("kfm", [128, 256], BF16)
    k.ubf = sbD("ubf", [128, DV], BF16)
    k.oscr = sbD("oscr", [128, DV], F32)
    k.obuf = sbD("obuf", [128, 2 * VD], BF16)
    k.osq = sbD("osq", [128, DV], F32)
    k.s0 = [sbD("s0a", [128, HH * DV], F32), sbD("s0b", [128, HH * DV], F32)]
    k.s1 = [sbD("s1a", [128, HH * DV], F32), sbD("s1b", [128, HH * DV], F32)]
    k.s0b = [sbD("s0ab", [128, HH * DV], BF16), sbD("s0bb", [128, HH * DV], BF16)]
    k.s1b = [sbD("s1ab", [128, HH * DV], BF16), sbD("s1bb", [128, HH * DV], BF16)]
    k.c["hsel"] = nc.inline_tensor(
        np.stack([np.repeat(np.eye(4, dtype=np.float32)[:, hh:hh + 1], 128, 1) for hh in range(4)], 0)
        .transpose(1, 0, 2).reshape(4, 512), name="c_hsel").ap()
    k.hsel = sbD("hsel", [4, 512], F32)
    k.s_lqkv = k.sem("s_lqkv")
    k.s_lhsel = k.sem("s_lhsel")
    k.s_spp = [k.sem("s_spp0"), k.sem("s_spp1")]
    k.s_hd = [k.sem("s_hd1"), k.sem("s_hd2")]
    k.s_pan = [k.sem("s_pan0"), k.sem("s_pan1")]
    k.s_so = [k.sem("s_so0"), k.sem("s_so1")]

    @k.block.sync
    def _(sync):
        sync.wait_ge(k.s_sc.h, k.A_done_sc)
        sync.dma_start(out=k.hsel[:, :], in_=k.c["hsel"]).then_inc(k.s_lhsel.inc(16), 16)

    @k.block.gpsimd
    def _(g):
        g.wait_ge(k.s_sc.h, k.A_done_sc)
        for kc in range(8):
            g.dma_start(out=k.wqb[:, kc * KD:(kc + 1) * KD],
                        in_=k.d["W_q"][128 * kc:128 * (kc + 1), :]).then_inc(k.s_lqkv.inc(16), 16)
            g.dma_start(out=k.wkb[:, kc * KD:(kc + 1) * KD],
                        in_=k.d["W_k"][128 * kc:128 * (kc + 1), :]).then_inc(k.s_lqkv.inc(16), 16)
            g.dma_start(out=k.wvb[:, kc * VD:(kc + 1) * VD],
                        in_=k.d["W_v"][128 * kc:128 * (kc + 1), :]).then_inc(k.s_lqkv.inc(16), 16)


def emit_chunk_lambda(k, c, chain):
    block, pb = k.block, k.pb
    s_mm, s_v, s_sc = k.s_mm, k.s_v, k.s_sc
    par = c % 2
    lo = 16 * par

    vnow = s_v.v

    @block.tensor
    def _(t):
        t.wait_ge(s_v.h, vnow)
        t.matmul(pb[5][:, 384:388], k.consts[:, 256:384], k.gkb[:, 8 * c:8 * c + 4],
                 start=True, stop=True)
        t.matmul(pb[5][:, 388:392], k.consts[:, 384:512], k.gkb[:, 8 * c:8 * c + 4],
                 start=True, stop=True).then_inc(s_mm.inc(), 1)

    mmv = s_mm.v

    @block.vector
    def _(v):
        v.wait_ge(s_mm.h, mmv)
        v.tensor_copy(out=k.lamv[:, lo:lo + 8], in_=pb[5][:, 384:392]).then_inc(s_v.inc(), 1)
        v.wait_ge(s_v.h, s_v.v)
        v.tensor_tensor(out=k.lamv[:, lo + 12:lo + 16], in0=k.lamv[:, lo:lo + 4],
                        in1=k.lamv[:, lo + 4:lo + 8], op=OP.add).then_inc(s_v.inc(), 1)

    vv = s_v.v

    @block.scalar
    def _(sc):
        sc.wait_ge(s_v.h, vv)
        sc.activation(out=k.lamlast[:, 4 * par:4 * par + 4], in_=k.lamv[:, lo + 12:lo + 16],
                      func=AF.Exp).then_inc(s_sc.inc(), 1)
        sc.wait_ge(s_sc.h, s_sc.v)
        sc.activation(out=k.lamv[:, lo + 12:lo + 16], in_=k.lamv[:, lo + 4:lo + 8],
                      func=AF.Exp)
        sc.activation(out=k.lamv[:, lo + 8:lo + 12], in_=k.lamv[:, lo:lo + 4],
                      func=AF.Exp).then_inc(s_sc.inc(), 1)

    scv = s_sc.v

    @block.tensor
    def _(t):
        t.wait_ge(s_sc.h, scv)
        t.transpose(pb[4][0:4, 384:512], k.lamv[:, lo:lo + 4],
                    k.consts[:, 0:128]).then_inc(s_mm.inc(), 1)

    mmv2 = s_mm.v

    @block.vector
    def _(v):
        v.wait_ge(s_mm.h, mmv2)
        v.tensor_copy(out=k.lamT[:, 128 * par:128 * (par + 1)],
                      in_=pb[4][0:4, 384:512]).then_inc(s_v.inc(), 1)


def emit_head(k, c, h, chain):
    block, pb = k.block, k.pb
    s_mm, s_v, s_sc = k.s_mm, k.s_v, k.s_sc
    par = c % 2
    lo = 16 * par
    j = 4 * c + h
    jp = j % 2
    cur, nxt = par, 1 - par
    bcol = k.gkb[:, 8 * c + 4 + h:8 * c + 5 + h]
    lamcol = k.lamv[:, lo + h:lo + h + 1]
    Lamcol = k.lamv[:, lo + 8 + h:lo + 9 + h]
    Lamrcol = k.lamv[:, lo + 12 + h:lo + 13 + h]
    Lamlastcol = k.lamlast[:, 4 * par + h:4 * par + h + 1]
    UVo, WFo, AQo, KTo, QFo = (k.uv[:, jp * DV:(jp + 1) * DV],
                               k.wfm[:, jp * 256:(jp + 1) * 256],
                               k.aqt[:, jp * 128:(jp + 1) * 128],
                               k.ktil[:, jp * 192:(jp + 1) * 192],
                               k.qfm[:, jp * 256:(jp + 1) * 256])

    if chain == 1:
        vproj = s_v.v

        @block.tensor
        def _(t):
            t.wait_ge(k.s_lqkv.h, k.s_lqkv.v)
            t.wait_ge(k.s_lhsel.h, k.s_lhsel.v)
            t.wait_ge(s_v.h, vproj)
            for kc in range(8):
                t.matmul(pb[0][:, 0:192], k.xs[:, kc * TOK + 128 * c:kc * TOK + 128 * (c + 1)],
                         k.wqb[:, kc * KD + 192 * h:kc * KD + 192 * (h + 1)],
                         start=(kc == 0), stop=(kc == 7))
            for kc in range(8):
                t.matmul(pb[0][:, 192:384], k.xs[:, kc * TOK + 128 * c:kc * TOK + 128 * (c + 1)],
                         k.wkb[:, kc * KD + 192 * h:kc * KD + 192 * (h + 1)],
                         start=(kc == 0), stop=(kc == 7))
            for kc in range(8):
                mi = t.matmul(pb[1][:, 0:384], k.xs[:, kc * TOK + 128 * c:kc * TOK + 128 * (c + 1)],
                              k.wvb[:, kc * VD + 384 * h:kc * VD + 384 * (h + 1)],
                              start=(kc == 0), stop=(kc == 7))
            mi.then_inc(s_mm.inc(), 1)

        m1 = s_mm.v

        @block.vector
        def _(v):
            v.wait_ge(s_mm.h, m1)
            if j >= 2:
                v.wait_ge(k.s_spp[j % 2].h, 16 * 5 * (j // 2))
            v.tensor_copy(out=k.q_tm[:, 0:192], in_=pb[0][:, 0:192])
            v.tensor_copy(out=k.k_tm[:, 0:192], in_=pb[0][:, 192:384]).then_inc(s_v.inc(), 1)
            v.wait_ge(s_v.h, s_v.v)
            v.tensor_tensor(out=k.qh_tm[:, :], in0=k.q_tm[:, 0:192], in1=k.q_tm[:, 0:192], op=OP.mult)
            v.tensor_tensor(out=k.kh_tm[:, :], in0=k.k_tm[:, 0:192], in1=k.k_tm[:, 0:192],
                            op=OP.mult).then_inc(s_v.inc(), 1)
            v.wait_ge(s_v.h, s_v.v)
            v.tensor_reduce(out=k.sqv[:, 0:1], in_=k.qh_tm[:, :], axis=AX, op=OP.add)
            v.tensor_reduce(out=k.sqv[:, 1:2], in_=k.kh_tm[:, :], axis=AX,
                            op=OP.add).then_inc(s_v.inc(), 1)

        v1 = s_v.v

        @block.scalar
        def _(sc):
            sc.wait_ge(s_v.h, v1)
            sc.activation(out=k.sqv[:, 2:4], in_=k.sqv[:, 0:2], func=AF.Ln,
                          bias=k.fconst[:, 0:1]).then_inc(s_sc.inc(), 1)
            sc.wait_ge(s_sc.h, s_sc.v)
            sc.activation(out=k.sqv[:, 4:6], in_=k.sqv[:, 2:4], func=AF.Exp,
                          scale=-0.5).then_inc(s_sc.inc(), 1)

        sc1 = s_sc.v

        @block.vector
        def _(v):
            v.wait_ge(s_sc.h, sc1)
            v.tensor_scalar(out=k.qh_tm[:, :], in0=k.q_tm[:, 0:192], scalar1=k.sqv[:, 4:5],
                            scalar2=None, op0=OP.mult)
            v.tensor_scalar(out=k.kh_tm[:, :], in0=k.k_tm[:, 0:192], scalar1=k.sqv[:, 5:6],
                            scalar2=None, op0=OP.mult).then_inc(s_v.inc(), 1)

        v2 = s_v.v

        @block.tensor
        def _(t):
            t.wait_ge(s_v.h, v2)
            t.transpose(pb[2][:, 0:128], k.qh_tm[:, 0:128], k.consts[:, 0:128])
            t.transpose(pb[2][0:64, 128:256], k.qh_tm[:, 128:192], k.consts[:, 0:128])
            t.transpose(pb[3][:, 0:128], k.kh_tm[:, 0:128], k.consts[:, 0:128])
            t.transpose(pb[3][0:64, 128:256], k.kh_tm[:, 128:192], k.consts[:, 0:128])
            t.matmul(pb[2][:, 256:384], k.hsel[:, 128 * h:128 * (h + 1)],
                     k.lamT[:, 128 * par:128 * (par + 1)],
                     start=True, stop=True).then_inc(s_mm.inc(), 1)

        m2 = s_mm.v

        @block.vector
        def _(v):
            v.wait_ge(s_mm.h, m2)
            v.tensor_copy(out=QFo[:, 0:128], in_=pb[2][:, 0:128])
            v.tensor_copy(out=QFo[0:64, 128:256], in_=pb[2][0:64, 128:256])
            v.tensor_copy(out=k.kfm[:, 0:128], in_=pb[3][:, 0:128])
            v.tensor_copy(out=k.kfm[0:64, 128:256], in_=pb[3][0:64, 128:256])
            v.tensor_scalar(out=k.dscr[:, :], in0=pb[2][:, 256:384], scalar1=lamcol,
                            scalar2=0.0, op0=OP.subtract, op1=OP.min)
            v.tensor_scalar(out=k.dloS[:, :], in0=pb[2][:, 256:384], scalar1=lamcol,
                            scalar2=0.0, op0=OP.subtract, op1=OP.max).then_inc(s_v.inc(), 1)

        v3 = s_v.v

        @block.scalar
        def _(sc):
            sc.wait_ge(s_v.h, v3)
            sc.activation(out=k.dupS[:, :], in_=k.dscr[:, :], func=AF.Exp).then_inc(s_sc.inc(), 1)
            sc.wait_ge(s_sc.h, s_sc.v)
            sc.activation(out=k.dscr[:, :], in_=k.dloS[:, :], func=AF.Exp,
                          scale=-1.0).then_inc(s_sc.inc(), 1)

        sc2 = s_sc.v

        @block.tensor
        def _(t):
            t.wait_ge(s_v.h, v3)
            t.matmul(pb[4][:, 0:128], k.kfm[:, 0:128], k.kfm[:, 0:128], start=True, stop=False)
            t.matmul(pb[4][:, 0:128], k.kfm[0:64, 128:256], k.kfm[0:64, 128:256], start=False, stop=True)
            t.matmul(pb[4][:, 128:256], k.kfm[:, 0:128], QFo[:, 0:128], start=True, stop=False)
            mi = t.matmul(pb[4][:, 128:256], k.kfm[0:64, 128:256], QFo[0:64, 128:256],
                          start=False, stop=True)
            mi.then_inc(s_mm.inc(), 1)

        m3 = s_mm.v

        @block.vector
        def _(v):
            v.wait_ge(s_mm.h, m3)
            v.wait_ge(s_sc.h, sc2)
            v.scalar_tensor_tensor(out=k.dloS[:, :], in0=k.dscr[:, :], scalar=bcol,
                                   in1=pb[4][:, 0:128], op0=OP.mult, op1=OP.mult)
            v.tensor_tensor(out=k.dscr2[:, :], in0=k.dupS[:, :], in1=pb[4][:, 128:256],
                            op=OP.mult).then_inc(s_v.inc(), 1)
            v.wait_ge(s_v.h, s_v.v)
            v.tensor_tensor(out=k.a_bf[:, jp * 128:(jp + 1) * 128], in0=k.dloS[:, :],
                            in1=k.constsb[:, 0:128], op=OP.mult)
            v.tensor_tensor(out=AQo, in0=k.dscr2[:, :],
                            in1=k.constsb[:, 128:256], op=OP.mult).then_inc(s_v.inc(), 1)

        v4 = s_v.v
        abf = k.a_bf[:, jp * 128:(jp + 1) * 128]

        @block.vector
        def _(v):
            v.wait_ge(s_v.h, v4)
            v.tensor_tensor(out=k.as_bf[:, :], in0=abf, in1=k.constsb[:, 384:512],
                            op=OP.mult).then_inc(s_v.inc(), 1)
            v.wait_ge(s_v.h, s_v.v)
            v.tensor_tensor(out=k.z0[:, :], in0=k.as_bf[:, :], in1=k.constsb[:, 256:384],
                            op=OP.add).then_inc(s_v.inc(), 1)

        vd = s_v.v

        @block.tensor
        def _(t):
            t.wait_ge(s_v.h, vd)
            t.transpose(pb[5][:, 0:64].bitcast(BF16), k.z0[:, :],
                        k.constsb[:, 256:384]).then_inc(s_mm.inc(), 1)

        md = s_mm.v

        @block.vector
        def _(v):
            v.wait_ge(s_mm.h, md)
            v.tensor_copy(out=k.zt0[:, :], in_=pb[5][:, 0:64].bitcast(BF16)).then_inc(s_v.inc(), 1)

        zc, ztc = k.z0, k.zt0
        zn, ztn = k.z1, k.zt1
        for lv in range(1, 7):
            vpre = s_v.v

            @block.vector
            def _(v, lv=lv):
                v.tensor_tensor(out=k.as_bf[:, :], in0=abf,
                                in1=k.constsb[:, 384 + 128 * lv:512 + 128 * lv],
                                op=OP.mult).then_inc(s_v.inc(), 1)

            va = s_v.v

            @block.tensor
            def _(t, ztc=ztc):
                t.wait_ge(s_v.h, va)
                t.matmul(pb[5][:, 0:128], k.as_bf[:, :], ztc[:, :],
                         start=True, stop=True).then_inc(s_mm.inc(), 1)

            mq = s_mm.v

            @block.vector
            def _(v):
                v.wait_ge(s_mm.h, mq)
                v.tensor_copy(out=k.q1s[:, :], in_=pb[5][:, 0:128]).then_inc(s_v.inc(), 1)

            vq = s_v.v

            @block.tensor
            def _(t, zc=zc):
                t.wait_ge(s_v.h, vq)
                t.matmul(pb[6][:, 0:128], k.q1s[:, :], zc[:, :], start=True, stop=True)
                t.matmul(pb[6][:, 128:256], zc[:, :], k.q1s[:, :],
                         start=True, stop=True).then_inc(s_mm.inc(), 1)

            mp = s_mm.v

            @block.vector
            def _(v, zc=zc, ztc=ztc, zn=zn, ztn=ztn):
                v.wait_ge(s_mm.h, mp)
                v.tensor_tensor(out=zn[:, :], in0=zc[:, :], in1=pb[6][:, 0:128], op=OP.add)
                v.tensor_tensor(out=ztn[:, :], in0=ztc[:, :], in1=pb[6][:, 128:256],
                                op=OP.add).then_inc(s_v.inc(), 1)

            zc, zn = zn, zc
            ztc, ztn = ztn, ztc

        ztfinal = ztc

        vr0 = s_v.v

        @block.vector
        def _(v):
            v.tensor_tensor(out=k.blcol[:, 0:1], in0=bcol, in1=Lamcol, op=OP.mult)
            v.tensor_scalar(out=k.r_pan[:, 0:384], in0=pb[1][:, 0:384], scalar1=bcol,
                            scalar2=None, op0=OP.mult)
            v.tensor_scalar(out=KTo, in0=k.kh_tm[:, :], scalar1=Lamrcol,
                            scalar2=None, op0=OP.mult).then_inc(s_v.inc(), 1)
            v.wait_ge(s_v.h, s_v.v)
            v.tensor_scalar(out=k.r_pan[:, 384:576], in0=k.kh_tm[:, :], scalar1=k.blcol[:, 0:1],
                            scalar2=None, op0=OP.mult).then_inc(s_v.inc(), 1)

        vr = s_v.v

        @block.tensor
        def _(t, ztfinal=ztfinal):
            t.wait_ge(s_v.h, vr)
            t.matmul(pb[7][:, 0:512], ztfinal[:, :], k.r_pan[:, 0:512], start=True, stop=True)
            t.matmul(pb[5][:, 128:192], ztfinal[:, :], k.r_pan[:, 512:576],
                     start=True, stop=True).then_inc(s_mm.inc(), 1)

        mpan = s_mm.v

        @block.vector
        def _(v):
            v.wait_ge(s_mm.h, mpan)
            v.tensor_copy(out=UVo, in_=pb[7][:, 0:384])
            v.tensor_copy(out=k.w_tm[:, jp * 192:jp * 192 + 128], in_=pb[7][:, 384:512])
            v.tensor_copy(out=k.w_tm[:, jp * 192 + 128:jp * 192 + 192],
                          in_=pb[5][:, 128:192]).then_inc(s_v.inc(), 1)

        vw = s_v.v

        @block.tensor
        def _(t):
            t.wait_ge(s_v.h, vw)
            t.transpose(pb[6][:, 0:64].bitcast(BF16), k.w_tm[:, jp * 192:jp * 192 + 128],
                        k.constsb[:, 256:384])
            t.transpose(pb[6][0:64, 64:128].bitcast(BF16), k.w_tm[:, jp * 192 + 128:jp * 192 + 192],
                        k.constsb[:, 256:384]).then_inc(s_mm.inc(), 1)

        mwf = s_mm.v

        @block.vector
        def _(v):
            v.wait_ge(s_mm.h, mwf)
            v.tensor_copy(out=WFo[:, 0:128], in_=pb[6][:, 0:64].bitcast(BF16))
            v.tensor_copy(out=WFo[0:64, 128:256],
                          in_=pb[6][0:64, 64:128].bitcast(BF16)).then_inc(s_v.inc(), 1)

        vwf = s_v.v

        @block.gpsimd
        def _(g):
            g.wait_ge(s_v.h, vwf)
            sp = k.s_spp[j % 2]
            g.dma_start(out=k.dr["sp_uv"][128 * j:128 * (j + 1), :], in_=UVo).then_inc(sp.inc(16), 16)
            g.dma_start(out=k.dr["sp_wfm"][128 * j:128 * (j + 1), :], in_=WFo).then_inc(sp.inc(16), 16)
            g.dma_start(out=k.dr["sp_aqt"][128 * j:128 * (j + 1), :], in_=AQo).then_inc(sp.inc(16), 16)
            g.dma_start(out=k.dr["sp_ktil"][128 * j:128 * (j + 1), :], in_=KTo).then_inc(sp.inc(16), 16)
            g.dma_start(out=k.dr["sp_qfm"][128 * j:128 * (j + 1), :], in_=QFo).then_inc(sp.inc(16), 16)

    else:
        @block.sync
        def _(sync):
            if j >= 2:
                sync.wait_ge(k.s_v.h, k.head_done_v[2][j - 2])
            sp = k.s_pan[jp]
            sync.dma_start(out=UVo, in_=k.dr["sp_uv"][128 * j:128 * (j + 1), :]).then_inc(sp.inc(16), 16)
            sync.dma_start(out=WFo, in_=k.dr["sp_wfm"][128 * j:128 * (j + 1), :]).then_inc(sp.inc(16), 16)
            sync.dma_start(out=AQo, in_=k.dr["sp_aqt"][128 * j:128 * (j + 1), :]).then_inc(sp.inc(16), 16)
            sync.dma_start(out=KTo, in_=k.dr["sp_ktil"][128 * j:128 * (j + 1), :]).then_inc(sp.inc(16), 16)
            sync.dma_start(out=QFo, in_=k.dr["sp_qfm"][128 * j:128 * (j + 1), :]).then_inc(sp.inc(16), 16)

    panwait = (k.s_pan[jp], 16 * 5 * (j // 2 + 1)) if chain == 2 else None

    vws = s_v.v

    @block.tensor
    def _(t):
        if panwait is not None:
            t.wait_ge(panwait[0].h, panwait[1])
        t.wait_ge(s_v.h, vws)
        t.matmul(pb[6][:, 0:384], WFo[:, 0:128], k.s0b[cur][:, DV * h:DV * (h + 1)],
                 start=True, stop=False)
        t.matmul(pb[6][:, 0:384], WFo[0:64, 128:256], k.s1b[cur][0:64, DV * h:DV * (h + 1)],
                 start=False, stop=True).then_inc(s_mm.inc(), 1)

    mws = s_mm.v

    @block.vector
    def _(v):
        v.wait_ge(s_mm.h, mws)
        v.tensor_tensor(out=k.ubf[:, :], in0=UVo, in1=pb[6][:, 0:384],
                        op=OP.subtract).then_inc(s_v.inc(), 1)

    vu = s_v.v

    if chain == 2:
        @block.tensor
        def _(t):
            t.wait_ge(s_v.h, vu)
            t.matmul(pb[2][:, 0:384], QFo[:, 0:128], k.s0b[cur][:, DV * h:DV * (h + 1)],
                     start=True, stop=False)
            t.matmul(pb[2][:, 0:384], QFo[0:64, 128:256], k.s1b[cur][0:64, DV * h:DV * (h + 1)],
                     start=False, stop=True)
            t.matmul(pb[3][:, 0:384], AQo, k.ubf[:, :], start=True, stop=True).then_inc(s_mm.inc(), 1)

        mo = s_mm.v

        @block.vector
        def _(v):
            v.wait_ge(s_mm.h, mo)
            if c >= 2 and h == 0:
                v.wait_ge(k.s_so[c % 2].h, 16 * (c // 2))
            v.tensor_scalar(out=k.oscr[:, :], in0=pb[2][:, 0:384], scalar1=Lamcol,
                            scalar2=None, op0=OP.mult).then_inc(s_v.inc(), 1)
            v.wait_ge(s_v.h, s_v.v)
            v.tensor_tensor(out=k.obuf[:, par * VD + DV * h:par * VD + DV * (h + 1)],
                            in0=k.oscr[:, :], in1=pb[3][:, 0:384], op=OP.add).then_inc(s_v.inc(), 1)

        vo = s_v.v

        @block.scalar
        def _(sc):
            sc.wait_ge(s_v.h, vo)
            sc.wait_ge(s_sc.h, s_sc.v)
            sc.activation(out=k.osq[:, :], in_=k.obuf[:, par * VD + DV * h:par * VD + DV * (h + 1)],
                          func=AF.Square,
                          accum_out=k.stat[:, 4 * c + h:4 * c + h + 1]).then_inc(s_sc.inc(), 1)

    @block.tensor
    def _(t):
        t.wait_ge(s_v.h, vu)
        t.matmul(pb[0][:, 0:384], KTo[:, 0:128], k.ubf[:, :], start=True, stop=True)
        t.matmul(pb[1][0:64, 0:384], KTo[:, 128:192], k.ubf[:, :],
                 start=True, stop=True).then_inc(s_mm.inc(), 1)

    msu = s_mm.v

    @block.vector
    def _(v):
        v.wait_ge(s_mm.h, msu)
        v.scalar_tensor_tensor(out=k.s0[nxt][:, DV * h:DV * (h + 1)],
                               in0=k.s0[cur][:, DV * h:DV * (h + 1)], scalar=Lamlastcol,
                               in1=pb[0][:, 0:384], op0=OP.mult, op1=OP.add)
        v.scalar_tensor_tensor(out=k.s1[nxt][0:64, DV * h:DV * (h + 1)],
                               in0=k.s1[cur][0:64, DV * h:DV * (h + 1)], scalar=Lamlastcol[0:64, :],
                               in1=pb[1][0:64, 0:384], op0=OP.mult, op1=OP.add).then_inc(s_v.inc(), 1)
        v.wait_ge(s_v.h, s_v.v)
        v.tensor_copy(out=k.s0b[nxt][:, DV * h:DV * (h + 1)], in_=k.s0[nxt][:, DV * h:DV * (h + 1)])
        v.tensor_copy(out=k.s1b[nxt][0:64, DV * h:DV * (h + 1)],
                      in_=k.s1[nxt][0:64, DV * h:DV * (h + 1)]).then_inc(s_v.inc(), 1)
    k.head_done_v[chain][j] = k.s_v.v


def emit_delta(k, chain):
    nc, block = k.nc, k.block
    if not hasattr(k, "head_done_v"):
        k.head_done_v = {1: {}, 2: {}}
    if chain == 1:
        delta_alloc(k)

        @block.vector
        def _(v):
            v.memset(k.s0[0][:, :], 0.0)
            v.memset(k.s1[0][:, :], 0.0)
            v.memset(k.s0b[0][:, :], 0.0)
            v.memset(k.s1b[0][:, :], 0.0)
            v.memset(k.wfm[64:128, 128:256], 0.0)
            v.memset(k.wfm[64:128, 384:512], 0.0)
            v.memset(k.qfm[64:128, 128:256], 0.0)
            v.memset(k.qfm[64:128, 384:512], 0.0)
            v.memset(k.s1[1][64:128, :], 0.0)
            v.memset(k.s1b[1][64:128, :], 0.0).then_inc(k.s_v.inc(), 1)
    else:
        @block.sync
        def _(sync):
            sync.wait_ge(k.s_spp[0].h, k.s_spp[0].v)
            sync.wait_ge(k.s_spp[1].h, k.s_spp[1].v)
            sync.wait_ge(k.s_v.h, k.s_v.v)

    for c in range(NCH):
        emit_chunk_lambda(k, c, chain)
        for h in range(HH):
            emit_head(k, c, h, chain)
        if chain == 2:
            par = c % 2
            vdone = k.s_v.v

            @block.gpsimd
            def _(g, c=c, par=par):
                g.wait_ge(k.s_v.h, vdone)
                g.dma_start(out=k.dr["sp_o"][128 * c:128 * (c + 1), :],
                            in_=k.obuf[:, par * VD:(par + 1) * VD]).then_inc(k.s_so[c % 2].inc(16), 16)

    if chain == 1:
        k.chain1_hd = k.head_done_v[1][4 * NCH - 1]
    else:
        k.chain2_done_so = (k.s_so[0].v, k.s_so[1].v)
        k.chain2_hd = k.head_done_v[2][4 * NCH - 1]
        dbo = k.dbg_tensor("o", [TOK, VD])
        if dbo is not None:
            @block.gpsimd
            def _(g):
                g.wait_ge(k.s_so[0].h, k.s_so[0].v)
                g.wait_ge(k.s_so[1].h, k.s_so[1].v)
                g.dma_start(out=dbo, in_=k.dr["sp_o"][:, :]).then_inc(k.s_sp.inc(16), 16)


def emit_state_ag(k):
    nc, block = k.nc, k.block
    s_v = k.s_v
    mock = getattr(k, "mock_ag", False)

    if mock:
        @block.vector
        def _(v):
            v.wait_ge(s_v.h, k.chain1_hd)
            v.memset(k.s0[0][:, :], 0.0)
            v.memset(k.s1[0][:, :], 0.0)
            v.memset(k.s0b[0][:, :], 0.0)
            v.memset(k.s1b[0][:, :], 0.0).then_inc(s_v.inc(), 1)
        return

    s_ag = k.sem("s_ag")
    s_agl = k.sem("s_agl")
    cin, cout = k.dr["cc_state_in"], k.dr["cc_state_out"]
    fin = NCH % 2

    @block.gpsimd
    def _(g):
        g.wait_ge(k.s_v.h, k.chain1_hd)
        g.dma_start(out=cin[0:128, :], in_=k.s0[fin][:, :]).then_inc(s_ag.inc(16), 16)
        g.dma_start(out=cin[128:192, :], in_=k.s1[fin][0:64, :]).then_inc(s_ag.inc(16), 16)
        g.wait_ge(s_ag.h, s_ag.v)
        g.collective_compute(
            "AllGather", mybir.AluOpType.bypass,
            replica_groups=[[0, 1, 2, 3, 4, 5, 6, 7]],
            ins=[cin.opt()], outs=[cout.opt()],
        ).then_inc(s_agl.inc(1), 1)

    agv = k.s_ag_done = k.s_ag = s_ag

    s_msk = k.sem("s_msk")

    @block.sync
    def _(sync):
        sync.dma_start(out=k.sval[0:1, 8:16], in_=k.d["smask"]).then_inc(s_msk.inc(16), 16)

    @block.tensor
    def _(t):
        t.wait_ge(s_msk.h, s_msk.v)
        t.matmul(k.pb[4][:, 0:8], k.consts[0:1, 128:256], k.sval[0:1, 8:16],
                 start=True, stop=True).then_inc(k.s_mm.inc(), 1)

    mmv = k.s_mm.v

    @block.vector
    def _(v):
        v.wait_ge(k.s_mm.h, mmv)
        v.tensor_copy(out=k.sval[:, 0:8], in_=k.pb[4][:, 0:8]).then_inc(s_v.inc(), 1)

    slab0 = k.sbD("slab0", [128, HH * DV], F32)
    slab1 = k.sbD("slab1", [128, HH * DV], F32)

    @block.vector
    def _(v):
        v.wait_ge(s_ag.h, 32)
        v.memset(k.s0[0][:, :], 0.0)
        v.memset(k.s1[0][:, :], 0.0).then_inc(s_v.inc(), 1)

    for ci in range(8):
        sidx, dst = ci % 2, (ci + 1) % 2
        lv = k.sem(f"s_agld{ci}")
        vprev = s_v.v

        @block.sync
        def _(sync, ci=ci, lv=lv, vprev=vprev):
            sync.wait_ge(s_agl.h, s_agl.v)
            sync.wait_ge(s_v.h, vprev)
            sync.dma_start(out=slab0[:, :], in_=k.dr["cc_state_out"][192 * ci:192 * ci + 128, :]).then_inc(lv.inc(16), 16)
            sync.dma_start(out=slab1[0:64, :], in_=k.dr["cc_state_out"][192 * ci + 128:192 * (ci + 1), :]).then_inc(lv.inc(16), 16)

        @block.vector
        def _(v, ci=ci, sidx=sidx, dst=dst, lv=lv, vprev=vprev):
            v.wait_ge(lv.h, lv.v)
            v.scalar_tensor_tensor(out=k.s0[dst][:, :], in0=slab0[:, :],
                                   scalar=k.sval[:, ci:ci + 1], in1=k.s0[sidx][:, :],
                                   op0=OP.mult, op1=OP.add)
            v.scalar_tensor_tensor(out=k.s1[dst][0:64, :], in0=slab1[0:64, :],
                                   scalar=k.sval[0:64, ci:ci + 1], in1=k.s1[sidx][0:64, :],
                                   op0=OP.mult, op1=OP.add).then_inc(s_v.inc(), 1)

    vfin = s_v.v

    @block.vector
    def _(v):
        v.wait_ge(s_v.h, vfin)
        v.tensor_copy(out=k.s0b[0][:, :], in_=k.s0[0][:, :])
        v.tensor_copy(out=k.s1b[0][0:64, :], in_=k.s1[0][0:64, :]).then_inc(s_v.inc(), 1)

    k.stateag_done_v = s_v.v


def emit_epilogue(k):
    nc, block, pb = k.nc, k.block, k.pb
    s_mm, s_v, s_sc = k.s_mm, k.s_v, k.s_sc

    scv0 = s_sc.v

    @block.scalar
    def _(sc):
        sc.wait_ge(s_sc.h, s_sc.v)
        sc.wait_ge(s_v.h, k.chain2_hd)
        sc.activation(out=k.stat[:, 64:128], in_=k.stat[:, 0:64], func=AF.Ln,
                      scale=1.0 / DV, bias=k.fconst[:, 1:2]).then_inc(s_sc.inc(), 1)
        sc.wait_ge(s_sc.h, s_sc.v)
        sc.activation(out=k.stat[:, 0:64], in_=k.stat[:, 64:128], func=AF.Exp,
                      scale=-0.5).then_inc(s_sc.inc(), 1)
    rs_done_sc = s_sc.v

    k.scopeD.close()
    scope1 = ExitStack()
    sb1 = lambda *a: scope1.enter_context(nc.sbuf_tensor(*a))
    wgb = sb1("wgb", [128, 8 * VD], BF16)
    otile = sb1("otile", [128, 2 * VD], BF16)
    gs = sb1("gs", [128, 2 * VD], BF16)
    og = sb1("og", [128, 2 * VD], BF16)
    t1s = sb1("t1s", [128, VD], F32)
    s_lwg2 = k.sem("s_lwg2")
    s_lo = [k.sem("s_lo0"), k.sem("s_lo1")]
    s_og = [k.sem("s_og0"), k.sem("s_og1")]

    @block.gpsimd
    def _(g):
        g.wait_ge(s_v.h, k.chain2_hd)
        g.wait_ge(k.s_so[0].h, k.s_so[0].v)
        g.wait_ge(k.s_so[1].h, k.s_so[1].v)
        for kc in range(8):
            g.dma_start(out=wgb[:, kc * VD:(kc + 1) * VD],
                        in_=k.d["W_g"][128 * kc:128 * (kc + 1), :]).then_inc(s_lwg2.inc(16), 16)

    ot_ready = {}
    og_done_v = {}
    tile_v_done = {}

    @block.sync
    def _(sync):
        sync.wait_ge(s_v.h, k.chain2_hd)
        sync.wait_ge(k.s_so[0].h, k.s_so[0].v)
        sync.wait_ge(k.s_so[1].h, k.s_so[1].v)

    for t in range(NT):
        par = t % 2

        @block.sync
        def _(sync, t=t, par=par):
            if t >= 2:
                sync.wait_ge(s_v.h, tile_v_done[t - 2])
            sync.dma_start(out=otile[:, par * VD:(par + 1) * VD],
                           in_=k.dr["sp_o"][128 * t:128 * (t + 1), :]).then_inc(s_lo[par].inc(16), 16)

        vprev_g = s_v.v

        @block.tensor
        def _(tt_, t=t, par=par):
            tt_.wait_ge(s_lwg2.h, s_lwg2.v)
            tt_.wait_ge(s_v.h, vprev_g)
            for nc3 in range(3):
                for kc in range(8):
                    mi = tt_.matmul(pb[nc3][:, 0:512],
                                    k.xs[:, kc * TOK + 128 * t:kc * TOK + 128 * (t + 1)],
                                    wgb[:, kc * VD + 512 * nc3:kc * VD + 512 * (nc3 + 1)],
                                    start=(kc == 0), stop=(kc == 7))
            mi.then_inc(s_mm.inc(), 1)

        mg = s_mm.v

        @block.scalar
        def _(sc, t=t, par=par):
            sc.wait_ge(s_mm.h, mg)
            if t == 0:
                sc.wait_ge(s_sc.h, rs_done_sc)
            for nc3 in range(3):
                si = sc.activation(out=gs[:, par * VD + 512 * nc3:par * VD + 512 * (nc3 + 1)],
                                   in_=pb[nc3][:, 0:512], func=AF.Silu)
            si.then_inc(s_sc.inc(), 1)

        sg = s_sc.v

        @block.vector
        def _(v, t=t, par=par):
            v.wait_ge(s_sc.h, sg)
            v.wait_ge(s_lo[par].h, s_lo[par].v)
            if t >= 2:
                v.wait_ge(s_og[par].h, 16 * (t // 2))
            for h in range(HH):
                vi = v.scalar_tensor_tensor(out=t1s[:, DV * h:DV * (h + 1)],
                                            in0=otile[:, par * VD + DV * h:par * VD + DV * (h + 1)],
                                            scalar=k.stat[:, 4 * t + h:4 * t + h + 1],
                                            in1=k.gnwb[:, :], op0=OP.mult, op1=OP.mult)
            vi.then_inc(s_v.inc(), 1)
            v.wait_ge(s_v.h, s_v.v)
            v.tensor_tensor(out=og[:, par * VD:(par + 1) * VD], in0=t1s[:, :],
                            in1=gs[:, par * VD:(par + 1) * VD], op=OP.mult).then_inc(s_v.inc(), 1)
        tile_v_done[t] = s_v.v

        @block.gpsimd
        def _(g, t=t, par=par):
            g.wait_ge(s_v.h, tile_v_done[t])
            g.dma_start(out=k.dr["sp_og"][128 * t:128 * (t + 1), :],
                        in_=og[:, par * VD:(par + 1) * VD]).then_inc(s_og[par].inc(16), 16)

    c1_v = s_v.v
    c1_og = (s_og[0].v, s_og[1].v)

    scope1.close()
    k.scope_xs.close()
    scope2 = ExitStack()
    k.scope2 = scope2
    sb2 = lambda *a: scope2.enter_context(nc.sbuf_tensor(*a))
    wsob = sb2("wsob", [128, 12 * DM], BF16)
    woutb = sb2("woutb", [128, 16 * DM], BF16)
    ogfm = sb2("ogfm", [128, 12 * TOK], BF16)
    ybuf = sb2("ybuf", [128, 8 * TOK], BF16)
    xres = sb2("xres", [128, 2 * DM], F32)
    pre1 = sb2("pre1", [128, DM], F32)
    sqscr = sb2("sqscr", [128, DM], F32)
    x1buf = sb2("x1buf", [128, 2 * DM], F32)
    lnscr = sb2("lnscr", [128, 16], F32)
    s_lso = k.sem("s_lso")
    s_ogt = k.sem("s_ogt")
    s_lxr = [k.sem("s_lxr0"), k.sem("s_lxr1")]
    s_x1 = [k.sem("s_x10"), k.sem("s_x11")]

    @block.gpsimd
    def _(g):
        g.wait_ge(s_v.h, c1_v)
        for kc in range(12):
            g.dma_start(out=wsob[:, kc * DM:(kc + 1) * DM],
                        in_=k.d["W_so"][128 * kc:128 * (kc + 1), :]).then_inc(s_lso.inc(16), 16)
        for kc in range(16):
            g.dma_start(out=woutb[:, kc * DM:(kc + 1) * DM],
                        in_=k.d["W_out"][128 * kc:128 * (kc + 1), :]).then_inc(s_lso.inc(16), 16)

    @block.sync
    def _(sync):
        sync.wait_ge(s_og[0].h, c1_og[0])
        sync.wait_ge(s_og[1].h, c1_og[1])
        sync.wait_ge(s_v.h, c1_v)
        for ic in range(12):
            sync.dma_start(out=ogfm[:, ic * TOK:(ic + 1) * TOK],
                           in_=k.dr["sp_og"][0:TOK, 128 * ic:128 * (ic + 1)],
                           transpose=True).then_inc(s_ogt.inc(16), 16)

    yjobs = [(oc, tg) for oc in range(8) for tg in range(4)]
    yev = {}
    for i, (oc, tg) in enumerate(yjobs):
        bank = pb[i % 2]

        @block.tensor
        def _(tt_, i=i, oc=oc, tg=tg, bank=bank):
            if i == 0:
                tt_.wait_ge(s_lso.h, s_lso.v)
                tt_.wait_ge(s_ogt.h, s_ogt.v)
            if i >= 2:
                tt_.wait_ge(s_v.h, yev[i - 2])
            for kc in range(12):
                mi = tt_.matmul(bank[:, 0:512],
                                wsob[:, kc * DM + oc * 128:kc * DM + (oc + 1) * 128],
                                ogfm[:, kc * TOK + 512 * tg:kc * TOK + 512 * (tg + 1)],
                                start=(kc == 0), stop=(kc == 11))
            mi.then_inc(s_mm.inc(), 1)

        my = s_mm.v

        @block.vector
        def _(v, i=i, oc=oc, tg=tg, bank=bank, my=my):
            v.wait_ge(s_mm.h, my)
            v.tensor_copy(out=ybuf[:, oc * TOK + 512 * tg:oc * TOK + 512 * (tg + 1)],
                          in_=bank[:, 0:512]).then_inc(s_v.inc(), 1)
        yev[i] = s_v.v

    y_done_v = s_v.v

    x1_v_done = {}
    for t in range(NT):
        par = t % 2

        @block.sync
        def _(sync, t=t, par=par):
            if t >= 2:
                sync.wait_ge(s_v.h, x1_v_done[t - 2])
            sync.dma_start(out=xres[:, par * DM:(par + 1) * DM],
                           in_=k.d["x_pad"][128 * t + 1:128 * (t + 1) + 1, :]).then_inc(s_lxr[par].inc(16), 16)

        vprev_a = s_v.v

        @block.tensor
        def _(tt_, t=t, par=par):
            tt_.wait_ge(s_v.h, vprev_a)
            for nc2 in range(2):
                for kc in range(16):
                    srcb = ybuf if kc < 8 else k.zs
                    kcc = kc % 8
                    mi = tt_.matmul(pb[2 + 2 * par + nc2][:, 0:512],
                                    srcb[:, kcc * TOK + 128 * t:kcc * TOK + 128 * (t + 1)],
                                    woutb[:, kc * DM + 512 * nc2:kc * DM + 512 * (nc2 + 1)],
                                    start=(kc == 0), stop=(kc == 15))
            mi.then_inc(s_mm.inc(), 1)

        ma = s_mm.v

        @block.vector
        def _(v, t=t, par=par, ma=ma):
            v.wait_ge(s_mm.h, ma)
            v.wait_ge(s_lxr[par].h, s_lxr[par].v)
            if t >= 2:
                v.wait_ge(s_x1[par].h, 16 * (t // 2))
            v.tensor_tensor(out=pre1[:, 0:512], in0=pb[2 + 2 * par][:, 0:512],
                            in1=xres[:, par * DM:par * DM + 512], op=OP.add)
            v.tensor_tensor(out=pre1[:, 512:1024], in0=pb[3 + 2 * par][:, 0:512],
                            in1=xres[:, par * DM + 512:par * DM + 1024], op=OP.add).then_inc(s_v.inc(), 1)
            v.wait_ge(s_v.h, s_v.v)
            v.tensor_reduce(out=lnscr[:, 0:1], in_=pre1[:, :], axis=AX, op=OP.add).then_inc(s_v.inc(), 1)

        vpre = s_v.v

        @block.scalar
        def _(sc, t=t):
            sc.wait_ge(s_v.h, vpre)
            sc.wait_ge(s_sc.h, s_sc.v)
            sc.activation(out=sqscr[:, :], in_=pre1[:, :], func=AF.Square,
                          accum_out=lnscr[:, 1:2]).then_inc(s_sc.inc(), 1)

        sqd = s_sc.v

        @block.vector
        def _(v, t=t):
            v.wait_ge(s_sc.h, sqd)
            v.tensor_scalar(out=lnscr[:, 2:3], in0=lnscr[:, 0:1], scalar1=1.0 / DM,
                            scalar2=None, op0=OP.mult)
            v.tensor_scalar(out=lnscr[:, 3:4], in0=lnscr[:, 1:2], scalar1=1.0 / DM,
                            scalar2=None, op0=OP.mult).then_inc(s_v.inc(), 1)
            v.wait_ge(s_v.h, s_v.v)
            v.tensor_tensor(out=lnscr[:, 4:5], in0=lnscr[:, 2:3], in1=lnscr[:, 2:3], op=OP.mult).then_inc(s_v.inc(), 1)
            v.wait_ge(s_v.h, s_v.v)
            v.tensor_tensor(out=lnscr[:, 5:6], in0=lnscr[:, 3:4], in1=lnscr[:, 4:5],
                            op=OP.subtract).then_inc(s_v.inc(), 1)

        vvar = s_v.v

        @block.scalar
        def _(sc, t=t):
            sc.wait_ge(s_v.h, vvar)
            sc.activation(out=lnscr[:, 6:7], in_=lnscr[:, 5:6], func=AF.Ln,
                          bias=k.fconst[:, 1:2]).then_inc(s_sc.inc(), 1)
            sc.wait_ge(s_sc.h, s_sc.v)
            sc.activation(out=lnscr[:, 7:8], in_=lnscr[:, 6:7], func=AF.Exp,
                          scale=-0.5).then_inc(s_sc.inc(), 1)

        srs = s_sc.v

        @block.vector
        def _(v, t=t, par=par):
            v.wait_ge(s_sc.h, srs)
            v.tensor_tensor(out=lnscr[:, 8:9], in0=lnscr[:, 2:3], in1=lnscr[:, 7:8], op=OP.mult).then_inc(s_v.inc(), 1)
            v.wait_ge(s_v.h, s_v.v)
            v.tensor_scalar(out=lnscr[:, 9:10], in0=lnscr[:, 8:9], scalar1=-1.0,
                            scalar2=None, op0=OP.mult).then_inc(s_v.inc(), 1)
            v.wait_ge(s_v.h, s_v.v)
            v.tensor_scalar(out=x1buf[:, par * DM:(par + 1) * DM], in0=pre1[:, :],
                            scalar1=lnscr[:, 7:8], scalar2=lnscr[:, 9:10],
                            op0=OP.mult, op1=OP.add).then_inc(s_v.inc(), 1)
            v.wait_ge(s_v.h, s_v.v)
            v.tensor_reduce(out=k.stat[:, 160 + t:161 + t], in_=x1buf[:, par * DM:(par + 1) * DM],
                            axis=AX, op=OP.max, apply_absolute_value=True).then_inc(s_v.inc(), 1)
        x1_v_done[t] = s_v.v

        @block.gpsimd
        def _(g, t=t, par=par):
            g.wait_ge(s_v.h, x1_v_done[t])
            g.dma_start(out=k.dr["sp_x1"][128 * t:128 * (t + 1), :],
                        in_=x1buf[:, par * DM:(par + 1) * DM]).then_inc(s_x1[par].inc(16), 16)

    k.epi_v = s_v.v
    k.epi_x1 = (s_x1[0].v, s_x1[1].v)
    k.s_x1_sems = s_x1

    dbx1 = k.dbg_tensor("x1", [TOK, DM])
    if dbx1 is not None:
        @block.gpsimd
        def _(g):
            g.wait_ge(s_x1[0].h, s_x1[0].v)
            g.wait_ge(s_x1[1].h, s_x1[1].v)
            g.dma_start(out=dbx1, in_=k.dr["sp_x1"][:, :]).then_inc(k.s_sp.inc(16), 16)


def emit_ffn(k):
    nc, block, pb = k.nc, k.block, k.pb
    s_mm, s_v, s_sc = k.s_mm, k.s_v, k.s_sc

    def agather(cin, cout, sem):
        k._agctr = getattr(k, "_agctr", 0) + 1
        csem = k.sem(f"s_agc{k._agctr}")
        k.block.materialize(k.realblock)

        @k.realblock.gpsimd
        def _(g):
            g.wait_ge(sem.h, sem.v)
            g.collective_compute(
                "AllGather", mybir.AluOpType.bypass,
                replica_groups=[[0, 1, 2, 3, 4, 5, 6, 7]],
                ins=[cin.opt()], outs=[cout.opt()],
            ).then_inc(csem.inc(1), 1)
        return csem

    def bcast_scalar(pairs, waitv):
        @block.tensor
        def _(t):
            t.wait_ge(s_v.h, waitv)
            for i, (srcc, _) in enumerate(pairs):
                mi = t.matmul(pb[6][:, 8 + i:9 + i], k.consts[0:1, 128:256],
                              k.sval[0:1, srcc:srcc + 1], start=True, stop=True)
            mi.then_inc(s_mm.inc(), 1)
        mmv = s_mm.v

        @block.vector
        def _(v):
            v.wait_ge(s_mm.h, mmv)
            for i, (_, dstc) in enumerate(pairs):
                vi = v.tensor_copy(out=k.sval[:, dstc:dstc + 1], in_=pb[6][:, 8 + i:9 + i])
            vi.then_inc(s_v.inc(), 1)
        return s_v.v

    k.scope2.close()
    scopeF = ExitStack()
    k.scopeF = scopeF
    wsh = scopeF.enter_context(nc.sbuf_tensor("wsh", [128, FI], F32))
    wi16 = scopeF.enter_context(nc.sbuf_tensor("wi16", [128, FI], I16))
    wdqb = scopeF.enter_context(nc.sbuf_tensor("wdqb", [128, FI], BF16))
    wsc = scopeF.enter_context(nc.sbuf_tensor("wsc", [128, 3 * 96], F32))
    wdqt = scopeF.enter_context(nc.sbuf_tensor("wdqt", [128, FI], BF16))
    s_lsh = k.sem("s_lsh")
    s_wq = k.sem("s_wq")
    block.materialize(k.realblock)
    dq = k.realblock

    for wt in range(7):
        if wt < 3:
            srcap = k.d["wg_shard"][128 * wt:128 * (wt + 1), :]
            ncols, nblk, dstrow = DM, 32, 128 * wt
        elif wt < 6:
            srcap = k.d["wu_shard"][128 * (wt - 3):128 * (wt - 2), :]
            ncols, nblk, dstrow = DM, 32, 384 + 128 * (wt - 3)
        else:
            srcap = k.d["wd_shard"]
            ncols, nblk, dstrow = FI, 96, 768

        vW = s_v.v

        @dq.sync
        def _(sync, wt=wt, srcap=srcap, ncols=ncols, vW=vW):
            if wt > 0:
                sync.wait_ge(s_v.h, vW)
            else:
                sync.wait_ge(s_v.h, k.epi_v)
                sync.wait_ge(k.s_x1_sems[0].h, k.epi_x1[0])
                sync.wait_ge(k.s_x1_sems[1].h, k.epi_x1[1])
            sync.dma_start(out=wsh[:, 0:ncols], in_=srcap).then_inc(s_lsh.inc(16), 16)

        ldW = s_lsh.v

        @dq.vector
        def _(v, wt=wt, ncols=ncols, nblk=nblk, ldW=ldW):
            v.wait_ge(s_lsh.h, ldW)
            if wt == 0:
                v.wait_ge(s_v.h, k.epi_v)
            v.tensor_reduce(out=wsc[:, 0:nblk], in_=wsh[:, 0:ncols].rearrange("p (a b) -> p a b", b=32),
                            axis=AX, op=OP.max, apply_absolute_value=True).then_inc(s_v.inc(), 1)
            v.wait_ge(s_v.h, s_v.v)
            v.tensor_scalar(out=wsc[:, 96:96 + nblk], in0=wsc[:, 0:nblk], scalar1=1.0 / 7.0,
                            scalar2=1e-12, op0=OP.mult, op1=OP.max).then_inc(s_v.inc(), 1)
            v.wait_ge(s_v.h, s_v.v)
            v.reciprocal(out=wsc[:, 192:192 + nblk], in_=wsc[:, 96:96 + nblk]).then_inc(s_v.inc(), 1)
            v.wait_ge(s_v.h, s_v.v)
            for b in range(nblk):
                vi = v.tensor_scalar(out=wi16[:, 32 * b:32 * (b + 1)], in0=wsh[:, 32 * b:32 * (b + 1)],
                                     scalar1=wsc[:, 192 + b:193 + b], scalar2=None, op0=OP.mult)
            vi.then_inc(s_v.inc(), 1)
            v.wait_ge(s_v.h, s_v.v)
            v.wait_ge(s_mm.h, s_mm.v)
            for b in range(nblk):
                vi = v.tensor_scalar(out=wdqb[:, 32 * b:32 * (b + 1)], in0=wi16[:, 32 * b:32 * (b + 1)],
                                     scalar1=wsc[:, 96 + b:97 + b], scalar2=None, op0=OP.mult)
            vi.then_inc(s_v.inc(), 1)

        wq_v = s_v.v

        nic = ncols // 128
        mm0 = s_mm.v
        v0 = s_v.v

        @dq.tensor
        def _(t, wq_v=wq_v, nic=nic, mm0=mm0, v0=v0):
            t.wait_ge(s_v.h, wq_v)
            for icc in range(nic):
                if icc >= 2:
                    t.wait_ge(s_v.h, v0 + icc - 1)
                t.transpose(pb[6 + icc % 2][:, 0:64].bitcast(BF16),
                            wdqb[:, 128 * icc:128 * (icc + 1)],
                            k.constsb[:, 256:384]).then_inc(s_mm.inc(), 1)

        @dq.vector
        def _(v, wt=wt, nic=nic, mm0=mm0):
            v.wait_ge(s_wq.h, 16 * wt)
            for icc in range(nic):
                v.wait_ge(s_mm.h, mm0 + icc + 1)
                v.tensor_copy(out=wdqt[:, 128 * icc:128 * (icc + 1)],
                              in_=pb[6 + icc % 2][:, 0:64].bitcast(BF16)).then_inc(s_v.inc(), 1)

        wqt_v = s_v.v

        @dq.gpsimd
        def _(g, wt=wt, nic=nic, wqt_v=wqt_v):
            g.wait_ge(s_v.h, wqt_v)
            flat1d = k.dr["cc_wg_in"].rearrange("r c -> (r c)")
            flat1dd = k.dr["cc_wd_in"].rearrange("r c -> (r c)")
            if wt < 3:
                dv = flat1d[0:393216].rearrange("(icc p o) -> p icc o", p=128, o=384)
                g.dma_start(out=dv[:, :, 128 * wt:128 * (wt + 1)][:, 0:8, :],
                            in_=wdqt[:, 0:1024].rearrange("p (icc o) -> p icc o", o=128),
                            allow_slow_non_contiguous=True).then_inc(s_wq.inc(16), 16)
            elif wt < 6:
                dv = flat1d[393216:786432].rearrange("(icc p o) -> p icc o", p=128, o=384)
                g.dma_start(out=dv[:, :, 128 * (wt - 3):128 * (wt - 2)][:, 0:8, :],
                            in_=wdqt[:, 0:1024].rearrange("p (icc o) -> p icc o", o=128),
                            allow_slow_non_contiguous=True).then_inc(s_wq.inc(16), 16)
            else:
                dv = flat1dd[0:393216].rearrange("(icc p o) -> p icc o", p=128, o=128)
                g.dma_start(out=dv[:, 0:24, :],
                            in_=wdqt[:, 0:3072].rearrange("p (icc o) -> p icc o", o=128),
                            allow_slow_non_contiguous=True).then_inc(s_wq.inc(16), 16)

        if wt == 5:
            k.s_wqc = agather(k.dr["cc_wg_in"], k.dr["cc_wg_out"], s_wq)

    k.wq_dma_v = s_wq.v
    k.s_wdc = agather(k.dr["cc_wd_in"], k.dr["cc_wd_out"], s_wq)
    k.s_wq = s_wq
    scopeF.close()

    s_ag1 = k.sem("s_ag1")
    vx = s_v.v

    @block.vector
    def _(v):
        v.wait_ge(s_v.h, vx)
        v.tensor_reduce(out=k.sval[:, 20:21], in_=k.stat[:, 160:176], axis=AX,
                        op=OP.max).then_inc(s_v.inc(), 1)

    vmx = s_v.v

    @block.tensor
    def _(t):
        t.wait_ge(s_v.h, vmx)
        t.transpose(pb[6][0:1, 0:128], k.sval[:, 20:21], k.consts[:, 0:128]).then_inc(s_mm.inc(), 1)

    mmx = s_mm.v

    @block.vector
    def _(v):
        v.wait_ge(s_mm.h, mmx)
        v.memset(k.sval[0:1, 22:23], 0.0)
        v.tensor_reduce(out=k.sval[0:1, 21:22], in_=pb[6][0:1, 0:128], axis=AX,
                        op=OP.max).then_inc(s_v.inc(), 1)

    vgm = s_v.v

    @block.gpsimd
    def _(g):
        g.wait_ge(s_v.h, vgm)
        g.dma_start(out=k.dr["cc_s0_in"], in_=k.sval[0:1, 21:23]).then_inc(s_ag1.inc(16), 16)

    s_ag1c = agather(k.dr["cc_s0_in"], k.dr["cc_s0_out"], s_ag1)

    def max_from_ag(cout, agsem, dstcol, n=8):
        k._rdctr = getattr(k, "_rdctr", 0) + 1
        s_rd = k.sem(f"s_rd{dstcol}_{k._rdctr}")

        @block.sync
        def _(sync):
            sync.wait_ge(agsem.h, agsem.v)
            sync.dma_start(out=k.sval[0:1, 24:24 + 2 * n], in_=cout).then_inc(s_rd.inc(16), 16)

        @block.vector
        def _(v):
            v.wait_ge(s_rd.h, s_rd.v)
            g2 = k.sval[0:1, 24:24 + 2 * n].rearrange("p (a b) -> p b a", b=2)
            v.tensor_reduce(out=k.sval[0:1, dstcol:dstcol + 1], in_=g2[:, 0:1, :], axis=AX, op=OP.max)
            v.tensor_reduce(out=k.sval[0:1, dstcol + 1:dstcol + 2], in_=g2[:, 1:2, :],
                            axis=AX, op=OP.max).then_inc(s_v.inc(), 1)
        return s_v.v

    max_from_ag(k.dr["cc_s0_out"], s_ag1c, 16)

    @block.vector
    def _(v):
        v.wait_ge(s_v.h, s_v.v)
        v.tensor_scalar(out=k.sval[0:1, 19:20], in0=k.sval[0:1, 16:17],
                        scalar1=1.0 / QMAX, scalar2=None, op0=OP.mult).then_inc(s_v.inc(), 1)
        v.wait_ge(s_v.h, s_v.v)
        v.reciprocal(out=k.sval[0:1, 20:21], in_=k.sval[0:1, 19:20]).then_inc(s_v.inc(), 1)

    bc_sx_v = bcast_scalar([(19, 8), (20, 9)], s_v.v)

    scope3 = ExitStack()
    k.scopeF = scope3
    sb3 = lambda *a: scope3.enter_context(nc.sbuf_tensor(*a))
    xqfm = sb3("xqfm", [128, 8 * TOK], BF16)
    wupT = sb3("wupT", [128, 8 * FI], BF16)
    wgT = sb3("wgT", [128, 8 * FI], BF16)
    x1t = sb3("x1t", [128, 2 * DM], F32)
    xqi = sb3("xqi", [128, DM], I16)
    xqb = sb3("xqb", [128, 2 * DM], BF16)
    evb = sb3("evb", [128, 2 * FI], F32)
    mxa = sb3("mxa", [128, 8], F32)
    s_lx1 = [k.sem("s_lx1a"), k.sem("s_lx1b")]
    s_xq = [k.sem("s_xqa"), k.sem("s_xqb")]
    s_xqt = k.sem("s_xqt")
    s_lwt = k.sem("s_lwt")
    s_spu = [k.sem("s_spua"), k.sem("s_spub")]
    s_spg = [k.sem("s_spga"), k.sem("s_spgb")]

    @block.scalar
    def _(sc):
        sc.wait_ge(k.s_wqc.h, k.s_wqc.v)
        ccf = k.dr["cc_wg_out"].rearrange("(r z) c -> r (z c)", z=768)
        for ic in range(8):
            gsrc = ccf[:, 0:393216].rearrange("r (i o) -> i r o", o=384)
            usrc = ccf[:, 393216:786432].rearrange("r (i o) -> i r o", o=384)
            sc.dma_start(out=wgT[:, ic * FI:(ic + 1) * FI].rearrange("p (r o) -> p r o", o=384),
                         in_=gsrc[128 * ic:128 * (ic + 1), :, :]).then_inc(s_lwt.inc(16), 16)
            sc.dma_start(out=wupT[:, ic * FI:(ic + 1) * FI].rearrange("p (r o) -> p r o", o=384),
                         in_=usrc[128 * ic:128 * (ic + 1), :, :]).then_inc(s_lwt.inc(16), 16)

    xq_v = {}
    for t in range(NT):
        par = t % 2

        @block.sync
        def _(sync, t=t, par=par):
            if t >= 2:
                sync.wait_ge(s_v.h, xq_v[t - 2])
            elif t == 0:
                sync.wait_ge(k.s_wq.h, k.wq_dma_v)
            sync.dma_start(out=x1t[:, par * DM:(par + 1) * DM],
                           in_=k.dr["sp_x1"][128 * t:128 * (t + 1), :]).then_inc(s_lx1[par].inc(16), 16)

        vt0 = s_v.v

        @block.vector
        def _(v, t=t, par=par, vt0=vt0):
            v.wait_ge(s_lx1[par].h, s_lx1[par].v)
            v.wait_ge(s_v.h, vt0)
            if t == 0:
                v.wait_ge(s_v.h, bc_sx_v)
            if t >= 2:
                v.wait_ge(s_xq[par].h, 16 * (t // 2))
            v.tensor_scalar(out=xqi[:, :], in0=x1t[:, par * DM:(par + 1) * DM],
                            scalar1=k.sval[:, 9:10], scalar2=None, op0=OP.mult).then_inc(s_v.inc(), 1)
            v.wait_ge(s_v.h, s_v.v)
            v.tensor_copy(out=xqb[:, par * DM:(par + 1) * DM], in_=xqi[:, :]).then_inc(s_v.inc(), 1)
        xq_v[t] = s_v.v

        @block.gpsimd
        def _(g, t=t, par=par):
            g.wait_ge(s_v.h, xq_v[t])
            g.dma_start(out=k.dr["sp_xq"][128 * t:128 * (t + 1), :],
                        in_=xqb[:, par * DM:(par + 1) * DM]).then_inc(s_xq[par].inc(16), 16)

    @block.scalar
    def _(sc):
        sc.wait_ge(s_xq[0].h, s_xq[0].v)
        sc.wait_ge(s_xq[1].h, s_xq[1].v)
        for ic in range(8):
            sc.dma_start(out=xqfm[:, ic * TOK:(ic + 1) * TOK],
                         in_=k.dr["sp_xq"][0:TOK, 128 * ic:128 * (ic + 1)],
                         transpose=True).then_inc(s_xqt.inc(16), 16)

    vinit = s_v.v

    @block.vector
    def _(v):
        v.wait_ge(s_v.h, vinit)
        v.memset(mxa[:, 0:4], 0.0).then_inc(s_v.inc(), 1)

    k.block.set_direct("tensor", "vector", "gpsimd", "sync", "scalar")
    vred = {}
    evc = {}
    runpar = [0, 0]
    nj = 0
    for t in range(NT):
        for w in range(2):
            par = t % 2
            wT = wupT if w == 0 else wgT
            spd = k.dr["sp_up"] if w == 0 else k.dr["sp_gate"]
            spsem = s_spu if w == 0 else s_spg
            for nc6 in range(6):
                bank = pb[nj % 2]

                @block.tensor
                def _(tt_, t=t, w=w, nc6=nc6, nj=nj, wT=wT, bank=bank):
                    if nj == 0:
                        tt_.wait_ge(s_xqt.h, s_xqt.v)
                        tt_.wait_ge(s_lwt.h, s_lwt.v)
                    if nj >= 2:
                        tt_.wait_ge(s_v.h, vred[nj - 2])
                        tt_.wait_ge(s_sc.h, evc[nj - 2])
                    for kc in range(8):
                        mi = tt_.matmul(bank[:, 0:512],
                                        xqfm[:, kc * TOK + 128 * t:kc * TOK + 128 * (t + 1)],
                                        wT[:, kc * FI + 512 * nc6:kc * FI + 512 * (nc6 + 1)],
                                        start=(kc == 0), stop=(kc == 7))
                    mi.then_inc(s_mm.inc(), 1)

                mmj = s_mm.v
                rsrc, rdst = runpar[w], 1 - runpar[w]
                runpar[w] = rdst

                vj0 = s_v.v

                @block.vector
                def _(v, w=w, bank=bank, mmj=mmj, rsrc=rsrc, rdst=rdst, vj0=vj0):
                    v.wait_ge(s_mm.h, mmj)
                    v.wait_ge(s_v.h, vj0)
                    v.tensor_reduce(out=mxa[:, 4:5], in_=bank[:, 0:512], axis=AX,
                                    op=OP.max, apply_absolute_value=True).then_inc(s_v.inc(), 1)
                    v.wait_ge(s_v.h, s_v.v)
                    v.tensor_tensor(out=mxa[:, 2 * rdst + w:2 * rdst + w + 1],
                                    in0=mxa[:, 2 * rsrc + w:2 * rsrc + w + 1],
                                    in1=mxa[:, 4:5], op=OP.max).then_inc(s_v.inc(), 1)
                vred[nj] = s_v.v

                jpar = (2 * t + w) % 2

                @block.scalar
                def _(sc, t=t, w=w, jpar=jpar, nc6=nc6, bank=bank, mmj=mmj):
                    sc.wait_ge(s_mm.h, mmj)
                    if nc6 == 0 and t >= 1:
                        sc.wait_ge(spsem[(t - 1) % 2].h, 16 * ((t - 1) // 2 + 1))
                    sc.copy(out=evb[:, jpar * FI + 512 * nc6:jpar * FI + 512 * (nc6 + 1)],
                            in_=bank[:, 0:512]).then_inc(s_sc.inc(), 1)
                evc[nj] = s_sc.v
                nj += 1

            jpar = (2 * t + w) % 2

            @block.gpsimd
            def _(g, t=t, jpar=jpar, spd=spd, spsem=spsem, scj=evc[nj - 1]):
                g.wait_ge(s_sc.h, scj)
                g.dma_start(out=spd[128 * t:128 * (t + 1), :],
                            in_=evb[:, jpar * FI:(jpar + 1) * FI]).then_inc(spsem[t % 2].inc(16), 16)


    s_ag2 = k.sem("s_ag2")
    vfin = s_v.v

    @block.vector
    def _(v):
        v.wait_ge(s_v.h, vfin)
        v.tensor_copy(out=k.sval[0:1, 24:25], in_=mxa[0:1, 0:1])

    @block.tensor
    def _(t):
        t.wait_ge(s_v.h, vfin)
        t.transpose(pb[6][0:2, 0:128], mxa[:, 0:2], k.consts[:, 0:128]).then_inc(s_mm.inc(), 1)

    mfin = s_mm.v

    @block.vector
    def _(v):
        v.wait_ge(s_mm.h, mfin)
        v.tensor_reduce(out=k.sval[0:2, 40:41], in_=pb[6][0:2, 0:128], axis=AX,
                        op=OP.max).then_inc(s_v.inc(), 1)

    vag2a = s_v.v

    @block.tensor
    def _(t):
        t.wait_ge(s_v.h, vag2a)
        t.transpose(pb[7][0:1, 0:2], k.sval[0:2, 40:41], k.consts[0:2, 0:2]).then_inc(s_mm.inc(), 1)

    mag2b = s_mm.v

    @block.vector
    def _(v):
        v.wait_ge(s_mm.h, mag2b)
        v.tensor_copy(out=k.sval[0:1, 21:23], in_=pb[7][0:1, 0:2]).then_inc(s_v.inc(), 1)

    vag2 = s_v.v

    @block.gpsimd
    def _(g):
        g.wait_ge(s_v.h, vag2)
        g.dma_start(out=k.dr["cc_s1_in"], in_=k.sval[0:1, 21:23]).then_inc(s_ag2.inc(16), 16)

    s_ag2c = agather(k.dr["cc_s1_in"], k.dr["cc_s1_out"], s_ag2)
    max_from_ag(k.dr["cc_s1_out"], s_ag2c, 16)

    @block.vector
    def _(v):
        v.wait_ge(s_v.h, s_v.v)
        v.tensor_tensor(out=k.sval[0:1, 19:20],
                        in0=k.sval[0:1, 16:17], in1=k.sval[0:1, 8:9], op=OP.mult)
        v.tensor_tensor(out=k.sval[0:1, 20:21], in0=k.sval[0:1, 17:18], in1=k.sval[0:1, 8:9],
                        op=OP.mult).then_inc(s_v.inc(), 1)
        v.wait_ge(s_v.h, s_v.v)
        v.tensor_scalar(out=k.sval[0:1, 21:22], in0=k.sval[0:1, 19:20],
                        scalar1=1.0 / QMAX, scalar2=None, op0=OP.mult)
        v.tensor_scalar(out=k.sval[0:1, 22:23], in0=k.sval[0:1, 20:21],
                        scalar1=1.0 / QMAX, scalar2=None, op0=OP.mult).then_inc(s_v.inc(), 1)
        v.wait_ge(s_v.h, s_v.v)
        v.reciprocal(out=k.sval[0:1, 30:31], in_=k.sval[0:1, 21:22])
        v.reciprocal(out=k.sval[0:1, 31:32], in_=k.sval[0:1, 22:23]).then_inc(s_v.inc(), 1)
        v.wait_ge(s_v.h, s_v.v)
        v.tensor_tensor(out=k.sval[0:1, 23:24], in0=k.sval[0:1, 8:9], in1=k.sval[0:1, 30:31],
                        op=OP.mult)
        v.tensor_tensor(out=k.sval[0:1, 24:25], in0=k.sval[0:1, 8:9], in1=k.sval[0:1, 31:32],
                        op=OP.mult).then_inc(s_v.inc(), 1)

    bc2 = bcast_scalar([(21, 10), (22, 11), (23, 12), (24, 13)], s_v.v)

    scope3.close()
    k.scope_zs.close()
    scope4 = ExitStack()
    k.scopeF = scope4
    sb4 = lambda *a: scope4.enter_context(nc.sbuf_tensor(*a))
    gint = sb4("gint", [128, NT * FI], I16)
    gtile = sb4("gtile", [128, 2 * FI], BF16)
    sgs = sb4("sgs", [128, 2 * FI], F32)
    sgu = sb4("sgu", [128, FI], U16)
    pts = sb4("pts", [128, FI], F32)
    upt = sb4("upt", [128, 2 * FI], BF16)
    upi = sb4("upi", [128, 2 * FI], I16)
    ots = sb4("ots", [128, 2 * FI], F32)
    mxb = sb4("mxb", [128, 8], F32)
    s_lg = [k.sem("s_lga"), k.sem("s_lgb")]
    s_lu = [k.sem("s_lua"), k.sem("s_lub")]
    s_ot = [k.sem("s_ota"), k.sem("s_otb")]

    vinitB = s_v.v

    @block.vector
    def _(v):
        v.wait_ge(s_v.h, vinitB)
        v.wait_ge(s_spu[0].h, s_spu[0].v)
        v.wait_ge(s_spu[1].h, s_spu[1].v)
        v.wait_ge(s_spg[0].h, s_spg[0].v)
        v.wait_ge(s_spg[1].h, s_spg[1].v)
        v.wait_ge(s_xq[0].h, s_xq[0].v)
        v.wait_ge(s_xq[1].h, s_xq[1].v)
        v.memset(mxb[:, 0:4], 0.0).then_inc(s_v.inc(), 1)
    vinitB2 = s_v.v

    runp = [0, 0]
    btile_v = {}
    ptsv = {}
    for t in range(NT):
        par = t % 2

        @block.sync
        def _(sync, t=t, par=par):
            if t >= 2:
                sync.wait_ge(s_v.h, btile_v[t - 2])
            else:
                sync.wait_ge(s_v.h, vinitB2)
            sync.dma_start(out=gtile[:, par * FI:(par + 1) * FI],
                           in_=k.dr["sp_gate"][128 * t:128 * (t + 1), :]).then_inc(s_lg[par].inc(16), 16)

        ldg = s_lg[par].v
        vprevB = btile_v[t - 2] if t >= 2 else vinitB2
        ptsvB = ptsv[t - 1] if t >= 1 else vinitB2

        @block.scalar
        def _(sc, t=t, par=par, ldg=ldg, vprevB=vprevB, ptsvB=ptsvB):
            sc.wait_ge(s_lg[par].h, ldg)
            sc.wait_ge(s_sc.h, s_sc.v)
            sc.wait_ge(s_v.h, vprevB)
            sc.activation(out=gint[:, t * FI:(t + 1) * FI], in_=gtile[:, par * FI:(par + 1) * FI],
                          func=AF.Copy, scale=k.sval[:, 13:14]).then_inc(s_sc.inc(), 1)
            sc.wait_ge(s_sc.h, s_sc.v)
            sc.activation(out=sgs[:, par * FI:(par + 1) * FI], in_=gint[:, t * FI:(t + 1) * FI],
                          func=AF.Sigmoid, scale=k.sval[:, 11:12]).then_inc(s_sc.inc(), 1)
            sc.wait_ge(s_sc.h, s_sc.v)
            sc.wait_ge(s_v.h, ptsvB)
            sc.activation(out=sgu[:, 0:FI], in_=sgs[:, par * FI:(par + 1) * FI],
                          func=AF.Copy, scale=65536.0).then_inc(s_sc.inc(), 1)

        scb = s_sc.v
        rs, rd = runp[0], 1 - runp[0]
        runp[0] = rd

        vb0 = s_v.v

        @block.vector
        def _(v, t=t, par=par, scb=scb, rs=rs, rd=rd, vb0=vb0):
            v.wait_ge(s_sc.h, scb)
            v.wait_ge(s_v.h, vb0)
            v.tensor_tensor(out=pts[:, :], in0=gint[:, t * FI:(t + 1) * FI],
                            in1=sgu[:, 0:FI],
                            op=OP.mult).then_inc(s_v.inc(), 1)
            ptsv[t] = s_v.v
            v.wait_ge(s_v.h, s_v.v)
            v.tensor_reduce(out=mxb[:, 4:5], in_=pts[:, :], axis=AX, op=OP.max,
                            apply_absolute_value=True).then_inc(s_v.inc(), 1)
            v.wait_ge(s_v.h, s_v.v)
            v.tensor_tensor(out=mxb[:, 2 * rd:2 * rd + 1], in0=mxb[:, 2 * rs:2 * rs + 1],
                            in1=mxb[:, 4:5], op=OP.max).then_inc(s_v.inc(), 1)
        btile_v[t] = s_v.v

    s_ag3 = k.sem("s_ag3")
    vfb = s_v.v

    @block.tensor
    def _(t):
        t.wait_ge(s_v.h, vfb)
        t.transpose(pb[6][0:1, 0:128], mxb[:, 0:1], k.consts[:, 0:128]).then_inc(s_mm.inc(), 1)

    mfb = s_mm.v

    @block.vector
    def _(v):
        v.wait_ge(s_mm.h, mfb)
        v.memset(k.sval[0:1, 22:23], 0.0)
        v.tensor_reduce(out=k.sval[0:1, 21:22], in_=pb[6][0:1, 0:128], axis=AX,
                        op=OP.max).then_inc(s_v.inc(), 1)

    vag3 = s_v.v

    @block.gpsimd
    def _(g):
        g.wait_ge(s_v.h, vag3)
        g.dma_start(out=k.dr["cc_s2_in"], in_=k.sval[0:1, 21:23]).then_inc(s_ag3.inc(16), 16)

    s_ag3c = agather(k.dr["cc_s2_in"], k.dr["cc_s2_out"], s_ag3)
    max_from_ag(k.dr["cc_s2_out"], s_ag3c, 16)

    @block.vector
    def _(v):
        v.wait_ge(s_v.h, s_v.v)
        v.tensor_scalar(out=k.sval[0:1, 26:27], in0=k.sval[0:1, 11:12],
                        scalar1=1.0 / 65536.0, scalar2=None, op0=OP.mult).then_inc(s_v.inc(), 1)
        v.wait_ge(s_v.h, s_v.v)
        v.tensor_tensor(out=k.sval[0:1, 27:28], in0=k.sval[0:1, 26:27], in1=k.sval[0:1, 16:17],
                        op=OP.mult).then_inc(s_v.inc(), 1)
        v.wait_ge(s_v.h, s_v.v)
        v.tensor_scalar(out=k.sval[0:1, 28:29], in0=k.sval[0:1, 27:28],
                        scalar1=1.0 / QMAX, scalar2=None, op0=OP.mult).then_inc(s_v.inc(), 1)
        v.wait_ge(s_v.h, s_v.v)
        v.reciprocal(out=k.sval[0:1, 32:33], in_=k.sval[0:1, 28:29]).then_inc(s_v.inc(), 1)
        v.wait_ge(s_v.h, s_v.v)
        v.tensor_tensor(out=k.sval[0:1, 29:30], in0=k.sval[0:1, 26:27], in1=k.sval[0:1, 32:33],
                        op=OP.mult).then_inc(s_v.inc(), 1)
    bc3 = bcast_scalar([(28, 14), (29, 15)], s_v.v)

    dtile_v = {}
    ptsv = {}
    for t in range(NT):
        par = t % 2

        @block.sync
        def _(sync, t=t, par=par):
            if t >= 2:
                sync.wait_ge(s_v.h, dtile_v[t - 2])
            sync.dma_start(out=upt[:, par * FI:(par + 1) * FI],
                           in_=k.dr["sp_up"][128 * t:128 * (t + 1), :]).then_inc(s_lu[par].inc(16), 16)

        ldu = s_lu[par].v
        vprevD = dtile_v[t - 2] if t >= 2 else None
        ptsvD = ptsv[t - 1] if t >= 1 else bc3

        @block.scalar
        def _(sc, t=t, par=par, ldu=ldu, vprevD=vprevD, ptsvD=ptsvD):
            sc.wait_ge(s_sc.h, s_sc.v)
            if vprevD is not None:
                sc.wait_ge(s_v.h, vprevD)
            else:
                sc.wait_ge(s_v.h, bc3)
            sc.activation(out=sgs[:, par * FI:(par + 1) * FI], in_=gint[:, t * FI:(t + 1) * FI],
                          func=AF.Sigmoid, scale=k.sval[:, 11:12]).then_inc(s_sc.inc(), 1)
            sc.wait_ge(s_sc.h, s_sc.v)
            sc.wait_ge(s_v.h, ptsvD)
            sc.activation(out=sgu[:, 0:FI], in_=sgs[:, par * FI:(par + 1) * FI],
                          func=AF.Copy, scale=65536.0).then_inc(s_sc.inc(), 1)
            sc.wait_ge(s_lu[par].h, ldu)
            sc.activation(out=upi[:, par * FI:(par + 1) * FI], in_=upt[:, par * FI:(par + 1) * FI],
                          func=AF.Copy, scale=k.sval[:, 12:13]).then_inc(s_sc.inc(), 1)

        scd = s_sc.v
        rs, rd = runp[1], 1 - runp[1]
        runp[1] = rd

        vd0 = s_v.v

        @block.vector
        def _(v, t=t, par=par, scd=scd, rs=rs, rd=rd, vd0=vd0):
            v.wait_ge(s_sc.h, scd)
            v.wait_ge(s_v.h, vd0)
            v.tensor_tensor(out=pts[:, :], in0=gint[:, t * FI:(t + 1) * FI],
                            in1=sgu[:, 0:FI],
                            op=OP.mult).then_inc(s_v.inc(), 1)
            ptsv[t] = s_v.v
            v.wait_ge(s_v.h, s_v.v)
            v.tensor_scalar(out=gint[:, t * FI:(t + 1) * FI], in0=pts[:, :],
                            scalar1=k.sval[:, 15:16], scalar2=None, op0=OP.mult).then_inc(s_v.inc(), 1)
            v.wait_ge(s_v.h, s_v.v)
            if t >= 2:
                v.wait_ge(s_ot[par].h, 16 * (t // 2))
            v.tensor_tensor(out=ots[:, par * FI:(par + 1) * FI],
                            in0=gint[:, t * FI:(t + 1) * FI],
                            in1=upi[:, par * FI:(par + 1) * FI],
                            op=OP.mult).then_inc(s_v.inc(), 1)
            v.wait_ge(s_v.h, s_v.v)
            v.tensor_reduce(out=mxb[:, 4:5], in_=ots[:, par * FI:(par + 1) * FI], axis=AX,
                            op=OP.max, apply_absolute_value=True).then_inc(s_v.inc(), 1)
            v.wait_ge(s_v.h, s_v.v)
            v.tensor_tensor(out=mxb[:, 2 * rd + 1:2 * rd + 2], in0=mxb[:, 2 * rs + 1:2 * rs + 2],
                            in1=mxb[:, 4:5], op=OP.max).then_inc(s_v.inc(), 1)
        dtile_v[t] = s_v.v

        @block.gpsimd
        def _(g, t=t, par=par):
            g.wait_ge(s_v.h, dtile_v[t])
            g.dma_start(out=k.dr["sp_oq"][128 * t:128 * (t + 1), :],
                        in_=ots[:, par * FI:(par + 1) * FI]).then_inc(s_ot[par].inc(16), 16)

    s_ag4 = k.sem("s_ag4")
    vfd = s_v.v

    @block.tensor
    def _(t):
        t.wait_ge(s_v.h, vfd)
        t.transpose(pb[6][0:1, 0:128], mxb[:, 1:2], k.consts[:, 0:128]).then_inc(s_mm.inc(), 1)

    mfd = s_mm.v

    @block.vector
    def _(v):
        v.wait_ge(s_mm.h, mfd)
        v.memset(k.sval[0:1, 22:23], 0.0)
        v.tensor_reduce(out=k.sval[0:1, 21:22], in_=pb[6][0:1, 0:128], axis=AX,
                        op=OP.max).then_inc(s_v.inc(), 1)

    vag4 = s_v.v

    @block.gpsimd
    def _(g):
        g.wait_ge(s_v.h, vag4)
        g.dma_start(out=k.dr["cc_s3_in"], in_=k.sval[0:1, 21:23]).then_inc(s_ag4.inc(16), 16)

    s_ag4c = agather(k.dr["cc_s3_in"], k.dr["cc_s3_out"], s_ag4)
    max_from_ag(k.dr["cc_s3_out"], s_ag4c, 16)

    @block.vector
    def _(v):
        v.wait_ge(s_v.h, s_v.v)
        v.tensor_tensor(out=k.sval[0:1, 26:27], in0=k.sval[0:1, 14:15], in1=k.sval[0:1, 10:11],
                        op=OP.mult).then_inc(s_v.inc(), 1)
        v.wait_ge(s_v.h, s_v.v)
        v.tensor_tensor(out=k.sval[0:1, 27:28], in0=k.sval[0:1, 26:27], in1=k.sval[0:1, 16:17],
                        op=OP.mult).then_inc(s_v.inc(), 1)
        v.wait_ge(s_v.h, s_v.v)
        v.tensor_scalar(out=k.sval[0:1, 28:29], in0=k.sval[0:1, 27:28],
                        scalar1=1.0 / QMAX, scalar2=None, op0=OP.mult).then_inc(s_v.inc(), 1)
        v.wait_ge(s_v.h, s_v.v)
        v.reciprocal(out=k.sval[0:1, 32:33], in_=k.sval[0:1, 28:29]).then_inc(s_v.inc(), 1)
        v.wait_ge(s_v.h, s_v.v)
        v.tensor_tensor(out=k.sval[0:1, 29:30], in0=k.sval[0:1, 26:27], in1=k.sval[0:1, 32:33],
                        op=OP.mult).then_inc(s_v.inc(), 1)
    bc4 = bcast_scalar([(28, 16), (29, 17)], s_v.v)

    scope4.close()
    scope5 = ExitStack()
    k.scopeF = scope5
    sb5 = lambda *a: scope5.enter_context(nc.sbuf_tensor(*a))
    ofm = sb5("ofm", [128, 24 * TOK], BF16)
    wdT = sb5("wdT", [128, 24 * DM], BF16)
    x1e = sb5("x1e", [128, 2 * DM], F32)
    pre2 = sb5("pre2", [128, DM], F32)
    sq2 = sb5("sq2", [128, DM], F32)
    outb = sb5("outb", [128, 2 * DM], F32)
    ln2s = sb5("ln2s", [128, 16], F32)
    oraw = sb5("oraw", [128, 4 * TOK], BF16)
    oq16 = sb5("oq16", [128, 4 * TOK], I16)
    s_oft = [k.sem(f"s_oft{i}") for i in range(4)]
    s_lwd = k.sem("s_lwd")
    s_lx2 = [k.sem("s_lx2a"), k.sem("s_lx2b")]

    @block.sync
    def _(sync):
        sync.wait_ge(k.s_wdc.h, k.s_wdc.v)
        ccf = k.dr["cc_wd_out"].rearrange("(r z) c -> r (z c)", z=384)
        dsrc = ccf[:, 0:393216].rearrange("r (i o) -> i r o", o=128)
        for ic in range(24):
            sync.dma_start(out=wdT[:, ic * DM:(ic + 1) * DM].rearrange("p (r o) -> p r o", o=128),
                           in_=dsrc[128 * ic:128 * (ic + 1), :, :]).then_inc(s_lwd.inc(16), 16)

    ofmcp = {}
    oqsc = {}
    for ic in range(26):
        par = ic % 4

        @block.scalar
        def _(sc, ic=ic, par=par):
            if ic < 24:
                if ic == 0:
                    sc.wait_ge(s_ot[0].h, s_ot[0].v)
                    sc.wait_ge(s_ot[1].h, s_ot[1].v)
                if ic >= 4:
                    sc.wait_ge(s_v.h, ofmcp[ic - 4])
                sc.dma_start(out=oraw[:, par * TOK:(par + 1) * TOK],
                             in_=k.dr["sp_oq"][0:TOK, 128 * ic:128 * (ic + 1)],
                             transpose=True).then_inc(s_oft[par].inc(16), 16)
            if ic >= 2:
                qp = (ic - 2) % 4
                sc.wait_ge(s_oft[qp].h, 16 * ((ic - 2) // 4 + 1))
                if ic == 2:
                    sc.wait_ge(s_v.h, bc4)
                sc.wait_ge(s_sc.h, s_sc.v)
                sc.activation(out=oq16[:, qp * TOK:(qp + 1) * TOK],
                              in_=oraw[:, qp * TOK:(qp + 1) * TOK],
                              func=AF.Copy, scale=k.sval[:, 17:18]).then_inc(s_sc.inc(), 1)
                oqsc[ic - 2] = s_sc.v

        if ic >= 2:
            @block.vector
            def _(v, ic=ic):
                qp = (ic - 2) % 4
                v.wait_ge(s_sc.h, oqsc[ic - 2])
                v.tensor_copy(out=ofm[:, (ic - 2) * TOK:(ic - 1) * TOK],
                              in_=oq16[:, qp * TOK:(qp + 1) * TOK]).then_inc(s_v.inc(), 1)
            ofmcp[ic - 2] = s_v.v
    ofm_done = ofmcp[23]

    e2_v = {}
    for t in range(NT):
        par = t % 2

        @block.sync
        def _(sync, t=t, par=par):
            if t >= 2:
                sync.wait_ge(s_v.h, e2_v[t - 2])
            sync.dma_start(out=x1e[:, par * DM:(par + 1) * DM],
                           in_=k.dr["sp_x1"][128 * t:128 * (t + 1), :]).then_inc(s_lx2[par].inc(16), 16)

        vprev2 = s_v.v

        @block.tensor
        def _(tt_, t=t, par=par, vprev2=vprev2):
            if t == 0:
                tt_.wait_ge(s_v.h, ofm_done)
                tt_.wait_ge(s_lwd.h, s_lwd.v)
            tt_.wait_ge(s_v.h, vprev2)
            for nc2 in range(2):
                for kc in range(24):
                    mi = tt_.matmul(pb[2 * par + nc2][:, 0:512],
                                    ofm[:, kc * TOK + 128 * t:kc * TOK + 128 * (t + 1)],
                                    wdT[:, kc * DM + 512 * nc2:kc * DM + 512 * (nc2 + 1)],
                                    start=(kc == 0), stop=(kc == 23))
            mi.then_inc(s_mm.inc(), 1)

        md = s_mm.v

        @block.vector
        def _(v, t=t, par=par, md=md):
            v.wait_ge(s_mm.h, md)
            v.wait_ge(s_lx2[par].h, s_lx2[par].v)
            v.scalar_tensor_tensor(out=pre2[:, 0:512], in0=pb[2 * par][:, 0:512],
                                   scalar=k.sval[:, 16:17], in1=x1e[:, par * DM:par * DM + 512],
                                   op0=OP.mult, op1=OP.add)
            v.scalar_tensor_tensor(out=pre2[:, 512:1024], in0=pb[2 * par + 1][:, 0:512],
                                   scalar=k.sval[:, 16:17], in1=x1e[:, par * DM + 512:par * DM + 1024],
                                   op0=OP.mult, op1=OP.add).then_inc(s_v.inc(), 1)
            v.wait_ge(s_v.h, s_v.v)
            v.tensor_reduce(out=ln2s[:, 0:1], in_=pre2[:, :], axis=AX, op=OP.add).then_inc(s_v.inc(), 1)

        vpre2 = s_v.v

        @block.scalar
        def _(sc, t=t, vpre2=vpre2):
            sc.wait_ge(s_v.h, vpre2)
            sc.wait_ge(s_sc.h, s_sc.v)
            sc.activation(out=sq2[:, :], in_=pre2[:, :], func=AF.Square,
                          accum_out=ln2s[:, 1:2]).then_inc(s_sc.inc(), 1)

        sq2d = s_sc.v

        @block.vector
        def _(v, t=t, sq2d=sq2d):
            v.wait_ge(s_sc.h, sq2d)
            v.tensor_scalar(out=ln2s[:, 2:3], in0=ln2s[:, 0:1], scalar1=1.0 / DM,
                            scalar2=None, op0=OP.mult)
            v.tensor_scalar(out=ln2s[:, 3:4], in0=ln2s[:, 1:2], scalar1=1.0 / DM,
                            scalar2=None, op0=OP.mult).then_inc(s_v.inc(), 1)
            v.wait_ge(s_v.h, s_v.v)
            v.tensor_tensor(out=ln2s[:, 4:5], in0=ln2s[:, 2:3], in1=ln2s[:, 2:3],
                            op=OP.mult).then_inc(s_v.inc(), 1)
            v.wait_ge(s_v.h, s_v.v)
            v.tensor_tensor(out=ln2s[:, 5:6], in0=ln2s[:, 3:4], in1=ln2s[:, 4:5],
                            op=OP.subtract).then_inc(s_v.inc(), 1)

        vvar2 = s_v.v

        @block.scalar
        def _(sc, t=t, vvar2=vvar2):
            sc.wait_ge(s_v.h, vvar2)
            sc.activation(out=ln2s[:, 6:7], in_=ln2s[:, 5:6], func=AF.Ln,
                          bias=k.fconst[:, 1:2]).then_inc(s_sc.inc(), 1)
            sc.wait_ge(s_sc.h, s_sc.v)
            sc.activation(out=ln2s[:, 7:8], in_=ln2s[:, 6:7], func=AF.Exp,
                          scale=-0.5).then_inc(s_sc.inc(), 1)

        srs2 = s_sc.v

        @block.vector
        def _(v, t=t, par=par, srs2=srs2):
            v.wait_ge(s_sc.h, srs2)
            v.tensor_tensor(out=ln2s[:, 8:9], in0=ln2s[:, 2:3], in1=ln2s[:, 7:8],
                            op=OP.mult).then_inc(s_v.inc(), 1)
            v.wait_ge(s_v.h, s_v.v)
            v.tensor_scalar(out=ln2s[:, 9:10], in0=ln2s[:, 8:9], scalar1=-1.0,
                            scalar2=None, op0=OP.mult).then_inc(s_v.inc(), 1)
            v.wait_ge(s_v.h, s_v.v)
            if t >= 2:
                v.wait_ge(k.s_st2[par].h, 16 * (t // 2))
            v.tensor_scalar(out=outb[:, par * DM:(par + 1) * DM], in0=pre2[:, :],
                            scalar1=ln2s[:, 7:8], scalar2=ln2s[:, 9:10],
                            op0=OP.mult, op1=OP.add).then_inc(s_v.inc(), 1)
        e2_v[t] = s_v.v

        @block.sync
        def _(sync, t=t, par=par):
            sync.wait_ge(s_v.h, e2_v[t])
            sync.dma_start(out=k.d["out"][128 * t:128 * (t + 1), :],
                           in_=outb[:, par * DM:(par + 1) * DM]).then_inc(k.s_st2[par].inc(16), 16)

    k.scope5 = scope5


def emit_dbg_ffn(k):
    block = k.block
    dsv = k.dbg_tensor("sval", [128, 48])
    dgr = k.dbg_tensor("gateraw", [TOK, FI])
    doi = k.dbg_tensor("oint", [TOK, FI])
    dwq = k.dbg_tensor("wdq", [8 * 1152, DM])
    if dsv is not None:
        @block.gpsimd
        def _(g):
            g.wait_ge(k.s_st2[0].h, k.s_st2[0].v)
            g.wait_ge(k.s_st2[1].h, k.s_st2[1].v)
            g.dma_start(out=dsv, in_=k.sval[:, :]).then_inc(k.s_sp.inc(16), 16)
            if dgr is not None:
                g.dma_start(out=dgr, in_=k.dr["sp_gate"][:, :]).then_inc(k.s_sp.inc(16), 16)
            if doi is not None:
                g.dma_start(out=doi, in_=k.dr["sp_up"][:, :]).then_inc(k.s_sp.inc(16), 16)
            if dwq is not None:
                g.dma_start(out=dwq, in_=k.dr["cc_wq_out"]).then_inc(k.s_sp.inc(16), 16)
            g.wait_ge(k.s_sp.h, k.s_sp.v)


def emit_finish(k):
    block = k.block

    @block.sync
    def _(sync):
        sync.wait_ge(k.s_st2[0].h, k.s_st2[0].v)
        sync.wait_ge(k.s_st2[1].h, k.s_st2[1].v)

    @block.gpsimd
    def _(g):
        g.wait_ge(k.s_st2[0].h, k.s_st2[0].v)
        g.wait_ge(k.s_st2[1].h, k.s_st2[1].v)

    k.scope5.close()



def emit_finish_dbg(k):
    block = k.block
    for scope in ("scopeF", "scope2", "scopeD", "scopeA", "scope_xs", "scope_zs"):
        s = getattr(k, scope, None)
        if s is not None:
            try:
                s.close()
            except Exception:
                pass

    @block.gpsimd
    def _(g):
        if k.s_sp.v:
            g.wait_ge(k.s_sp.h, k.s_sp.v)
        for s in getattr(k, "s_so", []) or []:
            if s.v:
                g.wait_ge(s.h, s.v)


def shard_inputs(inputs, core):
    b, half = core // 2, core % 2
    x = np.asarray(inputs["L0_attn_x"])
    t0 = half * TOK
    xp = np.zeros((TPAD, DM), np.float32)
    lo, hi = t0 - 1, t0 + TOK + 2
    slo, shi = max(lo, 0), min(hi, LSEQ)
    xp[slo - lo:shi - lo, :] = x[b, slo:shi, :]
    convw = np.concatenate([np.asarray(inputs["L0_attn_conv_x_w"])[:, 0, :],
                            np.asarray(inputs["L0_attn_conv_z_w"])[:, 0, :]], axis=0)
    convb = np.concatenate([np.asarray(inputs["L0_attn_conv_x_b"]),
                            np.asarray(inputs["L0_attn_conv_z_b"])], axis=0)[:, None]
    wgkb = np.concatenate([np.asarray(inputs["L0_attn_W_gk"]),
                           np.asarray(inputs["L0_attn_W_b"])], axis=1)
    gkb_bias = np.concatenate([np.asarray(inputs["L0_attn_dt_bias"]),
                               np.asarray(inputs["L0_attn_b_b"])])[None, :]
    smask = np.zeros((1, 8), np.float32)
    if half == 1:
        smask[0, core - 1] = 1.0
    m = {
        "x_pad": np.ascontiguousarray(xp),
        "W_in": np.asarray(inputs["L0_attn_W_in"]),
        "convw": np.ascontiguousarray(convw),
        "convb": np.ascontiguousarray(convb),
        "W_q": np.asarray(inputs["L0_attn_W_q"]),
        "W_k": np.asarray(inputs["L0_attn_W_k"]),
        "W_v": np.asarray(inputs["L0_attn_W_v"]),
        "W_g": np.asarray(inputs["L0_attn_W_g"]),
        "W_gkb": np.ascontiguousarray(wgkb),
        "gkb_bias": np.ascontiguousarray(gkb_bias),
        "A_log": np.asarray(inputs["L0_attn_A_log"])[None, :],
        "g_norm_w": np.asarray(inputs["L0_attn_g_norm_w"])[None, :],
        "W_so": np.asarray(inputs["L0_attn_W_so"]),
        "W_out": np.asarray(inputs["L0_attn_W_out"]),
        "w_gate": np.asarray(inputs["L0_ffn_w_gate"]),
        "w_up": np.asarray(inputs["L0_ffn_w_up"]),
        "w_down": np.asarray(inputs["L0_ffn_w_down"]),
        "smask": smask,
        "wg_shard": np.asarray(inputs["L0_ffn_w_gate"])[384 * core:384 * (core + 1), :],
        "wu_shard": np.asarray(inputs["L0_ffn_w_up"])[384 * core:384 * (core + 1), :],
        "wd_shard": np.asarray(inputs["L0_ffn_w_down"])[128 * core:128 * (core + 1), :],
    }
    return {kk: np.ascontiguousarray(v, dtype=np.float32) for kk, v in m.items()}


def kernel(**inputs):
    nc, _ = build_program()
    in_maps = [shard_inputs(inputs, cidx) for cidx in range(8)]
    res = run_bass_kernel_spmd(nc, in_maps, core_ids=list(range(8)))
    outs = [res.results[cidx]["out"] for cidx in range(8)]
    return np.stack([np.concatenate([outs[2 * b], outs[2 * b + 1]], axis=0)
                     for b in range(B)]).astype(np.float32)


if __name__ == "__main__":
    nc, _ = build_program(upto="A", dbg=("xs", "zs"))
    print("built OK")
